# version 03
# baseline (speedup 1.0000x reference)
import numpy as np
import ml_dtypes
from contextlib import ExitStack

import concourse.bass as bass
import concourse.mybir as mybir
from concourse.bass_utils import run_bass_kernel_spmd

F32 = mybir.dt.float32
BF16 = mybir.dt.bfloat16
AF = mybir.ActivationFunctionType
ALU = mybir.AluOpType
AX = mybir.AxisListType

ENGINES = ("sync", "tensor", "vector", "scalar", "gpsimd")
DMA_ENG = ("sync", "gpsimd")
NLANE = 8

NB = 4
T = 2048
D = 1024
NTILES = NB * T // 128
NSUP = NB * T // 512
MAGIC = 12582912.0


class Buf:
    __slots__ = ("writer", "readers", "name")

    def __init__(self, name=""):
        self.writer = None
        self.readers = []
        self.name = name


class Prog:
    def __init__(self, nc):
        self.nc = nc
        self.steps = []
        self.eng_steps = {e: [] for e in ENGINES}
        self.nbar = 0
        self._inc_count = {}

    def op(self, eng, fn, reads=(), writes=(), deps=(), inc=True):
        d = list(deps)
        for b in reads:
            if b.writer is not None:
                d.append(b.writer)
        for b in writes:
            d.extend(b.readers)
            if b.writer is not None:
                d.append(b.writer)
        seq = len(self.steps)
        st = {"eng": eng, "fn": fn, "deps": d, "inc": inc, "seq": seq, "bar": None}
        if inc:
            self._inc_count[eng] = self._inc_count.get(eng, 0) + 1
        self.steps.append(st)
        self.eng_steps[eng].append(st)
        tok = (eng, seq)
        for b in reads:
            b.readers.append(tok)
        for b in writes:
            b.writer = tok
            b.readers = []
        return tok

    def barrier(self):
        snap = {}
        for e in ENGINES:
            if e in DMA_ENG:
                n = self._inc_count.get(e, 0)
                snap[e] = [(n + NLANE - 1 - r) // NLANE for r in range(NLANE)]
            else:
                snap[e] = self._inc_count.get(e, 0)
        for e in ENGINES:
            st = {"eng": e, "fn": None, "deps": (), "inc": False,
                  "seq": len(self.steps), "bar": ("waits", snap)}
            self.steps.append(st)
            self.eng_steps[e].append(st)

    def _resolve(self):
        self.cover = {}
        self.cover_seq = {}
        self.total_incs = {}
        for e in ENGINES:
            arr = self.eng_steps[e]
            if e in DMA_ENG:
                lane_cnt = [0] * NLANE
                for st in arr:
                    if st["bar"] is not None:
                        continue
                    assert st["inc"], "DMA steps must inc"
                    r = (sum(lane_cnt)) % NLANE
                    lane_cnt[r] += 1
                    st["lane"] = r
                    self.cover[(e, st["seq"])] = (r, lane_cnt[r] * 16)
                    self.cover_seq[(e, st["seq"])] = st["seq"]
                self.total_incs[e] = list(lane_cnt)
            else:
                cum = 0
                cums = []
                for st in arr:
                    if st["inc"]:
                        cum += 1
                    cums.append(cum)
                self.total_incs[e] = cum
                nxt = [None] * len(arr)
                nxtseq = [None] * len(arr)
                cur = None
                curseq = None
                for i in range(len(arr) - 1, -1, -1):
                    if arr[i]["inc"]:
                        cur = cums[i]
                        curseq = arr[i]["seq"]
                    nxt[i] = cur
                    nxtseq[i] = curseq
                for i, st in enumerate(arr):
                    if nxt[i] is not None:
                        self.cover[(e, st["seq"])] = (0, nxt[i])
                        self.cover_seq[(e, st["seq"])] = nxtseq[i]

    def emit(self, blk, sems, bar_sem):
        self._resolve()
        for ename in ENGINES:
            def mk(ename):
                def run(eng):
                    waited = {}
                    for st in self.eng_steps[ename]:
                        if st["bar"] is not None:
                            kind, val = st["bar"]
                            assert kind == "waits"
                            for e2 in ENGINES:
                                if e2 in DMA_ENG:
                                    for r in range(NLANE):
                                        v2 = val[e2][r] * 16
                                        if v2 > waited.get((e2, r), 0):
                                            eng.wait_ge(sems[(e2, r)], v2)
                                            waited[(e2, r)] = v2
                                else:
                                    v2 = val[e2]
                                    if v2 > waited.get((e2, 0), 0):
                                        eng.wait_ge(sems[(e2, 0)], v2)
                                        waited[(e2, 0)] = v2
                            continue
                        need = {}
                        for (de, dseq) in st["deps"]:
                            if de == ename and ename == "tensor":
                                continue
                            cv = self.cover.get((de, dseq))
                            assert cv is not None, \
                                f"dep on ({de},{dseq}) has no covering inc"
                            lane, v = cv
                            if de == ename and de not in DMA_ENG:
                                cseq = self.cover_seq[(de, dseq)]
                                assert cseq < st["seq"], \
                                    f"self-dep covering inc not before step"
                            if v > need.get((de, lane), -1):
                                need[(de, lane)] = v
                        for key in sorted(need):
                            v = need[key]
                            if v > waited.get(key, 0):
                                eng.wait_ge(sems[key], v)
                                waited[key] = v
                        ins = st["fn"](eng)
                        if st["inc"]:
                            if ename in DMA_ENG:
                                ins.then_inc(sems[(ename, st["lane"])], 16)
                            else:
                                ins.then_inc(sems[(ename, 0)], 1)
                    if ename == "sync":
                        for r in range(NLANE):
                            eng.wait_ge(sems[("sync", r)],
                                        self.total_incs["sync"][r] * 16)
                return run
            getattr(blk, ename)(mk(ename))



_NAME_N = [0]


def _un(s):
    _NAME_N[0] += 1
    return f"{s}_{_NAME_N[0]}"



class Ctx:
    pass


def ln_tile(C, src_ap, out_ap, eps_ap, r_src, w_out, sq_ap, b_sq, stat_ap, b_stat,
            extra_out=None):
    P = C.P
    n = src_ap.shape[-1]
    sums = stat_ap[:, 0:1]
    sumsq = stat_ap[:, 1:2]
    mu = stat_ap[:, 2:3]
    var = stat_ap[:, 3:4]
    rstd = stat_ap[:, 4:5]
    nmr = stat_ap[:, 5:6]
    P.op("vector", lambda e, a=src_ap, o=sums: e.reduce_sum(o, a, AX.X),
         reads=[r_src], writes=[b_stat])
    P.op("scalar", lambda e, a=src_ap, o=sq_ap, ac=sumsq:
         e.activation(o, a, AF.Square, accum_out=ac),
         reads=[r_src], writes=[b_sq, b_stat])
    P.op("vector", lambda e, o=mu, a=sums: e.tensor_single_scalar(o, a, 1.0 / n, ALU.mult),
         reads=[b_stat], writes=[b_stat])
    P.op("vector", lambda e, o=var, a=mu: e.tensor_tensor(o, a, a, ALU.mult),
         reads=[b_stat], writes=[b_stat])
    P.op("vector", lambda e, o=var, a=sumsq, m=var:
         e.scalar_tensor_tensor(o, a, 1.0 / n, m, ALU.mult, ALU.subtract),
         reads=[b_stat], writes=[b_stat])
    P.op("scalar", lambda e, o=rstd, a=var, b=eps_ap: e.activation(o, a, AF.Ln, bias=b),
         reads=[b_stat, C.b_const], writes=[b_stat])
    P.op("scalar", lambda e, o=rstd, a=rstd: e.activation(o, a, AF.Exp, scale=-0.5),
         reads=[b_stat], writes=[b_stat])
    P.op("vector", lambda e, o=nmr, a=mu, r=rstd:
         e.scalar_tensor_tensor(o, a, -1.0, r, ALU.mult, ALU.mult),
         reads=[b_stat], writes=[b_stat])
    P.op("scalar", lambda e, o=out_ap, a=src_ap, r=rstd, b=nmr:
         e.activation(o, a, AF.Identity, bias=b, scale=r),
         reads=[r_src, b_stat], writes=[w_out])
    return rstd, nmr


def transpose128(C, src_ap, dst_ap, r_src, w_dst, dtype, copy_eng="vector"):
    P = C.P
    ncols = src_ap.shape[-1]
    if dtype == BF16:
        bank, bbuf = C.next_pbf()
        ident = C.idb_sb
        pslice = bank[0:ncols, 0:128]
    else:
        bank, bbuf = C.next_pf()
        ident = C.idf_sb
        pslice = bank[0:ncols, 0:128]
    P.op("tensor", lambda e, o=pslice, i=src_ap, idt=ident[:]: e.transpose(o, i, idt),
         reads=[r_src, C.b_const], writes=[bbuf])
    P.op(copy_eng, lambda e, o=dst_ap, i=pslice:
         (e.tensor_copy(o, i) if copy_eng == "vector" else e.activation(o, i, AF.Identity)),
         reads=[bbuf], writes=[w_dst])


def emit_l1ffn(C, src, b_src, dst, b_dst):
    nc, P = C.nc, C.P

    def quant_chain(u_ap, out_ap, r_u, w_out, s1, s2, s3, s4, bs, bufs):
        b1, b2, b3, b4, bb = bufs
        bmax = bs[:, 0:32]
        scl = bs[:, 32:64]
        rscl = bs[:, 64:96]
        u3d = u_ap.rearrange("p (a b) -> p a b", a=32)
        P.op("vector", lambda e: e.reduce_max(bmax, u3d, AX.X, apply_absolute_value=True),
             reads=[r_u], writes=[bb])
        P.op("vector", lambda e: e.tensor_scalar(scl, bmax, 1.0 / 7.5, 1e-30, ALU.mult, ALU.max),
             reads=[bb], writes=[bb])
        P.op("vector", lambda e: e.reciprocal(rscl, scl),
             reads=[bb], writes=[bb])
        rb = rscl.rearrange("p (a o) -> p a o", a=32).broadcast_to([128, 32, 32])
        sb = scl.rearrange("p (a o) -> p a o", a=32).broadcast_to([128, 32, 32])
        s1d = s1.rearrange("p (a b) -> p a b", a=32)
        P.op("vector", lambda e: e.tensor_tensor(s1d, u3d, rb, ALU.mult),
             reads=[r_u, bb], writes=[b1])
        P.op("scalar", lambda e: e.activation(s2, s1, AF.Abs),
             reads=[b1], writes=[b2])
        P.op("vector", lambda e: e.tensor_single_scalar(s3, s2, 2.0, ALU.is_ge),
             reads=[b2], writes=[b3])
        P.op("vector", lambda e: e.tensor_single_scalar(s2, s2, 4.0, ALU.is_ge),
             reads=[b2], writes=[b2])
        P.op("vector", lambda e: e.tensor_scalar(s4, s3, -4.0, 8.0, ALU.mult, ALU.add),
             reads=[b3], writes=[b4])
        P.op("vector", lambda e: e.scalar_tensor_tensor(s4, s2, -2.0, s4, ALU.mult, ALU.add),
             reads=[b2, b4], writes=[b4])
        P.op("vector", lambda e: e.tensor_scalar(s3, s3, 0.125, 0.125, ALU.mult, ALU.add),
             reads=[b3], writes=[b3])
        P.op("vector", lambda e: e.scalar_tensor_tensor(s3, s2, 0.25, s3, ALU.mult, ALU.add),
             reads=[b2, b3], writes=[b3])
        P.op("vector", lambda e: e.tensor_tensor(s2, s1, s4, ALU.mult),
             reads=[b1, b4], writes=[b2])
        P.op("vector", lambda e: e.tensor_scalar(s2, s2, MAGIC, MAGIC, ALU.add, ALU.subtract),
             reads=[b2], writes=[b2])
        P.op("vector", lambda e: e.tensor_tensor(s1, s2, s3, ALU.mult),
             reads=[b2, b3], writes=[b1])
        od = out_ap.rearrange("p (a b) -> p a b", a=32)
        P.op("vector", lambda e: e.tensor_tensor(od, s1d, sb, ALU.mult),
             reads=[b1, bb], writes=[w_out])

    with ExitStack() as ctx:
        E = ctx.enter_context
        wf = E(nc.sbuf_tensor("l1f_wf", [128, 8, 1024], F32))
        wdq = E(nc.sbuf_tensor("l1f_wdq", [128, 8, 1024], BF16))
        wdT = E(nc.sbuf_tensor("l1f_wdT", [128, 8, 1024], BF16))
        s1 = E(nc.sbuf_tensor("l1f_s1", [128, 1024], F32))
        s2 = E(nc.sbuf_tensor("l1f_s2", [128, 1024], F32))
        s3 = E(nc.sbuf_tensor("l1f_s3", [128, 1024], F32))
        s4 = E(nc.sbuf_tensor("l1f_s4", [128, 1024], F32))
        bs = E(nc.sbuf_tensor("l1f_bs", [128, 96], F32))
        b_wf, b_wdq, b_wdT = Buf("wf"), Buf("wdq"), Buf("wdT")
        b1, b2, b3, b4, bb = Buf(), Buf(), Buf(), Buf(), Buf()

        P.op("sync", lambda e: e.dma_start(
            out=wf[:], in_=C.dram["l1f_w"].rearrange("(c p) k -> p c k", p=128)),
            writes=[b_wf])
        for jc in range(8):
            quant_chain(wf[:, jc, :], wdq[:, jc, :], b_wf, b_wdq,
                        s1[:], s2[:], s3[:], s4[:], bs[:], (b1, b2, b3, b4, bb))
        for jc in range(8):
            for kc in range(8):
                transpose128(C, wdq[:, jc, kc * 128:(kc + 1) * 128],
                             wdT[:, kc, jc * 128:(jc + 1) * 128],
                             b_wdq, b_wdT, BF16)
        P.barrier()

        xt = E(nc.sbuf_tensor("l1f_xt", [128, 4, 1024], F32))
        u5 = E(nc.sbuf_tensor("l1f_u5", [128, 1024], F32))
        xdtm = E(nc.sbuf_tensor("l1f_xdtm", [128, 1024], BF16))
        xdfm = E(nc.sbuf_tensor("l1f_xdfm", [128, 8, 512], BF16))
        outst = E(nc.sbuf_tensor("l1f_outst", [128, 512], F32))
        otm = E(nc.sbuf_tensor("l1f_otm", [128, 4, 1024], F32))
        zt = E(nc.sbuf_tensor("l1f_zt", [128, 1024], F32))
        xo = E(nc.sbuf_tensor("l1f_xo", [128, 1024], F32))
        stat = E(nc.sbuf_tensor("l1f_stat", [128, 8], F32))
        b_xt = [Buf() for _ in range(4)]
        b_u5, b_xdtm, b_xdfm = Buf(), Buf(), Buf()
        b_outst, b_otm, b_zt, b_xo, b_stat = Buf(), Buf(), Buf(), Buf(), Buf()

        for s in range(NSUP):
            for i in range(4):
                tile = s * 4 + i
                P.op("sync", lambda e, t=tile, i=i: e.dma_start(
                    out=xt[:, i, :], in_=src[t * 128:(t + 1) * 128, :]),
                    reads=[b_src[tile]], writes=[b_xt[i]])
                ln_tile(C, xt[:, i, :], u5[:], C.epsc[:, 0:1], b_xt[i], b_u5,
                        s1[:], b1, stat[:], b_stat)
                quant_chain(u5[:], xdtm[:], b_u5, b_xdtm,
                            s1[:], s2[:], s3[:], s4[:], bs[:], (b1, b2, b3, b4, bb))
                for c in range(8):
                    transpose128(C, xdtm[:, c * 128:(c + 1) * 128],
                                 xdfm[:, c, i * 128:(i + 1) * 128],
                                 b_xdtm, b_xdfm, BF16)
            for jc in range(8):
                bank, bbk = C.next_pf()
                for kc in range(8):
                    P.op("tensor", lambda e, bk=bank, kc=kc, jc=jc: e.matmul(
                        bk[:], wdT[:, kc, jc * 128:(jc + 1) * 128], xdfm[:, kc, :],
                        start=(kc == 0), stop=(kc == 7)),
                        reads=[b_wdT, b_xdfm], writes=[bbk], inc=(kc == 7))
                P.op("scalar", lambda e, bk=bank, jc=jc: e.activation(
                    outst[:], bk[:], AF.Identity, bias=C.bias_sb["l1f_b"][:, jc:jc + 1]),
                    reads=[bbk, C.b_const], writes=[b_outst])
                for i in range(4):
                    transpose128(C, outst[:, i * 128:(i + 1) * 128],
                                 otm[:, i, jc * 128:(jc + 1) * 128],
                                 b_outst, b_otm, F32)
            for i in range(4):
                tile = s * 4 + i
                rstd, nmr = ln_tile(C, otm[:, i, :], zt[:], C.epsc[:, 0:1], b_otm, b_zt,
                                    s1[:], b1, stat[:], b_stat)
                P.op("vector", lambda e, i=i: e.tensor_tensor(
                    xo[:], zt[:], xt[:, i, :], ALU.add),
                    reads=[b_zt, b_xt[i]], writes=[b_xo])
                P.op("sync", lambda e, t=tile: e.dma_start(
                    out=dst[t * 128:(t + 1) * 128, :], in_=xo[:]),
                    reads=[b_xo], writes=[b_dst[tile]])
        P.barrier()


def attn_phaseB(C, qs, nst, colstart_fn, mask_diag, qfm, b_q, kfm, b_k, vtm, b_v,
                negc, b_negc, attnst, b_att, Pb, b_Pb, denb, b_den, rb, b_rb):
    P = C.P
    pf, bpf = C.pf, C.b_pf
    for hp in range(8):
        for st in range(nst):
            cs = colstart_fn(st)
            pspan = slice(cs, 512)
            span = slice(qs * 512 + cs, qs * 512 + 512)
            first = st == 0
            last = st == nst - 1
            for h in range(2):
                base = h * 64
                sbank, b_sbank = pf[h], bpf[h]
                P.op("tensor", lambda e, sb=sbank, hp=hp, st=st, base=base, sp=span, ps=pspan:
                     e.matmul(sb[:, ps], kfm[base:base + 64, hp, st * 128:(st + 1) * 128],
                              qfm[base:base + 64, hp, sp], start=True, stop=True,
                              skip_group_check=True),
                     reads=[b_q, b_k], writes=[b_sbank])
                P.op("scalar", lambda e, sb=sbank, hp=hp, h=h, ps=pspan:
                     e.activation(Pb[:, h, ps], sb[:, ps], AF.Exp,
                                  bias=negc[:, 2 * hp + h:2 * hp + h + 1], scale=0.125),
                     reads=[b_sbank, b_negc], writes=[b_Pb])
                if mask_diag and st >= 4 * qs:
                    P.op("vector", lambda e, h=h, cs=cs:
                         e.tensor_tensor(Pb[:, h, cs:cs + 128], Pb[:, h, cs:cs + 128],
                                         C.tri_sb[:], ALU.mult),
                         reads=[b_Pb, C.b_const], writes=[b_Pb])
                P.op("tensor", lambda e, hp=hp, st=st, h=h, base=base, ps=pspan, first=first, last=last:
                     e.matmul(pf[4][base:base + 64, ps],
                              vtm[:, st, hp, base:base + 64], Pb[:, h, ps],
                              start=first, stop=last,
                              tile_position=(0, base), skip_group_check=True),
                     reads=[b_v, b_Pb], writes=[C.b_pf[4]], inc=(last and h == 1))
                dpart = 32 * h
                P.op("tensor", lambda e, h=h, ps=pspan, dp=dpart, first=first, last=last:
                     e.matmul(pf[5][dp:dp + 1, ps], C.onescb_sb[:], Pb[:, h, ps],
                              start=first, stop=last,
                              tile_position=(0, dp), skip_group_check=True),
                     reads=[C.b_const, b_Pb], writes=[C.b_pf[5]], inc=(last and h == 1))
        if getattr(C, "want_taps", None) and getattr(C, "tap_stage2", None) \
                and hp == 0 and qs == 0 and getattr(C, "_tapped_pb", False) is False:
            C._tapped_pb = True
            dencp, bdc = C.tap_stage2
            P.op("vector", lambda e: e.tensor_copy(dencp[0:1, 0:512], pf[5][0:1, 0:512]),
                 reads=[C.b_pf[5]], writes=[bdc])
            P.op("vector", lambda e: e.tensor_copy(dencp[32:33, 0:512], pf[5][32:33, 0:512]),
                 reads=[C.b_pf[5]], writes=[bdc])
            tap(C, "den_raw", dencp[0:64, 0:512], bdc)
            avcp, bac = C.tap_stage
            P.op("vector", lambda e: e.tensor_copy(avcp[:, 0:512], pf[4][:, 0:512]),
                 reads=[C.b_pf[4]], writes=[bac])
            tap(C, "av_raw", avcp[:, 0:512], bac)
        for h in range(2):
            dp = 32 * h
            base = h * 64
            P.op("vector", lambda e, dp=dp: e.reciprocal(denb[dp:dp + 1, :], pf[5][dp:dp + 1, 0:512]),
                 reads=[C.b_pf[5]], writes=[b_den])
            bb, b_bb = pf[h], bpf[h]
            P.op("tensor", lambda e, dp=dp, base=base, bb=bb:
                 e.matmul(bb[base:base + 64, 0:512], C.onesf_sb[dp:dp + 1, 0:64],
                          denb[dp:dp + 1, :], start=True, stop=True,
                          tile_position=(dp, base), skip_group_check=True),
                 reads=[b_den, C.b_const], writes=[b_bb])
            P.op("vector", lambda e, base=base, bb=bb: e.tensor_copy(
                rb[base:base + 64, :], bb[base:base + 64, 0:512]),
                 reads=[b_bb], writes=[b_rb])
            P.op("vector", lambda e, base=base, hp=hp: e.tensor_tensor(
                attnst[base:base + 64, hp, :], pf[4][base:base + 64, 0:512],
                rb[base:base + 64, :], ALU.mult),
                 reads=[C.b_pf[4], b_rb], writes=[b_att])


def emit_l0attn(C, src, b_src, dst, b_dst):
    nc, P = C.nc, C.P
    with ExitStack() as bs:
        E = bs.enter_context
        wat = E(nc.sbuf_tensor("l0_watt", [128, 8, 3072], BF16))
        wpr = E(nc.sbuf_tensor("l0_wproj", [128, 8, 1024], BF16))
        qfm = E(nc.sbuf_tensor("l0_qfm", [128, 8, 2048], BF16))
        kfm = E(nc.sbuf_tensor("l0_kfm", [128, 8, 2048], BF16))
        vtm = E(nc.sbuf_tensor("l0_vtm", [128, 16, 8, 128], BF16))
        negc = E(nc.sbuf_tensor("l0_negc", [128, 16], F32))
        nrm = E(nc.sbuf_tensor("l0_nrm", [128, 32], F32))
        ncs = E(nc.sbuf_tensor("l0_ncs", [16, 8], F32))
        ncrow = E(nc.sbuf_tensor("l0_ncrow", [1, 16], F32))
        b_wat, b_wpr, b_q, b_k, b_v = Buf(), Buf(), Buf(), Buf(), Buf()
        b_negc, b_nrm, b_ncs, b_ncrow = Buf(), Buf(), Buf(), Buf()
        P.op("gpsimd", lambda e: e.dma_start(
            out=wat[:], in_=C.dram["w_attnT"].rearrange("(c p) o -> p c o", p=128)),
            writes=[b_wat])
        P.op("gpsimd", lambda e: e.dma_start(
            out=wpr[:], in_=C.dram["w_projT"].rearrange("(c p) o -> p c o", p=128)),
            writes=[b_wpr])
        bat = C.bias_sb["b_attn_fm"]
        bpj = C.bias_sb["b_proj_fm"]

        for b in range(NB):
            with ExitStack() as sA:
                EA = sA.enter_context
                xt = EA(nc.sbuf_tensor(_un("l0a_xt"), [128, 1024], F32))
                u0t = EA(nc.sbuf_tensor(_un("l0a_u0t"), [128, 1024], F32))
                ttm = EA(nc.sbuf_tensor(_un("l0a_ttm"), [128, 1024], BF16))
                tfm = EA(nc.sbuf_tensor(_un("l0a_tfm"), [128, 8, 512], BF16))
                vst = EA(nc.sbuf_tensor(_un("l0a_vst"), [128, 8, 512], BF16))
                sq = EA(nc.sbuf_tensor(_un("l0a_sq"), [128, 1024], F32))
                stat = EA(nc.sbuf_tensor(_un("l0a_stat"), [128, 8], F32))
                b_xt, b_u0t, b_ttm, b_tfm = Buf(), Buf(), Buf(), Buf()
                b_vst, b_sq, b_stat = Buf(), Buf(), Buf()
                P.op("vector", lambda e: e.memset(nrm[:], 0.0), writes=[b_nrm])
                for j in range(4):
                    for i in range(4):
                        tile = b * 16 + j * 4 + i
                        P.op("sync", lambda e, t=tile, xt=xt: e.dma_start(
                            out=xt[:], in_=src[t * 128:(t + 1) * 128, :]),
                            reads=[b_src[tile]], writes=[b_xt])
                        ln_tile(C, xt[:], u0t[:], C.epsc[:, 0:1], b_xt, b_u0t,
                                sq[:], b_sq, stat[:], b_stat)
                        ln_tile(C, u0t[:], ttm[:], C.epsc[:, 1:2], b_u0t, b_ttm,
                                sq[:], b_sq, stat[:], b_stat)
                        for c in range(8):
                            transpose128(C, ttm[:, c * 128:(c + 1) * 128],
                                         tfm[:, c, i * 128:(i + 1) * 128],
                                         b_ttm, b_tfm, BF16)
                        if getattr(C, "paranoia", 0) >= 2:
                            P.barrier()
                    span = slice(j * 512, (j + 1) * 512)
                    for oc in range(24):
                        bank, bbk = C.pf[oc % 3], C.b_pf[oc % 3]
                        for kc in range(8):
                            P.op("tensor", lambda e, bank=bank, kc=kc, oc=oc, tfm=tfm:
                                 e.matmul(bank[:], wat[:, kc, oc * 128:(oc + 1) * 128],
                                          tfm[:, kc, :], start=(kc == 0), stop=(kc == 7)),
                                 reads=[b_wat, b_tfm], writes=[bbk], inc=(kc == 7))
                        if oc < 8:
                            dstap, bd = qfm[:, oc, span], b_q
                        elif oc < 16:
                            dstap, bd = kfm[:, oc - 8, span], b_k
                        else:
                            dstap, bd = vst[:, oc - 16, :], b_vst
                        P.op("scalar", lambda e, bank=bank, oc=oc, dstap=dstap:
                             e.activation(dstap, bank[:], AF.Identity,
                                          bias=bat[:, oc:oc + 1]),
                             reads=[bbk, C.b_const], writes=[bd])
                        if getattr(C, "paranoia", 0) >= 2:
                            P.barrier()
                    for c in range(8):
                        for i in range(4):
                            st_idx = j * 4 + i
                            transpose128(C, vst[:, c, i * 128:(i + 1) * 128],
                                         vtm[:, st_idx, c, :],
                                         b_vst, b_v, BF16)
                    if getattr(C, "paranoia", 0) >= 1:
                        P.barrier()
                    for (tn, off, bsrc2) in ((qfm, 0, b_q), (kfm, 16, b_k)):
                        for c in range(8):
                            P.op("scalar", lambda e, tn=tn, c=c, span=span, sq=sq:
                                 e.activation(sq[:, 0:512], tn[:, c, span], AF.Square),
                                 reads=[bsrc2], writes=[b_sq])
                            for i in range(4):
                                P.op("tensor", lambda e, c=c, i=i, sq=sq: e.matmul(
                                    C.pf[3][:, i * 16 + 2 * c:i * 16 + 2 * c + 2],
                                    sq[:, i * 128:(i + 1) * 128], C.eh_sb[:],
                                    start=True, stop=True, skip_group_check=True),
                                    reads=[b_sq, C.b_const], writes=[C.b_pf[3]])
                        for i in range(4):
                            P.op("vector", lambda e, off=off, i=i: e.tensor_tensor(
                                nrm[:, off:off + 16], nrm[:, off:off + 16],
                                C.pf[3][:, i * 16:(i + 1) * 16], ALU.max),
                                reads=[b_nrm, C.b_pf[3]], writes=[b_nrm])
                P.op("tensor", lambda e: e.transpose(C.pf[2][0:16, 0:128],
                                                     nrm[:, 0:16], C.idf_sb[:]),
                     reads=[b_nrm, C.b_const], writes=[C.b_pf[2]])
                P.op("vector", lambda e: e.reduce_max(ncs[:, 0:1], C.pf[2][0:16, 0:128], AX.X),
                     reads=[C.b_pf[2]], writes=[b_ncs])
                P.op("tensor", lambda e: e.transpose(C.pf[2][0:16, 128:256],
                                                     nrm[:, 16:32], C.idf_sb[:]),
                     reads=[b_nrm, C.b_const], writes=[C.b_pf[2]])
                P.op("vector", lambda e: e.reduce_max(ncs[:, 1:2], C.pf[2][0:16, 128:256], AX.X),
                     reads=[C.b_pf[2]], writes=[b_ncs])
                P.op("scalar", lambda e: e.activation(ncs[:, 2:3], ncs[:, 0:1], AF.Sqrt),
                     reads=[b_ncs], writes=[b_ncs])
                P.op("scalar", lambda e: e.activation(ncs[:, 3:4], ncs[:, 1:2], AF.Sqrt),
                     reads=[b_ncs], writes=[b_ncs])
                P.op("vector", lambda e: e.tensor_tensor(ncs[:, 4:5], ncs[:, 2:3],
                                                         ncs[:, 3:4], ALU.mult),
                     reads=[b_ncs], writes=[b_ncs])
                P.op("vector", lambda e: e.tensor_single_scalar(ncs[:, 5:6], ncs[:, 4:5],
                                                                -0.125, ALU.mult),
                     reads=[b_ncs], writes=[b_ncs])
                P.op("tensor", lambda e: e.transpose(C.pf[2][0:1, 256:272],
                                                     ncs[:, 5:6], C.idf_sb[0:16, 0:16]),
                     reads=[b_ncs, C.b_const], writes=[C.b_pf[2]])
                P.op("vector", lambda e: e.tensor_copy(ncrow[:], C.pf[2][0:1, 256:272]),
                     reads=[C.b_pf[2]], writes=[b_ncrow])
                P.op("tensor", lambda e: e.matmul(C.pf[2][:, 272:288],
                                                  C.onesf_sb[0:1, 0:128], ncrow[:],
                                                  start=True, stop=True,
                                                  skip_group_check=True),
                     reads=[b_ncrow, C.b_const], writes=[C.b_pf[2]])
                P.op("vector", lambda e: e.tensor_copy(negc[:], C.pf[2][:, 272:288]),
                     reads=[C.b_pf[2]], writes=[b_negc])
                if b == 0:
                    tap(C, "l0_qfm", qfm[:], b_q)
                    tap(C, "l0_kfm", kfm[:], b_k)
                    tap(C, "l0_vtm", vtm[:], b_v)
                    tap(C, "l0_negc", negc[:], b_negc)
            P.barrier()
            with ExitStack() as sB:
                EB = sB.enter_context
                Pb = EB(nc.sbuf_tensor(_un("l0b_pb"), [128, 2, 512], BF16))
                attnst = EB(nc.sbuf_tensor(_un("l0b_attnst"), [128, 8, 512], BF16))
                denb = EB(nc.sbuf_tensor(_un("l0b_denb"), [64, 512], F32))
                rb = EB(nc.sbuf_tensor(_un("l0b_rb"), [128, 512], F32))
                projst = EB(nc.sbuf_tensor(_un("l0b_projst"), [128, 512], F32))
                u1s = EB(nc.sbuf_tensor(_un("l0b_u1s"), [128, 4, 1024], F32))
                xtB = EB(nc.sbuf_tensor(_un("l0b_xt"), [128, 1024], F32))
                sqB = EB(nc.sbuf_tensor(_un("l0b_sq"), [128, 1024], F32))
                statB = EB(nc.sbuf_tensor(_un("l0b_stat"), [128, 8], F32))
                b_Pb, b_att, b_den, b_rb = Buf(), Buf(), Buf(), Buf()
                b_projst, b_u1s, b_xtB, b_sqB, b_statB = Buf(), Buf(), Buf(), Buf(), Buf()
                for qs in range(4):
                    if getattr(C, "paranoia", 0) >= 1:
                        P.barrier()
                    if not getattr(C, "l0_skip_attn", False):
                        attn_phaseB(C, qs, 4 * qs + 4,
                                    lambda st, qs=qs: max(0, st - 4 * qs) * 128,
                                    True, qfm, b_q, kfm, b_k, vtm, b_v,
                                    negc, b_negc, attnst, b_att, Pb, b_Pb,
                                    denb, b_den, rb, b_rb)
                    if getattr(C, "paranoia", 0) >= 1:
                        P.barrier()
                    if b == 0 and qs == 0:
                        tap(C, "l0_attnst0", attnst[:], b_att)
                    for i in range(4):
                        tile = b * 16 + qs * 4 + i
                        P.op("sync", lambda e, t=tile: e.dma_start(
                            out=xtB[:], in_=src[t * 128:(t + 1) * 128, :]),
                            reads=[b_src[tile]], writes=[b_xtB])
                        ln_tile(C, xtB[:], u1s[:, i, :], C.epsc[:, 0:1], b_xtB, b_u1s,
                                sqB[:], b_sqB, statB[:], b_statB)
                    for oc in range(8):
                        bank, bbk = C.pf[2 + (oc % 2)], C.b_pf[2 + (oc % 2)]
                        for kc in range(8):
                            P.op("tensor", lambda e, bank=bank, kc=kc, oc=oc, attnst=attnst:
                                 e.matmul(bank[:], wpr[:, kc, oc * 128:(oc + 1) * 128],
                                          attnst[:, kc, :], start=(kc == 0), stop=(kc == 7)),
                                 reads=[b_wpr, b_att], writes=[bbk], inc=(kc == 7))
                        P.op("scalar", lambda e, bank=bank, oc=oc, projst=projst:
                             e.activation(projst[:], bank[:], AF.Identity,
                                          bias=bpj[:, oc:oc + 1]),
                             reads=[bbk, C.b_const], writes=[b_projst])
                        for i in range(4):
                            tb, btb = C.pf[4 + (i % 2)], C.b_pf[4 + (i % 2)]
                            P.op("tensor", lambda e, tb=tb, i=i, projst=projst: e.transpose(
                                tb[0:128, 0:128], projst[:, i * 128:(i + 1) * 128],
                                C.idf_sb[:]),
                                reads=[b_projst, C.b_const], writes=[btb])
                            P.op("vector", lambda e, tb=tb, i=i, oc=oc, u1s=u1s: e.tensor_tensor(
                                u1s[:, i, oc * 128:(oc + 1) * 128], tb[0:128, 0:128],
                                u1s[:, i, oc * 128:(oc + 1) * 128], ALU.add),
                                reads=[btb, b_u1s], writes=[b_u1s])
                    if b == 0 and qs == 0:
                        tap(C, "l0_u1s0", u1s[:], b_u1s)
                    for i in range(4):
                        tile = b * 16 + qs * 4 + i
                        P.op("sync", lambda e, t=tile, i=i, u1s=u1s: e.dma_start(
                            out=C.u1d[t * 128:(t + 1) * 128, :], in_=u1s[:, i, :]),
                            reads=[b_u1s], writes=[C.b_u1d[tile]])
            P.barrier()
    P.barrier()
    with ExitStack() as sC:
        EC = sC.enter_context
        wfc = EC(nc.sbuf_tensor("l0c_wfc", [128, 8, 4096], BF16))
        wml = EC(nc.sbuf_tensor("l0c_wml", [128, 32, 1024], BF16))
        u1sC = EC(nc.sbuf_tensor("l0c_u1", [128, 4, 1024], F32))
        ttmC = EC(nc.sbuf_tensor("l0c_ttm", [128, 1024], BF16))
        tfmC = EC(nc.sbuf_tensor("l0c_tfm", [128, 8, 512], BF16))
        hfmC = EC(nc.sbuf_tensor("l0c_hfm", [128, 32, 512], BF16))
        mlpstC = EC(nc.sbuf_tensor("l0c_mlpst", [128, 512], F32))
        ztC = EC(nc.sbuf_tensor("l0c_zt", [128, 1024], F32))
        xtC = EC(nc.sbuf_tensor("l0c_xt", [128, 1024], F32))
        xoC = EC(nc.sbuf_tensor("l0c_xo", [128, 1024], F32))
        sqC = EC(nc.sbuf_tensor("l0c_sq", [128, 1024], F32))
        statC = EC(nc.sbuf_tensor("l0c_stat", [128, 8], F32))
        b_wfc, b_wml, b_u1, b_ttm, b_tfm, b_hfm = (Buf() for _ in range(6))
        b_mlpst, b_zt, b_xt, b_xo, b_sq, b_stat = (Buf() for _ in range(6))
        P.op("gpsimd", lambda e: e.dma_start(
            out=wfc[:], in_=C.dram["w_fcT"].rearrange("(c p) o -> p c o", p=128)),
            writes=[b_wfc])
        P.op("gpsimd", lambda e: e.dma_start(
            out=wml[:], in_=C.dram["w_mlpT"].rearrange("(c p) o -> p c o", p=128)),
            writes=[b_wml])
        bfc = C.bias_sb["b_fc_fm"]
        bml = C.bias_sb["b_mlp_fm"]
        for s in range(NSUP):
            if getattr(C, "paranoia", 0) >= 1:
                P.barrier()
            for i in range(4):
                tile = s * 4 + i
                P.op("sync", lambda e, t=tile, i=i: e.dma_start(
                    out=u1sC[:, i, :], in_=C.u1d[t * 128:(t + 1) * 128, :]),
                    reads=[C.b_u1d[tile]], writes=[b_u1])
                ln_tile(C, u1sC[:, i, :], ttmC[:], C.epsc[:, 1:2], b_u1, b_ttm,
                        sqC[:], b_sq, statC[:], b_stat)
                for c in range(8):
                    transpose128(C, ttmC[:, c * 128:(c + 1) * 128],
                                 tfmC[:, c, i * 128:(i + 1) * 128],
                                 b_ttm, b_tfm, BF16)
            for oc in range(32 if not getattr(C, "l0_skip_mlp", False) else 0):
                bank, bbk = C.pf[oc % 3], C.b_pf[oc % 3]
                for kc in range(8):
                    P.op("tensor", lambda e, bank=bank, kc=kc, oc=oc:
                         e.matmul(bank[:], wfc[:, kc, oc * 128:(oc + 1) * 128],
                                  tfmC[:, kc, :], start=(kc == 0), stop=(kc == 7)),
                         reads=[b_wfc, b_tfm], writes=[bbk], inc=(kc == 7))
                P.op("scalar", lambda e, bank=bank, oc=oc:
                     e.activation(hfmC[:, oc, :], bank[:], AF.Gelu_apprx_tanh,
                                  bias=bfc[:, oc:oc + 1]),
                     reads=[bbk, C.b_const], writes=[b_hfm])
            for oc in range(8 if not getattr(C, "l0_skip_mlp", False) else 0):
                bank, bbk = C.pf[3], C.b_pf[3]
                for fc in range(32):
                    P.op("tensor", lambda e, bank=bank, fc=fc, oc=oc:
                         e.matmul(bank[:], wml[:, fc, oc * 128:(oc + 1) * 128],
                                  hfmC[:, fc, :], start=(fc == 0), stop=(fc == 31)),
                         reads=[b_wml, b_hfm], writes=[bbk], inc=(fc == 31))
                P.op("scalar", lambda e, bank=bank, oc=oc:
                     e.activation(mlpstC[:], bank[:], AF.Identity,
                                  bias=bml[:, oc:oc + 1]),
                     reads=[bbk, C.b_const], writes=[b_mlpst])
                for i in range(4):
                    tb, btb = C.pf[4 + (i % 2)], C.b_pf[4 + (i % 2)]
                    P.op("tensor", lambda e, tb=tb, i=i: e.transpose(
                        tb[0:128, 0:128], mlpstC[:, i * 128:(i + 1) * 128], C.idf_sb[:]),
                        reads=[b_mlpst, C.b_const], writes=[btb])
                    P.op("vector", lambda e, tb=tb, i=i, oc=oc: e.tensor_tensor(
                        u1sC[:, i, oc * 128:(oc + 1) * 128], tb[0:128, 0:128],
                        u1sC[:, i, oc * 128:(oc + 1) * 128], ALU.add),
                        reads=[btb, b_u1], writes=[b_u1])
            for i in range(4):
                tile = s * 4 + i
                ln_tile(C, u1sC[:, i, :], ztC[:], C.epsc[:, 0:1], b_u1, b_zt,
                        sqC[:], b_sq, statC[:], b_stat)
                P.op("sync", lambda e, t=tile: e.dma_start(
                    out=xtC[:], in_=src[t * 128:(t + 1) * 128, :]),
                    reads=[b_src[tile]], writes=[b_xt])
                P.op("vector", lambda e: e.tensor_tensor(xoC[:], ztC[:], xtC[:], ALU.add),
                     reads=[b_zt, b_xt], writes=[b_xo])
                P.op("sync", lambda e, t=tile: e.dma_start(
                    out=dst[t * 128:(t + 1) * 128, :], in_=xoC[:]),
                    reads=[b_xo], writes=[b_dst[tile]])
    P.barrier()


def emit_l0ffn(C, src, b_src, dst, b_dst):
    nc, P = C.nc, C.P
    with ExitStack() as bs:
        E = bs.enter_context
        wg = E(nc.sbuf_tensor("f_wg", [128, 8, 8], BF16))
        b2m = E(nc.sbuf_tensor("f_b2m", [8, 1024], BF16))
        gam = E(nc.sbuf_tensor("f_gam", [128, 1024], F32))
        bet = E(nc.sbuf_tensor("f_bet", [128, 1024], F32))
        b_wg, b_b2m, b_gam, b_bet, b_gbrow = (Buf() for _ in range(5))
        P.op("gpsimd", lambda e: e.dma_start(
            out=wg[:], in_=C.dram["wg"].rearrange("(c p) o -> p c o", p=128)),
            writes=[b_wg])
        P.op("gpsimd", lambda e: e.dma_start(out=b2m[:], in_=C.dram["b2_mat"].ap()),
             writes=[b_b2m])
        with nc.sbuf_tensor("f_gbrow", [1, 2, 1024], F32) as gbrow:
            P.op("sync", lambda e: e.dma_start(out=gbrow[0:1, 0, :], in_=C.dram["gamma"].ap()),
                 writes=[b_gbrow])
            P.op("sync", lambda e: e.dma_start(out=gbrow[0:1, 1, :], in_=C.dram["beta"].ap()),
                 writes=[b_gbrow])
            for (gi, tgt, bt) in ((0, gam, b_gam), (1, bet, b_bet)):
                for half in range(2):
                    P.op("tensor", lambda e, gi=gi, half=half: e.matmul(
                        C.pf[4][:, 0:512], C.onesf_sb[0:1, 0:128],
                        gbrow[0:1, gi, half * 512:(half + 1) * 512],
                        start=True, stop=True, skip_group_check=True),
                        reads=[b_gbrow, C.b_const], writes=[C.b_pf[4]])
                    P.op("vector", lambda e, tgt=tgt, half=half: e.tensor_copy(
                        tgt[:, half * 512:(half + 1) * 512], C.pf[4][:, 0:512]),
                        reads=[C.b_pf[4]], writes=[bt])
        P.barrier()

        with ExitStack() as sP:
            EP = sP.enter_context
            xt = EP(nc.sbuf_tensor("f_xt", [128, 1024], F32))
            ttm = EP(nc.sbuf_tensor("f_ttm", [128, 1024], BF16))
            u3st = EP(nc.sbuf_tensor("f_u3st", [128, 8, 512], BF16))
            lgst = EP(nc.sbuf_tensor("f_lgst", [8, 512], F32))
            lgtm = EP(nc.sbuf_tensor("f_lgtm", [128, 4, 8], F32))
            g1 = EP(nc.sbuf_tensor("f_g1", [128, 4, 8], F32))
            g2 = EP(nc.sbuf_tensor("f_g2", [128, 4, 8], F32))
            g3 = EP(nc.sbuf_tensor("f_g3", [128, 4, 8], F32))
            g4 = EP(nc.sbuf_tensor("f_g4", [128, 4, 8], F32))
            gsc = EP(nc.sbuf_tensor("f_gsc", [128, 20], F32))
            grst = EP(nc.sbuf_tensor("f_grst", [8, 4, 128], F32))
            sq = EP(nc.sbuf_tensor("f_sq", [128, 1024], F32))
            stat = EP(nc.sbuf_tensor("f_stat", [128, 8], F32))
            b_xt, b_ttm, b_u3st, b_lgst, b_lgtm = (Buf() for _ in range(5))
            b_g1, b_g2, b_g3, b_g4, b_gsc, b_grst, b_sq, b_stat = (Buf() for _ in range(8))
            for s in range(NSUP):
                for i in range(4):
                    tile = s * 4 + i
                    P.op("sync", lambda e, t=tile: e.dma_start(
                        out=xt[:], in_=src[t * 128:(t + 1) * 128, :]),
                        reads=[b_src[tile]], writes=[b_xt])
                    ln_tile(C, xt[:], ttm[:], C.epsc[:, 0:1], b_xt, b_ttm,
                            sq[:], b_sq, stat[:], b_stat)
                    for c in range(8):
                        transpose128(C, ttm[:, c * 128:(c + 1) * 128],
                                     u3st[:, c, i * 128:(i + 1) * 128],
                                     b_ttm, b_u3st, BF16)
                P.op("sync", lambda e, s=s: e.dma_start(
                    out=C.u3fmd[s].rearrange("p (c n) -> p c n", c=8), in_=u3st[:]),
                    reads=[b_u3st], writes=[C.b_u3fmd[s]])
                for c in range(8):
                    P.op("tensor", lambda e, c=c: e.matmul(
                        C.pf[3][0:8, 0:512], wg[:, c, :], u3st[:, c, :],
                        start=(c == 0), stop=(c == 7), skip_group_check=True),
                        reads=[b_wg, b_u3st], writes=[C.b_pf[3]], inc=(c == 7))
                P.op("scalar", lambda e: e.activation(
                    lgst[:], C.pf[3][0:8, 0:512], AF.Identity,
                    bias=C.bias_sb["bg_col"][:]),
                    reads=[C.b_pf[3], C.b_const], writes=[b_lgst])
                for i in range(4):
                    P.op("tensor", lambda e, i=i: e.transpose(
                        C.pf[2][0:128, i * 8:(i + 1) * 8],
                        lgst[0:8, i * 128:(i + 1) * 128], C.idf_sb[0:8, 0:8]),
                        reads=[b_lgst, C.b_const], writes=[C.b_pf[2]])
                    P.op("vector", lambda e, i=i: e.tensor_copy(
                        lgtm[:, i, :], C.pf[2][0:128, i * 8:(i + 1) * 8]),
                        reads=[C.b_pf[2]], writes=[b_lgtm])
                m1 = gsc[:, 0:4]
                m2 = gsc[:, 4:8]
                dd = gsc[:, 8:12]
                w1 = gsc[:, 12:16]
                w2 = gsc[:, 16:20]
                m1b = m1.rearrange("p (a o) -> p a o", a=4).broadcast_to([128, 4, 8])
                m2b = m2.rearrange("p (a o) -> p a o", a=4).broadcast_to([128, 4, 8])
                w1b = w1.rearrange("p (a o) -> p a o", a=4).broadcast_to([128, 4, 8])
                w2b = w2.rearrange("p (a o) -> p a o", a=4).broadcast_to([128, 4, 8])
                P.op("vector", lambda e: e.reduce_max(m1, lgtm[:], AX.X),
                     reads=[b_lgtm], writes=[b_gsc])
                P.op("vector", lambda e: e.tensor_tensor(g1[:], lgtm[:], m1b, ALU.is_equal),
                     reads=[b_lgtm, b_gsc], writes=[b_g1])
                P.op("vector", lambda e: e.scalar_tensor_tensor(
                    g2[:], g1[:], -1e30, lgtm[:], ALU.mult, ALU.add),
                    reads=[b_g1, b_lgtm], writes=[b_g2])
                P.op("vector", lambda e: e.reduce_max(m2, g2[:], AX.X),
                     reads=[b_g2], writes=[b_gsc])
                P.op("vector", lambda e: e.tensor_tensor(dd, m1, m2, ALU.subtract),
                     reads=[b_gsc], writes=[b_gsc])
                P.op("scalar", lambda e: e.activation(w1, dd, AF.Sigmoid),
                     reads=[b_gsc], writes=[b_gsc])
                P.op("scalar", lambda e: e.activation(w2, dd, AF.Sigmoid, scale=-1.0),
                     reads=[b_gsc], writes=[b_gsc])
                P.op("vector", lambda e: e.tensor_tensor(g3[:], g2[:], m2b, ALU.is_equal),
                     reads=[b_g2, b_gsc], writes=[b_g3])
                P.op("vector", lambda e: e.tensor_tensor(g4[:], g1[:], w1b, ALU.mult),
                     reads=[b_g1, b_gsc], writes=[b_g4])
                P.op("vector", lambda e: e.tensor_tensor(g1[:], g3[:], w2b, ALU.mult),
                     reads=[b_g3, b_gsc], writes=[b_g1])
                P.op("vector", lambda e: e.tensor_tensor(g4[:], g4[:], g1[:], ALU.add),
                     reads=[b_g4, b_g1], writes=[b_g4])
                for i in range(4):
                    P.op("tensor", lambda e, i=i: e.transpose(
                        C.pf[1][0:8, i * 128:(i + 1) * 128],
                        g4[:, i, :], C.idf_sb[:]),
                        reads=[b_g4, C.b_const], writes=[C.b_pf[1]])
                    P.op("vector", lambda e, i=i: e.tensor_copy(
                        grst[0:8, i, :], C.pf[1][0:8, i * 128:(i + 1) * 128]),
                        reads=[C.b_pf[1]], writes=[b_grst])
                P.op("sync", lambda e, s=s: e.dma_start(
                    out=C.gd[:, s * 512:(s + 1) * 512],
                    in_=grst[:].rearrange("p a n -> p (a n)")),
                    reads=[b_grst], writes=[C.b_gd[s]])
        P.barrier()

        with ExitStack() as sM:
            EM = sM.enter_context
            w1bufs = [EM(nc.sbuf_tensor(f"f_w1{i}", [128, 8, 2048], BF16)) for i in range(2)]
            w2sb = EM(nc.sbuf_tensor("f_w2", [128, 16, 1024], BF16))
            u3t = EM(nc.sbuf_tensor("f_u3t", [128, 2, 8, 512], BF16))
            hb = EM(nc.sbuf_tensor("f_h", [128, 16, 512], BF16))
            gbc = EM(nc.sbuf_tensor("f_gbc", [128, 512], BF16))
            g1r = EM(nc.sbuf_tensor("f_g1r", [1, 512], F32))
            g8 = EM(nc.sbuf_tensor("f_g8", [8, 2, 512], BF16))
            mixed = EM(nc.sbuf_tensor("f_mixed", [128, 2, 8, 512], F32))
            xt2 = EM(nc.sbuf_tensor("f_xt2", [128, 1024], F32))
            u3r = EM(nc.sbuf_tensor("f_u3r", [128, 1024], F32))
            yt = EM(nc.sbuf_tensor("f_yt", [128, 1024], F32))
            zt = EM(nc.sbuf_tensor("f_zt", [128, 1024], F32))
            xo = EM(nc.sbuf_tensor("f_xo", [128, 1024], F32))
            sq2 = EM(nc.sbuf_tensor("f_sq2", [128, 1024], F32))
            stat2 = EM(nc.sbuf_tensor("f_stat2", [128, 8], F32))
            b_w1 = [Buf(), Buf()]
            b_w2, b_u3t, b_h, b_gbc, b_g1r, b_g8, b_mixed = (Buf() for _ in range(7))
            b_xt2, b_u3r, b_yt, b_zt, b_xo, b_sq2, b_stat2 = (Buf() for _ in range(7))
            b1f = C.bias_sb["b1_fm"]
            for H in range(8):
                sts = [2 * H, 2 * H + 1]
                for si, s in enumerate(sts):
                    P.op("sync", lambda e, s=s, si=si: e.dma_start(
                        out=u3t[:, si, :, :],
                        in_=C.u3fmd[s].rearrange("p (c n) -> p c n", c=8)),
                        reads=[C.b_u3fmd[s]], writes=[b_u3t])
                    P.op("gpsimd", lambda e, s=s, si=si: e.dma_start(
                        out=g8[:, si, :], in_=C.gd[:, s * 512:(s + 1) * 512]),
                        reads=[C.b_gd[s]], writes=[b_g8])
                for e_ in range(8):
                    wb, bwb = w1bufs[e_ % 2], b_w1[e_ % 2]
                    P.op("gpsimd", lambda e, e_=e_, wb=wb: e.dma_start(
                        out=wb[:], in_=C.dram["moe_w1"][e_].rearrange(
                            "(c p) f -> p c f", p=128)),
                        writes=[bwb])
                    P.op("gpsimd", lambda e, e_=e_: e.dma_start(
                        out=w2sb[:], in_=C.dram["moe_w2"][e_].rearrange(
                            "(c p) f -> p c f", p=128)),
                        writes=[b_w2])
                    for si, s in enumerate(sts):
                        P.op("sync", lambda e, s=s, e_=e_: e.dma_start(
                            out=g1r[:], in_=C.gd[e_:e_ + 1, s * 512:(s + 1) * 512]),
                            reads=[C.b_gd[s]], writes=[b_g1r])
                        P.op("tensor", lambda e: e.matmul(
                            C.pf[4][:, 0:512], C.onesf_sb[0:1, 0:128], g1r[:],
                            start=True, stop=True, skip_group_check=True),
                            reads=[b_g1r, C.b_const], writes=[C.b_pf[4]])
                        P.op("vector", lambda e: e.tensor_copy(gbc[:], C.pf[4][:, 0:512]),
                             reads=[C.b_pf[4]], writes=[b_gbc])
                        for foc in range(16):
                            bank, bbk = C.pf[foc % 2], C.b_pf[foc % 2]
                            for kc in range(8):
                                P.op("tensor", lambda e, bank=bank, kc=kc, foc=foc, si=si, wb=wb:
                                     e.matmul(bank[:], wb[:, kc, foc * 128:(foc + 1) * 128],
                                              u3t[:, si, kc, :], start=(kc == 0),
                                              stop=(kc == 7)),
                                     reads=[bwb, b_u3t], writes=[bbk], inc=(kc == 7))
                            P.op("scalar", lambda e, bank=bank, foc=foc, e_=e_:
                                 e.activation(hb[:, foc, :], bank[:], AF.Relu,
                                              bias=b1f[:, e_, foc:foc + 1]),
                                 reads=[bbk, C.b_const], writes=[b_h])
                        hb3 = hb[:]
                        gbc3 = gbc[:].rearrange("p (o n) -> p o n", o=1
                                                ).broadcast_to([128, 16, 512])
                        P.op("vector", lambda e, hb3=hb3, gbc3=gbc3:
                             e.tensor_tensor(hb3, hb3, gbc3, ALU.mult),
                             reads=[b_h, b_gbc], writes=[b_h])
                        for dc in range(8):
                            bank, bbk = C.pf[2 + dc % 2], C.b_pf[2 + dc % 2]
                            for fc in range(16):
                                P.op("tensor", lambda e, bank=bank, fc=fc, dc=dc, e_=e_:
                                     e.matmul(bank[:], w2sb[:, fc, dc * 128:(dc + 1) * 128],
                                              hb[:, fc, :], start=(fc == 0),
                                              stop=(fc == 15 and e_ != 0)),
                                     reads=[b_w2, b_h], writes=[bbk],
                                     inc=(fc == 15 and e_ != 0))
                            if e_ == 0:
                                P.op("tensor", lambda e, bank=bank, dc=dc, si=si:
                                     e.matmul(bank[:], b2m[0:8, dc * 128:(dc + 1) * 128],
                                              g8[:, si, :], start=False, stop=True,
                                              skip_group_check=True),
                                     reads=[b_b2m, b_g8], writes=[bbk], inc=True)
                            if e_ == 0:
                                P.op("vector", lambda e, bank=bank, dc=dc, si=si:
                                     e.tensor_copy(mixed[:, si, dc, :], bank[:]),
                                     reads=[bbk], writes=[b_mixed])
                            else:
                                P.op("vector", lambda e, bank=bank, dc=dc, si=si:
                                     e.tensor_tensor(mixed[:, si, dc, :], bank[:],
                                                     mixed[:, si, dc, :], ALU.add),
                                     reads=[bbk, b_mixed], writes=[b_mixed])
                for si, s in enumerate(sts):
                    for i in range(4):
                        tile = s * 4 + i
                        P.op("sync", lambda e, t=tile: e.dma_start(
                            out=xt2[:], in_=src[t * 128:(t + 1) * 128, :]),
                            reads=[b_src[tile]], writes=[b_xt2])
                        ln_tile(C, xt2[:], u3r[:], C.epsc[:, 0:1], b_xt2, b_u3r,
                                sq2[:], b_sq2, stat2[:], b_stat2)
                        for dc in range(8):
                            tb, btb = C.pf[4 + (dc % 2)], C.b_pf[4 + (dc % 2)]
                            P.op("tensor", lambda e, tb=tb, si=si, dc=dc, i=i: e.transpose(
                                tb[0:128, 0:128], mixed[:, si, dc, i * 128:(i + 1) * 128],
                                C.idf_sb[:]),
                                reads=[b_mixed, C.b_const], writes=[btb])
                            P.op("vector", lambda e, tb=tb, dc=dc: e.tensor_tensor(
                                yt[:, dc * 128:(dc + 1) * 128], tb[0:128, 0:128],
                                u3r[:, dc * 128:(dc + 1) * 128], ALU.add),
                                reads=[btb, b_u3r], writes=[b_yt])
                        rstd, nmr = ln_tile(C, yt[:], zt[:], C.epsc[:, 0:1], b_yt, b_zt,
                                            sq2[:], b_sq2, stat2[:], b_stat2)
                        P.op("vector", lambda e: e.tensor_tensor(zt[:], zt[:], gam[:], ALU.mult),
                             reads=[b_zt, b_gam], writes=[b_zt])
                        P.op("vector", lambda e: e.tensor_tensor(zt[:], zt[:], bet[:], ALU.add),
                             reads=[b_zt, b_bet], writes=[b_zt])
                        ln_tile(C, zt[:], yt[:], C.epsc[:, 0:1], b_zt, b_yt,
                                sq2[:], b_sq2, stat2[:], b_stat2)
                        P.op("vector", lambda e: e.tensor_tensor(xo[:], yt[:], xt2[:], ALU.add),
                             reads=[b_yt, b_xt2], writes=[b_xo])
                        P.op("sync", lambda e, t=tile: e.dma_start(
                            out=dst[t * 128:(t + 1) * 128, :], in_=xo[:]),
                            reads=[b_xo], writes=[b_dst[tile]])
    P.barrier()


def tap(C, name, ap, buf):
    if not getattr(C, "want_taps", None) or name not in C.want_taps:
        return
    nc, P = C.nc, C.P
    sh = list(ap.shape)
    d = nc.dram_tensor("tap_" + name, sh, ap.dtype, kind="ExternalOutput")
    P.op("sync", lambda e: e.dma_start(out=d.ap(), in_=ap), reads=[buf])


def negc_calc(C, nrm, b_nrm, ncs, b_ncs, ncrow, b_ncrow, negc, b_negc):
    P = C.P
    P.op("tensor", lambda e: e.transpose(C.pf[2][0:16, 0:128], nrm[:, 0:16], C.idf_sb[:]),
         reads=[b_nrm, C.b_const], writes=[C.b_pf[2]])
    P.op("vector", lambda e: e.reduce_max(ncs[:, 0:1], C.pf[2][0:16, 0:128], AX.X),
         reads=[C.b_pf[2]], writes=[b_ncs])
    P.op("tensor", lambda e: e.transpose(C.pf[2][0:16, 128:256], nrm[:, 16:32], C.idf_sb[:]),
         reads=[b_nrm, C.b_const], writes=[C.b_pf[2]])
    P.op("vector", lambda e: e.reduce_max(ncs[:, 1:2], C.pf[2][0:16, 128:256], AX.X),
         reads=[C.b_pf[2]], writes=[b_ncs])
    P.op("scalar", lambda e: e.activation(ncs[:, 2:3], ncs[:, 0:1], AF.Sqrt),
         reads=[b_ncs], writes=[b_ncs])
    P.op("scalar", lambda e: e.activation(ncs[:, 3:4], ncs[:, 1:2], AF.Sqrt),
         reads=[b_ncs], writes=[b_ncs])
    P.op("vector", lambda e: e.tensor_tensor(ncs[:, 4:5], ncs[:, 2:3], ncs[:, 3:4], ALU.mult),
         reads=[b_ncs], writes=[b_ncs])
    P.op("vector", lambda e: e.tensor_single_scalar(ncs[:, 5:6], ncs[:, 4:5], -0.125, ALU.mult),
         reads=[b_ncs], writes=[b_ncs])
    P.op("tensor", lambda e: e.transpose(C.pf[2][0:1, 256:272], ncs[:, 5:6],
                                         C.idf_sb[0:16, 0:16]),
         reads=[b_ncs, C.b_const], writes=[C.b_pf[2]])
    P.op("vector", lambda e: e.tensor_copy(ncrow[:], C.pf[2][0:1, 256:272]),
         reads=[C.b_pf[2]], writes=[b_ncrow])
    P.op("tensor", lambda e: e.matmul(C.pf[2][:, 272:288], C.onesf_sb[0:1, 0:128],
                                      ncrow[:], start=True, stop=True,
                                      skip_group_check=True),
         reads=[b_ncrow, C.b_const], writes=[C.b_pf[2]])
    P.op("vector", lambda e: e.tensor_copy(negc[:], C.pf[2][:, 272:288]),
         reads=[C.b_pf[2]], writes=[b_negc])


def emit_l1attn(C, src, b_src, dst, b_dst):
    nc, P = C.nc, C.P
    with ExitStack() as bs:
        E = bs.enter_context
        wq = E(nc.sbuf_tensor("l1_wq", [128, 8, 1024], BF16))
        wdn = E(nc.sbuf_tensor("l1_wdn", [128, 8, 512], BF16))
        wku = E(nc.sbuf_tensor("l1_wku", [128, 4, 1024], BF16))
        wvu = E(nc.sbuf_tensor("l1_wvu", [128, 4, 1024], BF16))
        qfm = E(nc.sbuf_tensor("l1_qfm", [128, 8, 2048], BF16))
        kfm = E(nc.sbuf_tensor("l1_kfm", [128, 8, 2048], BF16))
        vtm = E(nc.sbuf_tensor("l1_vtm", [128, 16, 8, 128], BF16))
        negc = E(nc.sbuf_tensor("l1_negc", [128, 16], F32))
        nrm = E(nc.sbuf_tensor("l1_nrm", [128, 32], F32))
        ncs = E(nc.sbuf_tensor("l1_ncs", [16, 8], F32))
        ncrow = E(nc.sbuf_tensor("l1_ncrow", [1, 16], F32))
        cos_sb = E(nc.sbuf_tensor("l1_cos", [128, 16, 32], F32))
        sin_sb = E(nc.sbuf_tensor("l1_sin", [128, 16, 32], F32))
        b_wq, b_wdn, b_wku, b_wvu = (Buf() for _ in range(4))
        b_q, b_k, b_v, b_negc, b_nrm, b_ncs, b_ncrow = (Buf() for _ in range(7))
        for (sb, nm, bw) in [(wq, "wqP", b_wq), (wdn, "w_down", b_wdn),
                             (wku, "wk_upP", b_wku), (wvu, "wv_up", b_wvu)]:
            P.op("gpsimd", lambda e, sb=sb, nm=nm: e.dma_start(
                out=sb[:], in_=C.dram[nm].rearrange("(c p) o -> p c o", p=128)),
                writes=[bw])
        P.op("sync", lambda e: e.dma_start(
            out=cos_sb[:], in_=C.dram["ropecos"].rearrange("(t p) d -> p t d", p=128)),
            writes=[C.b_const])
        P.op("sync", lambda e: e.dma_start(
            out=sin_sb[:], in_=C.dram["ropesin"].rearrange("(t p) d -> p t d", p=128)),
            writes=[C.b_const])

        for b in range(NB):
            with ExitStack() as sA:
                EA = sA.enter_context
                xt = EA(nc.sbuf_tensor(_un("l1a_xt"), [128, 1024], F32))
                ttm = EA(nc.sbuf_tensor(_un("l1a_ttm"), [128, 1024], BF16))
                tfm = EA(nc.sbuf_tensor(_un("l1a_tfm"), [128, 8, 512], BF16))
                latfm = EA(nc.sbuf_tensor(_un("l1a_lat"), [128, 4, 512], BF16))
                qst = EA(nc.sbuf_tensor(_un("l1a_qst"), [128, 512], F32))
                qtm = EA(nc.sbuf_tensor(_un("l1a_qtm"), [128, 4, 1024], F32))
                rout = EA(nc.sbuf_tensor(_un("l1a_rout"), [128, 1024], BF16))
                rscr = EA(nc.sbuf_tensor(_un("l1a_rscr"), [128, 1024], F32))
                vst = EA(nc.sbuf_tensor(_un("l1a_vst"), [128, 8, 512], BF16))
                sq = EA(nc.sbuf_tensor(_un("l1a_sq"), [128, 1024], F32))
                nrmt = EA(nc.sbuf_tensor(_un("l1a_nrmt"), [128, 16], F32))
                stat = EA(nc.sbuf_tensor(_un("l1a_stat"), [128, 8], F32))
                b_xt, b_ttm, b_tfm, b_lat, b_qst = (Buf() for _ in range(5))
                b_qtm, b_rout, b_rscr, b_vst, b_sq, b_nrmt, b_stat = (Buf() for _ in range(7))
                P.op("vector", lambda e: e.memset(nrm[:], 0.0), writes=[b_nrm])

                def fm_grid(wsb, b_w, nkc, noc, rhs, b_rhs, evac):
                    for oc in range(noc):
                        bank, bbk = C.pf[oc % 3], C.b_pf[oc % 3]
                        for kc in range(nkc):
                            P.op("tensor", lambda e, bank=bank, kc=kc, oc=oc:
                                 e.matmul(bank[:], wsb[:, kc, oc * 128:(oc + 1) * 128],
                                          rhs[:, kc, :], start=(kc == 0),
                                          stop=(kc == nkc - 1)),
                                 reads=[b_w, b_rhs], writes=[bbk], inc=(kc == nkc - 1))
                        evac(oc, bank, bbk)

                def tm_rope_path(j, srcname, b_srcw, off):
                    pass

                for j in range(4):
                    span = slice(j * 512, (j + 1) * 512)
                    for i in range(4):
                        tile = b * 16 + j * 4 + i
                        P.op("sync", lambda e, t=tile, xt=xt: e.dma_start(
                            out=xt[:], in_=src[t * 128:(t + 1) * 128, :]),
                            reads=[b_src[tile]], writes=[b_xt])
                        ln_tile(C, xt[:], ttm[:], C.epsc[:, 0:1], b_xt, b_ttm,
                                sq[:], b_sq, stat[:], b_stat)
                        for c in range(8):
                            transpose128(C, ttm[:, c * 128:(c + 1) * 128],
                                         tfm[:, c, i * 128:(i + 1) * 128],
                                         b_ttm, b_tfm, BF16)

                    def evac_lat(oc, bank, bbk):
                        P.op("scalar", lambda e, bank=bank, oc=oc, latfm=latfm: e.activation(
                            latfm[:, oc, :], bank[:], AF.Identity),
                            reads=[bbk], writes=[b_lat])
                    fm_grid(wdn, b_wdn, 8, 4, tfm, b_tfm, evac_lat)

                    def evac_v(oc, bank, bbk):
                        P.op("scalar", lambda e, bank=bank, oc=oc, vst=vst: e.activation(
                            vst[:, oc, :], bank[:], AF.Identity),
                            reads=[bbk], writes=[b_vst])
                    fm_grid(wvu, b_wvu, 4, 8, latfm, b_lat, evac_v)
                    for c in range(8):
                        for i in range(4):
                            transpose128(C, vst[:, c, i * 128:(i + 1) * 128],
                                         vtm[:, j * 4 + i, c, :], b_vst, b_v, BF16)

                    for (wsb, b_w, nkc, rhsbuf, b_rhs, dstfm, b_dst2, off) in (
                            (wq, b_wq, 8, tfm, b_tfm, qfm, b_q, 0),
                            (wku, b_wku, 4, latfm, b_lat, kfm, b_k, 16)):
                        def evac_q(oc, bank, bbk):
                            P.op("scalar", lambda e, bank=bank, qst=qst: e.activation(
                                qst[:], bank[:], AF.Identity),
                                reads=[bbk], writes=[b_qst])
                            for i in range(4):
                                tb, btb = C.pf[3 + (i % 2)], C.b_pf[3 + (i % 2)]
                                P.op("tensor", lambda e, tb=tb, i=i, qst=qst: e.transpose(
                                    tb[0:128, 0:128], qst[:, i * 128:(i + 1) * 128],
                                    C.idf_sb[:]),
                                    reads=[b_qst, C.b_const], writes=[btb])
                                P.op("vector", lambda e, tb=tb, i=i, oc=oc, qtm=qtm:
                                     e.tensor_copy(qtm[:, i, oc * 128:(oc + 1) * 128],
                                                   tb[0:128, 0:128]),
                                     reads=[btb], writes=[b_qtm])
                        fm_grid(wsb, b_w, nkc, 8, rhsbuf, b_rhs, evac_q)
                        for i in range(4):
                            q3 = qtm[:, i, :].rearrange("p (h two d) -> p h two d",
                                                        h=16, two=2)
                            x1 = q3[:, :, 0, :]
                            x2 = q3[:, :, 1, :]
                            ti = j * 4 + i
                            cb = cos_sb[:, ti, :].rearrange("p (o d) -> p o d", o=1
                                                            ).broadcast_to([128, 16, 32])
                            sb_ = sin_sb[:, ti, :].rearrange("p (o d) -> p o d", o=1
                                                             ).broadcast_to([128, 16, 32])
                            r1 = rscr[:, 0:512].rearrange("p (h d) -> p h d", h=16)
                            r2 = rscr[:, 512:1024].rearrange("p (h d) -> p h d", h=16)
                            ro = rout[:].rearrange("p (h two d) -> p h two d", h=16, two=2)
                            P.op("vector", lambda e, x1=x1, cb=cb, r1=r1:
                                 e.tensor_tensor(r1, x1, cb, ALU.mult),
                                 reads=[b_qtm, C.b_const], writes=[b_rscr])
                            P.op("vector", lambda e, x2=x2, sbx=sb_, r2=r2:
                                 e.tensor_tensor(r2, x2, sbx, ALU.mult),
                                 reads=[b_qtm, C.b_const], writes=[b_rscr])
                            P.op("vector", lambda e, r1=r1, r2=r2, ro=ro:
                                 e.tensor_tensor(ro[:, :, 0, :], r1, r2, ALU.subtract),
                                 reads=[b_rscr], writes=[b_rout])
                            P.op("vector", lambda e, x1=x1, sbx=sb_, r1=r1:
                                 e.tensor_tensor(r1, x1, sbx, ALU.mult),
                                 reads=[b_qtm, C.b_const], writes=[b_rscr])
                            P.op("vector", lambda e, x2=x2, cb=cb, r2=r2:
                                 e.tensor_tensor(r2, x2, cb, ALU.mult),
                                 reads=[b_qtm, C.b_const], writes=[b_rscr])
                            P.op("vector", lambda e, r1=r1, r2=r2, ro=ro:
                                 e.tensor_tensor(ro[:, :, 1, :], r1, r2, ALU.add),
                                 reads=[b_rscr], writes=[b_rout])
                            P.op("vector", lambda e, i=i, sq=sq, qtm=qtm: e.tensor_tensor(
                                sq[:], qtm[:, i, :], qtm[:, i, :], ALU.mult),
                                reads=[b_qtm], writes=[b_sq])
                            P.op("vector", lambda e, sq=sq, nrmt=nrmt: e.reduce_sum(
                                nrmt[:], sq[:].rearrange("p (h d) -> p h d", h=16), AX.X),
                                reads=[b_sq], writes=[b_nrmt])
                            P.op("vector", lambda e, off=off, nrmt=nrmt: e.tensor_tensor(
                                nrm[:, off:off + 16], nrm[:, off:off + 16],
                                nrmt[:], ALU.max),
                                reads=[b_nrm, b_nrmt], writes=[b_nrm])
                            for c in range(8):
                                transpose128(C, rout[:, c * 128:(c + 1) * 128],
                                             dstfm[:, c, (j * 4 + i) * 128:(j * 4 + i + 1) * 128],
                                             b_rout, b_dst2, BF16)
                negc_calc(C, nrm, b_nrm, ncs, b_ncs, ncrow, b_ncrow, negc, b_negc)
                if b == 0:
                    tap(C, "l1_qfm", qfm[:], b_q)
                    tap(C, "l1_kfm", kfm[:], b_k)
                    tap(C, "l1_vtm", vtm[:], b_v)
                    tap(C, "l1_negc", negc[:], b_negc)
            P.barrier()
            with ExitStack() as sB:
                EB = sB.enter_context
                wo = EB(nc.sbuf_tensor(_un("l1b_wo"), [128, 8, 1024], BF16))
                b_wo = Buf()
                P.op("gpsimd", lambda e: e.dma_start(
                    out=wo[:], in_=C.dram["wo"].rearrange("(c p) o -> p c o", p=128)),
                    writes=[b_wo])
                Pb = EB(nc.sbuf_tensor(_un("l1b_pb"), [128, 2, 512], BF16))
                attnst = EB(nc.sbuf_tensor(_un("l1b_attnst"), [128, 8, 512], BF16))
                denb = EB(nc.sbuf_tensor(_un("l1b_denb"), [64, 512], F32))
                rb = EB(nc.sbuf_tensor(_un("l1b_rb"), [128, 512], F32))
                wost = EB(nc.sbuf_tensor(_un("l1b_wost"), [128, 512], F32))
                otm = EB(nc.sbuf_tensor(_un("l1b_otm"), [128, 4, 1024], F32))
                zt = EB(nc.sbuf_tensor(_un("l1b_zt"), [128, 1024], F32))
                xtB = EB(nc.sbuf_tensor(_un("l1b_xt"), [128, 1024], F32))
                xoB = EB(nc.sbuf_tensor(_un("l1b_xo"), [128, 1024], F32))
                sqB = EB(nc.sbuf_tensor(_un("l1b_sq"), [128, 1024], F32))
                statB = EB(nc.sbuf_tensor(_un("l1b_stat"), [128, 8], F32))
                b_Pb, b_att, b_den, b_rb, b_wost = (Buf() for _ in range(5))
                b_otm, b_zt, b_xtB, b_xoB, b_sqB, b_statB = (Buf() for _ in range(6))
                C.tap_stage = (wost, b_wost)
                C.tap_stage2 = (zt, b_zt)
                for qs in range(4):
                    attn_phaseB(C, qs, 4 * qs + 4,
                                lambda st, qs=qs: max(0, (st & ~1) - 4 * qs) * 128,
                                False, qfm, b_q, kfm, b_k, vtm, b_v,
                                negc, b_negc, attnst, b_att, Pb, b_Pb,
                                denb, b_den, rb, b_rb)
                    if b == 0 and qs == 0:
                        tap(C, "l1_attnst0", attnst[:], b_att)
                    for oc in range(8):
                        bank, bbk = C.pf[2 + (oc % 2)], C.b_pf[2 + (oc % 2)]
                        for kc in range(8):
                            P.op("tensor", lambda e, bank=bank, kc=kc, oc=oc, wo=wo, attnst=attnst:
                                 e.matmul(bank[:], wo[:, kc, oc * 128:(oc + 1) * 128],
                                          attnst[:, kc, :], start=(kc == 0), stop=(kc == 7)),
                                 reads=[b_wo, b_att], writes=[bbk], inc=(kc == 7))
                        P.op("scalar", lambda e, bank=bank, wost=wost: e.activation(
                            wost[:], bank[:], AF.Identity),
                            reads=[bbk], writes=[b_wost])
                        for i in range(4):
                            tb, btb = C.pf[4 + (i % 2)], C.b_pf[4 + (i % 2)]
                            P.op("tensor", lambda e, tb=tb, i=i, wost=wost: e.transpose(
                                tb[0:128, 0:128], wost[:, i * 128:(i + 1) * 128],
                                C.idf_sb[:]),
                                reads=[b_wost, C.b_const], writes=[btb])
                            P.op("vector", lambda e, tb=tb, i=i, oc=oc, otm=otm: e.tensor_copy(
                                otm[:, i, oc * 128:(oc + 1) * 128], tb[0:128, 0:128]),
                                reads=[btb], writes=[b_otm])
                    for i in range(4):
                        tile = b * 16 + qs * 4 + i
                        ln_tile(C, otm[:, i, :], zt[:], C.epsc[:, 0:1], b_otm, b_zt,
                                sqB[:], b_sqB, statB[:], b_statB)
                        P.op("sync", lambda e, t=tile, xtB=xtB: e.dma_start(
                            out=xtB[:], in_=src[t * 128:(t + 1) * 128, :]),
                            reads=[b_src[tile]], writes=[b_xtB])
                        P.op("vector", lambda e, xoB=xoB, zt=zt, xtB=xtB: e.tensor_tensor(xoB[:], zt[:], xtB[:], ALU.add),
                             reads=[b_zt, b_xtB], writes=[b_xoB])
                        P.op("sync", lambda e, t=tile, xoB=xoB: e.dma_start(
                            out=dst[t * 128:(t + 1) * 128, :], in_=xoB[:]),
                            reads=[b_xoB], writes=[b_dst[tile]])
            P.barrier()
    P.barrier()


def build_program(blocks=("l0a", "l0f", "l1a", "l1f"), want_taps=None, paranoia=0, **cflags):
    nc = bass.Bass("TRN2", target_bir_lowering=False)
    C = Ctx()
    C.nc = nc
    C.want_taps = want_taps
    C.paranoia = paranoia
    for k_, v_ in cflags.items():
        setattr(C, k_, v_)
    C.dram = {}

    def din(name, shape, dtype=F32):
        C.dram[name] = nc.dram_tensor(name, list(shape), dtype, kind="ExternalInput")
        return C.dram[name]

    din("x", [NB * T, D])
    dout = nc.dram_tensor("out", [NB * T, D], F32, kind="ExternalOutput")
    C.dram["out"] = dout

    din("idb", [128, 128], BF16)
    din("idf", [128, 128], F32)
    din("onesf", [128, 128], F32)
    din("onescb", [128, 1], BF16)
    din("epsc", [128, 2], F32)
    if "l0a" in blocks or "l1a" in blocks:
        din("eh", [128, 2], F32)
    if "l0a" in blocks:
        din("tri", [128, 128], BF16)
        din("w_attnT", [D, 3 * D])
        din("b_attn_fm", [128, 24])
        din("w_projT", [D, D])
        din("b_proj_fm", [128, 8])
        din("w_fcT", [D, 4 * D])
        din("b_fc_fm", [128, 32])
        din("w_mlpT", [4 * D, D])
        din("b_mlp_fm", [128, 8])
    if "l0f" in blocks:
        din("wg", [D, 8])
        din("bg_col", [8, 1])
        din("moe_w1", [8, D, 2 * D])
        din("b1_fm", [128, 8, 16])
        din("moe_w2", [8, 2 * D, D])
        din("b2_mat", [8, D])
        din("gamma", [1, D])
        din("beta", [1, D])
    if "l1a" in blocks:
        din("ropecos", [T, 32])
        din("ropesin", [T, 32])
        din("wqP", [D, D])
        din("w_down", [D, 512])
        din("wk_upP", [512, D])
        din("wv_up", [512, D])
        din("wo", [D, D])
    if "l1f" in blocks:
        din("l1f_w", [D, D])
        din("l1f_b_fm", [128, 8])

    need_xres = len(blocks) > 1
    if need_xres:
        xres = nc.dram_tensor("xres", [NB * T, D], F32, kind="Internal")
    if "l0a" in blocks:
        u1d = nc.dram_tensor("u1d", [NB * T, D], F32, kind="Internal")
        C.u1d = u1d
        C.b_u1d = [Buf() for _ in range(NTILES)]
    if "l0f" in blocks:
        u3fmd = nc.dram_tensor("u3fmd", [NSUP, 128, 8 * 512], BF16, kind="Internal")
        gd = nc.dram_tensor("gd", [8, NB * T], F32, kind="Internal")
        C.u3fmd, C.gd = u3fmd, gd
        C.b_u3fmd = [Buf() for _ in range(NSUP)]
        C.b_gd = [Buf() for _ in range(NSUP)]

    with ExitStack() as ctx:
        E = ctx.enter_context
        P = Prog(nc)
        C.P = P
        sems = {}
        for e in ENGINES:
            if e in DMA_ENG:
                for r in range(NLANE):
                    sems[(e, r)] = E(nc.semaphore(f"sem_{e}{r}"))
            else:
                sems[(e, 0)] = E(nc.semaphore("sem_" + e))
        bar_sem = E(nc.semaphore("sem_bar"))

        C.idb_sb = E(nc.sbuf_tensor("idb_sb", [128, 128], BF16))
        C.idf_sb = E(nc.sbuf_tensor("idf_sb", [128, 128], F32))
        C.onesf_sb = E(nc.sbuf_tensor("onesf_sb", [128, 128], F32))
        C.onescb_sb = E(nc.sbuf_tensor("onescb_sb", [128, 1], BF16))
        C.epsc = E(nc.sbuf_tensor("epsc_sb", [128, 2], F32))
        C.b_const = Buf("const")
        loads = [(C.idb_sb, "idb"), (C.idf_sb, "idf"), (C.onesf_sb, "onesf"),
                 (C.onescb_sb, "onescb"), (C.epsc, "epsc")]
        C.bias_sb = {}
        if "l0a" in blocks or "l1a" in blocks:
            C.eh_sb = E(nc.sbuf_tensor("eh_sb", [128, 2], F32))
            loads.append((C.eh_sb, "eh"))
        if "l0a" in blocks:
            C.tri_sb = E(nc.sbuf_tensor("tri_sb", [128, 128], BF16))
            loads.append((C.tri_sb, "tri"))
            for nm, sh in [("b_attn_fm", [128, 24]), ("b_proj_fm", [128, 8]),
                           ("b_fc_fm", [128, 32]), ("b_mlp_fm", [128, 8])]:
                C.bias_sb[nm] = E(nc.sbuf_tensor(nm + "_sb", sh, F32))
                loads.append((C.bias_sb[nm], nm))
        if "l0f" in blocks:
            for nm, sh in [("bg_col", [8, 1]), ("b1_fm", [128, 8, 16])]:
                C.bias_sb[nm] = E(nc.sbuf_tensor(nm + "_sb", sh, F32))
                loads.append((C.bias_sb[nm], nm))
        if "l1f" in blocks:
            C.bias_sb["l1f_b"] = E(nc.sbuf_tensor("l1f_b_sb", [128, 8], F32))
            loads.append((C.bias_sb["l1f_b"], "l1f_b_fm"))

        C.pf = [E(nc.psum_tensor(f"pf{i}", [128, 512], F32)) for i in range(6)]
        C.pbf = [E(nc.psum_tensor(f"pbf{i}", [128, 1024], BF16)) for i in range(2)]
        C.b_pf = [Buf(f"pf{i}") for i in range(6)]
        C.b_pbf = [Buf(f"pbf{i}") for i in range(2)]
        C._pf_rot = 0
        C._pbf_rot = 0

        def next_pf():
            i = C._pf_rot % len(C.pf)
            C._pf_rot += 1
            return C.pf[i], C.b_pf[i]

        def next_pbf():
            i = C._pbf_rot % len(C.pbf)
            C._pbf_rot += 1
            return C.pbf[i], C.b_pbf[i]

        C.next_pf = next_pf
        C.next_pbf = next_pbf

        blk = E(nc.Block())

        for sb, dr_name in loads:
            P.op("sync", lambda e, sb=sb, d=C.dram[dr_name]: e.dma_start(
                out=sb[:], in_=d.ap()), writes=[C.b_const])

        def dram_bufs(n):
            return [Buf() for _ in range(n)]

        b_x = dram_bufs(NTILES)
        b_out = dram_bufs(NTILES)
        b_xres = dram_bufs(NTILES) if need_xres else None

        seq = []
        for bi, name in enumerate(blocks):
            src = C.dram["x"].ap() if bi == 0 else xres.ap()
            bsrc = b_x if bi == 0 else b_xres
            last = bi == len(blocks) - 1
            dstap = dout.ap() if last else xres.ap()
            bdst = b_out if last else b_xres
            seq.append((name, src, bsrc, dstap, bdst))

        for name, src, bsrc, dstap, bdst in seq:
            if name == "l1f":
                emit_l1ffn(C, src, bsrc, dstap, bdst)
            elif name == "l0a":
                emit_l0attn(C, src, bsrc, dstap, bdst)
            elif name == "l0f":
                emit_l0ffn(C, src, bsrc, dstap, bdst)
            elif name == "l1a":
                emit_l1attn(C, src, bsrc, dstap, bdst)

        P.emit(blk, sems, bar_sem)
    return nc


def _bias_fm(b):
    nck = b.shape[0] // 128
    return np.ascontiguousarray(b.reshape(nck, 128).T).astype(np.float32)


def make_weight_arrays(inputs, blocks=("l0a", "l0f", "l1a", "l1f")):
    g = {}
    g["idb"] = np.eye(128, dtype=ml_dtypes.bfloat16)
    g["idf"] = np.eye(128, dtype=np.float32)
    g["onesf"] = np.ones((128, 128), dtype=np.float32)
    g["onescb"] = np.ones((128, 1), dtype=ml_dtypes.bfloat16)
    g["epsc"] = np.broadcast_to(np.array([1e-5, 1e-10], dtype=np.float32), (128, 2)).copy()
    if "l0a" in blocks or "l1a" in blocks:
        eh = np.zeros((128, 2), dtype=np.float32)
        eh[0:64, 0] = 1.0
        eh[64:128, 1] = 1.0
        g["eh"] = eh
    if "l0a" in blocks:
        s_le_q = (np.arange(128)[:, None] <= np.arange(128)[None, :])
        g["tri"] = s_le_q.astype(ml_dtypes.bfloat16)
        g["w_attnT"] = np.ascontiguousarray(inputs["L0_attn_w_attn"].T)
        g["b_attn_fm"] = _bias_fm(inputs["L0_attn_b_attn"])
        g["w_projT"] = np.ascontiguousarray(inputs["L0_attn_w_proj"].T)
        g["b_proj_fm"] = _bias_fm(inputs["L0_attn_b_proj"])
        g["w_fcT"] = np.ascontiguousarray(inputs["L0_attn_w_fc"].T)
        g["b_fc_fm"] = _bias_fm(inputs["L0_attn_b_fc"])
        g["w_mlpT"] = np.ascontiguousarray(inputs["L0_attn_w_mlp_proj"].T)
        g["b_mlp_fm"] = _bias_fm(inputs["L0_attn_b_mlp_proj"])
    if "l0f" in blocks:
        g["wg"] = np.ascontiguousarray(inputs["L0_ffn_Wg"])
        g["bg_col"] = np.ascontiguousarray(inputs["L0_ffn_bg"].reshape(8, 1))
        g["moe_w1"] = np.ascontiguousarray(inputs["L0_ffn_W1"])
        g["b1_fm"] = np.ascontiguousarray(
            inputs["L0_ffn_b1"].reshape(8, 16, 128).transpose(2, 0, 1))
        g["moe_w2"] = np.ascontiguousarray(inputs["L0_ffn_W2"])
        g["b2_mat"] = np.ascontiguousarray(inputs["L0_ffn_b2"])
        g["gamma"] = np.ascontiguousarray(inputs["L0_ffn_gamma"].reshape(1, D))
        g["beta"] = np.ascontiguousarray(inputs["L0_ffn_beta"].reshape(1, D))
    if "l1a" in blocks:
        inv = 1.0 / 10000.0 ** (np.arange(0, 64, 2, dtype=np.float32) / 64)
        ang = np.outer(np.arange(T, dtype=np.float32), inv)
        g["ropecos"] = np.cos(ang).astype(np.float32)
        g["ropesin"] = np.sin(ang).astype(np.float32)
        perm = np.concatenate([
            np.concatenate([h * 64 + 2 * np.arange(32), h * 64 + 2 * np.arange(32) + 1])
            for h in range(16)])
        g["wqP"] = np.ascontiguousarray(inputs["L1_attn_wq"][:, perm])
        g["w_down"] = np.ascontiguousarray(inputs["L1_attn_w_down"])
        g["wk_upP"] = np.ascontiguousarray(inputs["L1_attn_wk_up"][:, perm])
        g["wv_up"] = np.ascontiguousarray(inputs["L1_attn_wv_up"])
        g["wo"] = np.ascontiguousarray(inputs["L1_attn_wo"])
    if "l1f" in blocks:
        g["l1f_w"] = np.ascontiguousarray(inputs["L1_ffn_weight"])
        g["l1f_b_fm"] = _bias_fm(inputs["L1_ffn_bias"])
    return g


FUSED = True


def kernel(**inputs):
    blocks = ("l0a", "l0f", "l1a", "l1f")
    x = np.asarray(inputs["L0_attn_x"], dtype=np.float32).reshape(32 * T, D)
    if FUSED:
        g = make_weight_arrays(inputs, blocks)
        nc = build_program(blocks)
        in_maps = []
        for c in range(8):
            m = dict(g)
            m["x"] = np.ascontiguousarray(x[c * NB * T:(c + 1) * NB * T])
            in_maps.append(m)
        res = run_bass_kernel_spmd(nc, in_maps, core_ids=list(range(8)))
        out = np.concatenate([r["out"] for r in res.results], axis=0)
        return out.reshape(32, T, D)
    cur = x
    for bname in blocks:
        g = make_weight_arrays(inputs, (bname,))
        nc = build_program((bname,))
        in_maps = []
        for c in range(8):
            m = dict(g)
            m["x"] = np.ascontiguousarray(cur[c * NB * T:(c + 1) * NB * T])
            in_maps.append(m)
        res = run_bass_kernel_spmd(nc, in_maps, core_ids=list(range(8)))
        cur = np.concatenate([r["out"] for r in res.results], axis=0)
    return cur.reshape(32, T, D)


# version 04
# speedup vs baseline: 1.2705x; 1.2705x over previous
import numpy as np
import ml_dtypes
from contextlib import ExitStack

import concourse.bass as bass
import concourse.mybir as mybir
from concourse.bass_utils import run_bass_kernel_spmd

F32 = mybir.dt.float32
BF16 = mybir.dt.bfloat16
AF = mybir.ActivationFunctionType
ALU = mybir.AluOpType
AX = mybir.AxisListType

ENGINES = ("sync", "tensor", "vector", "scalar", "gpsimd")
DMA_ENG = ("sync", "gpsimd")
NLANE = 8

NB = 4
T = 2048
D = 1024
NTILES = NB * T // 128
NSUP = NB * T // 512
MAGIC = 12582912.0


class Buf:
    __slots__ = ("writer", "readers", "name")

    def __init__(self, name=""):
        self.writer = None
        self.readers = []
        self.name = name


class Prog:
    def __init__(self, nc):
        self.nc = nc
        self.steps = []
        self.eng_steps = {e: [] for e in ENGINES}
        self.nbar = 0
        self._inc_count = {}

    def op(self, eng, fn, reads=(), writes=(), deps=(), inc=True):
        d = list(deps)
        for b in reads:
            if b.writer is not None:
                d.append(b.writer)
        for b in writes:
            d.extend(b.readers)
            if b.writer is not None:
                d.append(b.writer)
        seq = len(self.steps)
        st = {"eng": eng, "fn": fn, "deps": d, "inc": inc, "seq": seq, "bar": None}
        if inc:
            self._inc_count[eng] = self._inc_count.get(eng, 0) + 1
        self.steps.append(st)
        self.eng_steps[eng].append(st)
        tok = (eng, seq)
        for b in reads:
            b.readers.append(tok)
        for b in writes:
            b.writer = tok
            b.readers = []
        return tok

    def barrier(self):
        snap = {}
        for e in ENGINES:
            if e in DMA_ENG:
                n = self._inc_count.get(e, 0)
                snap[e] = [(n + NLANE - 1 - r) // NLANE for r in range(NLANE)]
            else:
                snap[e] = self._inc_count.get(e, 0)
        for e in ENGINES:
            st = {"eng": e, "fn": None, "deps": (), "inc": False,
                  "seq": len(self.steps), "bar": ("waits", snap)}
            self.steps.append(st)
            self.eng_steps[e].append(st)

    def _resolve(self):
        self.cover = {}
        self.cover_seq = {}
        self.total_incs = {}
        for e in ENGINES:
            arr = self.eng_steps[e]
            if e in DMA_ENG:
                lane_cnt = [0] * NLANE
                for st in arr:
                    if st["bar"] is not None:
                        continue
                    assert st["inc"], "DMA steps must inc"
                    r = (sum(lane_cnt)) % NLANE
                    lane_cnt[r] += 1
                    st["lane"] = r
                    self.cover[(e, st["seq"])] = (r, lane_cnt[r] * 16)
                    self.cover_seq[(e, st["seq"])] = st["seq"]
                self.total_incs[e] = list(lane_cnt)
            else:
                cum = 0
                cums = []
                for st in arr:
                    if st["inc"]:
                        cum += 1
                    cums.append(cum)
                self.total_incs[e] = cum
                nxt = [None] * len(arr)
                nxtseq = [None] * len(arr)
                cur = None
                curseq = None
                for i in range(len(arr) - 1, -1, -1):
                    if arr[i]["inc"]:
                        cur = cums[i]
                        curseq = arr[i]["seq"]
                    nxt[i] = cur
                    nxtseq[i] = curseq
                for i, st in enumerate(arr):
                    if nxt[i] is not None:
                        self.cover[(e, st["seq"])] = (0, nxt[i])
                        self.cover_seq[(e, st["seq"])] = nxtseq[i]

    def emit(self, blk, sems, bar_sem):
        self._resolve()
        for ename in ENGINES:
            def mk(ename):
                def run(eng):
                    waited = {}
                    for st in self.eng_steps[ename]:
                        if st["bar"] is not None:
                            kind, val = st["bar"]
                            assert kind == "waits"
                            for e2 in ENGINES:
                                if e2 in DMA_ENG:
                                    for r in range(NLANE):
                                        v2 = val[e2][r] * 16
                                        if v2 > waited.get((e2, r), 0):
                                            eng.wait_ge(sems[(e2, r)], v2)
                                            waited[(e2, r)] = v2
                                else:
                                    v2 = val[e2]
                                    if v2 > waited.get((e2, 0), 0):
                                        eng.wait_ge(sems[(e2, 0)], v2)
                                        waited[(e2, 0)] = v2
                            continue
                        need = {}
                        for (de, dseq) in st["deps"]:
                            if de == ename and ename == "tensor":
                                continue
                            cv = self.cover.get((de, dseq))
                            assert cv is not None, \
                                f"dep on ({de},{dseq}) has no covering inc"
                            lane, v = cv
                            cseq = self.cover_seq[(de, dseq)]
                            assert cseq < st["seq"], \
                                f"dep of step {st['seq']} covered only by later step {cseq} on {de}"
                            if v > need.get((de, lane), -1):
                                need[(de, lane)] = v
                        for key in sorted(need):
                            v = need[key]
                            if v > waited.get(key, 0):
                                eng.wait_ge(sems[key], v)
                                waited[key] = v
                        ins = st["fn"](eng)
                        if st["inc"]:
                            if ename in DMA_ENG:
                                ins.then_inc(sems[(ename, st["lane"])], 16)
                            else:
                                ins.then_inc(sems[(ename, 0)], 1)
                    if ename == "sync":
                        for r in range(NLANE):
                            eng.wait_ge(sems[("sync", r)],
                                        self.total_incs["sync"][r] * 16)
                return run
            getattr(blk, ename)(mk(ename))



_NAME_N = [0]


def _un(s):
    _NAME_N[0] += 1
    return f"{s}_{_NAME_N[0]}"



class Ctx:
    pass


def ln_tile(C, src_ap, out_ap, eps_ap, r_src, w_out, sq_ap, b_sq, stat_ap, b_stat,
            extra_out=None):
    P = C.P
    n = src_ap.shape[-1]
    sums = stat_ap[:, 0:1]
    sumsq = stat_ap[:, 1:2]
    mu = stat_ap[:, 2:3]
    var = stat_ap[:, 3:4]
    rstd = stat_ap[:, 4:5]
    nmr = stat_ap[:, 5:6]
    P.op("vector", lambda e, a=src_ap, o=sums: e.reduce_sum(o, a, AX.X),
         reads=[r_src], writes=[b_stat])
    if out_ap.dtype == F32:
        P.op("scalar", lambda e, a=src_ap, o=out_ap, ac=sumsq:
             e.activation(o, a, AF.Square, accum_out=ac),
             reads=[r_src], writes=[w_out, b_stat])
    else:
        P.op("scalar", lambda e, a=src_ap, o=sq_ap, ac=sumsq:
             e.activation(o, a, AF.Square, accum_out=ac),
             reads=[r_src], writes=[b_sq, b_stat])
    P.op("vector", lambda e, o=mu, a=sums: e.tensor_single_scalar(o, a, 1.0 / n, ALU.mult),
         reads=[b_stat], writes=[b_stat])
    P.op("vector", lambda e, o=var, a=mu: e.tensor_tensor(o, a, a, ALU.mult),
         reads=[b_stat], writes=[b_stat])
    P.op("vector", lambda e, o=var, a=sumsq, m=var:
         e.scalar_tensor_tensor(o, a, 1.0 / n, m, ALU.mult, ALU.subtract),
         reads=[b_stat], writes=[b_stat])
    P.op("scalar", lambda e, o=rstd, a=var, b=eps_ap: e.activation(o, a, AF.Ln, bias=b),
         reads=[b_stat, C.b_const], writes=[b_stat])
    P.op("scalar", lambda e, o=rstd, a=rstd: e.activation(o, a, AF.Exp, scale=-0.5),
         reads=[b_stat], writes=[b_stat])
    P.op("vector", lambda e, o=nmr, a=mu, r=rstd:
         e.scalar_tensor_tensor(o, a, -1.0, r, ALU.mult, ALU.mult),
         reads=[b_stat], writes=[b_stat])
    P.op("scalar", lambda e, o=out_ap, a=src_ap, r=rstd, b=nmr:
         e.activation(o, a, AF.Identity, bias=b, scale=r),
         reads=[r_src, b_stat], writes=[w_out])
    return rstd, nmr


def transpose128(C, src_ap, dst_ap, r_src, w_dst, dtype, copy_eng="vector"):
    P = C.P
    ncols = src_ap.shape[-1]
    if dtype == BF16:
        bank, bbuf = C.next_pbf()
        ident = C.idb_sb
        pslice = bank[0:ncols, 0:128]
    else:
        bank, bbuf = C.next_pf()
        ident = C.idf_sb
        pslice = bank[0:ncols, 0:128]
    P.op("tensor", lambda e, o=pslice, i=src_ap, idt=ident[:]: e.transpose(o, i, idt),
         reads=[r_src, C.b_const], writes=[bbuf])
    P.op(copy_eng, lambda e, o=dst_ap, i=pslice:
         (e.tensor_copy(o, i) if copy_eng == "vector" else e.activation(o, i, AF.Identity)),
         reads=[bbuf], writes=[w_dst])


def emit_l1ffn(C, src, b_src, dst, b_dst):
    nc, P = C.nc, C.P

    def quant_chain(u_ap, out_ap, r_u, w_out, s1, s2, s3, s4, bs, bufs):
        b1, b2, b3, b4, bb = bufs
        bmax = bs[:, 0:32]
        scl = bs[:, 32:64]
        rscl = bs[:, 64:96]
        u3d = u_ap.rearrange("p (a b) -> p a b", a=32)
        P.op("vector", lambda e: e.reduce_max(bmax, u3d, AX.X, apply_absolute_value=True),
             reads=[r_u], writes=[bb])
        P.op("vector", lambda e: e.tensor_scalar(scl, bmax, 1.0 / 7.5, 1e-30, ALU.mult, ALU.max),
             reads=[bb], writes=[bb])
        P.op("vector", lambda e: e.reciprocal(rscl, scl),
             reads=[bb], writes=[bb])
        rb = rscl.rearrange("p (a o) -> p a o", a=32).broadcast_to([128, 32, 32])
        sb = scl.rearrange("p (a o) -> p a o", a=32).broadcast_to([128, 32, 32])
        s1d = s1.rearrange("p (a b) -> p a b", a=32)
        P.op("vector", lambda e: e.tensor_tensor(s1d, u3d, rb, ALU.mult),
             reads=[r_u, bb], writes=[b1])
        P.op("scalar", lambda e: e.activation(s2, s1, AF.Abs),
             reads=[b1], writes=[b2])
        P.op("vector", lambda e: e.tensor_single_scalar(s3, s2, 2.0, ALU.is_ge),
             reads=[b2], writes=[b3])
        P.op("vector", lambda e: e.tensor_single_scalar(s2, s2, 4.0, ALU.is_ge),
             reads=[b2], writes=[b2])
        P.op("vector", lambda e: e.tensor_scalar(s4, s3, -4.0, 8.0, ALU.mult, ALU.add),
             reads=[b3], writes=[b4])
        P.op("vector", lambda e: e.scalar_tensor_tensor(s4, s2, -2.0, s4, ALU.mult, ALU.add),
             reads=[b2, b4], writes=[b4])
        P.op("vector", lambda e: e.tensor_scalar(s3, s3, 0.125, 0.125, ALU.mult, ALU.add),
             reads=[b3], writes=[b3])
        P.op("vector", lambda e: e.scalar_tensor_tensor(s3, s2, 0.25, s3, ALU.mult, ALU.add),
             reads=[b2, b3], writes=[b3])
        P.op("vector", lambda e: e.tensor_tensor(s2, s1, s4, ALU.mult),
             reads=[b1, b4], writes=[b2])
        P.op("vector", lambda e: e.tensor_scalar(s2, s2, MAGIC, MAGIC, ALU.add, ALU.subtract),
             reads=[b2], writes=[b2])
        P.op("vector", lambda e: e.tensor_tensor(s1, s2, s3, ALU.mult),
             reads=[b2, b3], writes=[b1])
        od = out_ap.rearrange("p (a b) -> p a b", a=32)
        P.op("vector", lambda e: e.tensor_tensor(od, s1d, sb, ALU.mult),
             reads=[b1, bb], writes=[w_out])

    with ExitStack() as ctx:
        E = ctx.enter_context
        wf = E(nc.sbuf_tensor("l1f_wf", [128, 8, 1024], F32))
        wdq = E(nc.sbuf_tensor("l1f_wdq", [128, 8, 1024], BF16))
        wdT = E(nc.sbuf_tensor("l1f_wdT", [128, 8, 1024], BF16))
        s1 = E(nc.sbuf_tensor("l1f_s1", [128, 1024], F32))
        s2 = E(nc.sbuf_tensor("l1f_s2", [128, 1024], F32))
        s3 = E(nc.sbuf_tensor("l1f_s3", [128, 1024], F32))
        s4 = E(nc.sbuf_tensor("l1f_s4", [128, 1024], F32))
        bs = E(nc.sbuf_tensor("l1f_bs", [128, 96], F32))
        b_wf, b_wdq, b_wdT = Buf("wf"), Buf("wdq"), Buf("wdT")
        b1, b2, b3, b4, bb = Buf(), Buf(), Buf(), Buf(), Buf()

        P.op("sync", lambda e: e.dma_start(
            out=wf[:], in_=C.dram["l1f_w"].rearrange("(c p) k -> p c k", p=128)),
            writes=[b_wf])
        for jc in range(8):
            quant_chain(wf[:, jc, :], wdq[:, jc, :], b_wf, b_wdq,
                        s1[:], s2[:], s3[:], s4[:], bs[:], (b1, b2, b3, b4, bb))
        for jc in range(8):
            for kc in range(8):
                transpose128(C, wdq[:, jc, kc * 128:(kc + 1) * 128],
                             wdT[:, kc, jc * 128:(jc + 1) * 128],
                             b_wdq, b_wdT, BF16)
        P.barrier()

        xt = E(nc.sbuf_tensor("l1f_xt", [128, 4, 1024], F32))
        u5 = E(nc.sbuf_tensor("l1f_u5", [128, 1024], F32))
        xdtm = E(nc.sbuf_tensor("l1f_xdtm", [128, 1024], BF16))
        xdfm = E(nc.sbuf_tensor("l1f_xdfm", [128, 8, 512], BF16))
        outst = E(nc.sbuf_tensor("l1f_outst", [128, 512], F32))
        otm = E(nc.sbuf_tensor("l1f_otm", [128, 4, 1024], F32))
        zt = E(nc.sbuf_tensor("l1f_zt", [128, 1024], F32))
        xo = E(nc.sbuf_tensor("l1f_xo", [128, 1024], F32))
        stat = E(nc.sbuf_tensor("l1f_stat", [128, 8], F32))
        b_xt = [Buf() for _ in range(4)]
        b_u5, b_xdtm, b_xdfm = Buf(), Buf(), Buf()
        b_outst, b_otm, b_zt, b_xo, b_stat = Buf(), Buf(), Buf(), Buf(), Buf()

        for s in range(NSUP):
            for i in range(4):
                tile = s * 4 + i
                P.op("sync", lambda e, t=tile, i=i: e.dma_start(
                    out=xt[:, i, :], in_=src[t * 128:(t + 1) * 128, :]),
                    reads=[b_src[tile]], writes=[b_xt[i]])
                ln_tile(C, xt[:, i, :], u5[:], C.epsc[:, 0:1], b_xt[i], b_u5,
                        s1[:], b1, stat[:], b_stat)
                quant_chain(u5[:], xdtm[:], b_u5, b_xdtm,
                            s1[:], s2[:], s3[:], s4[:], bs[:], (b1, b2, b3, b4, bb))
                for c in range(8):
                    transpose128(C, xdtm[:, c * 128:(c + 1) * 128],
                                 xdfm[:, c, i * 128:(i + 1) * 128],
                                 b_xdtm, b_xdfm, BF16)
            for jc in range(8):
                bank, bbk = C.next_pf()
                for kc in range(8):
                    P.op("tensor", lambda e, bk=bank, kc=kc, jc=jc: e.matmul(
                        bk[:], wdT[:, kc, jc * 128:(jc + 1) * 128], xdfm[:, kc, :],
                        start=(kc == 0), stop=(kc == 7)),
                        reads=[b_wdT, b_xdfm], writes=[bbk], inc=(kc == 7))
                P.op("scalar", lambda e, bk=bank, jc=jc: e.activation(
                    outst[:], bk[:], AF.Identity, bias=C.bias_sb["l1f_b"][:, jc:jc + 1]),
                    reads=[bbk, C.b_const], writes=[b_outst])
                for i in range(4):
                    transpose128(C, outst[:, i * 128:(i + 1) * 128],
                                 otm[:, i, jc * 128:(jc + 1) * 128],
                                 b_outst, b_otm, F32)
            for i in range(4):
                tile = s * 4 + i
                rstd, nmr = ln_tile(C, otm[:, i, :], zt[:], C.epsc[:, 0:1], b_otm, b_zt,
                                    s1[:], b1, stat[:], b_stat)
                P.op("vector", lambda e, i=i: e.tensor_tensor(
                    xo[:], zt[:], xt[:, i, :], ALU.add),
                    reads=[b_zt, b_xt[i]], writes=[b_xo])
                P.op("sync", lambda e, t=tile: e.dma_start(
                    out=dst[t * 128:(t + 1) * 128, :], in_=xo[:]),
                    reads=[b_xo], writes=[b_dst[tile]])
        P.barrier()


def attn_phaseB(C, qs, nst, colstart_fn, mask_diag, qfm, b_q, kfm, b_k, vtm, b_v,
                negc, b_negc, attnst, b_att, Pb, b_Pb, denb, b_den, rb, b_rb):
    P = C.P
    pf, bpf = C.pf, C.b_pf

    for hp in range(8):
        def emit_S(st):
            cs = colstart_fn(st)
            pspan = slice(cs, 512)
            span = slice(qs * 512 + cs, qs * 512 + 512)
            pbase = (st % 2) * 2
            for h in range(2):
                base = h * 64
                sbank, b_sbank = pf[pbase + h], bpf[pbase + h]
                P.op("tensor", lambda e, sb=sbank, hp=hp, st=st, base=base, sp=span, ps=pspan:
                     e.matmul(sb[:, ps], kfm[base:base + 64, hp, st * 128:(st + 1) * 128],
                              qfm[base:base + 64, hp, sp], start=True, stop=True,
                              skip_group_check=True),
                     reads=[b_q, b_k], writes=[b_sbank])

        def emit_exp(st):
            cs = colstart_fn(st)
            pspan = slice(cs, 512)
            pbase = (st % 2) * 2
            par = st % 2
            for h in range(2):
                sbank, b_sbank = pf[pbase + h], bpf[pbase + h]
                P.op("scalar", lambda e, sb=sbank, hp=hp, h=h, ps=pspan, par=par:
                     e.activation(Pb[:, par, h, ps], sb[:, ps], AF.Exp,
                                  bias=negc[:, 2 * hp + h:2 * hp + h + 1], scale=0.125),
                     reads=[b_sbank, b_negc], writes=[b_Pb])
                if mask_diag and st >= 4 * qs:
                    P.op("vector", lambda e, h=h, cs=cs, par=par:
                         e.tensor_tensor(Pb[:, par, h, cs:cs + 128],
                                         Pb[:, par, h, cs:cs + 128],
                                         C.tri_sb[:], ALU.mult),
                         reads=[b_Pb, C.b_const], writes=[b_Pb])

        def emit_av(st):
            cs = colstart_fn(st)
            pspan = slice(cs, 512)
            par = st % 2
            first = st == 0
            last = st == nst - 1
            for h in range(2):
                base = h * 64
                P.op("tensor", lambda e, hp=hp, st=st, h=h, base=base, ps=pspan,
                     first=first, last=last, par=par:
                     e.matmul(pf[4][base:base + 64, ps],
                              vtm[:, st, hp, base:base + 64], Pb[:, par, h, ps],
                              start=first, stop=last,
                              tile_position=(0, base), skip_group_check=True),
                     reads=[b_v, b_Pb], writes=[C.b_pf[4]], inc=True)
                dp = 32 * h
                P.op("tensor", lambda e, h=h, ps=pspan, dp=dp, first=first, last=last, par=par:
                     e.matmul(pf[5][dp:dp + 1, ps], C.onescb_sb[:], Pb[:, par, h, ps],
                              start=first, stop=last,
                              tile_position=(0, dp), skip_group_check=True),
                     reads=[C.b_const, b_Pb], writes=[C.b_pf[5]], inc=True)

        emit_S(0)
        for st in range(nst):
            emit_exp(st)
            if st + 1 < nst:
                emit_S(st + 1)
            emit_av(st)

        for h in range(2):
            dp = 32 * h
            base = h * 64
            P.op("vector", lambda e, dp=dp: e.reciprocal(denb[dp:dp + 1, :], pf[5][dp:dp + 1, 0:512]),
                 reads=[C.b_pf[5]], writes=[b_den])
            bb, b_bb = pf[h], bpf[h]
            P.op("tensor", lambda e, dp=dp, base=base, bb=bb:
                 e.matmul(bb[base:base + 64, 0:512], C.onesf_sb[dp:dp + 1, 0:64],
                          denb[dp:dp + 1, :], start=True, stop=True,
                          tile_position=(dp, base), skip_group_check=True),
                 reads=[b_den, C.b_const], writes=[b_bb])
            P.op("vector", lambda e, base=base, bb=bb: e.tensor_copy(
                rb[base:base + 64, :], bb[base:base + 64, 0:512]),
                 reads=[b_bb], writes=[b_rb])
            P.op("vector", lambda e, base=base, hp=hp: e.tensor_tensor(
                attnst[base:base + 64, hp, :], pf[4][base:base + 64, 0:512],
                rb[base:base + 64, :], ALU.mult),
                 reads=[C.b_pf[4], b_rb], writes=[b_att])


def emit_l0attn(C, src, b_src, dst, b_dst):
    nc, P = C.nc, C.P
    with ExitStack() as bs:
        E = bs.enter_context
        wat = E(nc.sbuf_tensor("l0_watt", [128, 8, 3072], BF16))
        wpr = E(nc.sbuf_tensor("l0_wproj", [128, 8, 1024], BF16))
        qfm = E(nc.sbuf_tensor("l0_qfm", [128, 8, 2048], BF16))
        kfm = E(nc.sbuf_tensor("l0_kfm", [128, 8, 2048], BF16))
        vtm = E(nc.sbuf_tensor("l0_vtm", [128, 16, 8, 128], BF16))
        negc = E(nc.sbuf_tensor("l0_negc", [128, 16], F32))
        nrm = E(nc.sbuf_tensor("l0_nrm", [128, 32], F32))
        ncs = E(nc.sbuf_tensor("l0_ncs", [16, 8], F32))
        ncrow = E(nc.sbuf_tensor("l0_ncrow", [1, 16], F32))
        b_wat, b_wpr, b_q, b_k, b_v = Buf(), Buf(), Buf(), Buf(), Buf()
        b_negc, b_nrm, b_ncs, b_ncrow = Buf(), Buf(), Buf(), Buf()
        P.op("gpsimd", lambda e: e.dma_start(
            out=wat[:], in_=C.dram["w_attnT"].rearrange("(c p) o -> p c o", p=128)),
            writes=[b_wat])
        P.op("gpsimd", lambda e: e.dma_start(
            out=wpr[:], in_=C.dram["w_projT"].rearrange("(c p) o -> p c o", p=128)),
            writes=[b_wpr])
        bat = C.bias_sb["b_attn_fm"]
        bpj = C.bias_sb["b_proj_fm"]

        for b in range(NB):
            with ExitStack() as sA:
                EA = sA.enter_context
                xt = EA(nc.sbuf_tensor(_un("l0a_xt"), [128, 2, 1024], F32))
                u0t = EA(nc.sbuf_tensor(_un("l0a_u0t"), [128, 1024], F32))
                ttm = EA(nc.sbuf_tensor(_un("l0a_ttm"), [128, 1024], BF16))
                tfm = EA(nc.sbuf_tensor(_un("l0a_tfm"), [128, 2, 8, 512], BF16))
                vst = EA(nc.sbuf_tensor(_un("l0a_vst"), [128, 8, 512], BF16))
                sq = EA(nc.sbuf_tensor(_un("l0a_sq"), [128, 1024], F32))
                stat = EA(nc.sbuf_tensor(_un("l0a_stat"), [128, 8], F32))
                b_xt = [Buf(), Buf()]
                b_u0t, b_ttm, b_tfm, b_vst, b_sq, b_stat = (Buf() for _ in range(6))
                P.op("vector", lambda e: e.memset(nrm[:], 0.0), writes=[b_nrm])

                def prep(j):
                    par = j % 2
                    for i in range(4):
                        tile = b * 16 + j * 4 + i
                        P.op("sync", lambda e, t=tile, xt=xt, par=par: e.dma_start(
                            out=xt[:, par, :], in_=src[t * 128:(t + 1) * 128, :]),
                            reads=[b_src[tile]], writes=[b_xt[par]])
                        ln_tile(C, xt[:, par, :], u0t[:], C.epsc[:, 0:1], b_xt[par],
                                b_u0t, sq[:], b_sq, stat[:], b_stat)
                        ln_tile(C, u0t[:], ttm[:], C.epsc[:, 1:2], b_u0t, b_ttm,
                                sq[:], b_sq, stat[:], b_stat)
                        for c in range(8):
                            transpose128(C, ttm[:, c * 128:(c + 1) * 128],
                                         tfm[:, par, c, i * 128:(i + 1) * 128],
                                         b_ttm, b_tfm, BF16)

                prep(0)
                u0v = u0t[:].rearrange("p (s n) -> p s n", s=2)
                for j in range(4):
                    par = j % 2
                    span = slice(j * 512, (j + 1) * 512)
                    for oc in range(24):
                        bank, bbk = C.pf[oc % 3], C.b_pf[oc % 3]
                        for kc in range(8):
                            P.op("tensor", lambda e, bank=bank, kc=kc, oc=oc, tfm=tfm, par=par:
                                 e.matmul(bank[:], wat[:, kc, oc * 128:(oc + 1) * 128],
                                          tfm[:, par, kc, :], start=(kc == 0), stop=(kc == 7)),
                                 reads=[b_wat, b_tfm], writes=[bbk], inc=(kc == 7))
                        if oc < 8:
                            dstap, bd = qfm[:, oc, span], b_q
                        elif oc < 16:
                            dstap, bd = kfm[:, oc - 8, span], b_k
                        else:
                            dstap, bd = vst[:, oc - 16, :], b_vst
                        P.op("scalar", lambda e, bank=bank, oc=oc, dstap=dstap:
                             e.activation(dstap, bank[:], AF.Identity,
                                          bias=bat[:, oc:oc + 1]),
                             reads=[bbk, C.b_const], writes=[bd])
                        if oc < 16:
                            slot = oc % 2
                            P.op("scalar", lambda e, d=dstap, slot=slot, u0t=u0t:
                                 e.activation(u0v[:, slot, :], d, AF.Square),
                                 reads=[bd], writes=[b_u0t])
                            for i in range(4):
                                c = oc % 8
                                P.op("tensor", lambda e, slot=slot, i=i, c=c, u0t=u0t:
                                     e.matmul(
                                         C.pf[3][:, i * 16 + 2 * c:i * 16 + 2 * c + 2],
                                         u0v[:, slot, i * 128:(i + 1) * 128], C.eh_sb[:],
                                         start=True, stop=True, skip_group_check=True),
                                     reads=[b_u0t, C.b_const], writes=[C.b_pf[3]])
                        if oc == 7 or oc == 15:
                            off = 0 if oc == 7 else 16
                            for i in range(4):
                                P.op("vector", lambda e, off=off, i=i: e.tensor_tensor(
                                    nrm[:, off:off + 16], nrm[:, off:off + 16],
                                    C.pf[3][:, i * 16:(i + 1) * 16], ALU.max),
                                    reads=[b_nrm, C.b_pf[3]], writes=[b_nrm])
                    for c in range(8):
                        for i in range(4):
                            transpose128(C, vst[:, c, i * 128:(i + 1) * 128],
                                         vtm[:, j * 4 + i, c, :], b_vst, b_v, BF16)
                    if j < 3:
                        prep(j + 1)
                negc_calc(C, nrm, b_nrm, ncs, b_ncs, ncrow, b_ncrow, negc, b_negc)
            P.barrier()
            with ExitStack() as sB:
                EB = sB.enter_context
                Pb = EB(nc.sbuf_tensor(_un("l0b_pb"), [128, 2, 2, 512], BF16))
                attnst = EB(nc.sbuf_tensor(_un("l0b_attnst"), [128, 8, 512], BF16))
                denb = EB(nc.sbuf_tensor(_un("l0b_denb"), [64, 512], F32))
                rb = EB(nc.sbuf_tensor(_un("l0b_rb"), [128, 512], F32))
                projst = EB(nc.sbuf_tensor(_un("l0b_projst"), [128, 512], F32))
                u1s = EB(nc.sbuf_tensor(_un("l0b_u1s"), [128, 4, 1024], F32))
                xtB = EB(nc.sbuf_tensor(_un("l0b_xt"), [128, 1024], F32))
                sqB = EB(nc.sbuf_tensor(_un("l0b_sq"), [128, 1024], F32))
                statB = EB(nc.sbuf_tensor(_un("l0b_stat"), [128, 8], F32))
                b_Pb, b_att, b_den, b_rb = Buf(), Buf(), Buf(), Buf()
                b_projst, b_u1s, b_xtB, b_sqB, b_statB = Buf(), Buf(), Buf(), Buf(), Buf()
                for qs in range(4):
                    attn_phaseB(C, qs, 4 * qs + 4,
                                lambda st, qs=qs: max(0, st - 4 * qs) * 128,
                                True, qfm, b_q, kfm, b_k, vtm, b_v,
                                negc, b_negc, attnst, b_att, Pb, b_Pb,
                                denb, b_den, rb, b_rb)
                    for i in range(4):
                        tile = b * 16 + qs * 4 + i
                        P.op("sync", lambda e, t=tile, i=i, xtB=xtB: e.dma_start(
                            out=xtB[:], in_=src[t * 128:(t + 1) * 128, :]),
                            reads=[b_src[tile]], writes=[b_xtB])
                        ln_tile(C, xtB[:], u1s[:, i, :], C.epsc[:, 0:1], b_xtB, b_u1s,
                                sqB[:], b_sqB, statB[:], b_statB)
                    for oc in range(8):
                        bank, bbk = C.pf[2 + (oc % 2)], C.b_pf[2 + (oc % 2)]
                        for kc in range(8):
                            P.op("tensor", lambda e, bank=bank, kc=kc, oc=oc, attnst=attnst:
                                 e.matmul(bank[:], wpr[:, kc, oc * 128:(oc + 1) * 128],
                                          attnst[:, kc, :], start=(kc == 0), stop=(kc == 7)),
                                 reads=[b_wpr, b_att], writes=[bbk], inc=(kc == 7))
                        P.op("scalar", lambda e, bank=bank, oc=oc, projst=projst:
                             e.activation(projst[:], bank[:], AF.Identity,
                                          bias=bpj[:, oc:oc + 1]),
                             reads=[bbk, C.b_const], writes=[b_projst])
                        for i in range(4):
                            tb, btb = C.pf[4 + (i % 2)], C.b_pf[4 + (i % 2)]
                            P.op("tensor", lambda e, tb=tb, i=i, projst=projst: e.transpose(
                                tb[0:128, 0:128], projst[:, i * 128:(i + 1) * 128],
                                C.idf_sb[:]),
                                reads=[b_projst, C.b_const], writes=[btb])
                            P.op("vector", lambda e, tb=tb, i=i, oc=oc, u1s=u1s: e.tensor_tensor(
                                u1s[:, i, oc * 128:(oc + 1) * 128], tb[0:128, 0:128],
                                u1s[:, i, oc * 128:(oc + 1) * 128], ALU.add),
                                reads=[btb, b_u1s], writes=[b_u1s])
                    for i in range(4):
                        tile = b * 16 + qs * 4 + i
                        P.op("sync", lambda e, t=tile, i=i, u1s=u1s: e.dma_start(
                            out=C.u1d[t * 128:(t + 1) * 128, :], in_=u1s[:, i, :]),
                            reads=[b_u1s], writes=[C.b_u1d[tile]])
            P.barrier()
    P.barrier()
    with ExitStack() as sC:
        EC = sC.enter_context
        wfc = EC(nc.sbuf_tensor("l0c_wfc", [128, 8, 4096], BF16))
        wml = EC(nc.sbuf_tensor("l0c_wml", [128, 32, 1024], BF16))
        u1sC = EC(nc.sbuf_tensor("l0c_u1", [128, 4, 1024], F32))
        ut = EC(nc.sbuf_tensor("l0c_ut", [128, 1024], F32))
        ttmC = EC(nc.sbuf_tensor("l0c_ttm", [128, 1024], BF16))
        tfmC = EC(nc.sbuf_tensor("l0c_tfm", [128, 2, 8, 512], BF16))
        hfmC = EC(nc.sbuf_tensor("l0c_hfm", [128, 32, 512], BF16))
        sqC = EC(nc.sbuf_tensor("l0c_sq", [128, 1024], F32))
        statC = EC(nc.sbuf_tensor("l0c_stat", [128, 8], F32))
        mlpstC = ttmC[:].bitcast(F32)
        b_wfc, b_wml, b_u1, b_ut, b_ttm2, b_tfm2, b_hfm = (Buf() for _ in range(7))
        b_mlpst = b_ttm2
        b_zt, b_xt2, b_xo2, b_sq2, b_stat2 = (Buf() for _ in range(5))
        P.op("gpsimd", lambda e: e.dma_start(
            out=wfc[:], in_=C.dram["w_fcT"].rearrange("(c p) o -> p c o", p=128)),
            writes=[b_wfc])
        P.op("gpsimd", lambda e: e.dma_start(
            out=wml[:], in_=C.dram["w_mlpT"].rearrange("(c p) o -> p c o", p=128)),
            writes=[b_wml])
        bfc = C.bias_sb["b_fc_fm"]
        bml = C.bias_sb["b_mlp_fm"]

        def prepC(s):
            par = s % 2
            for i in range(4):
                tile = s * 4 + i
                P.op("sync", lambda e, t=tile, ut=ut: e.dma_start(
                    out=ut[:], in_=C.u1d[t * 128:(t + 1) * 128, :]),
                    reads=[C.b_u1d[tile]], writes=[b_ut])
                ln_tile(C, ut[:], ttmC[:], C.epsc[:, 1:2], b_ut, b_ttm2,
                        sqC[:], b_sq2, statC[:], b_stat2)
                for c in range(8):
                    transpose128(C, ttmC[:, c * 128:(c + 1) * 128],
                                 tfmC[:, par, c, i * 128:(i + 1) * 128],
                                 b_ttm2, b_tfm2, BF16)

        prepC(0)
        for s in range(NSUP):
            par = s % 2
            for oc in range(32):
                bank, bbk = C.pf[oc % 3], C.b_pf[oc % 3]
                for kc in range(8):
                    P.op("tensor", lambda e, bank=bank, kc=kc, oc=oc, par=par:
                         e.matmul(bank[:], wfc[:, kc, oc * 128:(oc + 1) * 128],
                                  tfmC[:, par, kc, :], start=(kc == 0), stop=(kc == 7)),
                         reads=[b_wfc, b_tfm2], writes=[bbk], inc=(kc == 7))
                P.op("scalar", lambda e, bank=bank, oc=oc:
                     e.activation(hfmC[:, oc, :], bank[:], AF.Gelu_apprx_tanh,
                                  bias=bfc[:, oc:oc + 1]),
                     reads=[bbk, C.b_const], writes=[b_hfm])
            if s < NSUP - 1:
                prepC(s + 1)
            for i in range(4):
                tile = s * 4 + i
                P.op("sync", lambda e, t=tile, i=i: e.dma_start(
                    out=u1sC[:, i, :], in_=C.u1d[t * 128:(t + 1) * 128, :]),
                    reads=[C.b_u1d[tile]], writes=[b_u1])
            for oc in range(8):
                bank, bbk = C.pf[3], C.b_pf[3]
                for fc in range(32):
                    P.op("tensor", lambda e, bank=bank, fc=fc, oc=oc:
                         e.matmul(bank[:], wml[:, fc, oc * 128:(oc + 1) * 128],
                                  hfmC[:, fc, :], start=(fc == 0), stop=(fc == 31)),
                         reads=[b_wml, b_hfm], writes=[bbk], inc=(fc == 31))
                P.op("scalar", lambda e, bank=bank, oc=oc:
                     e.activation(mlpstC, bank[:], AF.Identity,
                                  bias=bml[:, oc:oc + 1]),
                     reads=[bbk, C.b_const], writes=[b_mlpst])
                for i in range(4):
                    tb, btb = C.pf[4 + (i % 2)], C.b_pf[4 + (i % 2)]
                    P.op("tensor", lambda e, tb=tb, i=i: e.transpose(
                        tb[0:128, 0:128], mlpstC[:, i * 128:(i + 1) * 128], C.idf_sb[:]),
                        reads=[b_mlpst, C.b_const], writes=[btb])
                    P.op("vector", lambda e, tb=tb, i=i, oc=oc: e.tensor_tensor(
                        u1sC[:, i, oc * 128:(oc + 1) * 128], tb[0:128, 0:128],
                        u1sC[:, i, oc * 128:(oc + 1) * 128], ALU.add),
                        reads=[btb, b_u1], writes=[b_u1])
            for i in range(4):
                tile = s * 4 + i
                ln_tile(C, u1sC[:, i, :], ut[:], C.epsc[:, 0:1], b_u1, b_ut,
                        sqC[:], b_sq2, statC[:], b_stat2)
                P.op("sync", lambda e, t=tile: e.dma_start(
                    out=sqC[:], in_=src[t * 128:(t + 1) * 128, :]),
                    reads=[b_src[tile]], writes=[b_sq2])
                P.op("vector", lambda e: e.tensor_tensor(ut[:], ut[:], sqC[:], ALU.add),
                     reads=[b_ut, b_sq2], writes=[b_ut])
                P.op("sync", lambda e, t=tile: e.dma_start(
                    out=dst[t * 128:(t + 1) * 128, :], in_=ut[:]),
                    reads=[b_ut], writes=[b_dst[tile]])
    P.barrier()


def emit_l0ffn(C, src, b_src, dst, b_dst):
    nc, P = C.nc, C.P
    with ExitStack() as bs:
        E = bs.enter_context
        wg = E(nc.sbuf_tensor("f_wg", [128, 8, 8], BF16))
        b2m = E(nc.sbuf_tensor("f_b2m", [8, 1024], BF16))
        gam = E(nc.sbuf_tensor("f_gam", [128, 1024], F32))
        bet = E(nc.sbuf_tensor("f_bet", [128, 1024], F32))
        b_wg, b_b2m, b_gam, b_bet, b_gbrow = (Buf() for _ in range(5))
        P.op("gpsimd", lambda e: e.dma_start(
            out=wg[:], in_=C.dram["wg"].rearrange("(c p) o -> p c o", p=128)),
            writes=[b_wg])
        P.op("gpsimd", lambda e: e.dma_start(out=b2m[:], in_=C.dram["b2_mat"].ap()),
             writes=[b_b2m])
        with nc.sbuf_tensor("f_gbrow", [1, 2, 1024], F32) as gbrow:
            P.op("sync", lambda e: e.dma_start(out=gbrow[0:1, 0, :], in_=C.dram["gamma"].ap()),
                 writes=[b_gbrow])
            P.op("sync", lambda e: e.dma_start(out=gbrow[0:1, 1, :], in_=C.dram["beta"].ap()),
                 writes=[b_gbrow])
            for (gi, tgt, bt) in ((0, gam, b_gam), (1, bet, b_bet)):
                for half in range(2):
                    P.op("tensor", lambda e, gi=gi, half=half: e.matmul(
                        C.pf[4][:, 0:512], C.onesf_sb[0:1, 0:128],
                        gbrow[0:1, gi, half * 512:(half + 1) * 512],
                        start=True, stop=True, skip_group_check=True),
                        reads=[b_gbrow, C.b_const], writes=[C.b_pf[4]])
                    P.op("vector", lambda e, tgt=tgt, half=half: e.tensor_copy(
                        tgt[:, half * 512:(half + 1) * 512], C.pf[4][:, 0:512]),
                        reads=[C.b_pf[4]], writes=[bt])
        P.barrier()

        with ExitStack() as sP:
            EP = sP.enter_context
            xt = EP(nc.sbuf_tensor("f_xt", [128, 1024], F32))
            ttm = EP(nc.sbuf_tensor("f_ttm", [128, 1024], BF16))
            u3st = EP(nc.sbuf_tensor("f_u3st", [128, 8, 512], BF16))
            lgst = EP(nc.sbuf_tensor("f_lgst", [8, 512], F32))
            lgtm = EP(nc.sbuf_tensor("f_lgtm", [128, 4, 8], F32))
            g1 = EP(nc.sbuf_tensor("f_g1", [128, 4, 8], F32))
            g2 = EP(nc.sbuf_tensor("f_g2", [128, 4, 8], F32))
            g3 = EP(nc.sbuf_tensor("f_g3", [128, 4, 8], F32))
            g4 = EP(nc.sbuf_tensor("f_g4", [128, 4, 8], F32))
            gsc = EP(nc.sbuf_tensor("f_gsc", [128, 20], F32))
            grst = EP(nc.sbuf_tensor("f_grst", [8, 4, 128], F32))
            sq = EP(nc.sbuf_tensor("f_sq", [128, 1024], F32))
            stat = EP(nc.sbuf_tensor("f_stat", [128, 8], F32))
            b_xt, b_ttm, b_u3st, b_lgst, b_lgtm = (Buf() for _ in range(5))
            b_g1, b_g2, b_g3, b_g4, b_gsc, b_grst, b_sq, b_stat = (Buf() for _ in range(8))
            for s in range(NSUP):
                for i in range(4):
                    tile = s * 4 + i
                    P.op("sync", lambda e, t=tile: e.dma_start(
                        out=xt[:], in_=src[t * 128:(t + 1) * 128, :]),
                        reads=[b_src[tile]], writes=[b_xt])
                    ln_tile(C, xt[:], ttm[:], C.epsc[:, 0:1], b_xt, b_ttm,
                            sq[:], b_sq, stat[:], b_stat)
                    for c in range(8):
                        transpose128(C, ttm[:, c * 128:(c + 1) * 128],
                                     u3st[:, c, i * 128:(i + 1) * 128],
                                     b_ttm, b_u3st, BF16)
                P.op("sync", lambda e, s=s: e.dma_start(
                    out=C.u3fmd[s].rearrange("p (c n) -> p c n", c=8), in_=u3st[:]),
                    reads=[b_u3st], writes=[C.b_u3fmd[s]])
                for c in range(8):
                    P.op("tensor", lambda e, c=c: e.matmul(
                        C.pf[3][0:8, 0:512], wg[:, c, :], u3st[:, c, :],
                        start=(c == 0), stop=(c == 7), skip_group_check=True),
                        reads=[b_wg, b_u3st], writes=[C.b_pf[3]], inc=(c == 7))
                P.op("scalar", lambda e: e.activation(
                    lgst[:], C.pf[3][0:8, 0:512], AF.Identity,
                    bias=C.bias_sb["bg_col"][:]),
                    reads=[C.b_pf[3], C.b_const], writes=[b_lgst])
                for i in range(4):
                    P.op("tensor", lambda e, i=i: e.transpose(
                        C.pf[2][0:128, i * 8:(i + 1) * 8],
                        lgst[0:8, i * 128:(i + 1) * 128], C.idf_sb[0:8, 0:8]),
                        reads=[b_lgst, C.b_const], writes=[C.b_pf[2]])
                    P.op("vector", lambda e, i=i: e.tensor_copy(
                        lgtm[:, i, :], C.pf[2][0:128, i * 8:(i + 1) * 8]),
                        reads=[C.b_pf[2]], writes=[b_lgtm])
                m1 = gsc[:, 0:4]
                m2 = gsc[:, 4:8]
                dd = gsc[:, 8:12]
                w1 = gsc[:, 12:16]
                w2 = gsc[:, 16:20]
                m1b = m1.rearrange("p (a o) -> p a o", a=4).broadcast_to([128, 4, 8])
                m2b = m2.rearrange("p (a o) -> p a o", a=4).broadcast_to([128, 4, 8])
                w1b = w1.rearrange("p (a o) -> p a o", a=4).broadcast_to([128, 4, 8])
                w2b = w2.rearrange("p (a o) -> p a o", a=4).broadcast_to([128, 4, 8])
                P.op("vector", lambda e: e.reduce_max(m1, lgtm[:], AX.X),
                     reads=[b_lgtm], writes=[b_gsc])
                P.op("vector", lambda e: e.tensor_tensor(g1[:], lgtm[:], m1b, ALU.is_equal),
                     reads=[b_lgtm, b_gsc], writes=[b_g1])
                P.op("vector", lambda e: e.scalar_tensor_tensor(
                    g2[:], g1[:], -1e30, lgtm[:], ALU.mult, ALU.add),
                    reads=[b_g1, b_lgtm], writes=[b_g2])
                P.op("vector", lambda e: e.reduce_max(m2, g2[:], AX.X),
                     reads=[b_g2], writes=[b_gsc])
                P.op("vector", lambda e: e.tensor_tensor(dd, m1, m2, ALU.subtract),
                     reads=[b_gsc], writes=[b_gsc])
                P.op("scalar", lambda e: e.activation(w1, dd, AF.Sigmoid),
                     reads=[b_gsc], writes=[b_gsc])
                P.op("scalar", lambda e: e.activation(w2, dd, AF.Sigmoid, scale=-1.0),
                     reads=[b_gsc], writes=[b_gsc])
                P.op("vector", lambda e: e.tensor_tensor(g3[:], g2[:], m2b, ALU.is_equal),
                     reads=[b_g2, b_gsc], writes=[b_g3])
                P.op("vector", lambda e: e.tensor_tensor(g4[:], g1[:], w1b, ALU.mult),
                     reads=[b_g1, b_gsc], writes=[b_g4])
                P.op("vector", lambda e: e.tensor_tensor(g1[:], g3[:], w2b, ALU.mult),
                     reads=[b_g3, b_gsc], writes=[b_g1])
                P.op("vector", lambda e: e.tensor_tensor(g4[:], g4[:], g1[:], ALU.add),
                     reads=[b_g4, b_g1], writes=[b_g4])
                for i in range(4):
                    P.op("tensor", lambda e, i=i: e.transpose(
                        C.pf[1][0:8, i * 128:(i + 1) * 128],
                        g4[:, i, :], C.idf_sb[:]),
                        reads=[b_g4, C.b_const], writes=[C.b_pf[1]])
                    P.op("vector", lambda e, i=i: e.tensor_copy(
                        grst[0:8, i, :], C.pf[1][0:8, i * 128:(i + 1) * 128]),
                        reads=[C.b_pf[1]], writes=[b_grst])
                P.op("sync", lambda e, s=s: e.dma_start(
                    out=C.gd[:, s * 512:(s + 1) * 512],
                    in_=grst[:].rearrange("p a n -> p (a n)")),
                    reads=[b_grst], writes=[C.b_gd[s]])
        P.barrier()

        with ExitStack() as sM:
            EM = sM.enter_context
            w1bufs = [EM(nc.sbuf_tensor(f"f_w1{i}", [128, 8, 2048], BF16)) for i in range(2)]
            w2sb = EM(nc.sbuf_tensor("f_w2", [128, 16, 1024], BF16))
            u3t = EM(nc.sbuf_tensor("f_u3t", [128, 2, 2, 8, 512], BF16))
            hb = EM(nc.sbuf_tensor("f_h", [128, 16, 512], BF16))
            gbc = EM(nc.sbuf_tensor("f_gbc", [128, 512], BF16))
            g1r = EM(nc.sbuf_tensor("f_g1r", [1, 512], F32))
            g8 = EM(nc.sbuf_tensor("f_g8", [8, 2, 512], BF16))
            mixed = EM(nc.sbuf_tensor("f_mixed", [128, 2, 8, 512], F32))
            xt2 = EM(nc.sbuf_tensor("f_xt2", [128, 1024], F32))
            u3r = EM(nc.sbuf_tensor("f_u3r", [128, 1024], F32))
            zt = EM(nc.sbuf_tensor("f_zt", [128, 1024], F32))
            sq2 = zt
            stat2 = EM(nc.sbuf_tensor("f_stat2", [128, 8], F32))
            b_w1 = [Buf(), Buf()]
            b_w2, b_u3t, b_h, b_gbc, b_g1r, b_g8, b_mixed = (Buf() for _ in range(7))
            b_xt2, b_u3r, b_zt, b_stat2 = (Buf() for _ in range(4))
            b_sq2 = b_zt
            b1f = C.bias_sb["b1_fm"]
            for H in range(8):
                sts = [2 * H, 2 * H + 1]
                hpar = H % 2
                for si, s in enumerate(sts):
                    P.op("sync", lambda e, s=s, si=si, hpar=hpar: e.dma_start(
                        out=u3t[:, hpar, si, :, :],
                        in_=C.u3fmd[s].rearrange("p (c n) -> p c n", c=8)),
                        reads=[C.b_u3fmd[s]], writes=[b_u3t])
                    P.op("gpsimd", lambda e, s=s, si=si: e.dma_start(
                        out=g8[:, si, :], in_=C.gd[:, s * 512:(s + 1) * 512]),
                        reads=[C.b_gd[s]], writes=[b_g8])
                for e_ in range(8):
                    wb, bwb = w1bufs[e_ % 2], b_w1[e_ % 2]
                    P.op("gpsimd", lambda e, e_=e_, wb=wb: e.dma_start(
                        out=wb[:], in_=C.dram["moe_w1"][e_].rearrange(
                            "(c p) f -> p c f", p=128)),
                        writes=[bwb])
                    P.op("gpsimd", lambda e, e_=e_: e.dma_start(
                        out=w2sb[:], in_=C.dram["moe_w2"][e_].rearrange(
                            "(c p) f -> p c f", p=128)),
                        writes=[b_w2])
                    for si, s in enumerate(sts):
                        P.op("sync", lambda e, s=s, e_=e_: e.dma_start(
                            out=g1r[:], in_=C.gd[e_:e_ + 1, s * 512:(s + 1) * 512]),
                            reads=[C.b_gd[s]], writes=[b_g1r])
                        P.op("tensor", lambda e: e.matmul(
                            C.pf[4][:, 0:512], C.onesf_sb[0:1, 0:128], g1r[:],
                            start=True, stop=True, skip_group_check=True),
                            reads=[b_g1r, C.b_const], writes=[C.b_pf[4]])
                        P.op("vector", lambda e: e.tensor_copy(gbc[:], C.pf[4][:, 0:512]),
                             reads=[C.b_pf[4]], writes=[b_gbc])
                        for foc in range(16):
                            bank, bbk = C.pf[foc % 2], C.b_pf[foc % 2]
                            for kc in range(8):
                                P.op("tensor", lambda e, bank=bank, kc=kc, foc=foc, si=si, wb=wb, hpar=hpar:
                                     e.matmul(bank[:], wb[:, kc, foc * 128:(foc + 1) * 128],
                                              u3t[:, hpar, si, kc, :], start=(kc == 0),
                                              stop=(kc == 7)),
                                     reads=[bwb, b_u3t], writes=[bbk], inc=(kc == 7))
                            P.op("scalar", lambda e, bank=bank, foc=foc, e_=e_:
                                 e.activation(hb[:, foc, :], bank[:], AF.Relu,
                                              bias=b1f[:, e_, foc:foc + 1]),
                                 reads=[bbk, C.b_const], writes=[b_h])
                            P.op("vector", lambda e, foc=foc:
                                 e.tensor_tensor(hb[:, foc, :], hb[:, foc, :],
                                                 gbc[:], ALU.mult),
                                 reads=[b_h, b_gbc], writes=[b_h])
                        for dc in range(8):
                            bank, bbk = C.pf[2 + dc % 2], C.b_pf[2 + dc % 2]
                            for fc in range(16):
                                P.op("tensor", lambda e, bank=bank, fc=fc, dc=dc, e_=e_:
                                     e.matmul(bank[:], w2sb[:, fc, dc * 128:(dc + 1) * 128],
                                              hb[:, fc, :], start=(fc == 0),
                                              stop=(fc == 15 and e_ != 0)),
                                     reads=[b_w2, b_h], writes=[bbk],
                                     inc=(fc == 15 and e_ != 0))
                            if e_ == 0:
                                P.op("tensor", lambda e, bank=bank, dc=dc, si=si:
                                     e.matmul(bank[:], b2m[0:8, dc * 128:(dc + 1) * 128],
                                              g8[:, si, :], start=False, stop=True,
                                              skip_group_check=True),
                                     reads=[b_b2m, b_g8], writes=[bbk], inc=True)
                            if e_ == 0:
                                P.op("vector", lambda e, bank=bank, dc=dc, si=si:
                                     e.tensor_copy(mixed[:, si, dc, :], bank[:]),
                                     reads=[bbk], writes=[b_mixed])
                            else:
                                P.op("vector", lambda e, bank=bank, dc=dc, si=si:
                                     e.tensor_tensor(mixed[:, si, dc, :], bank[:],
                                                     mixed[:, si, dc, :], ALU.add),
                                     reads=[bbk, b_mixed], writes=[b_mixed])
                for si, s in enumerate(sts):
                    for i in range(4):
                        tile = s * 4 + i
                        P.op("sync", lambda e, t=tile: e.dma_start(
                            out=xt2[:], in_=src[t * 128:(t + 1) * 128, :]),
                            reads=[b_src[tile]], writes=[b_xt2])
                        ln_tile(C, xt2[:], u3r[:], C.epsc[:, 0:1], b_xt2, b_u3r,
                                sq2[:], b_sq2, stat2[:], b_stat2)
                        for dc in range(8):
                            tb, btb = C.pf[4 + (dc % 2)], C.b_pf[4 + (dc % 2)]
                            P.op("tensor", lambda e, tb=tb, si=si, dc=dc, i=i: e.transpose(
                                tb[0:128, 0:128], mixed[:, si, dc, i * 128:(i + 1) * 128],
                                C.idf_sb[:]),
                                reads=[b_mixed, C.b_const], writes=[btb])
                            P.op("vector", lambda e, tb=tb, dc=dc: e.tensor_tensor(
                                u3r[:, dc * 128:(dc + 1) * 128], tb[0:128, 0:128],
                                u3r[:, dc * 128:(dc + 1) * 128], ALU.add),
                                reads=[btb, b_u3r], writes=[b_u3r])
                        rstd, nmr = ln_tile(C, u3r[:], zt[:], C.epsc[:, 0:1], b_u3r, b_zt,
                                            sq2[:], b_sq2, stat2[:], b_stat2)
                        P.op("vector", lambda e: e.tensor_tensor(zt[:], zt[:], gam[:], ALU.mult),
                             reads=[b_zt, b_gam], writes=[b_zt])
                        P.op("vector", lambda e: e.tensor_tensor(zt[:], zt[:], bet[:], ALU.add),
                             reads=[b_zt, b_bet], writes=[b_zt])
                        ln_tile(C, zt[:], u3r[:], C.epsc[:, 0:1], b_zt, b_u3r,
                                sq2[:], b_sq2, stat2[:], b_stat2)
                        P.op("vector", lambda e: e.tensor_tensor(u3r[:], u3r[:], xt2[:], ALU.add),
                             reads=[b_u3r, b_xt2], writes=[b_u3r])
                        P.op("sync", lambda e, t=tile: e.dma_start(
                            out=dst[t * 128:(t + 1) * 128, :], in_=u3r[:]),
                            reads=[b_u3r], writes=[b_dst[tile]])
    P.barrier()


def tap(C, name, ap, buf):
    if not getattr(C, "want_taps", None) or name not in C.want_taps:
        return
    nc, P = C.nc, C.P
    sh = list(ap.shape)
    d = nc.dram_tensor("tap_" + name, sh, ap.dtype, kind="ExternalOutput")
    P.op("sync", lambda e: e.dma_start(out=d.ap(), in_=ap), reads=[buf])


def negc_calc(C, nrm, b_nrm, ncs, b_ncs, ncrow, b_ncrow, negc, b_negc):
    P = C.P
    P.op("tensor", lambda e: e.transpose(C.pf[2][0:16, 0:128], nrm[:, 0:16], C.idf_sb[:]),
         reads=[b_nrm, C.b_const], writes=[C.b_pf[2]])
    P.op("vector", lambda e: e.reduce_max(ncs[:, 0:1], C.pf[2][0:16, 0:128], AX.X),
         reads=[C.b_pf[2]], writes=[b_ncs])
    P.op("tensor", lambda e: e.transpose(C.pf[2][0:16, 128:256], nrm[:, 16:32], C.idf_sb[:]),
         reads=[b_nrm, C.b_const], writes=[C.b_pf[2]])
    P.op("vector", lambda e: e.reduce_max(ncs[:, 1:2], C.pf[2][0:16, 128:256], AX.X),
         reads=[C.b_pf[2]], writes=[b_ncs])
    P.op("scalar", lambda e: e.activation(ncs[:, 2:3], ncs[:, 0:1], AF.Sqrt),
         reads=[b_ncs], writes=[b_ncs])
    P.op("scalar", lambda e: e.activation(ncs[:, 3:4], ncs[:, 1:2], AF.Sqrt),
         reads=[b_ncs], writes=[b_ncs])
    P.op("vector", lambda e: e.tensor_tensor(ncs[:, 4:5], ncs[:, 2:3], ncs[:, 3:4], ALU.mult),
         reads=[b_ncs], writes=[b_ncs])
    P.op("vector", lambda e: e.tensor_single_scalar(ncs[:, 5:6], ncs[:, 4:5], -0.125, ALU.mult),
         reads=[b_ncs], writes=[b_ncs])
    P.op("tensor", lambda e: e.transpose(C.pf[2][0:1, 256:272], ncs[:, 5:6],
                                         C.idf_sb[0:16, 0:16]),
         reads=[b_ncs, C.b_const], writes=[C.b_pf[2]])
    P.op("vector", lambda e: e.tensor_copy(ncrow[:], C.pf[2][0:1, 256:272]),
         reads=[C.b_pf[2]], writes=[b_ncrow])
    P.op("tensor", lambda e: e.matmul(C.pf[2][:, 272:288], C.onesf_sb[0:1, 0:128],
                                      ncrow[:], start=True, stop=True,
                                      skip_group_check=True),
         reads=[b_ncrow, C.b_const], writes=[C.b_pf[2]])
    P.op("vector", lambda e: e.tensor_copy(negc[:], C.pf[2][:, 272:288]),
         reads=[C.b_pf[2]], writes=[b_negc])


def emit_l1attn(C, src, b_src, dst, b_dst):
    nc, P = C.nc, C.P
    with ExitStack() as bs:
        E = bs.enter_context
        wq = E(nc.sbuf_tensor("l1_wq", [128, 8, 1024], BF16))
        wdn = E(nc.sbuf_tensor("l1_wdn", [128, 8, 512], BF16))
        wku = E(nc.sbuf_tensor("l1_wku", [128, 4, 1024], BF16))
        wvu = E(nc.sbuf_tensor("l1_wvu", [128, 4, 1024], BF16))
        qfm = E(nc.sbuf_tensor("l1_qfm", [128, 8, 2048], BF16))
        kfm = E(nc.sbuf_tensor("l1_kfm", [128, 8, 2048], BF16))
        vtm = E(nc.sbuf_tensor("l1_vtm", [128, 16, 8, 128], BF16))
        negc = E(nc.sbuf_tensor("l1_negc", [128, 16], F32))
        nrm = E(nc.sbuf_tensor("l1_nrm", [128, 32], F32))
        ncs = E(nc.sbuf_tensor("l1_ncs", [16, 8], F32))
        ncrow = E(nc.sbuf_tensor("l1_ncrow", [1, 16], F32))
        cos_sb = E(nc.sbuf_tensor("l1_cos", [128, 16, 32], F32))
        sin_sb = E(nc.sbuf_tensor("l1_sin", [128, 16, 32], F32))
        b_wq, b_wdn, b_wku, b_wvu = (Buf() for _ in range(4))
        b_q, b_k, b_v, b_negc, b_nrm, b_ncs, b_ncrow = (Buf() for _ in range(7))
        for (sb, nm, bw) in [(wq, "wqP", b_wq), (wdn, "w_down", b_wdn),
                             (wku, "wk_upP", b_wku), (wvu, "wv_up", b_wvu)]:
            P.op("gpsimd", lambda e, sb=sb, nm=nm: e.dma_start(
                out=sb[:], in_=C.dram[nm].rearrange("(c p) o -> p c o", p=128)),
                writes=[bw])
        P.op("sync", lambda e: e.dma_start(
            out=cos_sb[:], in_=C.dram["ropecos"].rearrange("(t p) d -> p t d", p=128)),
            writes=[C.b_const])
        P.op("sync", lambda e: e.dma_start(
            out=sin_sb[:], in_=C.dram["ropesin"].rearrange("(t p) d -> p t d", p=128)),
            writes=[C.b_const])

        for b in range(NB):
            with ExitStack() as sA:
                EA = sA.enter_context
                xt = EA(nc.sbuf_tensor(_un("l1a_xt"), [128, 1024], F32))
                ttm = EA(nc.sbuf_tensor(_un("l1a_ttm"), [128, 1024], BF16))
                tfm = EA(nc.sbuf_tensor(_un("l1a_tfm"), [128, 8, 512], BF16))
                latfm = EA(nc.sbuf_tensor(_un("l1a_lat"), [128, 4, 512], BF16))
                qst = EA(nc.sbuf_tensor(_un("l1a_qst"), [128, 512], F32))
                qtm = EA(nc.sbuf_tensor(_un("l1a_qtm"), [128, 4, 1024], F32))
                rout = EA(nc.sbuf_tensor(_un("l1a_rout"), [128, 1024], BF16))
                rscr = EA(nc.sbuf_tensor(_un("l1a_rscr"), [128, 1024], F32))
                vst = EA(nc.sbuf_tensor(_un("l1a_vst"), [128, 8, 512], BF16))
                sq = EA(nc.sbuf_tensor(_un("l1a_sq"), [128, 1024], F32))
                nrmt = EA(nc.sbuf_tensor(_un("l1a_nrmt"), [128, 16], F32))
                stat = EA(nc.sbuf_tensor(_un("l1a_stat"), [128, 8], F32))
                b_xt, b_ttm, b_tfm, b_lat, b_qst = (Buf() for _ in range(5))
                b_qtm, b_rout, b_rscr, b_vst, b_sq, b_nrmt, b_stat = (Buf() for _ in range(7))
                P.op("vector", lambda e: e.memset(nrm[:], 0.0), writes=[b_nrm])

                def fm_grid(wsb, b_w, nkc, noc, rhs, b_rhs, evac):
                    for oc in range(noc):
                        bank, bbk = C.pf[oc % 3], C.b_pf[oc % 3]
                        for kc in range(nkc):
                            P.op("tensor", lambda e, bank=bank, kc=kc, oc=oc:
                                 e.matmul(bank[:], wsb[:, kc, oc * 128:(oc + 1) * 128],
                                          rhs[:, kc, :], start=(kc == 0),
                                          stop=(kc == nkc - 1)),
                                 reads=[b_w, b_rhs], writes=[bbk], inc=(kc == nkc - 1))
                        evac(oc, bank, bbk)

                def tm_rope_path(j, srcname, b_srcw, off):
                    pass

                for j in range(4):
                    span = slice(j * 512, (j + 1) * 512)
                    for i in range(4):
                        tile = b * 16 + j * 4 + i
                        P.op("sync", lambda e, t=tile, xt=xt: e.dma_start(
                            out=xt[:], in_=src[t * 128:(t + 1) * 128, :]),
                            reads=[b_src[tile]], writes=[b_xt])
                        ln_tile(C, xt[:], ttm[:], C.epsc[:, 0:1], b_xt, b_ttm,
                                sq[:], b_sq, stat[:], b_stat)
                        for c in range(8):
                            transpose128(C, ttm[:, c * 128:(c + 1) * 128],
                                         tfm[:, c, i * 128:(i + 1) * 128],
                                         b_ttm, b_tfm, BF16)

                    def evac_lat(oc, bank, bbk):
                        P.op("scalar", lambda e, bank=bank, oc=oc, latfm=latfm: e.activation(
                            latfm[:, oc, :], bank[:], AF.Identity),
                            reads=[bbk], writes=[b_lat])
                    fm_grid(wdn, b_wdn, 8, 4, tfm, b_tfm, evac_lat)

                    def evac_v(oc, bank, bbk):
                        P.op("scalar", lambda e, bank=bank, oc=oc, vst=vst: e.activation(
                            vst[:, oc, :], bank[:], AF.Identity),
                            reads=[bbk], writes=[b_vst])
                    fm_grid(wvu, b_wvu, 4, 8, latfm, b_lat, evac_v)
                    for c in range(8):
                        for i in range(4):
                            transpose128(C, vst[:, c, i * 128:(i + 1) * 128],
                                         vtm[:, j * 4 + i, c, :], b_vst, b_v, BF16)

                    for (wsb, b_w, nkc, rhsbuf, b_rhs, dstfm, b_dst2, off) in (
                            (wq, b_wq, 8, tfm, b_tfm, qfm, b_q, 0),
                            (wku, b_wku, 4, latfm, b_lat, kfm, b_k, 16)):
                        def evac_q(oc, bank, bbk):
                            P.op("scalar", lambda e, bank=bank, qst=qst: e.activation(
                                qst[:], bank[:], AF.Identity),
                                reads=[bbk], writes=[b_qst])
                            for i in range(4):
                                tb, btb = C.pf[3 + (i % 2)], C.b_pf[3 + (i % 2)]
                                P.op("tensor", lambda e, tb=tb, i=i, qst=qst: e.transpose(
                                    tb[0:128, 0:128], qst[:, i * 128:(i + 1) * 128],
                                    C.idf_sb[:]),
                                    reads=[b_qst, C.b_const], writes=[btb])
                                P.op("vector", lambda e, tb=tb, i=i, oc=oc, qtm=qtm:
                                     e.tensor_copy(qtm[:, i, oc * 128:(oc + 1) * 128],
                                                   tb[0:128, 0:128]),
                                     reads=[btb], writes=[b_qtm])
                        fm_grid(wsb, b_w, nkc, 8, rhsbuf, b_rhs, evac_q)
                        for i in range(4):
                            q3 = qtm[:, i, :].rearrange("p (h two d) -> p h two d",
                                                        h=16, two=2)
                            x1 = q3[:, :, 0, :]
                            x2 = q3[:, :, 1, :]
                            ti = j * 4 + i
                            cb = cos_sb[:, ti, :].rearrange("p (o d) -> p o d", o=1
                                                            ).broadcast_to([128, 16, 32])
                            sb_ = sin_sb[:, ti, :].rearrange("p (o d) -> p o d", o=1
                                                             ).broadcast_to([128, 16, 32])
                            r1 = rscr[:, 0:512].rearrange("p (h d) -> p h d", h=16)
                            r2 = rscr[:, 512:1024].rearrange("p (h d) -> p h d", h=16)
                            ro = rout[:].rearrange("p (h two d) -> p h two d", h=16, two=2)
                            P.op("vector", lambda e, x1=x1, cb=cb, r1=r1:
                                 e.tensor_tensor(r1, x1, cb, ALU.mult),
                                 reads=[b_qtm, C.b_const], writes=[b_rscr])
                            P.op("vector", lambda e, x2=x2, sbx=sb_, r2=r2:
                                 e.tensor_tensor(r2, x2, sbx, ALU.mult),
                                 reads=[b_qtm, C.b_const], writes=[b_rscr])
                            P.op("vector", lambda e, r1=r1, r2=r2, ro=ro:
                                 e.tensor_tensor(ro[:, :, 0, :], r1, r2, ALU.subtract),
                                 reads=[b_rscr], writes=[b_rout])
                            P.op("vector", lambda e, x1=x1, sbx=sb_, r1=r1:
                                 e.tensor_tensor(r1, x1, sbx, ALU.mult),
                                 reads=[b_qtm, C.b_const], writes=[b_rscr])
                            P.op("vector", lambda e, x2=x2, cb=cb, r2=r2:
                                 e.tensor_tensor(r2, x2, cb, ALU.mult),
                                 reads=[b_qtm, C.b_const], writes=[b_rscr])
                            P.op("vector", lambda e, r1=r1, r2=r2, ro=ro:
                                 e.tensor_tensor(ro[:, :, 1, :], r1, r2, ALU.add),
                                 reads=[b_rscr], writes=[b_rout])
                            P.op("vector", lambda e, i=i, sq=sq, qtm=qtm: e.tensor_tensor(
                                sq[:], qtm[:, i, :], qtm[:, i, :], ALU.mult),
                                reads=[b_qtm], writes=[b_sq])
                            P.op("vector", lambda e, sq=sq, nrmt=nrmt: e.reduce_sum(
                                nrmt[:], sq[:].rearrange("p (h d) -> p h d", h=16), AX.X),
                                reads=[b_sq], writes=[b_nrmt])
                            P.op("vector", lambda e, off=off, nrmt=nrmt: e.tensor_tensor(
                                nrm[:, off:off + 16], nrm[:, off:off + 16],
                                nrmt[:], ALU.max),
                                reads=[b_nrm, b_nrmt], writes=[b_nrm])
                            for c in range(8):
                                transpose128(C, rout[:, c * 128:(c + 1) * 128],
                                             dstfm[:, c, (j * 4 + i) * 128:(j * 4 + i + 1) * 128],
                                             b_rout, b_dst2, BF16)
                negc_calc(C, nrm, b_nrm, ncs, b_ncs, ncrow, b_ncrow, negc, b_negc)
                if b == 0:
                    tap(C, "l1_qfm", qfm[:], b_q)
                    tap(C, "l1_kfm", kfm[:], b_k)
                    tap(C, "l1_vtm", vtm[:], b_v)
                    tap(C, "l1_negc", negc[:], b_negc)
            P.barrier()
            with ExitStack() as sB:
                EB = sB.enter_context
                wo = EB(nc.sbuf_tensor(_un("l1b_wo"), [128, 8, 1024], BF16))
                b_wo = Buf()
                P.op("gpsimd", lambda e: e.dma_start(
                    out=wo[:], in_=C.dram["wo"].rearrange("(c p) o -> p c o", p=128)),
                    writes=[b_wo])
                Pb = EB(nc.sbuf_tensor(_un("l1b_pb"), [128, 2, 2, 512], BF16))
                attnst = EB(nc.sbuf_tensor(_un("l1b_attnst"), [128, 8, 512], BF16))
                denb = EB(nc.sbuf_tensor(_un("l1b_denb"), [64, 512], F32))
                rb = EB(nc.sbuf_tensor(_un("l1b_rb"), [128, 512], F32))
                wost = EB(nc.sbuf_tensor(_un("l1b_wost"), [128, 512], F32))
                otm = EB(nc.sbuf_tensor(_un("l1b_otm"), [128, 4, 1024], F32))
                zt = EB(nc.sbuf_tensor(_un("l1b_zt"), [128, 1024], F32))
                xtB = EB(nc.sbuf_tensor(_un("l1b_xt"), [128, 1024], F32))
                sqB = EB(nc.sbuf_tensor(_un("l1b_sq"), [128, 1024], F32))
                statB = EB(nc.sbuf_tensor(_un("l1b_stat"), [128, 8], F32))
                b_Pb, b_att, b_den, b_rb, b_wost = (Buf() for _ in range(5))
                b_otm, b_zt, b_xtB, b_sqB, b_statB = (Buf() for _ in range(5))
                C.tap_stage = (wost, b_wost)
                C.tap_stage2 = (zt, b_zt)
                for qs in range(4):
                    attn_phaseB(C, qs, 4 * qs + 4,
                                lambda st, qs=qs: max(0, (st & ~1) - 4 * qs) * 128,
                                False, qfm, b_q, kfm, b_k, vtm, b_v,
                                negc, b_negc, attnst, b_att, Pb, b_Pb,
                                denb, b_den, rb, b_rb)
                    if b == 0 and qs == 0:
                        tap(C, "l1_attnst0", attnst[:], b_att)
                    for oc in range(8):
                        bank, bbk = C.pf[2 + (oc % 2)], C.b_pf[2 + (oc % 2)]
                        for kc in range(8):
                            P.op("tensor", lambda e, bank=bank, kc=kc, oc=oc, wo=wo, attnst=attnst:
                                 e.matmul(bank[:], wo[:, kc, oc * 128:(oc + 1) * 128],
                                          attnst[:, kc, :], start=(kc == 0), stop=(kc == 7)),
                                 reads=[b_wo, b_att], writes=[bbk], inc=(kc == 7))
                        P.op("scalar", lambda e, bank=bank, wost=wost: e.activation(
                            wost[:], bank[:], AF.Identity),
                            reads=[bbk], writes=[b_wost])
                        for i in range(4):
                            tb, btb = C.pf[4 + (i % 2)], C.b_pf[4 + (i % 2)]
                            P.op("tensor", lambda e, tb=tb, i=i, wost=wost: e.transpose(
                                tb[0:128, 0:128], wost[:, i * 128:(i + 1) * 128],
                                C.idf_sb[:]),
                                reads=[b_wost, C.b_const], writes=[btb])
                            P.op("vector", lambda e, tb=tb, i=i, oc=oc, otm=otm: e.tensor_copy(
                                otm[:, i, oc * 128:(oc + 1) * 128], tb[0:128, 0:128]),
                                reads=[btb], writes=[b_otm])
                    for i in range(4):
                        tile = b * 16 + qs * 4 + i
                        ln_tile(C, otm[:, i, :], zt[:], C.epsc[:, 0:1], b_otm, b_zt,
                                sqB[:], b_sqB, statB[:], b_statB)
                        P.op("sync", lambda e, t=tile, xtB=xtB: e.dma_start(
                            out=xtB[:], in_=src[t * 128:(t + 1) * 128, :]),
                            reads=[b_src[tile]], writes=[b_xtB])
                        P.op("vector", lambda e, zt=zt, xtB=xtB: e.tensor_tensor(xtB[:], zt[:], xtB[:], ALU.add),
                             reads=[b_zt, b_xtB], writes=[b_xtB])
                        P.op("sync", lambda e, t=tile, xtB=xtB: e.dma_start(
                            out=dst[t * 128:(t + 1) * 128, :], in_=xtB[:]),
                            reads=[b_xtB], writes=[b_dst[tile]])
            P.barrier()
    P.barrier()


def build_program(blocks=("l0a", "l0f", "l1a", "l1f"), want_taps=None, paranoia=0, **cflags):
    nc = bass.Bass("TRN2", target_bir_lowering=False)
    C = Ctx()
    C.nc = nc
    C.want_taps = want_taps
    C.paranoia = paranoia
    for k_, v_ in cflags.items():
        setattr(C, k_, v_)
    C.dram = {}

    def din(name, shape, dtype=F32):
        C.dram[name] = nc.dram_tensor(name, list(shape), dtype, kind="ExternalInput")
        return C.dram[name]

    din("x", [NB * T, D])
    dout = nc.dram_tensor("out", [NB * T, D], F32, kind="ExternalOutput")
    C.dram["out"] = dout

    din("idb", [128, 128], BF16)
    din("idf", [128, 128], F32)
    din("onesf", [128, 128], F32)
    din("onescb", [128, 1], BF16)
    din("epsc", [128, 2], F32)
    if "l0a" in blocks or "l1a" in blocks:
        din("eh", [128, 2], F32)
    if "l0a" in blocks:
        din("tri", [128, 128], BF16)
        din("w_attnT", [D, 3 * D])
        din("b_attn_fm", [128, 24])
        din("w_projT", [D, D])
        din("b_proj_fm", [128, 8])
        din("w_fcT", [D, 4 * D])
        din("b_fc_fm", [128, 32])
        din("w_mlpT", [4 * D, D])
        din("b_mlp_fm", [128, 8])
    if "l0f" in blocks:
        din("wg", [D, 8])
        din("bg_col", [8, 1])
        din("moe_w1", [8, D, 2 * D])
        din("b1_fm", [128, 8, 16])
        din("moe_w2", [8, 2 * D, D])
        din("b2_mat", [8, D])
        din("gamma", [1, D])
        din("beta", [1, D])
    if "l1a" in blocks:
        din("ropecos", [T, 32])
        din("ropesin", [T, 32])
        din("wqP", [D, D])
        din("w_down", [D, 512])
        din("wk_upP", [512, D])
        din("wv_up", [512, D])
        din("wo", [D, D])
    if "l1f" in blocks:
        din("l1f_w", [D, D])
        din("l1f_b_fm", [128, 8])

    need_xres = len(blocks) > 1
    if need_xres:
        xres = nc.dram_tensor("xres", [NB * T, D], F32, kind="Internal")
    if "l0a" in blocks:
        u1d = nc.dram_tensor("u1d", [NB * T, D], F32, kind="Internal")
        C.u1d = u1d
        C.b_u1d = [Buf() for _ in range(NTILES)]
    if "l0f" in blocks:
        u3fmd = nc.dram_tensor("u3fmd", [NSUP, 128, 8 * 512], BF16, kind="Internal")
        gd = nc.dram_tensor("gd", [8, NB * T], F32, kind="Internal")
        C.u3fmd, C.gd = u3fmd, gd
        C.b_u3fmd = [Buf() for _ in range(NSUP)]
        C.b_gd = [Buf() for _ in range(NSUP)]

    with ExitStack() as ctx:
        E = ctx.enter_context
        P = Prog(nc)
        C.P = P
        sems = {}
        for e in ENGINES:
            if e in DMA_ENG:
                for r in range(NLANE):
                    sems[(e, r)] = E(nc.semaphore(f"sem_{e}{r}"))
            else:
                sems[(e, 0)] = E(nc.semaphore("sem_" + e))
        bar_sem = E(nc.semaphore("sem_bar"))

        C.idb_sb = E(nc.sbuf_tensor("idb_sb", [128, 128], BF16))
        C.idf_sb = E(nc.sbuf_tensor("idf_sb", [128, 128], F32))
        C.onesf_sb = E(nc.sbuf_tensor("onesf_sb", [128, 128], F32))
        C.onescb_sb = E(nc.sbuf_tensor("onescb_sb", [128, 1], BF16))
        C.epsc = E(nc.sbuf_tensor("epsc_sb", [128, 2], F32))
        C.b_const = Buf("const")
        loads = [(C.idb_sb, "idb"), (C.idf_sb, "idf"), (C.onesf_sb, "onesf"),
                 (C.onescb_sb, "onescb"), (C.epsc, "epsc")]
        C.bias_sb = {}
        if "l0a" in blocks or "l1a" in blocks:
            C.eh_sb = E(nc.sbuf_tensor("eh_sb", [128, 2], F32))
            loads.append((C.eh_sb, "eh"))
        if "l0a" in blocks:
            C.tri_sb = E(nc.sbuf_tensor("tri_sb", [128, 128], BF16))
            loads.append((C.tri_sb, "tri"))
            for nm, sh in [("b_attn_fm", [128, 24]), ("b_proj_fm", [128, 8]),
                           ("b_fc_fm", [128, 32]), ("b_mlp_fm", [128, 8])]:
                C.bias_sb[nm] = E(nc.sbuf_tensor(nm + "_sb", sh, F32))
                loads.append((C.bias_sb[nm], nm))
        if "l0f" in blocks:
            for nm, sh in [("bg_col", [8, 1]), ("b1_fm", [128, 8, 16])]:
                C.bias_sb[nm] = E(nc.sbuf_tensor(nm + "_sb", sh, F32))
                loads.append((C.bias_sb[nm], nm))
        if "l1f" in blocks:
            C.bias_sb["l1f_b"] = E(nc.sbuf_tensor("l1f_b_sb", [128, 8], F32))
            loads.append((C.bias_sb["l1f_b"], "l1f_b_fm"))

        C.pf = [E(nc.psum_tensor(f"pf{i}", [128, 512], F32)) for i in range(6)]
        C.pbf = [E(nc.psum_tensor(f"pbf{i}", [128, 1024], BF16)) for i in range(2)]
        C.b_pf = [Buf(f"pf{i}") for i in range(6)]
        C.b_pbf = [Buf(f"pbf{i}") for i in range(2)]
        C._pf_rot = 0
        C._pbf_rot = 0

        def next_pf():
            i = C._pf_rot % len(C.pf)
            C._pf_rot += 1
            return C.pf[i], C.b_pf[i]

        def next_pbf():
            i = C._pbf_rot % len(C.pbf)
            C._pbf_rot += 1
            return C.pbf[i], C.b_pbf[i]

        C.next_pf = next_pf
        C.next_pbf = next_pbf

        blk = E(nc.Block())

        for sb, dr_name in loads:
            P.op("sync", lambda e, sb=sb, d=C.dram[dr_name]: e.dma_start(
                out=sb[:], in_=d.ap()), writes=[C.b_const])

        def dram_bufs(n):
            return [Buf() for _ in range(n)]

        b_x = dram_bufs(NTILES)
        b_out = dram_bufs(NTILES)
        b_xres = dram_bufs(NTILES) if need_xres else None

        seq = []
        for bi, name in enumerate(blocks):
            src = C.dram["x"].ap() if bi == 0 else xres.ap()
            bsrc = b_x if bi == 0 else b_xres
            last = bi == len(blocks) - 1
            dstap = dout.ap() if last else xres.ap()
            bdst = b_out if last else b_xres
            seq.append((name, src, bsrc, dstap, bdst))

        for name, src, bsrc, dstap, bdst in seq:
            if name == "l1f":
                emit_l1ffn(C, src, bsrc, dstap, bdst)
            elif name == "l0a":
                emit_l0attn(C, src, bsrc, dstap, bdst)
            elif name == "l0f":
                emit_l0ffn(C, src, bsrc, dstap, bdst)
            elif name == "l1a":
                emit_l1attn(C, src, bsrc, dstap, bdst)

        P.emit(blk, sems, bar_sem)
    return nc


def _bias_fm(b):
    nck = b.shape[0] // 128
    return np.ascontiguousarray(b.reshape(nck, 128).T).astype(np.float32)


def make_weight_arrays(inputs, blocks=("l0a", "l0f", "l1a", "l1f")):
    g = {}
    g["idb"] = np.eye(128, dtype=ml_dtypes.bfloat16)
    g["idf"] = np.eye(128, dtype=np.float32)
    g["onesf"] = np.ones((128, 128), dtype=np.float32)
    g["onescb"] = np.ones((128, 1), dtype=ml_dtypes.bfloat16)
    g["epsc"] = np.broadcast_to(np.array([1e-5, 1e-10], dtype=np.float32), (128, 2)).copy()
    if "l0a" in blocks or "l1a" in blocks:
        eh = np.zeros((128, 2), dtype=np.float32)
        eh[0:64, 0] = 1.0
        eh[64:128, 1] = 1.0
        g["eh"] = eh
    if "l0a" in blocks:
        s_le_q = (np.arange(128)[:, None] <= np.arange(128)[None, :])
        g["tri"] = s_le_q.astype(ml_dtypes.bfloat16)
        g["w_attnT"] = np.ascontiguousarray(inputs["L0_attn_w_attn"].T)
        g["b_attn_fm"] = _bias_fm(inputs["L0_attn_b_attn"])
        g["w_projT"] = np.ascontiguousarray(inputs["L0_attn_w_proj"].T)
        g["b_proj_fm"] = _bias_fm(inputs["L0_attn_b_proj"])
        g["w_fcT"] = np.ascontiguousarray(inputs["L0_attn_w_fc"].T)
        g["b_fc_fm"] = _bias_fm(inputs["L0_attn_b_fc"])
        g["w_mlpT"] = np.ascontiguousarray(inputs["L0_attn_w_mlp_proj"].T)
        g["b_mlp_fm"] = _bias_fm(inputs["L0_attn_b_mlp_proj"])
    if "l0f" in blocks:
        g["wg"] = np.ascontiguousarray(inputs["L0_ffn_Wg"])
        g["bg_col"] = np.ascontiguousarray(inputs["L0_ffn_bg"].reshape(8, 1))
        g["moe_w1"] = np.ascontiguousarray(inputs["L0_ffn_W1"])
        g["b1_fm"] = np.ascontiguousarray(
            inputs["L0_ffn_b1"].reshape(8, 16, 128).transpose(2, 0, 1))
        g["moe_w2"] = np.ascontiguousarray(inputs["L0_ffn_W2"])
        g["b2_mat"] = np.ascontiguousarray(inputs["L0_ffn_b2"])
        g["gamma"] = np.ascontiguousarray(inputs["L0_ffn_gamma"].reshape(1, D))
        g["beta"] = np.ascontiguousarray(inputs["L0_ffn_beta"].reshape(1, D))
    if "l1a" in blocks:
        inv = 1.0 / 10000.0 ** (np.arange(0, 64, 2, dtype=np.float32) / 64)
        ang = np.outer(np.arange(T, dtype=np.float32), inv)
        g["ropecos"] = np.cos(ang).astype(np.float32)
        g["ropesin"] = np.sin(ang).astype(np.float32)
        perm = np.concatenate([
            np.concatenate([h * 64 + 2 * np.arange(32), h * 64 + 2 * np.arange(32) + 1])
            for h in range(16)])
        g["wqP"] = np.ascontiguousarray(inputs["L1_attn_wq"][:, perm])
        g["w_down"] = np.ascontiguousarray(inputs["L1_attn_w_down"])
        g["wk_upP"] = np.ascontiguousarray(inputs["L1_attn_wk_up"][:, perm])
        g["wv_up"] = np.ascontiguousarray(inputs["L1_attn_wv_up"])
        g["wo"] = np.ascontiguousarray(inputs["L1_attn_wo"])
    if "l1f" in blocks:
        g["l1f_w"] = np.ascontiguousarray(inputs["L1_ffn_weight"])
        g["l1f_b_fm"] = _bias_fm(inputs["L1_ffn_bias"])
    return g


FUSED = True


def kernel(**inputs):
    blocks = ("l0a", "l0f", "l1a", "l1f")
    x = np.asarray(inputs["L0_attn_x"], dtype=np.float32).reshape(32 * T, D)
    if FUSED:
        g = make_weight_arrays(inputs, blocks)
        nc = build_program(blocks)
        in_maps = []
        for c in range(8):
            m = dict(g)
            m["x"] = np.ascontiguousarray(x[c * NB * T:(c + 1) * NB * T])
            in_maps.append(m)
        res = run_bass_kernel_spmd(nc, in_maps, core_ids=list(range(8)))
        out = np.concatenate([r["out"] for r in res.results], axis=0)
        return out.reshape(32, T, D)
    cur = x
    for bname in blocks:
        g = make_weight_arrays(inputs, (bname,))
        nc = build_program((bname,))
        in_maps = []
        for c in range(8):
            m = dict(g)
            m["x"] = np.ascontiguousarray(cur[c * NB * T:(c + 1) * NB * T])
            in_maps.append(m)
        res = run_bass_kernel_spmd(nc, in_maps, core_ids=list(range(8)))
        cur = np.concatenate([r["out"] for r in res.results], axis=0)
    return cur.reshape(32, T, D)


# version 05
# speedup vs baseline: 1.3164x; 1.0362x over previous
import numpy as np
import ml_dtypes
from contextlib import ExitStack

import concourse.bass as bass
import concourse.mybir as mybir
from concourse.bass_utils import run_bass_kernel_spmd

F32 = mybir.dt.float32
BF16 = mybir.dt.bfloat16
AF = mybir.ActivationFunctionType
ALU = mybir.AluOpType
AX = mybir.AxisListType

ENGINES = ("sync", "tensor", "vector", "scalar", "gpsimd")
DMA_ENG = ("sync", "gpsimd")
NLANE = 8

NB = 4
T = 2048
D = 1024
NTILES = NB * T // 128
NSUP = NB * T // 512
MAGIC = 12582912.0


class Buf:
    __slots__ = ("writer", "readers", "name")

    def __init__(self, name=""):
        self.writer = None
        self.readers = []
        self.name = name


class Prog:
    def __init__(self, nc):
        self.nc = nc
        self.steps = []
        self.eng_steps = {e: [] for e in ENGINES}
        self.nbar = 0
        self._inc_count = {}

    def op(self, eng, fn, reads=(), writes=(), deps=(), inc=True):
        d = list(deps)
        for b in reads:
            if b.writer is not None:
                d.append(b.writer)
        for b in writes:
            d.extend(b.readers)
            if b.writer is not None:
                d.append(b.writer)
        seq = len(self.steps)
        st = {"eng": eng, "fn": fn, "deps": d, "inc": inc, "seq": seq, "bar": None}
        if inc:
            self._inc_count[eng] = self._inc_count.get(eng, 0) + 1
        self.steps.append(st)
        self.eng_steps[eng].append(st)
        tok = (eng, seq)
        for b in reads:
            b.readers.append(tok)
        for b in writes:
            b.writer = tok
            b.readers = []
        return tok

    def barrier(self):
        snap = {}
        for e in ENGINES:
            if e in DMA_ENG:
                n = self._inc_count.get(e, 0)
                snap[e] = [(n + NLANE - 1 - r) // NLANE for r in range(NLANE)]
            else:
                snap[e] = self._inc_count.get(e, 0)
        for e in ENGINES:
            st = {"eng": e, "fn": None, "deps": (), "inc": False,
                  "seq": len(self.steps), "bar": ("waits", snap)}
            self.steps.append(st)
            self.eng_steps[e].append(st)

    def _resolve(self):
        self.cover = {}
        self.cover_seq = {}
        self.total_incs = {}
        for e in ENGINES:
            arr = self.eng_steps[e]
            if e in DMA_ENG:
                lane_cnt = [0] * NLANE
                for st in arr:
                    if st["bar"] is not None:
                        continue
                    assert st["inc"], "DMA steps must inc"
                    r = (sum(lane_cnt)) % NLANE
                    lane_cnt[r] += 1
                    st["lane"] = r
                    self.cover[(e, st["seq"])] = (r, lane_cnt[r] * 16)
                    self.cover_seq[(e, st["seq"])] = st["seq"]
                self.total_incs[e] = list(lane_cnt)
            else:
                cum = 0
                cums = []
                for st in arr:
                    if st["inc"]:
                        cum += 1
                    cums.append(cum)
                self.total_incs[e] = cum
                nxt = [None] * len(arr)
                nxtseq = [None] * len(arr)
                cur = None
                curseq = None
                for i in range(len(arr) - 1, -1, -1):
                    if arr[i]["inc"]:
                        cur = cums[i]
                        curseq = arr[i]["seq"]
                    nxt[i] = cur
                    nxtseq[i] = curseq
                for i, st in enumerate(arr):
                    if nxt[i] is not None:
                        self.cover[(e, st["seq"])] = (0, nxt[i])
                        self.cover_seq[(e, st["seq"])] = nxtseq[i]

    def emit(self, blk, sems, bar_sem):
        self._resolve()
        for ename in ENGINES:
            def mk(ename):
                def run(eng):
                    waited = {}
                    for st in self.eng_steps[ename]:
                        if st["bar"] is not None:
                            kind, val = st["bar"]
                            assert kind == "waits"
                            for e2 in ENGINES:
                                if e2 in DMA_ENG:
                                    for r in range(NLANE):
                                        v2 = val[e2][r] * 16
                                        if v2 > waited.get((e2, r), 0):
                                            eng.wait_ge(sems[(e2, r)], v2)
                                            waited[(e2, r)] = v2
                                else:
                                    v2 = val[e2]
                                    if v2 > waited.get((e2, 0), 0):
                                        eng.wait_ge(sems[(e2, 0)], v2)
                                        waited[(e2, 0)] = v2
                            continue
                        need = {}
                        for (de, dseq) in st["deps"]:
                            if de == ename and ename == "tensor":
                                continue
                            cv = self.cover.get((de, dseq))
                            assert cv is not None, \
                                f"dep on ({de},{dseq}) has no covering inc"
                            lane, v = cv
                            cseq = self.cover_seq[(de, dseq)]
                            assert cseq < st["seq"], \
                                f"dep of step {st['seq']} covered only by later step {cseq} on {de}"
                            if v > need.get((de, lane), -1):
                                need[(de, lane)] = v
                        for key in sorted(need):
                            v = need[key]
                            if v > waited.get(key, 0):
                                eng.wait_ge(sems[key], v)
                                waited[key] = v
                        ins = st["fn"](eng)
                        if st["inc"]:
                            if ename in DMA_ENG:
                                ins.then_inc(sems[(ename, st["lane"])], 16)
                            else:
                                ins.then_inc(sems[(ename, 0)], 1)
                    if ename == "sync":
                        for r in range(NLANE):
                            eng.wait_ge(sems[("sync", r)],
                                        self.total_incs["sync"][r] * 16)
                return run
            getattr(blk, ename)(mk(ename))



_NAME_N = [0]


def _un(s):
    _NAME_N[0] += 1
    return f"{s}_{_NAME_N[0]}"



class Ctx:
    pass


def ln_tile(C, src_ap, out_ap, eps_ap, r_src, w_out, sq_ap, b_sq, stat_ap, b_stat,
            extra_out=None):
    P = C.P
    n = src_ap.shape[-1]
    sums = stat_ap[:, 0:1]
    sumsq = stat_ap[:, 1:2]
    mu = stat_ap[:, 2:3]
    var = stat_ap[:, 3:4]
    rstd = stat_ap[:, 4:5]
    nmr = stat_ap[:, 5:6]
    P.op("vector", lambda e, a=src_ap, o=sums: e.reduce_sum(o, a, AX.X),
         reads=[r_src], writes=[b_stat])
    if out_ap.dtype == F32:
        P.op("scalar", lambda e, a=src_ap, o=out_ap, ac=sumsq:
             e.activation(o, a, AF.Square, accum_out=ac),
             reads=[r_src], writes=[w_out, b_stat])
    else:
        P.op("scalar", lambda e, a=src_ap, o=sq_ap, ac=sumsq:
             e.activation(o, a, AF.Square, accum_out=ac),
             reads=[r_src], writes=[b_sq, b_stat])
    P.op("vector", lambda e, o=mu, a=sums: e.tensor_single_scalar(o, a, 1.0 / n, ALU.mult),
         reads=[b_stat], writes=[b_stat])
    P.op("vector", lambda e, o=var, a=mu: e.tensor_tensor(o, a, a, ALU.mult),
         reads=[b_stat], writes=[b_stat])
    P.op("vector", lambda e, o=var, a=sumsq, m=var:
         e.scalar_tensor_tensor(o, a, 1.0 / n, m, ALU.mult, ALU.subtract),
         reads=[b_stat], writes=[b_stat])
    P.op("scalar", lambda e, o=rstd, a=var, b=eps_ap: e.activation(o, a, AF.Ln, bias=b),
         reads=[b_stat, C.b_const], writes=[b_stat])
    P.op("scalar", lambda e, o=rstd, a=rstd: e.activation(o, a, AF.Exp, scale=-0.5),
         reads=[b_stat], writes=[b_stat])
    P.op("vector", lambda e, o=nmr, a=mu, r=rstd:
         e.scalar_tensor_tensor(o, a, -1.0, r, ALU.mult, ALU.mult),
         reads=[b_stat], writes=[b_stat])
    P.op("scalar", lambda e, o=out_ap, a=src_ap, r=rstd, b=nmr:
         e.activation(o, a, AF.Identity, bias=b, scale=r),
         reads=[r_src, b_stat], writes=[w_out])
    return rstd, nmr


def transpose128(C, src_ap, dst_ap, r_src, w_dst, dtype, copy_eng="vector"):
    P = C.P
    ncols = src_ap.shape[-1]
    if dtype == BF16:
        bank, bbuf = C.next_pbf()
        ident = C.idb_sb
        pslice = bank[0:ncols, 0:128]
    else:
        bank, bbuf = C.next_pf()
        ident = C.idf_sb
        pslice = bank[0:ncols, 0:128]
    P.op("tensor", lambda e, o=pslice, i=src_ap, idt=ident[:]: e.transpose(o, i, idt),
         reads=[r_src, C.b_const], writes=[bbuf])
    P.op(copy_eng, lambda e, o=dst_ap, i=pslice:
         (e.tensor_copy(o, i) if copy_eng == "vector" else e.activation(o, i, AF.Identity)),
         reads=[bbuf], writes=[w_dst])


def emit_l1ffn(C, src, b_src, dst, b_dst):
    nc, P = C.nc, C.P

    def quant_chain(u_ap, out_ap, r_u, w_out, s1, s2, s3, s4, bs, bufs):
        b1, b2, b3, b4, bb = bufs
        bmax = bs[:, 0:32]
        scl = bs[:, 32:64]
        rscl = bs[:, 64:96]
        u3d = u_ap.rearrange("p (a b) -> p a b", a=32)
        P.op("vector", lambda e: e.reduce_max(bmax, u3d, AX.X, apply_absolute_value=True),
             reads=[r_u], writes=[bb])
        P.op("vector", lambda e: e.tensor_scalar(scl, bmax, 1.0 / 7.5, 1e-30, ALU.mult, ALU.max),
             reads=[bb], writes=[bb])
        P.op("vector", lambda e: e.reciprocal(rscl, scl),
             reads=[bb], writes=[bb])
        rb = rscl.rearrange("p (a o) -> p a o", a=32).broadcast_to([128, 32, 32])
        sb = scl.rearrange("p (a o) -> p a o", a=32).broadcast_to([128, 32, 32])
        s1d = s1.rearrange("p (a b) -> p a b", a=32)
        P.op("vector", lambda e: e.tensor_tensor(s1d, u3d, rb, ALU.mult),
             reads=[r_u, bb], writes=[b1])
        P.op("scalar", lambda e: e.activation(s2, s1, AF.Abs),
             reads=[b1], writes=[b2])
        P.op("vector", lambda e: e.tensor_single_scalar(s3, s2, 2.0, ALU.is_ge),
             reads=[b2], writes=[b3])
        P.op("vector", lambda e: e.tensor_single_scalar(s2, s2, 4.0, ALU.is_ge),
             reads=[b2], writes=[b2])
        P.op("vector", lambda e: e.tensor_scalar(s4, s3, -4.0, 8.0, ALU.mult, ALU.add),
             reads=[b3], writes=[b4])
        P.op("vector", lambda e: e.scalar_tensor_tensor(s4, s2, -2.0, s4, ALU.mult, ALU.add),
             reads=[b2, b4], writes=[b4])
        P.op("vector", lambda e: e.tensor_scalar(s3, s3, 0.125, 0.125, ALU.mult, ALU.add),
             reads=[b3], writes=[b3])
        P.op("vector", lambda e: e.scalar_tensor_tensor(s3, s2, 0.25, s3, ALU.mult, ALU.add),
             reads=[b2, b3], writes=[b3])
        P.op("vector", lambda e: e.tensor_tensor(s2, s1, s4, ALU.mult),
             reads=[b1, b4], writes=[b2])
        P.op("vector", lambda e: e.tensor_scalar(s2, s2, MAGIC, MAGIC, ALU.add, ALU.subtract),
             reads=[b2], writes=[b2])
        P.op("vector", lambda e: e.tensor_tensor(s1, s2, s3, ALU.mult),
             reads=[b2, b3], writes=[b1])
        od = out_ap.rearrange("p (a b) -> p a b", a=32)
        P.op("vector", lambda e: e.tensor_tensor(od, s1d, sb, ALU.mult),
             reads=[b1, bb], writes=[w_out])

    with ExitStack() as ctx:
        E = ctx.enter_context
        wf = E(nc.sbuf_tensor("l1f_wf", [128, 8, 1024], F32))
        wdq = E(nc.sbuf_tensor("l1f_wdq", [128, 8, 1024], BF16))
        wdT = E(nc.sbuf_tensor("l1f_wdT", [128, 8, 1024], BF16))
        s1 = E(nc.sbuf_tensor("l1f_s1", [128, 1024], F32))
        s2 = E(nc.sbuf_tensor("l1f_s2", [128, 1024], F32))
        s3 = E(nc.sbuf_tensor("l1f_s3", [128, 1024], F32))
        s4 = E(nc.sbuf_tensor("l1f_s4", [128, 1024], F32))
        bs = E(nc.sbuf_tensor("l1f_bs", [128, 96], F32))
        b_wf, b_wdq, b_wdT = Buf("wf"), Buf("wdq"), Buf("wdT")
        b1, b2, b3, b4, bb = Buf(), Buf(), Buf(), Buf(), Buf()

        P.op("sync", lambda e: e.dma_start(
            out=wf[:], in_=C.dram["l1f_w"].rearrange("(c p) k -> p c k", p=128)),
            writes=[b_wf])
        for jc in range(8):
            quant_chain(wf[:, jc, :], wdq[:, jc, :], b_wf, b_wdq,
                        s1[:], s2[:], s3[:], s4[:], bs[:], (b1, b2, b3, b4, bb))
        for jc in range(8):
            for kc in range(8):
                transpose128(C, wdq[:, jc, kc * 128:(kc + 1) * 128],
                             wdT[:, kc, jc * 128:(jc + 1) * 128],
                             b_wdq, b_wdT, BF16)
        P.barrier()

        xt = E(nc.sbuf_tensor("l1f_xt", [128, 4, 1024], F32))
        u5 = E(nc.sbuf_tensor("l1f_u5", [128, 1024], F32))
        xdtm = E(nc.sbuf_tensor("l1f_xdtm", [128, 1024], BF16))
        xdfm = E(nc.sbuf_tensor("l1f_xdfm", [128, 8, 512], BF16))
        outst = E(nc.sbuf_tensor("l1f_outst", [128, 512], F32))
        otm = E(nc.sbuf_tensor("l1f_otm", [128, 4, 1024], F32))
        zt = E(nc.sbuf_tensor("l1f_zt", [128, 1024], F32))
        xo = E(nc.sbuf_tensor("l1f_xo", [128, 1024], F32))
        stat = E(nc.sbuf_tensor("l1f_stat", [128, 8], F32))
        b_xt = [Buf() for _ in range(4)]
        b_u5, b_xdtm, b_xdfm = Buf(), Buf(), Buf()
        b_outst, b_otm, b_zt, b_xo, b_stat = Buf(), Buf(), Buf(), Buf(), Buf()

        for s in range(NSUP):
            for i in range(4):
                tile = s * 4 + i
                P.op("sync", lambda e, t=tile, i=i: e.dma_start(
                    out=xt[:, i, :], in_=src[t * 128:(t + 1) * 128, :]),
                    reads=[b_src[tile]], writes=[b_xt[i]])
                ln_tile(C, xt[:, i, :], u5[:], C.epsc[:, 0:1], b_xt[i], b_u5,
                        s1[:], b1, stat[:], b_stat)
                quant_chain(u5[:], xdtm[:], b_u5, b_xdtm,
                            s1[:], s2[:], s3[:], s4[:], bs[:], (b1, b2, b3, b4, bb))
                for c in range(8):
                    transpose128(C, xdtm[:, c * 128:(c + 1) * 128],
                                 xdfm[:, c, i * 128:(i + 1) * 128],
                                 b_xdtm, b_xdfm, BF16)
            for jc in range(8):
                bank, bbk = C.next_pf()
                for kc in range(8):
                    P.op("tensor", lambda e, bk=bank, kc=kc, jc=jc: e.matmul(
                        bk[:], wdT[:, kc, jc * 128:(jc + 1) * 128], xdfm[:, kc, :],
                        start=(kc == 0), stop=(kc == 7)),
                        reads=[b_wdT, b_xdfm], writes=[bbk], inc=(kc == 7))
                P.op("scalar", lambda e, bk=bank, jc=jc: e.activation(
                    outst[:], bk[:], AF.Identity, bias=C.bias_sb["l1f_b"][:, jc:jc + 1]),
                    reads=[bbk, C.b_const], writes=[b_outst])
                for i in range(4):
                    transpose128(C, outst[:, i * 128:(i + 1) * 128],
                                 otm[:, i, jc * 128:(jc + 1) * 128],
                                 b_outst, b_otm, F32)
            for i in range(4):
                tile = s * 4 + i
                rstd, nmr = ln_tile(C, otm[:, i, :], zt[:], C.epsc[:, 0:1], b_otm, b_zt,
                                    s1[:], b1, stat[:], b_stat)
                P.op("vector", lambda e, i=i: e.tensor_tensor(
                    xo[:], zt[:], xt[:, i, :], ALU.add),
                    reads=[b_zt, b_xt[i]], writes=[b_xo])
                P.op("sync", lambda e, t=tile: e.dma_start(
                    out=dst[t * 128:(t + 1) * 128, :], in_=xo[:]),
                    reads=[b_xo], writes=[b_dst[tile]])
        P.barrier()


def attn_phaseB(C, qs, nst, colstart_fn, mask_diag, qfm, b_q, kfm, b_k, vtm, b_v,
                negc, b_negc, attnst, b_att, Pb, b_Pb, denb, b_den, rb, b_rb):
    P = C.P
    pf, bpf = C.pf, C.b_pf

    for hp in range(8):
        def emit_S(st):
            cs = colstart_fn(st)
            pspan = slice(cs, 512)
            span = slice(qs * 512 + cs, qs * 512 + 512)
            pbase = (st % 2) * 2
            for h in range(2):
                base = h * 64
                sbank, b_sbank = pf[pbase + h], bpf[pbase + h]
                P.op("tensor", lambda e, sb=sbank, hp=hp, st=st, base=base, sp=span, ps=pspan:
                     e.matmul(sb[:, ps], kfm[base:base + 64, hp, st * 128:(st + 1) * 128],
                              qfm[base:base + 64, hp, sp], start=True, stop=True,
                              skip_group_check=True),
                     reads=[b_q, b_k], writes=[b_sbank])

        def emit_exp(st):
            cs = colstart_fn(st)
            pspan = slice(cs, 512)
            pbase = (st % 2) * 2
            par = st % 2
            for h in range(2):
                sbank, b_sbank = pf[pbase + h], bpf[pbase + h]
                P.op("scalar", lambda e, sb=sbank, hp=hp, h=h, ps=pspan, par=par:
                     e.activation(Pb[:, par, h, ps], sb[:, ps], AF.Exp,
                                  bias=negc[:, 2 * hp + h:2 * hp + h + 1], scale=0.125),
                     reads=[b_sbank, b_negc], writes=[b_Pb])
                if mask_diag and st >= 4 * qs:
                    P.op("vector", lambda e, h=h, cs=cs, par=par:
                         e.tensor_tensor(Pb[:, par, h, cs:cs + 128],
                                         Pb[:, par, h, cs:cs + 128],
                                         C.tri_sb[:], ALU.mult),
                         reads=[b_Pb, C.b_const], writes=[b_Pb])

        def emit_av(st):
            cs = colstart_fn(st)
            pspan = slice(cs, 512)
            par = st % 2
            first = st == 0
            last = st == nst - 1
            for h in range(2):
                base = h * 64
                P.op("tensor", lambda e, hp=hp, st=st, h=h, base=base, ps=pspan,
                     first=first, last=last, par=par:
                     e.matmul(pf[4][base:base + 64, ps],
                              vtm[:, st, hp, base:base + 64], Pb[:, par, h, ps],
                              start=first, stop=last,
                              tile_position=(0, base), skip_group_check=True),
                     reads=[b_v, b_Pb], writes=[C.b_pf[4]], inc=True)
                dp = 32 * h
                P.op("tensor", lambda e, h=h, ps=pspan, dp=dp, first=first, last=last, par=par:
                     e.matmul(pf[5][dp:dp + 1, ps], C.onescb_sb[:], Pb[:, par, h, ps],
                              start=first, stop=last,
                              tile_position=(0, dp), skip_group_check=True),
                     reads=[C.b_const, b_Pb], writes=[C.b_pf[5]], inc=True)

        emit_S(0)
        for st in range(nst):
            emit_exp(st)
            if st + 1 < nst:
                emit_S(st + 1)
            emit_av(st)

        for h in range(2):
            dp = 32 * h
            base = h * 64
            P.op("scalar", lambda e, dp=dp: e.activation(
                denb[dp:dp + 1, :], pf[5][dp:dp + 1, 0:512], AF.Ln),
                 reads=[C.b_pf[5]], writes=[b_den])
            P.op("scalar", lambda e, dp=dp: e.activation(
                denb[dp:dp + 1, :], denb[dp:dp + 1, :], AF.Exp, scale=-1.0),
                 reads=[b_den], writes=[b_den])
            bb, b_bb = pf[2 + h], bpf[2 + h]
            P.op("tensor", lambda e, dp=dp, base=base, bb=bb:
                 e.matmul(bb[base:base + 64, 0:512], C.onesf_sb[dp:dp + 1, 0:64],
                          denb[dp:dp + 1, :], start=True, stop=True,
                          tile_position=(dp, base), skip_group_check=True),
                 reads=[b_den, C.b_const], writes=[b_bb])
            P.op("vector", lambda e, base=base, bb=bb: e.tensor_copy(
                rb[base:base + 64, :], bb[base:base + 64, 0:512]),
                 reads=[b_bb], writes=[b_rb])
            P.op("vector", lambda e, base=base, hp=hp: e.tensor_tensor(
                attnst[base:base + 64, hp, :], pf[4][base:base + 64, 0:512],
                rb[base:base + 64, :], ALU.mult),
                 reads=[C.b_pf[4], b_rb], writes=[b_att])


def emit_l0attn(C, src, b_src, dst, b_dst):
    nc, P = C.nc, C.P
    with ExitStack() as bs:
        E = bs.enter_context
        wat = E(nc.sbuf_tensor("l0_watt", [128, 8, 3072], BF16))
        wpr = E(nc.sbuf_tensor("l0_wproj", [128, 8, 1024], BF16))
        qfm = E(nc.sbuf_tensor("l0_qfm", [128, 8, 2048], BF16))
        kfm = E(nc.sbuf_tensor("l0_kfm", [128, 8, 2048], BF16))
        vtm = E(nc.sbuf_tensor("l0_vtm", [128, 16, 8, 128], BF16))
        negc = E(nc.sbuf_tensor("l0_negc", [128, 16], F32))
        nrm = E(nc.sbuf_tensor("l0_nrm", [128, 32], F32))
        ncs = E(nc.sbuf_tensor("l0_ncs", [16, 8], F32))
        ncrow = E(nc.sbuf_tensor("l0_ncrow", [1, 16], F32))
        b_wat, b_wpr, b_q, b_k, b_v = Buf(), Buf(), Buf(), Buf(), Buf()
        b_negc, b_nrm, b_ncs, b_ncrow = Buf(), Buf(), Buf(), Buf()
        P.op("gpsimd", lambda e: e.dma_start(
            out=wat[:], in_=C.dram["w_attnT"].rearrange("(c p) o -> p c o", p=128)),
            writes=[b_wat])
        P.op("gpsimd", lambda e: e.dma_start(
            out=wpr[:], in_=C.dram["w_projT"].rearrange("(c p) o -> p c o", p=128)),
            writes=[b_wpr])
        bat = C.bias_sb["b_attn_fm"]
        bpj = C.bias_sb["b_proj_fm"]

        for b in range(NB):
            with ExitStack() as sA:
                EA = sA.enter_context
                xt = EA(nc.sbuf_tensor(_un("l0a_xt"), [128, 2, 1024], F32))
                u0t = EA(nc.sbuf_tensor(_un("l0a_u0t"), [128, 1024], F32))
                ttm = EA(nc.sbuf_tensor(_un("l0a_ttm"), [128, 1024], BF16))
                tfm = EA(nc.sbuf_tensor(_un("l0a_tfm"), [128, 2, 8, 512], BF16))
                vst = EA(nc.sbuf_tensor(_un("l0a_vst"), [128, 8, 512], BF16))
                sq = EA(nc.sbuf_tensor(_un("l0a_sq"), [128, 1024], F32))
                stat = EA(nc.sbuf_tensor(_un("l0a_stat"), [128, 8], F32))
                b_xt = [Buf(), Buf()]
                b_u0t, b_ttm, b_tfm, b_vst, b_sq, b_stat = (Buf() for _ in range(6))
                P.op("vector", lambda e: e.memset(nrm[:], 0.0), writes=[b_nrm])

                def prep(j):
                    par = j % 2
                    for i in range(4):
                        tile = b * 16 + j * 4 + i
                        P.op("sync", lambda e, t=tile, xt=xt, par=par: e.dma_start(
                            out=xt[:, par, :], in_=src[t * 128:(t + 1) * 128, :]),
                            reads=[b_src[tile]], writes=[b_xt[par]])
                        ln_tile(C, xt[:, par, :], u0t[:], C.epsc[:, 0:1], b_xt[par],
                                b_u0t, sq[:], b_sq, stat[:], b_stat)
                        ln_tile(C, u0t[:], ttm[:], C.epsc[:, 1:2], b_u0t, b_ttm,
                                sq[:], b_sq, stat[:], b_stat)
                        for c in range(8):
                            transpose128(C, ttm[:, c * 128:(c + 1) * 128],
                                         tfm[:, par, c, i * 128:(i + 1) * 128],
                                         b_ttm, b_tfm, BF16)

                prep(0)
                ttv = ttm[:].rearrange("p (s n) -> p s n", s=2)
                for j in range(4):
                    par = j % 2
                    span = slice(j * 512, (j + 1) * 512)
                    for oc in range(24):
                        bank, bbk = C.pf[oc % 3], C.b_pf[oc % 3]
                        for kc in range(8):
                            P.op("tensor", lambda e, bank=bank, kc=kc, oc=oc, tfm=tfm, par=par:
                                 e.matmul(bank[:], wat[:, kc, oc * 128:(oc + 1) * 128],
                                          tfm[:, par, kc, :], start=(kc == 0), stop=(kc == 7)),
                                 reads=[b_wat, b_tfm], writes=[bbk], inc=(kc == 7))
                        if oc < 8:
                            dstap, bd = qfm[:, oc, span], b_q
                        elif oc < 16:
                            dstap, bd = kfm[:, oc - 8, span], b_k
                        else:
                            dstap, bd = vst[:, oc - 16, :], b_vst
                        P.op("scalar", lambda e, bank=bank, oc=oc, dstap=dstap:
                             e.activation(dstap, bank[:], AF.Identity,
                                          bias=bat[:, oc:oc + 1]),
                             reads=[bbk, C.b_const], writes=[bd])
                        if oc < 16:
                            slot = oc % 2
                            P.op("scalar", lambda e, d=dstap, slot=slot, ttm=ttm:
                                 e.activation(ttv[:, slot, :], d, AF.Square),
                                 reads=[bd], writes=[b_ttm])
                            for i in range(4):
                                c = oc % 8
                                P.op("tensor", lambda e, slot=slot, i=i, c=c, ttm=ttm:
                                     e.matmul(
                                         C.pf[3][:, i * 16 + 2 * c:i * 16 + 2 * c + 2],
                                         ttv[:, slot, i * 128:(i + 1) * 128], C.ehb_sb[:],
                                         start=True, stop=True, skip_group_check=True),
                                     reads=[b_ttm, C.b_const], writes=[C.b_pf[3]])
                        if oc == 7 or oc == 15:
                            off = 0 if oc == 7 else 16
                            for i in range(4):
                                P.op("vector", lambda e, off=off, i=i: e.tensor_tensor(
                                    nrm[:, off:off + 16], nrm[:, off:off + 16],
                                    C.pf[3][:, i * 16:(i + 1) * 16], ALU.max),
                                    reads=[b_nrm, C.b_pf[3]], writes=[b_nrm])
                    for c in range(8):
                        for i in range(4):
                            transpose128(C, vst[:, c, i * 128:(i + 1) * 128],
                                         vtm[:, j * 4 + i, c, :], b_vst, b_v, BF16)
                    if j < 3:
                        prep(j + 1)
                negc_calc(C, nrm, b_nrm, ncs, b_ncs, ncrow, b_ncrow, negc, b_negc)
            P.barrier()
            with ExitStack() as sB:
                EB = sB.enter_context
                Pb = EB(nc.sbuf_tensor(_un("l0b_pb"), [128, 2, 2, 512], BF16))
                attnst = EB(nc.sbuf_tensor(_un("l0b_attnst"), [128, 8, 512], BF16))
                denb = EB(nc.sbuf_tensor(_un("l0b_denb"), [64, 512], F32))
                rb = EB(nc.sbuf_tensor(_un("l0b_rb"), [128, 512], F32))
                projst = EB(nc.sbuf_tensor(_un("l0b_projst"), [128, 512], F32))
                u1s = EB(nc.sbuf_tensor(_un("l0b_u1s"), [128, 4, 1024], F32))
                xtB = EB(nc.sbuf_tensor(_un("l0b_xt"), [128, 1024], F32))
                sqB = EB(nc.sbuf_tensor(_un("l0b_sq"), [128, 1024], F32))
                statB = EB(nc.sbuf_tensor(_un("l0b_stat"), [128, 8], F32))
                b_Pb, b_att, b_den, b_rb = Buf(), Buf(), Buf(), Buf()
                b_projst, b_u1s, b_xtB, b_sqB, b_statB = Buf(), Buf(), Buf(), Buf(), Buf()
                for qs in range(4):
                    attn_phaseB(C, qs, 4 * qs + 4,
                                lambda st, qs=qs: max(0, st - 4 * qs) * 128,
                                True, qfm, b_q, kfm, b_k, vtm, b_v,
                                negc, b_negc, attnst, b_att, Pb, b_Pb,
                                denb, b_den, rb, b_rb)
                    for i in range(4):
                        tile = b * 16 + qs * 4 + i
                        P.op("sync", lambda e, t=tile, i=i, xtB=xtB: e.dma_start(
                            out=xtB[:], in_=src[t * 128:(t + 1) * 128, :]),
                            reads=[b_src[tile]], writes=[b_xtB])
                        ln_tile(C, xtB[:], u1s[:, i, :], C.epsc[:, 0:1], b_xtB, b_u1s,
                                sqB[:], b_sqB, statB[:], b_statB)
                    for oc in range(8):
                        bank, bbk = C.pf[2 + (oc % 2)], C.b_pf[2 + (oc % 2)]
                        for kc in range(8):
                            P.op("tensor", lambda e, bank=bank, kc=kc, oc=oc, attnst=attnst:
                                 e.matmul(bank[:], wpr[:, kc, oc * 128:(oc + 1) * 128],
                                          attnst[:, kc, :], start=(kc == 0), stop=(kc == 7)),
                                 reads=[b_wpr, b_att], writes=[bbk], inc=(kc == 7))
                        P.op("scalar", lambda e, bank=bank, oc=oc, projst=projst:
                             e.activation(projst[:], bank[:], AF.Identity,
                                          bias=bpj[:, oc:oc + 1]),
                             reads=[bbk, C.b_const], writes=[b_projst])
                        for i in range(4):
                            tb, btb = C.pf[4 + (i % 2)], C.b_pf[4 + (i % 2)]
                            P.op("tensor", lambda e, tb=tb, i=i, projst=projst: e.transpose(
                                tb[0:128, 0:128], projst[:, i * 128:(i + 1) * 128],
                                C.idf_sb[:]),
                                reads=[b_projst, C.b_const], writes=[btb])
                            P.op("vector", lambda e, tb=tb, i=i, oc=oc, u1s=u1s: e.tensor_tensor(
                                u1s[:, i, oc * 128:(oc + 1) * 128], tb[0:128, 0:128],
                                u1s[:, i, oc * 128:(oc + 1) * 128], ALU.add),
                                reads=[btb, b_u1s], writes=[b_u1s])
                    for i in range(4):
                        tile = b * 16 + qs * 4 + i
                        P.op("sync", lambda e, t=tile, i=i, u1s=u1s: e.dma_start(
                            out=C.u1d[t * 128:(t + 1) * 128, :], in_=u1s[:, i, :]),
                            reads=[b_u1s], writes=[C.b_u1d[tile]])
            P.barrier()
    P.barrier()
    with ExitStack() as sC:
        EC = sC.enter_context
        wfc = EC(nc.sbuf_tensor("l0c_wfc", [128, 8, 4096], BF16))
        wml = EC(nc.sbuf_tensor("l0c_wml", [128, 32, 1024], BF16))
        u1sC = EC(nc.sbuf_tensor("l0c_u1", [128, 4, 1024], F32))
        ut = EC(nc.sbuf_tensor("l0c_ut", [128, 1024], F32))
        ttmC = EC(nc.sbuf_tensor("l0c_ttm", [128, 1024], BF16))
        tfmC = EC(nc.sbuf_tensor("l0c_tfm", [128, 2, 8, 512], BF16))
        hfmC = EC(nc.sbuf_tensor("l0c_hfm", [128, 32, 512], BF16))
        sqC = EC(nc.sbuf_tensor("l0c_sq", [128, 1024], F32))
        statC = EC(nc.sbuf_tensor("l0c_stat", [128, 8], F32))
        mlpstC = ttmC[:].bitcast(F32)
        b_wfc, b_wml, b_u1, b_ut, b_ttm2, b_tfm2, b_hfm = (Buf() for _ in range(7))
        b_mlpst = b_ttm2
        b_zt, b_xt2, b_xo2, b_sq2, b_stat2 = (Buf() for _ in range(5))
        P.op("gpsimd", lambda e: e.dma_start(
            out=wfc[:], in_=C.dram["w_fcT"].rearrange("(c p) o -> p c o", p=128)),
            writes=[b_wfc])
        P.op("gpsimd", lambda e: e.dma_start(
            out=wml[:], in_=C.dram["w_mlpT"].rearrange("(c p) o -> p c o", p=128)),
            writes=[b_wml])
        bfc = C.bias_sb["b_fc_fm"]
        bml = C.bias_sb["b_mlp_fm"]

        def prepC(s):
            par = s % 2
            for i in range(4):
                tile = s * 4 + i
                P.op("sync", lambda e, t=tile, ut=ut: e.dma_start(
                    out=ut[:], in_=C.u1d[t * 128:(t + 1) * 128, :]),
                    reads=[C.b_u1d[tile]], writes=[b_ut])
                ln_tile(C, ut[:], ttmC[:], C.epsc[:, 1:2], b_ut, b_ttm2,
                        sqC[:], b_sq2, statC[:], b_stat2)
                for c in range(8):
                    transpose128(C, ttmC[:, c * 128:(c + 1) * 128],
                                 tfmC[:, par, c, i * 128:(i + 1) * 128],
                                 b_ttm2, b_tfm2, BF16)

        prepC(0)
        for s in range(NSUP):
            par = s % 2
            for oc in range(32):
                bank, bbk = C.pf[oc % 3], C.b_pf[oc % 3]
                for kc in range(8):
                    P.op("tensor", lambda e, bank=bank, kc=kc, oc=oc, par=par:
                         e.matmul(bank[:], wfc[:, kc, oc * 128:(oc + 1) * 128],
                                  tfmC[:, par, kc, :], start=(kc == 0), stop=(kc == 7)),
                         reads=[b_wfc, b_tfm2], writes=[bbk], inc=(kc == 7))
                P.op("scalar", lambda e, bank=bank, oc=oc:
                     e.activation(hfmC[:, oc, :], bank[:], AF.Gelu_apprx_tanh,
                                  bias=bfc[:, oc:oc + 1]),
                     reads=[bbk, C.b_const], writes=[b_hfm])
            if s < NSUP - 1:
                prepC(s + 1)
            for i in range(4):
                tile = s * 4 + i
                P.op("sync", lambda e, t=tile, i=i: e.dma_start(
                    out=u1sC[:, i, :], in_=C.u1d[t * 128:(t + 1) * 128, :]),
                    reads=[C.b_u1d[tile]], writes=[b_u1])
            for oc in range(8):
                bank, bbk = C.pf[3 + (oc % 2)], C.b_pf[3 + (oc % 2)]
                for fc in range(32):
                    P.op("tensor", lambda e, bank=bank, fc=fc, oc=oc:
                         e.matmul(bank[:], wml[:, fc, oc * 128:(oc + 1) * 128],
                                  hfmC[:, fc, :], start=(fc == 0), stop=(fc == 31)),
                         reads=[b_wml, b_hfm], writes=[bbk], inc=(fc == 31))
                P.op("scalar", lambda e, bank=bank, oc=oc:
                     e.activation(mlpstC, bank[:], AF.Identity,
                                  bias=bml[:, oc:oc + 1]),
                     reads=[bbk, C.b_const], writes=[b_mlpst])
                for i in range(4):
                    tb, btb = C.pf[5], C.b_pf[5]
                    P.op("tensor", lambda e, tb=tb, i=i: e.transpose(
                        tb[0:128, 0:128], mlpstC[:, i * 128:(i + 1) * 128], C.idf_sb[:]),
                        reads=[b_mlpst, C.b_const], writes=[btb])
                    P.op("vector", lambda e, tb=tb, i=i, oc=oc: e.tensor_tensor(
                        u1sC[:, i, oc * 128:(oc + 1) * 128], tb[0:128, 0:128],
                        u1sC[:, i, oc * 128:(oc + 1) * 128], ALU.add),
                        reads=[btb, b_u1], writes=[b_u1])
            for i in range(4):
                tile = s * 4 + i
                ln_tile(C, u1sC[:, i, :], ut[:], C.epsc[:, 0:1], b_u1, b_ut,
                        sqC[:], b_sq2, statC[:], b_stat2)
                P.op("sync", lambda e, t=tile: e.dma_start(
                    out=sqC[:], in_=src[t * 128:(t + 1) * 128, :]),
                    reads=[b_src[tile]], writes=[b_sq2])
                P.op("vector", lambda e: e.tensor_tensor(ut[:], ut[:], sqC[:], ALU.add),
                     reads=[b_ut, b_sq2], writes=[b_ut])
                P.op("sync", lambda e, t=tile: e.dma_start(
                    out=dst[t * 128:(t + 1) * 128, :], in_=ut[:]),
                    reads=[b_ut], writes=[b_dst[tile]])
    P.barrier()


def emit_l0ffn(C, src, b_src, dst, b_dst):
    nc, P = C.nc, C.P
    with ExitStack() as bs:
        E = bs.enter_context
        wg = E(nc.sbuf_tensor("f_wg", [128, 8, 8], BF16))
        b2m = E(nc.sbuf_tensor("f_b2m", [8, 1024], BF16))
        gam = E(nc.sbuf_tensor("f_gam", [128, 1024], F32))
        bet = E(nc.sbuf_tensor("f_bet", [128, 1024], F32))
        b_wg, b_b2m, b_gam, b_bet, b_gbrow = (Buf() for _ in range(5))
        P.op("gpsimd", lambda e: e.dma_start(
            out=wg[:], in_=C.dram["wg"].rearrange("(c p) o -> p c o", p=128)),
            writes=[b_wg])
        P.op("gpsimd", lambda e: e.dma_start(out=b2m[:], in_=C.dram["b2_mat"].ap()),
             writes=[b_b2m])
        with nc.sbuf_tensor("f_gbrow", [1, 2, 1024], F32) as gbrow:
            P.op("sync", lambda e: e.dma_start(out=gbrow[0:1, 0, :], in_=C.dram["gamma"].ap()),
                 writes=[b_gbrow])
            P.op("sync", lambda e: e.dma_start(out=gbrow[0:1, 1, :], in_=C.dram["beta"].ap()),
                 writes=[b_gbrow])
            for (gi, tgt, bt) in ((0, gam, b_gam), (1, bet, b_bet)):
                for half in range(2):
                    P.op("tensor", lambda e, gi=gi, half=half: e.matmul(
                        C.pf[4][:, 0:512], C.onesf_sb[0:1, 0:128],
                        gbrow[0:1, gi, half * 512:(half + 1) * 512],
                        start=True, stop=True, skip_group_check=True),
                        reads=[b_gbrow, C.b_const], writes=[C.b_pf[4]])
                    P.op("vector", lambda e, tgt=tgt, half=half: e.tensor_copy(
                        tgt[:, half * 512:(half + 1) * 512], C.pf[4][:, 0:512]),
                        reads=[C.b_pf[4]], writes=[bt])
        P.barrier()

        with ExitStack() as sP:
            EP = sP.enter_context
            xt = EP(nc.sbuf_tensor("f_xt", [128, 1024], F32))
            ttm = EP(nc.sbuf_tensor("f_ttm", [128, 1024], BF16))
            u3st = EP(nc.sbuf_tensor("f_u3st", [128, 8, 512], BF16))
            lgst = EP(nc.sbuf_tensor("f_lgst", [8, 512], F32))
            lgtm = EP(nc.sbuf_tensor("f_lgtm", [128, 4, 8], F32))
            g1 = EP(nc.sbuf_tensor("f_g1", [128, 4, 8], F32))
            g2 = EP(nc.sbuf_tensor("f_g2", [128, 4, 8], F32))
            g3 = EP(nc.sbuf_tensor("f_g3", [128, 4, 8], F32))
            g4 = EP(nc.sbuf_tensor("f_g4", [128, 4, 8], F32))
            gsc = EP(nc.sbuf_tensor("f_gsc", [128, 20], F32))
            grst = EP(nc.sbuf_tensor("f_grst", [8, 4, 128], F32))
            sq = EP(nc.sbuf_tensor("f_sq", [128, 1024], F32))
            stat = EP(nc.sbuf_tensor("f_stat", [128, 8], F32))
            b_xt, b_ttm, b_u3st, b_lgst, b_lgtm = (Buf() for _ in range(5))
            b_g1, b_g2, b_g3, b_g4, b_gsc, b_grst, b_sq, b_stat = (Buf() for _ in range(8))
            for s in range(NSUP):
                for i in range(4):
                    tile = s * 4 + i
                    P.op("sync", lambda e, t=tile: e.dma_start(
                        out=xt[:], in_=src[t * 128:(t + 1) * 128, :]),
                        reads=[b_src[tile]], writes=[b_xt])
                    ln_tile(C, xt[:], ttm[:], C.epsc[:, 0:1], b_xt, b_ttm,
                            sq[:], b_sq, stat[:], b_stat)
                    for c in range(8):
                        transpose128(C, ttm[:, c * 128:(c + 1) * 128],
                                     u3st[:, c, i * 128:(i + 1) * 128],
                                     b_ttm, b_u3st, BF16)
                P.op("sync", lambda e, s=s: e.dma_start(
                    out=C.u3fmd[s].rearrange("p (c n) -> p c n", c=8), in_=u3st[:]),
                    reads=[b_u3st], writes=[C.b_u3fmd[s]])
                for c in range(8):
                    P.op("tensor", lambda e, c=c: e.matmul(
                        C.pf[3][0:8, 0:512], wg[:, c, :], u3st[:, c, :],
                        start=(c == 0), stop=(c == 7), skip_group_check=True),
                        reads=[b_wg, b_u3st], writes=[C.b_pf[3]], inc=(c == 7))
                P.op("scalar", lambda e: e.activation(
                    lgst[:], C.pf[3][0:8, 0:512], AF.Identity,
                    bias=C.bias_sb["bg_col"][:]),
                    reads=[C.b_pf[3], C.b_const], writes=[b_lgst])
                for i in range(4):
                    P.op("tensor", lambda e, i=i: e.transpose(
                        C.pf[2][0:128, i * 8:(i + 1) * 8],
                        lgst[0:8, i * 128:(i + 1) * 128], C.idf_sb[0:8, 0:8]),
                        reads=[b_lgst, C.b_const], writes=[C.b_pf[2]])
                    P.op("vector", lambda e, i=i: e.tensor_copy(
                        lgtm[:, i, :], C.pf[2][0:128, i * 8:(i + 1) * 8]),
                        reads=[C.b_pf[2]], writes=[b_lgtm])
                m1 = gsc[:, 0:4]
                m2 = gsc[:, 4:8]
                dd = gsc[:, 8:12]
                w1 = gsc[:, 12:16]
                w2 = gsc[:, 16:20]
                m1b = m1.rearrange("p (a o) -> p a o", a=4).broadcast_to([128, 4, 8])
                m2b = m2.rearrange("p (a o) -> p a o", a=4).broadcast_to([128, 4, 8])
                w1b = w1.rearrange("p (a o) -> p a o", a=4).broadcast_to([128, 4, 8])
                w2b = w2.rearrange("p (a o) -> p a o", a=4).broadcast_to([128, 4, 8])
                P.op("vector", lambda e: e.reduce_max(m1, lgtm[:], AX.X),
                     reads=[b_lgtm], writes=[b_gsc])
                P.op("vector", lambda e: e.tensor_tensor(g1[:], lgtm[:], m1b, ALU.is_equal),
                     reads=[b_lgtm, b_gsc], writes=[b_g1])
                P.op("vector", lambda e: e.scalar_tensor_tensor(
                    g2[:], g1[:], -1e30, lgtm[:], ALU.mult, ALU.add),
                    reads=[b_g1, b_lgtm], writes=[b_g2])
                P.op("vector", lambda e: e.reduce_max(m2, g2[:], AX.X),
                     reads=[b_g2], writes=[b_gsc])
                P.op("vector", lambda e: e.tensor_tensor(dd, m1, m2, ALU.subtract),
                     reads=[b_gsc], writes=[b_gsc])
                P.op("scalar", lambda e: e.activation(w1, dd, AF.Sigmoid),
                     reads=[b_gsc], writes=[b_gsc])
                P.op("scalar", lambda e: e.activation(w2, dd, AF.Sigmoid, scale=-1.0),
                     reads=[b_gsc], writes=[b_gsc])
                P.op("vector", lambda e: e.tensor_tensor(g3[:], g2[:], m2b, ALU.is_equal),
                     reads=[b_g2, b_gsc], writes=[b_g3])
                P.op("vector", lambda e: e.tensor_tensor(g4[:], g1[:], w1b, ALU.mult),
                     reads=[b_g1, b_gsc], writes=[b_g4])
                P.op("vector", lambda e: e.tensor_tensor(g1[:], g3[:], w2b, ALU.mult),
                     reads=[b_g3, b_gsc], writes=[b_g1])
                P.op("vector", lambda e: e.tensor_tensor(g4[:], g4[:], g1[:], ALU.add),
                     reads=[b_g4, b_g1], writes=[b_g4])
                for i in range(4):
                    P.op("tensor", lambda e, i=i: e.transpose(
                        C.pf[1][0:8, i * 128:(i + 1) * 128],
                        g4[:, i, :], C.idf_sb[:]),
                        reads=[b_g4, C.b_const], writes=[C.b_pf[1]])
                    P.op("vector", lambda e, i=i: e.tensor_copy(
                        grst[0:8, i, :], C.pf[1][0:8, i * 128:(i + 1) * 128]),
                        reads=[C.b_pf[1]], writes=[b_grst])
                P.op("sync", lambda e, s=s: e.dma_start(
                    out=C.gd[:, s * 512:(s + 1) * 512],
                    in_=grst[:].rearrange("p a n -> p (a n)")),
                    reads=[b_grst], writes=[C.b_gd[s]])
        P.barrier()

        with ExitStack() as sM:
            EM = sM.enter_context
            w1bufs = [EM(nc.sbuf_tensor(f"f_w1{i}", [128, 8, 2048], BF16)) for i in range(2)]
            w2sb = EM(nc.sbuf_tensor("f_w2", [128, 16, 1024], BF16))
            u3t = EM(nc.sbuf_tensor("f_u3t", [128, 2, 2, 8, 512], BF16))
            hb = EM(nc.sbuf_tensor("f_h", [128, 16, 512], BF16))
            gbc = EM(nc.sbuf_tensor("f_gbc", [128, 512], BF16))
            g1r = EM(nc.sbuf_tensor("f_g1r", [1, 512], F32))
            g8 = EM(nc.sbuf_tensor("f_g8", [8, 2, 512], BF16))
            mixed = EM(nc.sbuf_tensor("f_mixed", [128, 2, 8, 512], F32))
            xt2 = EM(nc.sbuf_tensor("f_xt2", [128, 1024], F32))
            u3r = EM(nc.sbuf_tensor("f_u3r", [128, 1024], F32))
            zt = EM(nc.sbuf_tensor("f_zt", [128, 1024], F32))
            sq2 = zt
            stat2 = EM(nc.sbuf_tensor("f_stat2", [128, 8], F32))
            b_w1 = [Buf(), Buf()]
            b_w2, b_u3t, b_h, b_gbc, b_g1r, b_g8, b_mixed = (Buf() for _ in range(7))
            b_xt2, b_u3r, b_zt, b_stat2 = (Buf() for _ in range(4))
            b_sq2 = b_zt
            b1f = C.bias_sb["b1_fm"]
            for H in range(8):
                sts = [2 * H, 2 * H + 1]
                hpar = H % 2
                for si, s in enumerate(sts):
                    P.op("sync", lambda e, s=s, si=si, hpar=hpar: e.dma_start(
                        out=u3t[:, hpar, si, :, :],
                        in_=C.u3fmd[s].rearrange("p (c n) -> p c n", c=8)),
                        reads=[C.b_u3fmd[s]], writes=[b_u3t])
                    P.op("gpsimd", lambda e, s=s, si=si: e.dma_start(
                        out=g8[:, si, :], in_=C.gd[:, s * 512:(s + 1) * 512]),
                        reads=[C.b_gd[s]], writes=[b_g8])
                for e_ in range(8):
                    wb, bwb = w1bufs[e_ % 2], b_w1[e_ % 2]
                    P.op("gpsimd", lambda e, e_=e_, wb=wb: e.dma_start(
                        out=wb[:], in_=C.dram["moe_w1"][e_].rearrange(
                            "(c p) f -> p c f", p=128)),
                        writes=[bwb])
                    P.op("gpsimd", lambda e, e_=e_: e.dma_start(
                        out=w2sb[:], in_=C.dram["moe_w2"][e_].rearrange(
                            "(c p) f -> p c f", p=128)),
                        writes=[b_w2])
                    for si, s in enumerate(sts):
                        P.op("sync", lambda e, s=s, e_=e_: e.dma_start(
                            out=g1r[:], in_=C.gd[e_:e_ + 1, s * 512:(s + 1) * 512]),
                            reads=[C.b_gd[s]], writes=[b_g1r])
                        P.op("tensor", lambda e: e.matmul(
                            C.pf[4][:, 0:512], C.onesf_sb[0:1, 0:128], g1r[:],
                            start=True, stop=True, skip_group_check=True),
                            reads=[b_g1r, C.b_const], writes=[C.b_pf[4]])
                        P.op("vector", lambda e: e.tensor_copy(gbc[:], C.pf[4][:, 0:512]),
                             reads=[C.b_pf[4]], writes=[b_gbc])
                        for foc in range(16):
                            bank, bbk = C.pf[foc % 2], C.b_pf[foc % 2]
                            for kc in range(8):
                                P.op("tensor", lambda e, bank=bank, kc=kc, foc=foc, si=si, wb=wb, hpar=hpar:
                                     e.matmul(bank[:], wb[:, kc, foc * 128:(foc + 1) * 128],
                                              u3t[:, hpar, si, kc, :], start=(kc == 0),
                                              stop=(kc == 7)),
                                     reads=[bwb, b_u3t], writes=[bbk], inc=(kc == 7))
                            P.op("scalar", lambda e, bank=bank, foc=foc, e_=e_:
                                 e.activation(hb[:, foc, :], bank[:], AF.Relu,
                                              bias=b1f[:, e_, foc:foc + 1]),
                                 reads=[bbk, C.b_const], writes=[b_h])
                            P.op("vector", lambda e, foc=foc:
                                 e.tensor_tensor(hb[:, foc, :], hb[:, foc, :],
                                                 gbc[:], ALU.mult),
                                 reads=[b_h, b_gbc], writes=[b_h])
                        for dc in range(8):
                            bank, bbk = C.pf[2 + dc % 2], C.b_pf[2 + dc % 2]
                            for fc in range(16):
                                P.op("tensor", lambda e, bank=bank, fc=fc, dc=dc, e_=e_:
                                     e.matmul(bank[:], w2sb[:, fc, dc * 128:(dc + 1) * 128],
                                              hb[:, fc, :], start=(fc == 0),
                                              stop=(fc == 15 and e_ != 0)),
                                     reads=[b_w2, b_h], writes=[bbk],
                                     inc=(fc == 15 and e_ != 0))
                            if e_ == 0:
                                P.op("tensor", lambda e, bank=bank, dc=dc, si=si:
                                     e.matmul(bank[:], b2m[0:8, dc * 128:(dc + 1) * 128],
                                              g8[:, si, :], start=False, stop=True,
                                              skip_group_check=True),
                                     reads=[b_b2m, b_g8], writes=[bbk], inc=True)
                            if e_ == 0:
                                P.op("vector", lambda e, bank=bank, dc=dc, si=si:
                                     e.tensor_copy(mixed[:, si, dc, :], bank[:]),
                                     reads=[bbk], writes=[b_mixed])
                            else:
                                P.op("vector", lambda e, bank=bank, dc=dc, si=si:
                                     e.tensor_tensor(mixed[:, si, dc, :], bank[:],
                                                     mixed[:, si, dc, :], ALU.add),
                                     reads=[bbk, b_mixed], writes=[b_mixed])
                for si, s in enumerate(sts):
                    for i in range(4):
                        tile = s * 4 + i
                        P.op("sync", lambda e, t=tile: e.dma_start(
                            out=xt2[:], in_=src[t * 128:(t + 1) * 128, :]),
                            reads=[b_src[tile]], writes=[b_xt2])
                        ln_tile(C, xt2[:], u3r[:], C.epsc[:, 0:1], b_xt2, b_u3r,
                                sq2[:], b_sq2, stat2[:], b_stat2)
                        for dc in range(8):
                            tb, btb = C.pf[4 + (dc % 2)], C.b_pf[4 + (dc % 2)]
                            P.op("tensor", lambda e, tb=tb, si=si, dc=dc, i=i: e.transpose(
                                tb[0:128, 0:128], mixed[:, si, dc, i * 128:(i + 1) * 128],
                                C.idf_sb[:]),
                                reads=[b_mixed, C.b_const], writes=[btb])
                            P.op("vector", lambda e, tb=tb, dc=dc: e.tensor_tensor(
                                u3r[:, dc * 128:(dc + 1) * 128], tb[0:128, 0:128],
                                u3r[:, dc * 128:(dc + 1) * 128], ALU.add),
                                reads=[btb, b_u3r], writes=[b_u3r])
                        rstd, nmr = ln_tile(C, u3r[:], zt[:], C.epsc[:, 0:1], b_u3r, b_zt,
                                            sq2[:], b_sq2, stat2[:], b_stat2)
                        P.op("vector", lambda e: e.tensor_tensor(zt[:], zt[:], gam[:], ALU.mult),
                             reads=[b_zt, b_gam], writes=[b_zt])
                        P.op("vector", lambda e: e.tensor_tensor(zt[:], zt[:], bet[:], ALU.add),
                             reads=[b_zt, b_bet], writes=[b_zt])
                        ln_tile(C, zt[:], u3r[:], C.epsc[:, 0:1], b_zt, b_u3r,
                                sq2[:], b_sq2, stat2[:], b_stat2)
                        P.op("vector", lambda e: e.tensor_tensor(u3r[:], u3r[:], xt2[:], ALU.add),
                             reads=[b_u3r, b_xt2], writes=[b_u3r])
                        P.op("sync", lambda e, t=tile: e.dma_start(
                            out=dst[t * 128:(t + 1) * 128, :], in_=u3r[:]),
                            reads=[b_u3r], writes=[b_dst[tile]])
    P.barrier()


def tap(C, name, ap, buf):
    if not getattr(C, "want_taps", None) or name not in C.want_taps:
        return
    nc, P = C.nc, C.P
    sh = list(ap.shape)
    d = nc.dram_tensor("tap_" + name, sh, ap.dtype, kind="ExternalOutput")
    P.op("sync", lambda e: e.dma_start(out=d.ap(), in_=ap), reads=[buf])


def negc_calc(C, nrm, b_nrm, ncs, b_ncs, ncrow, b_ncrow, negc, b_negc):
    P = C.P
    P.op("tensor", lambda e: e.transpose(C.pf[2][0:16, 0:128], nrm[:, 0:16], C.idf_sb[:]),
         reads=[b_nrm, C.b_const], writes=[C.b_pf[2]])
    P.op("vector", lambda e: e.reduce_max(ncs[:, 0:1], C.pf[2][0:16, 0:128], AX.X),
         reads=[C.b_pf[2]], writes=[b_ncs])
    P.op("tensor", lambda e: e.transpose(C.pf[2][0:16, 128:256], nrm[:, 16:32], C.idf_sb[:]),
         reads=[b_nrm, C.b_const], writes=[C.b_pf[2]])
    P.op("vector", lambda e: e.reduce_max(ncs[:, 1:2], C.pf[2][0:16, 128:256], AX.X),
         reads=[C.b_pf[2]], writes=[b_ncs])
    P.op("scalar", lambda e: e.activation(ncs[:, 2:3], ncs[:, 0:1], AF.Sqrt),
         reads=[b_ncs], writes=[b_ncs])
    P.op("scalar", lambda e: e.activation(ncs[:, 3:4], ncs[:, 1:2], AF.Sqrt),
         reads=[b_ncs], writes=[b_ncs])
    P.op("vector", lambda e: e.tensor_tensor(ncs[:, 4:5], ncs[:, 2:3], ncs[:, 3:4], ALU.mult),
         reads=[b_ncs], writes=[b_ncs])
    P.op("vector", lambda e: e.tensor_single_scalar(ncs[:, 5:6], ncs[:, 4:5], -0.125, ALU.mult),
         reads=[b_ncs], writes=[b_ncs])
    P.op("tensor", lambda e: e.transpose(C.pf[2][0:1, 256:272], ncs[:, 5:6],
                                         C.idf_sb[0:16, 0:16]),
         reads=[b_ncs, C.b_const], writes=[C.b_pf[2]])
    P.op("vector", lambda e: e.tensor_copy(ncrow[:], C.pf[2][0:1, 256:272]),
         reads=[C.b_pf[2]], writes=[b_ncrow])
    P.op("tensor", lambda e: e.matmul(C.pf[2][:, 272:288], C.onesf_sb[0:1, 0:128],
                                      ncrow[:], start=True, stop=True,
                                      skip_group_check=True),
         reads=[b_ncrow, C.b_const], writes=[C.b_pf[2]])
    P.op("vector", lambda e: e.tensor_copy(negc[:], C.pf[2][:, 272:288]),
         reads=[C.b_pf[2]], writes=[b_negc])


def emit_l1attn(C, src, b_src, dst, b_dst):
    nc, P = C.nc, C.P
    with ExitStack() as bs:
        E = bs.enter_context
        wq = E(nc.sbuf_tensor("l1_wq", [128, 8, 1024], BF16))
        wdn = E(nc.sbuf_tensor("l1_wdn", [128, 8, 512], BF16))
        wku = E(nc.sbuf_tensor("l1_wku", [128, 4, 1024], BF16))
        wvu = E(nc.sbuf_tensor("l1_wvu", [128, 4, 1024], BF16))
        qfm = E(nc.sbuf_tensor("l1_qfm", [128, 8, 2048], BF16))
        kfm = E(nc.sbuf_tensor("l1_kfm", [128, 8, 2048], BF16))
        vtm = E(nc.sbuf_tensor("l1_vtm", [128, 16, 8, 128], BF16))
        negc = E(nc.sbuf_tensor("l1_negc", [128, 16], F32))
        nrm = E(nc.sbuf_tensor("l1_nrm", [128, 32], F32))
        ncs = E(nc.sbuf_tensor("l1_ncs", [16, 8], F32))
        ncrow = E(nc.sbuf_tensor("l1_ncrow", [1, 16], F32))
        cos_sb = E(nc.sbuf_tensor("l1_cos", [128, 16, 32], F32))
        sin_sb = E(nc.sbuf_tensor("l1_sin", [128, 16, 32], F32))
        b_wq, b_wdn, b_wku, b_wvu = (Buf() for _ in range(4))
        b_q, b_k, b_v, b_negc, b_nrm, b_ncs, b_ncrow = (Buf() for _ in range(7))
        for (sb, nm, bw) in [(wq, "wqP", b_wq), (wdn, "w_down", b_wdn),
                             (wku, "wk_upP", b_wku), (wvu, "wv_up", b_wvu)]:
            P.op("gpsimd", lambda e, sb=sb, nm=nm: e.dma_start(
                out=sb[:], in_=C.dram[nm].rearrange("(c p) o -> p c o", p=128)),
                writes=[bw])
        P.op("sync", lambda e: e.dma_start(
            out=cos_sb[:], in_=C.dram["ropecos"].rearrange("(t p) d -> p t d", p=128)),
            writes=[C.b_const])
        P.op("sync", lambda e: e.dma_start(
            out=sin_sb[:], in_=C.dram["ropesin"].rearrange("(t p) d -> p t d", p=128)),
            writes=[C.b_const])

        for b in range(NB):
            with ExitStack() as sA:
                EA = sA.enter_context
                xt = EA(nc.sbuf_tensor(_un("l1a_xt"), [128, 1024], F32))
                ttm = EA(nc.sbuf_tensor(_un("l1a_ttm"), [128, 1024], BF16))
                tfm = EA(nc.sbuf_tensor(_un("l1a_tfm"), [128, 8, 512], BF16))
                latfm = EA(nc.sbuf_tensor(_un("l1a_lat"), [128, 4, 512], BF16))
                qst = EA(nc.sbuf_tensor(_un("l1a_qst"), [128, 512], F32))
                qtm = EA(nc.sbuf_tensor(_un("l1a_qtm"), [128, 4, 1024], F32))
                rout = EA(nc.sbuf_tensor(_un("l1a_rout"), [128, 1024], BF16))
                rscr = EA(nc.sbuf_tensor(_un("l1a_rscr"), [128, 1024], F32))
                vst = EA(nc.sbuf_tensor(_un("l1a_vst"), [128, 8, 512], BF16))
                sq = EA(nc.sbuf_tensor(_un("l1a_sq"), [128, 1024], F32))
                nrmt = EA(nc.sbuf_tensor(_un("l1a_nrmt"), [128, 16], F32))
                stat = EA(nc.sbuf_tensor(_un("l1a_stat"), [128, 8], F32))
                b_xt, b_ttm, b_tfm, b_lat, b_qst = (Buf() for _ in range(5))
                b_qtm, b_rout, b_rscr, b_vst, b_sq, b_nrmt, b_stat = (Buf() for _ in range(7))
                P.op("vector", lambda e: e.memset(nrm[:], 0.0), writes=[b_nrm])

                def fm_grid(wsb, b_w, nkc, noc, rhs, b_rhs, evac):
                    for oc in range(noc):
                        bank, bbk = C.pf[oc % 3], C.b_pf[oc % 3]
                        for kc in range(nkc):
                            P.op("tensor", lambda e, bank=bank, kc=kc, oc=oc:
                                 e.matmul(bank[:], wsb[:, kc, oc * 128:(oc + 1) * 128],
                                          rhs[:, kc, :], start=(kc == 0),
                                          stop=(kc == nkc - 1)),
                                 reads=[b_w, b_rhs], writes=[bbk], inc=(kc == nkc - 1))
                        evac(oc, bank, bbk)

                def tm_rope_path(j, srcname, b_srcw, off):
                    pass

                for j in range(4):
                    span = slice(j * 512, (j + 1) * 512)
                    for i in range(4):
                        tile = b * 16 + j * 4 + i
                        P.op("sync", lambda e, t=tile, xt=xt: e.dma_start(
                            out=xt[:], in_=src[t * 128:(t + 1) * 128, :]),
                            reads=[b_src[tile]], writes=[b_xt])
                        ln_tile(C, xt[:], ttm[:], C.epsc[:, 0:1], b_xt, b_ttm,
                                sq[:], b_sq, stat[:], b_stat)
                        for c in range(8):
                            transpose128(C, ttm[:, c * 128:(c + 1) * 128],
                                         tfm[:, c, i * 128:(i + 1) * 128],
                                         b_ttm, b_tfm, BF16)

                    def evac_lat(oc, bank, bbk):
                        P.op("scalar", lambda e, bank=bank, oc=oc, latfm=latfm: e.activation(
                            latfm[:, oc, :], bank[:], AF.Identity),
                            reads=[bbk], writes=[b_lat])
                    fm_grid(wdn, b_wdn, 8, 4, tfm, b_tfm, evac_lat)

                    def evac_v(oc, bank, bbk):
                        P.op("scalar", lambda e, bank=bank, oc=oc, vst=vst: e.activation(
                            vst[:, oc, :], bank[:], AF.Identity),
                            reads=[bbk], writes=[b_vst])
                    fm_grid(wvu, b_wvu, 4, 8, latfm, b_lat, evac_v)
                    for c in range(8):
                        for i in range(4):
                            transpose128(C, vst[:, c, i * 128:(i + 1) * 128],
                                         vtm[:, j * 4 + i, c, :], b_vst, b_v, BF16)

                    for (wsb, b_w, nkc, rhsbuf, b_rhs, dstfm, b_dst2, off) in (
                            (wq, b_wq, 8, tfm, b_tfm, qfm, b_q, 0),
                            (wku, b_wku, 4, latfm, b_lat, kfm, b_k, 16)):
                        def evac_q(oc, bank, bbk):
                            P.op("scalar", lambda e, bank=bank, qst=qst: e.activation(
                                qst[:], bank[:], AF.Identity),
                                reads=[bbk], writes=[b_qst])
                            for i in range(4):
                                tb, btb = C.pf[3 + (i % 2)], C.b_pf[3 + (i % 2)]
                                P.op("tensor", lambda e, tb=tb, i=i, qst=qst: e.transpose(
                                    tb[0:128, 0:128], qst[:, i * 128:(i + 1) * 128],
                                    C.idf_sb[:]),
                                    reads=[b_qst, C.b_const], writes=[btb])
                                P.op("vector", lambda e, tb=tb, i=i, oc=oc, qtm=qtm:
                                     e.tensor_copy(qtm[:, i, oc * 128:(oc + 1) * 128],
                                                   tb[0:128, 0:128]),
                                     reads=[btb], writes=[b_qtm])
                        fm_grid(wsb, b_w, nkc, 8, rhsbuf, b_rhs, evac_q)
                        for i in range(4):
                            q3 = qtm[:, i, :].rearrange("p (h two d) -> p h two d",
                                                        h=16, two=2)
                            x1 = q3[:, :, 0, :]
                            x2 = q3[:, :, 1, :]
                            ti = j * 4 + i
                            cb = cos_sb[:, ti, :].rearrange("p (o d) -> p o d", o=1
                                                            ).broadcast_to([128, 16, 32])
                            sb_ = sin_sb[:, ti, :].rearrange("p (o d) -> p o d", o=1
                                                             ).broadcast_to([128, 16, 32])
                            r1 = rscr[:, 0:512].rearrange("p (h d) -> p h d", h=16)
                            r2 = rscr[:, 512:1024].rearrange("p (h d) -> p h d", h=16)
                            ro = rout[:].rearrange("p (h two d) -> p h two d", h=16, two=2)
                            P.op("vector", lambda e, x1=x1, cb=cb, r1=r1:
                                 e.tensor_tensor(r1, x1, cb, ALU.mult),
                                 reads=[b_qtm, C.b_const], writes=[b_rscr])
                            P.op("vector", lambda e, x2=x2, sbx=sb_, r2=r2:
                                 e.tensor_tensor(r2, x2, sbx, ALU.mult),
                                 reads=[b_qtm, C.b_const], writes=[b_rscr])
                            P.op("vector", lambda e, r1=r1, r2=r2, ro=ro:
                                 e.tensor_tensor(ro[:, :, 0, :], r1, r2, ALU.subtract),
                                 reads=[b_rscr], writes=[b_rout])
                            P.op("vector", lambda e, x1=x1, sbx=sb_, r1=r1:
                                 e.tensor_tensor(r1, x1, sbx, ALU.mult),
                                 reads=[b_qtm, C.b_const], writes=[b_rscr])
                            P.op("vector", lambda e, x2=x2, cb=cb, r2=r2:
                                 e.tensor_tensor(r2, x2, cb, ALU.mult),
                                 reads=[b_qtm, C.b_const], writes=[b_rscr])
                            P.op("vector", lambda e, r1=r1, r2=r2, ro=ro:
                                 e.tensor_tensor(ro[:, :, 1, :], r1, r2, ALU.add),
                                 reads=[b_rscr], writes=[b_rout])
                            P.op("vector", lambda e, i=i, sq=sq, qtm=qtm: e.tensor_tensor(
                                sq[:], qtm[:, i, :], qtm[:, i, :], ALU.mult),
                                reads=[b_qtm], writes=[b_sq])
                            P.op("vector", lambda e, sq=sq, nrmt=nrmt: e.reduce_sum(
                                nrmt[:], sq[:].rearrange("p (h d) -> p h d", h=16), AX.X),
                                reads=[b_sq], writes=[b_nrmt])
                            P.op("vector", lambda e, off=off, nrmt=nrmt: e.tensor_tensor(
                                nrm[:, off:off + 16], nrm[:, off:off + 16],
                                nrmt[:], ALU.max),
                                reads=[b_nrm, b_nrmt], writes=[b_nrm])
                            for c in range(8):
                                transpose128(C, rout[:, c * 128:(c + 1) * 128],
                                             dstfm[:, c, (j * 4 + i) * 128:(j * 4 + i + 1) * 128],
                                             b_rout, b_dst2, BF16)
                negc_calc(C, nrm, b_nrm, ncs, b_ncs, ncrow, b_ncrow, negc, b_negc)
                if b == 0:
                    tap(C, "l1_qfm", qfm[:], b_q)
                    tap(C, "l1_kfm", kfm[:], b_k)
                    tap(C, "l1_vtm", vtm[:], b_v)
                    tap(C, "l1_negc", negc[:], b_negc)
            P.barrier()
            with ExitStack() as sB:
                EB = sB.enter_context
                wo = EB(nc.sbuf_tensor(_un("l1b_wo"), [128, 8, 1024], BF16))
                b_wo = Buf()
                P.op("gpsimd", lambda e: e.dma_start(
                    out=wo[:], in_=C.dram["wo"].rearrange("(c p) o -> p c o", p=128)),
                    writes=[b_wo])
                Pb = EB(nc.sbuf_tensor(_un("l1b_pb"), [128, 2, 2, 512], BF16))
                attnst = EB(nc.sbuf_tensor(_un("l1b_attnst"), [128, 8, 512], BF16))
                denb = EB(nc.sbuf_tensor(_un("l1b_denb"), [64, 512], F32))
                rb = EB(nc.sbuf_tensor(_un("l1b_rb"), [128, 512], F32))
                wost = EB(nc.sbuf_tensor(_un("l1b_wost"), [128, 512], F32))
                otm = EB(nc.sbuf_tensor(_un("l1b_otm"), [128, 4, 1024], F32))
                zt = EB(nc.sbuf_tensor(_un("l1b_zt"), [128, 1024], F32))
                xtB = EB(nc.sbuf_tensor(_un("l1b_xt"), [128, 1024], F32))
                sqB = EB(nc.sbuf_tensor(_un("l1b_sq"), [128, 1024], F32))
                statB = EB(nc.sbuf_tensor(_un("l1b_stat"), [128, 8], F32))
                b_Pb, b_att, b_den, b_rb, b_wost = (Buf() for _ in range(5))
                b_otm, b_zt, b_xtB, b_sqB, b_statB = (Buf() for _ in range(5))
                C.tap_stage = (wost, b_wost)
                C.tap_stage2 = (zt, b_zt)
                for qs in range(4):
                    attn_phaseB(C, qs, 4 * qs + 4,
                                lambda st, qs=qs: max(0, (st & ~1) - 4 * qs) * 128,
                                False, qfm, b_q, kfm, b_k, vtm, b_v,
                                negc, b_negc, attnst, b_att, Pb, b_Pb,
                                denb, b_den, rb, b_rb)
                    if b == 0 and qs == 0:
                        tap(C, "l1_attnst0", attnst[:], b_att)
                    for oc in range(8):
                        bank, bbk = C.pf[2 + (oc % 2)], C.b_pf[2 + (oc % 2)]
                        for kc in range(8):
                            P.op("tensor", lambda e, bank=bank, kc=kc, oc=oc, wo=wo, attnst=attnst:
                                 e.matmul(bank[:], wo[:, kc, oc * 128:(oc + 1) * 128],
                                          attnst[:, kc, :], start=(kc == 0), stop=(kc == 7)),
                                 reads=[b_wo, b_att], writes=[bbk], inc=(kc == 7))
                        P.op("scalar", lambda e, bank=bank, wost=wost: e.activation(
                            wost[:], bank[:], AF.Identity),
                            reads=[bbk], writes=[b_wost])
                        for i in range(4):
                            tb, btb = C.pf[4 + (i % 2)], C.b_pf[4 + (i % 2)]
                            P.op("tensor", lambda e, tb=tb, i=i, wost=wost: e.transpose(
                                tb[0:128, 0:128], wost[:, i * 128:(i + 1) * 128],
                                C.idf_sb[:]),
                                reads=[b_wost, C.b_const], writes=[btb])
                            P.op("vector", lambda e, tb=tb, i=i, oc=oc, otm=otm: e.tensor_copy(
                                otm[:, i, oc * 128:(oc + 1) * 128], tb[0:128, 0:128]),
                                reads=[btb], writes=[b_otm])
                    for i in range(4):
                        tile = b * 16 + qs * 4 + i
                        ln_tile(C, otm[:, i, :], zt[:], C.epsc[:, 0:1], b_otm, b_zt,
                                sqB[:], b_sqB, statB[:], b_statB)
                        P.op("sync", lambda e, t=tile, xtB=xtB: e.dma_start(
                            out=xtB[:], in_=src[t * 128:(t + 1) * 128, :]),
                            reads=[b_src[tile]], writes=[b_xtB])
                        P.op("vector", lambda e, zt=zt, xtB=xtB: e.tensor_tensor(xtB[:], zt[:], xtB[:], ALU.add),
                             reads=[b_zt, b_xtB], writes=[b_xtB])
                        P.op("sync", lambda e, t=tile, xtB=xtB: e.dma_start(
                            out=dst[t * 128:(t + 1) * 128, :], in_=xtB[:]),
                            reads=[b_xtB], writes=[b_dst[tile]])
            P.barrier()
    P.barrier()


def build_program(blocks=("l0a", "l0f", "l1a", "l1f"), want_taps=None, paranoia=0, **cflags):
    nc = bass.Bass("TRN2", target_bir_lowering=False)
    C = Ctx()
    C.nc = nc
    C.want_taps = want_taps
    C.paranoia = paranoia
    for k_, v_ in cflags.items():
        setattr(C, k_, v_)
    C.dram = {}

    def din(name, shape, dtype=F32):
        C.dram[name] = nc.dram_tensor(name, list(shape), dtype, kind="ExternalInput")
        return C.dram[name]

    din("x", [NB * T, D])
    dout = nc.dram_tensor("out", [NB * T, D], F32, kind="ExternalOutput")
    C.dram["out"] = dout

    din("idb", [128, 128], BF16)
    din("idf", [128, 128], F32)
    din("onesf", [128, 128], F32)
    din("onescb", [128, 1], BF16)
    din("epsc", [128, 2], F32)
    if "l0a" in blocks or "l1a" in blocks:
        din("eh", [128, 2], F32)
        din("ehb", [128, 2], BF16)
    if "l0a" in blocks:
        din("tri", [128, 128], BF16)
        din("w_attnT", [D, 3 * D])
        din("b_attn_fm", [128, 24])
        din("w_projT", [D, D])
        din("b_proj_fm", [128, 8])
        din("w_fcT", [D, 4 * D])
        din("b_fc_fm", [128, 32])
        din("w_mlpT", [4 * D, D])
        din("b_mlp_fm", [128, 8])
    if "l0f" in blocks:
        din("wg", [D, 8])
        din("bg_col", [8, 1])
        din("moe_w1", [8, D, 2 * D])
        din("b1_fm", [128, 8, 16])
        din("moe_w2", [8, 2 * D, D])
        din("b2_mat", [8, D])
        din("gamma", [1, D])
        din("beta", [1, D])
    if "l1a" in blocks:
        din("ropecos", [T, 32])
        din("ropesin", [T, 32])
        din("wqP", [D, D])
        din("w_down", [D, 512])
        din("wk_upP", [512, D])
        din("wv_up", [512, D])
        din("wo", [D, D])
    if "l1f" in blocks:
        din("l1f_w", [D, D])
        din("l1f_b_fm", [128, 8])

    need_xres = len(blocks) > 1
    if need_xres:
        xres = nc.dram_tensor("xres", [NB * T, D], F32, kind="Internal")
    if "l0a" in blocks:
        u1d = nc.dram_tensor("u1d", [NB * T, D], F32, kind="Internal")
        C.u1d = u1d
        C.b_u1d = [Buf() for _ in range(NTILES)]
    if "l0f" in blocks:
        u3fmd = nc.dram_tensor("u3fmd", [NSUP, 128, 8 * 512], BF16, kind="Internal")
        gd = nc.dram_tensor("gd", [8, NB * T], F32, kind="Internal")
        C.u3fmd, C.gd = u3fmd, gd
        C.b_u3fmd = [Buf() for _ in range(NSUP)]
        C.b_gd = [Buf() for _ in range(NSUP)]

    with ExitStack() as ctx:
        E = ctx.enter_context
        P = Prog(nc)
        C.P = P
        sems = {}
        for e in ENGINES:
            if e in DMA_ENG:
                for r in range(NLANE):
                    sems[(e, r)] = E(nc.semaphore(f"sem_{e}{r}"))
            else:
                sems[(e, 0)] = E(nc.semaphore("sem_" + e))
        bar_sem = E(nc.semaphore("sem_bar"))

        C.idb_sb = E(nc.sbuf_tensor("idb_sb", [128, 128], BF16))
        C.idf_sb = E(nc.sbuf_tensor("idf_sb", [128, 128], F32))
        C.onesf_sb = E(nc.sbuf_tensor("onesf_sb", [128, 128], F32))
        C.onescb_sb = E(nc.sbuf_tensor("onescb_sb", [128, 1], BF16))
        C.epsc = E(nc.sbuf_tensor("epsc_sb", [128, 2], F32))
        C.b_const = Buf("const")
        loads = [(C.idb_sb, "idb"), (C.idf_sb, "idf"), (C.onesf_sb, "onesf"),
                 (C.onescb_sb, "onescb"), (C.epsc, "epsc")]
        C.bias_sb = {}
        if "l0a" in blocks or "l1a" in blocks:
            C.eh_sb = E(nc.sbuf_tensor("eh_sb", [128, 2], F32))
            loads.append((C.eh_sb, "eh"))
            C.ehb_sb = E(nc.sbuf_tensor("ehb_sb", [128, 2], BF16))
            loads.append((C.ehb_sb, "ehb"))
        if "l0a" in blocks:
            C.tri_sb = E(nc.sbuf_tensor("tri_sb", [128, 128], BF16))
            loads.append((C.tri_sb, "tri"))
            for nm, sh in [("b_attn_fm", [128, 24]), ("b_proj_fm", [128, 8]),
                           ("b_fc_fm", [128, 32]), ("b_mlp_fm", [128, 8])]:
                C.bias_sb[nm] = E(nc.sbuf_tensor(nm + "_sb", sh, F32))
                loads.append((C.bias_sb[nm], nm))
        if "l0f" in blocks:
            for nm, sh in [("bg_col", [8, 1]), ("b1_fm", [128, 8, 16])]:
                C.bias_sb[nm] = E(nc.sbuf_tensor(nm + "_sb", sh, F32))
                loads.append((C.bias_sb[nm], nm))
        if "l1f" in blocks:
            C.bias_sb["l1f_b"] = E(nc.sbuf_tensor("l1f_b_sb", [128, 8], F32))
            loads.append((C.bias_sb["l1f_b"], "l1f_b_fm"))

        C.pf = [E(nc.psum_tensor(f"pf{i}", [128, 512], F32)) for i in range(6)]
        C.pbf = [E(nc.psum_tensor(f"pbf{i}", [128, 1024], BF16)) for i in range(2)]
        C.b_pf = [Buf(f"pf{i}") for i in range(6)]
        C.b_pbf = [Buf(f"pbf{i}") for i in range(2)]
        C._pf_rot = 0
        C._pbf_rot = 0

        def next_pf():
            i = C._pf_rot % len(C.pf)
            C._pf_rot += 1
            return C.pf[i], C.b_pf[i]

        def next_pbf():
            i = C._pbf_rot % len(C.pbf)
            C._pbf_rot += 1
            return C.pbf[i], C.b_pbf[i]

        C.next_pf = next_pf
        C.next_pbf = next_pbf

        blk = E(nc.Block())

        for sb, dr_name in loads:
            P.op("sync", lambda e, sb=sb, d=C.dram[dr_name]: e.dma_start(
                out=sb[:], in_=d.ap()), writes=[C.b_const])

        def dram_bufs(n):
            return [Buf() for _ in range(n)]

        b_x = dram_bufs(NTILES)
        b_out = dram_bufs(NTILES)
        b_xres = dram_bufs(NTILES) if need_xres else None

        seq = []
        for bi, name in enumerate(blocks):
            src = C.dram["x"].ap() if bi == 0 else xres.ap()
            bsrc = b_x if bi == 0 else b_xres
            last = bi == len(blocks) - 1
            dstap = dout.ap() if last else xres.ap()
            bdst = b_out if last else b_xres
            seq.append((name, src, bsrc, dstap, bdst))

        for name, src, bsrc, dstap, bdst in seq:
            if name == "l1f":
                emit_l1ffn(C, src, bsrc, dstap, bdst)
            elif name == "l0a":
                emit_l0attn(C, src, bsrc, dstap, bdst)
            elif name == "l0f":
                emit_l0ffn(C, src, bsrc, dstap, bdst)
            elif name == "l1a":
                emit_l1attn(C, src, bsrc, dstap, bdst)

        P.emit(blk, sems, bar_sem)
    return nc


def _bias_fm(b):
    nck = b.shape[0] // 128
    return np.ascontiguousarray(b.reshape(nck, 128).T).astype(np.float32)


def make_weight_arrays(inputs, blocks=("l0a", "l0f", "l1a", "l1f")):
    g = {}
    g["idb"] = np.eye(128, dtype=ml_dtypes.bfloat16)
    g["idf"] = np.eye(128, dtype=np.float32)
    g["onesf"] = np.ones((128, 128), dtype=np.float32)
    g["onescb"] = np.ones((128, 1), dtype=ml_dtypes.bfloat16)
    g["epsc"] = np.broadcast_to(np.array([1e-5, 1e-10], dtype=np.float32), (128, 2)).copy()
    if "l0a" in blocks or "l1a" in blocks:
        eh = np.zeros((128, 2), dtype=np.float32)
        eh[0:64, 0] = 1.0
        eh[64:128, 1] = 1.0
        g["eh"] = eh
        g["ehb"] = eh.astype(ml_dtypes.bfloat16)
    if "l0a" in blocks:
        s_le_q = (np.arange(128)[:, None] <= np.arange(128)[None, :])
        g["tri"] = s_le_q.astype(ml_dtypes.bfloat16)
        g["w_attnT"] = np.ascontiguousarray(inputs["L0_attn_w_attn"].T)
        g["b_attn_fm"] = _bias_fm(inputs["L0_attn_b_attn"])
        g["w_projT"] = np.ascontiguousarray(inputs["L0_attn_w_proj"].T)
        g["b_proj_fm"] = _bias_fm(inputs["L0_attn_b_proj"])
        g["w_fcT"] = np.ascontiguousarray(inputs["L0_attn_w_fc"].T)
        g["b_fc_fm"] = _bias_fm(inputs["L0_attn_b_fc"])
        g["w_mlpT"] = np.ascontiguousarray(inputs["L0_attn_w_mlp_proj"].T)
        g["b_mlp_fm"] = _bias_fm(inputs["L0_attn_b_mlp_proj"])
    if "l0f" in blocks:
        g["wg"] = np.ascontiguousarray(inputs["L0_ffn_Wg"])
        g["bg_col"] = np.ascontiguousarray(inputs["L0_ffn_bg"].reshape(8, 1))
        g["moe_w1"] = np.ascontiguousarray(inputs["L0_ffn_W1"])
        g["b1_fm"] = np.ascontiguousarray(
            inputs["L0_ffn_b1"].reshape(8, 16, 128).transpose(2, 0, 1))
        g["moe_w2"] = np.ascontiguousarray(inputs["L0_ffn_W2"])
        g["b2_mat"] = np.ascontiguousarray(inputs["L0_ffn_b2"])
        g["gamma"] = np.ascontiguousarray(inputs["L0_ffn_gamma"].reshape(1, D))
        g["beta"] = np.ascontiguousarray(inputs["L0_ffn_beta"].reshape(1, D))
    if "l1a" in blocks:
        inv = 1.0 / 10000.0 ** (np.arange(0, 64, 2, dtype=np.float32) / 64)
        ang = np.outer(np.arange(T, dtype=np.float32), inv)
        g["ropecos"] = np.cos(ang).astype(np.float32)
        g["ropesin"] = np.sin(ang).astype(np.float32)
        perm = np.concatenate([
            np.concatenate([h * 64 + 2 * np.arange(32), h * 64 + 2 * np.arange(32) + 1])
            for h in range(16)])
        g["wqP"] = np.ascontiguousarray(inputs["L1_attn_wq"][:, perm])
        g["w_down"] = np.ascontiguousarray(inputs["L1_attn_w_down"])
        g["wk_upP"] = np.ascontiguousarray(inputs["L1_attn_wk_up"][:, perm])
        g["wv_up"] = np.ascontiguousarray(inputs["L1_attn_wv_up"])
        g["wo"] = np.ascontiguousarray(inputs["L1_attn_wo"])
    if "l1f" in blocks:
        g["l1f_w"] = np.ascontiguousarray(inputs["L1_ffn_weight"])
        g["l1f_b_fm"] = _bias_fm(inputs["L1_ffn_bias"])
    return g


FUSED = True


def kernel(**inputs):
    blocks = ("l0a", "l0f", "l1a", "l1f")
    x = np.asarray(inputs["L0_attn_x"], dtype=np.float32).reshape(32 * T, D)
    if FUSED:
        g = make_weight_arrays(inputs, blocks)
        nc = build_program(blocks)
        in_maps = []
        for c in range(8):
            m = dict(g)
            m["x"] = np.ascontiguousarray(x[c * NB * T:(c + 1) * NB * T])
            in_maps.append(m)
        res = run_bass_kernel_spmd(nc, in_maps, core_ids=list(range(8)))
        out = np.concatenate([r["out"] for r in res.results], axis=0)
        return out.reshape(32, T, D)
    cur = x
    for bname in blocks:
        g = make_weight_arrays(inputs, (bname,))
        nc = build_program((bname,))
        in_maps = []
        for c in range(8):
            m = dict(g)
            m["x"] = np.ascontiguousarray(cur[c * NB * T:(c + 1) * NB * T])
            in_maps.append(m)
        res = run_bass_kernel_spmd(nc, in_maps, core_ids=list(range(8)))
        cur = np.concatenate([r["out"] for r in res.results], axis=0)
    return cur.reshape(32, T, D)


# version 06
# speedup vs baseline: 1.3637x; 1.0359x over previous
import numpy as np
import ml_dtypes
from contextlib import ExitStack

import concourse.bass as bass
import concourse.mybir as mybir
from concourse.bass_utils import run_bass_kernel_spmd

F32 = mybir.dt.float32
BF16 = mybir.dt.bfloat16
AF = mybir.ActivationFunctionType
ALU = mybir.AluOpType
AX = mybir.AxisListType

ENGINES = ("sync", "tensor", "vector", "scalar", "gpsimd")
DMA_ENG = ("sync", "gpsimd")
NLANE = 8

NB = 4
T = 2048
D = 1024
NTILES = NB * T // 128
NSUP = NB * T // 512
MAGIC = 12582912.0


class Buf:
    __slots__ = ("writer", "readers", "name")

    def __init__(self, name=""):
        self.writer = None
        self.readers = []
        self.name = name


class Prog:
    def __init__(self, nc):
        self.nc = nc
        self.steps = []
        self.eng_steps = {e: [] for e in ENGINES}
        self.nbar = 0
        self._inc_count = {}

    def op(self, eng, fn, reads=(), writes=(), deps=(), inc=True):
        d = list(deps)
        for b in reads:
            if b.writer is not None:
                d.append(b.writer)
        for b in writes:
            d.extend(b.readers)
            if b.writer is not None:
                d.append(b.writer)
        seq = len(self.steps)
        st = {"eng": eng, "fn": fn, "deps": d, "inc": inc, "seq": seq, "bar": None}
        if inc:
            self._inc_count[eng] = self._inc_count.get(eng, 0) + 1
        self.steps.append(st)
        self.eng_steps[eng].append(st)
        tok = (eng, seq)
        for b in reads:
            b.readers.append(tok)
        for b in writes:
            b.writer = tok
            b.readers = []
        return tok

    def barrier(self):
        snap = {}
        for e in ENGINES:
            if e in DMA_ENG:
                n = self._inc_count.get(e, 0)
                snap[e] = [(n + NLANE - 1 - r) // NLANE for r in range(NLANE)]
            else:
                snap[e] = self._inc_count.get(e, 0)
        for e in ENGINES:
            st = {"eng": e, "fn": None, "deps": (), "inc": False,
                  "seq": len(self.steps), "bar": ("waits", snap)}
            self.steps.append(st)
            self.eng_steps[e].append(st)

    def _resolve(self):
        self.cover = {}
        self.cover_seq = {}
        self.total_incs = {}
        for e in ENGINES:
            arr = self.eng_steps[e]
            if e in DMA_ENG:
                lane_cnt = [0] * NLANE
                for st in arr:
                    if st["bar"] is not None:
                        continue
                    assert st["inc"], "DMA steps must inc"
                    r = (sum(lane_cnt)) % NLANE
                    lane_cnt[r] += 1
                    st["lane"] = r
                    self.cover[(e, st["seq"])] = (r, lane_cnt[r] * 16)
                    self.cover_seq[(e, st["seq"])] = st["seq"]
                self.total_incs[e] = list(lane_cnt)
            else:
                cum = 0
                cums = []
                for st in arr:
                    if st["inc"]:
                        cum += 1
                    cums.append(cum)
                self.total_incs[e] = cum
                nxt = [None] * len(arr)
                nxtseq = [None] * len(arr)
                cur = None
                curseq = None
                for i in range(len(arr) - 1, -1, -1):
                    if arr[i]["inc"]:
                        cur = cums[i]
                        curseq = arr[i]["seq"]
                    nxt[i] = cur
                    nxtseq[i] = curseq
                for i, st in enumerate(arr):
                    if nxt[i] is not None:
                        self.cover[(e, st["seq"])] = (0, nxt[i])
                        self.cover_seq[(e, st["seq"])] = nxtseq[i]

    def emit(self, blk, sems, bar_sem):
        self._resolve()
        for ename in ENGINES:
            def mk(ename):
                def run(eng):
                    waited = {}
                    for st in self.eng_steps[ename]:
                        if st["bar"] is not None:
                            kind, val = st["bar"]
                            assert kind == "waits"
                            for e2 in ENGINES:
                                if e2 in DMA_ENG:
                                    for r in range(NLANE):
                                        v2 = val[e2][r] * 16
                                        if v2 > waited.get((e2, r), 0):
                                            eng.wait_ge(sems[(e2, r)], v2)
                                            waited[(e2, r)] = v2
                                else:
                                    v2 = val[e2]
                                    if v2 > waited.get((e2, 0), 0):
                                        eng.wait_ge(sems[(e2, 0)], v2)
                                        waited[(e2, 0)] = v2
                            continue
                        need = {}
                        for (de, dseq) in st["deps"]:
                            if de == ename and ename == "tensor":
                                continue
                            cv = self.cover.get((de, dseq))
                            assert cv is not None, \
                                f"dep on ({de},{dseq}) has no covering inc"
                            lane, v = cv
                            cseq = self.cover_seq[(de, dseq)]
                            assert cseq < st["seq"], \
                                f"dep of step {st['seq']} covered only by later step {cseq} on {de}"
                            if v > need.get((de, lane), -1):
                                need[(de, lane)] = v
                        for key in sorted(need):
                            v = need[key]
                            if v > waited.get(key, 0):
                                eng.wait_ge(sems[key], v)
                                waited[key] = v
                        ins = st["fn"](eng)
                        if st["inc"]:
                            if ename in DMA_ENG:
                                ins.then_inc(sems[(ename, st["lane"])], 16)
                            else:
                                ins.then_inc(sems[(ename, 0)], 1)
                    if ename == "sync":
                        for r in range(NLANE):
                            eng.wait_ge(sems[("sync", r)],
                                        self.total_incs["sync"][r] * 16)
                return run
            getattr(blk, ename)(mk(ename))



_NAME_N = [0]


def _un(s):
    _NAME_N[0] += 1
    return f"{s}_{_NAME_N[0]}"



class Ctx:
    pass


def ln_tile(C, src_ap, out_ap, eps_ap, r_src, w_out, sq_ap, b_sq, stat_ap, b_stat,
            extra_out=None):
    P = C.P
    n = src_ap.shape[-1]
    sums = stat_ap[:, 0:1]
    sumsq = stat_ap[:, 1:2]
    mu = stat_ap[:, 2:3]
    var = stat_ap[:, 3:4]
    rstd = stat_ap[:, 4:5]
    nmr = stat_ap[:, 5:6]
    P.op("vector", lambda e, a=src_ap, o=sums: e.reduce_sum(o, a, AX.X),
         reads=[r_src], writes=[b_stat])
    if out_ap.dtype == F32:
        P.op("scalar", lambda e, a=src_ap, o=out_ap, ac=sumsq:
             e.activation(o, a, AF.Square, accum_out=ac),
             reads=[r_src], writes=[w_out, b_stat])
    else:
        P.op("scalar", lambda e, a=src_ap, o=sq_ap, ac=sumsq:
             e.activation(o, a, AF.Square, accum_out=ac),
             reads=[r_src], writes=[b_sq, b_stat])
    P.op("vector", lambda e, o=mu, a=sums: e.tensor_single_scalar(o, a, 1.0 / n, ALU.mult),
         reads=[b_stat], writes=[b_stat])
    P.op("vector", lambda e, o=var, a=mu: e.tensor_tensor(o, a, a, ALU.mult),
         reads=[b_stat], writes=[b_stat])
    P.op("vector", lambda e, o=var, a=sumsq, m=var:
         e.scalar_tensor_tensor(o, a, 1.0 / n, m, ALU.mult, ALU.subtract),
         reads=[b_stat], writes=[b_stat])
    P.op("scalar", lambda e, o=rstd, a=var, b=eps_ap: e.activation(o, a, AF.Ln, bias=b),
         reads=[b_stat, C.b_const], writes=[b_stat])
    P.op("scalar", lambda e, o=rstd, a=rstd: e.activation(o, a, AF.Exp, scale=-0.5),
         reads=[b_stat], writes=[b_stat])
    P.op("vector", lambda e, o=nmr, a=mu, r=rstd:
         e.scalar_tensor_tensor(o, a, -1.0, r, ALU.mult, ALU.mult),
         reads=[b_stat], writes=[b_stat])
    P.op("scalar", lambda e, o=out_ap, a=src_ap, r=rstd, b=nmr:
         e.activation(o, a, AF.Identity, bias=b, scale=r),
         reads=[r_src, b_stat], writes=[w_out])
    return rstd, nmr


def transpose128(C, src_ap, dst_ap, r_src, w_dst, dtype, copy_eng="vector"):
    P = C.P
    ncols = src_ap.shape[-1]
    if dtype == BF16:
        bank, bbuf = C.next_pbf()
        ident = C.idb_sb
        pslice = bank[:].bitcast(BF16)[0:ncols, 0:128]
    else:
        bank, bbuf = C.next_pf()
        ident = C.idf_sb
        pslice = bank[0:ncols, 0:128]
    P.op("tensor", lambda e, o=pslice, i=src_ap, idt=ident[:]: e.transpose(o, i, idt),
         reads=[r_src, C.b_const], writes=[bbuf])
    P.op(copy_eng, lambda e, o=dst_ap, i=pslice:
         (e.tensor_copy(o, i) if copy_eng == "vector" else e.activation(o, i, AF.Identity)),
         reads=[bbuf], writes=[w_dst])


def emit_l1ffn(C, src, b_src, dst, b_dst):
    nc, P = C.nc, C.P

    def quant_chain(u_ap, out_ap, r_u, w_out, s1, s2, s3, s4, bs, bufs):
        b1, b2, b3, b4, bb = bufs
        bmax = bs[:, 0:32]
        scl = bs[:, 32:64]
        rscl = bs[:, 64:96]
        u3d = u_ap.rearrange("p (a b) -> p a b", a=32)
        P.op("vector", lambda e: e.reduce_max(bmax, u3d, AX.X, apply_absolute_value=True),
             reads=[r_u], writes=[bb])
        P.op("vector", lambda e: e.tensor_scalar(scl, bmax, 1.0 / 7.5, 1e-30, ALU.mult, ALU.max),
             reads=[bb], writes=[bb])
        P.op("vector", lambda e: e.reciprocal(rscl, scl),
             reads=[bb], writes=[bb])
        rb = rscl.rearrange("p (a o) -> p a o", a=32).broadcast_to([128, 32, 32])
        sb = scl.rearrange("p (a o) -> p a o", a=32).broadcast_to([128, 32, 32])
        s1d = s1.rearrange("p (a b) -> p a b", a=32)
        P.op("vector", lambda e: e.tensor_tensor(s1d, u3d, rb, ALU.mult),
             reads=[r_u, bb], writes=[b1])
        P.op("scalar", lambda e: e.activation(s2, s1, AF.Abs),
             reads=[b1], writes=[b2])
        P.op("vector", lambda e: e.tensor_single_scalar(s3, s2, 2.0, ALU.is_ge),
             reads=[b2], writes=[b3])
        P.op("vector", lambda e: e.tensor_single_scalar(s2, s2, 4.0, ALU.is_ge),
             reads=[b2], writes=[b2])
        P.op("vector", lambda e: e.tensor_scalar(s4, s3, -4.0, 8.0, ALU.mult, ALU.add),
             reads=[b3], writes=[b4])
        P.op("vector", lambda e: e.scalar_tensor_tensor(s4, s2, -2.0, s4, ALU.mult, ALU.add),
             reads=[b2, b4], writes=[b4])
        P.op("vector", lambda e: e.tensor_scalar(s3, s3, 0.125, 0.125, ALU.mult, ALU.add),
             reads=[b3], writes=[b3])
        P.op("vector", lambda e: e.scalar_tensor_tensor(s3, s2, 0.25, s3, ALU.mult, ALU.add),
             reads=[b2, b3], writes=[b3])
        P.op("vector", lambda e: e.tensor_tensor(s2, s1, s4, ALU.mult),
             reads=[b1, b4], writes=[b2])
        P.op("vector", lambda e: e.tensor_scalar(s2, s2, MAGIC, MAGIC, ALU.add, ALU.subtract),
             reads=[b2], writes=[b2])
        P.op("vector", lambda e: e.tensor_tensor(s1, s2, s3, ALU.mult),
             reads=[b2, b3], writes=[b1])
        od = out_ap.rearrange("p (a b) -> p a b", a=32)
        P.op("vector", lambda e: e.tensor_tensor(od, s1d, sb, ALU.mult),
             reads=[b1, bb], writes=[w_out])

    with ExitStack() as ctx:
        E = ctx.enter_context
        wf = E(nc.sbuf_tensor("l1f_wf", [128, 8, 1024], F32))
        wdq = E(nc.sbuf_tensor("l1f_wdq", [128, 8, 1024], BF16))
        wdT = E(nc.sbuf_tensor("l1f_wdT", [128, 8, 1024], BF16))
        s1 = E(nc.sbuf_tensor("l1f_s1", [128, 1024], F32))
        s2 = E(nc.sbuf_tensor("l1f_s2", [128, 1024], F32))
        s3 = E(nc.sbuf_tensor("l1f_s3", [128, 1024], F32))
        s4 = E(nc.sbuf_tensor("l1f_s4", [128, 1024], F32))
        bs = E(nc.sbuf_tensor("l1f_bs", [128, 96], F32))
        b_wf, b_wdq, b_wdT = Buf("wf"), Buf("wdq"), Buf("wdT")
        b1, b2, b3, b4, bb = Buf(), Buf(), Buf(), Buf(), Buf()

        P.op("sync", lambda e: e.dma_start(
            out=wf[:], in_=C.dram["l1f_w"].rearrange("(c p) k -> p c k", p=128)),
            writes=[b_wf])
        for jc in range(8):
            quant_chain(wf[:, jc, :], wdq[:, jc, :], b_wf, b_wdq,
                        s1[:], s2[:], s3[:], s4[:], bs[:], (b1, b2, b3, b4, bb))
        for jc in range(8):
            for kc in range(8):
                transpose128(C, wdq[:, jc, kc * 128:(kc + 1) * 128],
                             wdT[:, kc, jc * 128:(jc + 1) * 128],
                             b_wdq, b_wdT, BF16)
        P.barrier()

        xt = E(nc.sbuf_tensor("l1f_xt", [128, 4, 1024], F32))
        u5 = E(nc.sbuf_tensor("l1f_u5", [128, 1024], F32))
        xdtm = E(nc.sbuf_tensor("l1f_xdtm", [128, 1024], BF16))
        xdfm = E(nc.sbuf_tensor("l1f_xdfm", [128, 8, 512], BF16))
        outst = E(nc.sbuf_tensor("l1f_outst", [128, 512], F32))
        otm = E(nc.sbuf_tensor("l1f_otm", [128, 4, 1024], F32))
        zt = E(nc.sbuf_tensor("l1f_zt", [128, 1024], F32))
        xo = E(nc.sbuf_tensor("l1f_xo", [128, 1024], F32))
        stat = E(nc.sbuf_tensor("l1f_stat", [128, 8], F32))
        b_xt = [Buf() for _ in range(4)]
        b_u5, b_xdtm, b_xdfm = Buf(), Buf(), Buf()
        b_outst, b_otm, b_zt, b_xo, b_stat = Buf(), Buf(), Buf(), Buf(), Buf()

        for s in range(NSUP):
            for i in range(4):
                tile = s * 4 + i
                P.op("sync", lambda e, t=tile, i=i: e.dma_start(
                    out=xt[:, i, :], in_=src[t * 128:(t + 1) * 128, :]),
                    reads=[b_src[tile]], writes=[b_xt[i]])
                ln_tile(C, xt[:, i, :], u5[:], C.epsc[:, 0:1], b_xt[i], b_u5,
                        s1[:], b1, stat[:], b_stat)
                quant_chain(u5[:], xdtm[:], b_u5, b_xdtm,
                            s1[:], s2[:], s3[:], s4[:], bs[:], (b1, b2, b3, b4, bb))
                for c in range(8):
                    transpose128(C, xdtm[:, c * 128:(c + 1) * 128],
                                 xdfm[:, c, i * 128:(i + 1) * 128],
                                 b_xdtm, b_xdfm, BF16)
            for jc in range(8):
                bank, bbk = C.next_pf()
                for kc in range(8):
                    P.op("tensor", lambda e, bk=bank, kc=kc, jc=jc: e.matmul(
                        bk[:], wdT[:, kc, jc * 128:(jc + 1) * 128], xdfm[:, kc, :],
                        start=(kc == 0), stop=(kc == 7)),
                        reads=[b_wdT, b_xdfm], writes=[bbk], inc=(kc == 7))
                P.op("scalar", lambda e, bk=bank, jc=jc: e.activation(
                    outst[:], bk[:], AF.Identity, bias=C.bias_sb["l1f_b"][:, jc:jc + 1]),
                    reads=[bbk, C.b_const], writes=[b_outst])
                for i in range(4):
                    transpose128(C, outst[:, i * 128:(i + 1) * 128],
                                 otm[:, i, jc * 128:(jc + 1) * 128],
                                 b_outst, b_otm, F32)
            for i in range(4):
                tile = s * 4 + i
                rstd, nmr = ln_tile(C, otm[:, i, :], zt[:], C.epsc[:, 0:1], b_otm, b_zt,
                                    s1[:], b1, stat[:], b_stat)
                P.op("vector", lambda e, i=i: e.tensor_tensor(
                    xo[:], zt[:], xt[:, i, :], ALU.add),
                    reads=[b_zt, b_xt[i]], writes=[b_xo])
                P.op("sync", lambda e, t=tile: e.dma_start(
                    out=dst[t * 128:(t + 1) * 128, :], in_=xo[:]),
                    reads=[b_xo], writes=[b_dst[tile]])
        P.barrier()


def attn_phaseB(C, qs, nst, colstart_fn, mask_diag, qfm, b_q, kfm, b_k, vtm, b_v,
                negc, b_negc, attnst, b_att, Pb, b_Pb, denb, b_den, rb, b_rb):
    P = C.P
    pf, bpf = C.pf, C.b_pf
    SPAR = ((0, 1), (2, 3), (6, 7))

    for hp in range(8):
        def emit_S(st):
            cs = colstart_fn(st)
            pspan = slice(cs, 512)
            span = slice(qs * 512 + cs, qs * 512 + 512)
            for h in range(2):
                base = h * 64
                bi = SPAR[st % 3][h]
                P.op("tensor", lambda e, bi=bi, hp=hp, st=st, base=base, sp=span, ps=pspan:
                     e.matmul(pf[bi][:, ps], kfm[base:base + 64, hp, st * 128:(st + 1) * 128],
                              qfm[base:base + 64, hp, sp], start=True, stop=True,
                              skip_group_check=True),
                     reads=[b_q, b_k], writes=[bpf[bi]])

        def emit_exp(st):
            cs = colstart_fn(st)
            pspan = slice(cs, 512)
            par = st % 2
            for h in range(2):
                bi = SPAR[st % 3][h]
                P.op("scalar", lambda e, bi=bi, hp=hp, h=h, ps=pspan, par=par:
                     e.activation(Pb[:, par, h, ps], pf[bi][:, ps], AF.Exp,
                                  bias=negc[:, 2 * hp + h:2 * hp + h + 1], scale=0.125),
                     reads=[bpf[bi], b_negc], writes=[b_Pb])
                if mask_diag and st >= 4 * qs:
                    P.op("vector", lambda e, h=h, cs=cs, par=par:
                         e.tensor_tensor(Pb[:, par, h, cs:cs + 128],
                                         Pb[:, par, h, cs:cs + 128],
                                         C.tri_sb[:], ALU.mult),
                         reads=[b_Pb, C.b_const], writes=[b_Pb])

        def emit_av(st):
            cs = colstart_fn(st)
            ps = slice(cs, 512)
            par = st % 2
            first = st == 0
            last = st == nst - 1
            P.op("tensor", lambda e, hp=hp, st=st, ps=ps, first=first, last=last, par=par:
                 e.matmul(pf[4][0:65, ps], vtm[:, st, hp, 0:65], Pb[:, par, 0, ps],
                          start=first, stop=last, tile_position=(0, 0),
                          skip_group_check=True),
                 reads=[b_v, b_Pb], writes=[C.b_pf[4]], inc=True)
            P.op("tensor", lambda e, hp=hp, st=st, ps=ps, first=first, last=last, par=par:
                 e.matmul(pf[5][64:128, ps], vtm[:, st, hp, 65:129], Pb[:, par, 1, ps],
                          start=first, stop=last, tile_position=(0, 64),
                          skip_group_check=True),
                 reads=[b_v, b_Pb], writes=[C.b_pf[5]], inc=True)
            P.op("tensor", lambda e, ps=ps, first=first, last=last, par=par:
                 e.matmul(pf[5][0:1, ps], C.onescb_sb[:], Pb[:, par, 1, ps],
                          start=first, stop=last, tile_position=(0, 0),
                          skip_group_check=True),
                 reads=[C.b_const, b_Pb], writes=[C.b_pf[5]], inc=True)

        emit_S(0)
        if nst > 1:
            emit_S(1)
        for st in range(nst):
            emit_exp(st)
            if st + 2 < nst:
                emit_S(st + 2)
            emit_av(st)

        P.op("scalar", lambda e: e.activation(denb[64:65, :], pf[4][64:65, 0:512], AF.Ln),
             reads=[C.b_pf[4]], writes=[b_den])
        P.op("scalar", lambda e: e.activation(denb[64:65, :], denb[64:65, :], AF.Exp, scale=-1.0),
             reads=[b_den], writes=[b_den])
        P.op("scalar", lambda e: e.activation(denb[0:1, :], pf[5][0:1, 0:512], AF.Ln),
             reads=[C.b_pf[5]], writes=[b_den])
        P.op("scalar", lambda e: e.activation(denb[0:1, :], denb[0:1, :], AF.Exp, scale=-1.0),
             reads=[b_den], writes=[b_den])
        P.op("tensor", lambda e: e.matmul(pf[0][0:64, 0:512], C.onesf_sb[64:65, 0:64],
                                          denb[64:65, :], start=True, stop=True,
                                          tile_position=(64, 0), skip_group_check=True),
             reads=[b_den, C.b_const], writes=[bpf[0]])
        P.op("tensor", lambda e: e.matmul(pf[1][64:128, 0:512], C.onesf_sb[0:1, 0:64],
                                          denb[0:1, :], start=True, stop=True,
                                          tile_position=(0, 64), skip_group_check=True),
             reads=[b_den, C.b_const], writes=[bpf[1]])
        P.op("vector", lambda e: e.tensor_copy(rb[0:64, :], pf[0][0:64, 0:512]),
             reads=[bpf[0]], writes=[b_rb])
        P.op("vector", lambda e: e.tensor_copy(rb[64:128, :], pf[1][64:128, 0:512]),
             reads=[bpf[1]], writes=[b_rb])
        P.op("vector", lambda e, hp=hp: e.tensor_tensor(
            attnst[0:64, hp, :], pf[4][0:64, 0:512], rb[0:64, :], ALU.mult),
             reads=[C.b_pf[4], b_rb], writes=[b_att])
        P.op("vector", lambda e, hp=hp: e.tensor_tensor(
            attnst[64:128, hp, :], pf[5][64:128, 0:512], rb[64:128, :], ALU.mult),
             reads=[C.b_pf[5], b_rb], writes=[b_att])


def emit_l0attn(C, src, b_src, dst, b_dst):
    nc, P = C.nc, C.P
    with ExitStack() as bs:
        E = bs.enter_context
        wat = E(nc.sbuf_tensor("l0_watt", [128, 8, 3072], BF16))
        wpr = E(nc.sbuf_tensor("l0_wproj", [128, 8, 1024], BF16))
        qfm = E(nc.sbuf_tensor("l0_qfm", [128, 8, 2048], BF16))
        kfm = E(nc.sbuf_tensor("l0_kfm", [128, 8, 2048], BF16))
        vtm = E(nc.sbuf_tensor("l0_vtm", [128, 16, 8, 130], BF16))
        negc = E(nc.sbuf_tensor("l0_negc", [128, 16], F32))
        nrm = E(nc.sbuf_tensor("l0_nrm", [128, 32], F32))
        ncs = E(nc.sbuf_tensor("l0_ncs", [16, 8], F32))
        ncrow = E(nc.sbuf_tensor("l0_ncrow", [1, 16], F32))
        b_wat, b_wpr, b_q, b_k, b_v = Buf(), Buf(), Buf(), Buf(), Buf()
        b_negc, b_nrm, b_ncs, b_ncrow = Buf(), Buf(), Buf(), Buf()
        P.op("vector", lambda e: e.memset(vtm[:, :, :, 64:65], 1.0), writes=[b_v])
        P.op("gpsimd", lambda e: e.dma_start(
            out=wat[:], in_=C.dram["w_attnT"].rearrange("(c p) o -> p c o", p=128)),
            writes=[b_wat])
        P.op("gpsimd", lambda e: e.dma_start(
            out=wpr[:], in_=C.dram["w_projT"].rearrange("(c p) o -> p c o", p=128)),
            writes=[b_wpr])
        bat = C.bias_sb["b_attn_fm"]
        bpj = C.bias_sb["b_proj_fm"]

        for b in range(NB):
            with ExitStack() as sA:
                EA = sA.enter_context
                xt = EA(nc.sbuf_tensor(_un("l0a_xt"), [128, 2, 1024], F32))
                u0t = EA(nc.sbuf_tensor(_un("l0a_u0t"), [128, 1024], F32))
                ttm = EA(nc.sbuf_tensor(_un("l0a_ttm"), [128, 1024], BF16))
                tfm = EA(nc.sbuf_tensor(_un("l0a_tfm"), [128, 2, 8, 512], BF16))
                vst = EA(nc.sbuf_tensor(_un("l0a_vst"), [128, 8, 512], BF16))
                sq = EA(nc.sbuf_tensor(_un("l0a_sq"), [128, 1024], F32))
                stat = EA(nc.sbuf_tensor(_un("l0a_stat"), [128, 8], F32))
                b_xt = [Buf(), Buf()]
                b_u0t, b_ttm, b_tfm, b_vst, b_sq, b_stat = (Buf() for _ in range(6))
                P.op("vector", lambda e: e.memset(nrm[:], 0.0), writes=[b_nrm])

                def prep(j):
                    par = j % 2
                    for i in range(4):
                        tile = b * 16 + j * 4 + i
                        P.op("sync", lambda e, t=tile, xt=xt, par=par: e.dma_start(
                            out=xt[:, par, :], in_=src[t * 128:(t + 1) * 128, :]),
                            reads=[b_src[tile]], writes=[b_xt[par]])
                        ln_tile(C, xt[:, par, :], u0t[:], C.epsc[:, 0:1], b_xt[par],
                                b_u0t, sq[:], b_sq, stat[:], b_stat)
                        ln_tile(C, u0t[:], ttm[:], C.epsc[:, 1:2], b_u0t, b_ttm,
                                sq[:], b_sq, stat[:], b_stat)
                        for c in range(8):
                            transpose128(C, ttm[:, c * 128:(c + 1) * 128],
                                         tfm[:, par, c, i * 128:(i + 1) * 128],
                                         b_ttm, b_tfm, BF16)

                prep(0)
                ttv = ttm[:].rearrange("p (s n) -> p s n", s=2)
                for j in range(4):
                    par = j % 2
                    span = slice(j * 512, (j + 1) * 512)
                    for oc in range(24):
                        bank, bbk = C.pf[oc % 3], C.b_pf[oc % 3]
                        for kc in range(8):
                            P.op("tensor", lambda e, bank=bank, kc=kc, oc=oc, tfm=tfm, par=par:
                                 e.matmul(bank[:], wat[:, kc, oc * 128:(oc + 1) * 128],
                                          tfm[:, par, kc, :], start=(kc == 0), stop=(kc == 7)),
                                 reads=[b_wat, b_tfm], writes=[bbk], inc=(kc == 7))
                        if oc < 8:
                            dstap, bd = qfm[:, oc, span], b_q
                        elif oc < 16:
                            dstap, bd = kfm[:, oc - 8, span], b_k
                        else:
                            dstap, bd = vst[:, oc - 16, :], b_vst
                        P.op("scalar", lambda e, bank=bank, oc=oc, dstap=dstap:
                             e.activation(dstap, bank[:], AF.Identity,
                                          bias=bat[:, oc:oc + 1]),
                             reads=[bbk, C.b_const], writes=[bd])
                        if oc < 16:
                            slot = oc % 2
                            P.op("scalar", lambda e, d=dstap, slot=slot, ttm=ttm:
                                 e.activation(ttv[:, slot, :], d, AF.Square),
                                 reads=[bd], writes=[b_ttm])
                            for i in range(4):
                                c = oc % 8
                                P.op("tensor", lambda e, slot=slot, i=i, c=c, ttm=ttm:
                                     e.matmul(
                                         C.pf[3][:, i * 16 + 2 * c:i * 16 + 2 * c + 2],
                                         ttv[:, slot, i * 128:(i + 1) * 128], C.ehb_sb[:],
                                         start=True, stop=True, skip_group_check=True),
                                     reads=[b_ttm, C.b_const], writes=[C.b_pf[3]])
                        if oc == 7 or oc == 15:
                            off = 0 if oc == 7 else 16
                            for i in range(4):
                                P.op("vector", lambda e, off=off, i=i: e.tensor_tensor(
                                    nrm[:, off:off + 16], nrm[:, off:off + 16],
                                    C.pf[3][:, i * 16:(i + 1) * 16], ALU.max),
                                    reads=[b_nrm, C.b_pf[3]], writes=[b_nrm])
                    for c in range(8):
                        for i in range(4):
                            transpose128_v(C, vst[:, c, i * 128:(i + 1) * 128],
                                           vtm, j * 4 + i, c, b_vst, b_v)
                    if j < 3:
                        prep(j + 1)
                negc_calc(C, nrm, b_nrm, ncs, b_ncs, ncrow, b_ncrow, negc, b_negc)
            P.barrier()
            with ExitStack() as sB:
                EB = sB.enter_context
                Pb = EB(nc.sbuf_tensor(_un("l0b_pb"), [128, 2, 2, 512], BF16))
                attnst = EB(nc.sbuf_tensor(_un("l0b_attnst"), [128, 8, 512], BF16))
                denb = EB(nc.sbuf_tensor(_un("l0b_denb"), [128, 512], F32))
                rb = EB(nc.sbuf_tensor(_un("l0b_rb"), [128, 512], F32))
                projst = EB(nc.sbuf_tensor(_un("l0b_projst"), [128, 512], F32))
                u1s = EB(nc.sbuf_tensor(_un("l0b_u1s"), [128, 4, 1024], F32))
                xtB = EB(nc.sbuf_tensor(_un("l0b_xt"), [128, 1024], F32))
                sqB = EB(nc.sbuf_tensor(_un("l0b_sq"), [128, 1024], F32))
                statB = EB(nc.sbuf_tensor(_un("l0b_stat"), [128, 8], F32))
                b_Pb, b_att, b_den, b_rb = Buf(), Buf(), Buf(), Buf()
                b_projst, b_u1s, b_xtB, b_sqB, b_statB = Buf(), Buf(), Buf(), Buf(), Buf()
                for qs in range(4):
                    attn_phaseB(C, qs, 4 * qs + 4,
                                lambda st, qs=qs: max(0, st - 4 * qs) * 128,
                                True, qfm, b_q, kfm, b_k, vtm, b_v,
                                negc, b_negc, attnst, b_att, Pb, b_Pb,
                                denb, b_den, rb, b_rb)
                    for i in range(4):
                        tile = b * 16 + qs * 4 + i
                        P.op("sync", lambda e, t=tile, i=i, xtB=xtB: e.dma_start(
                            out=xtB[:], in_=src[t * 128:(t + 1) * 128, :]),
                            reads=[b_src[tile]], writes=[b_xtB])
                        ln_tile(C, xtB[:], u1s[:, i, :], C.epsc[:, 0:1], b_xtB, b_u1s,
                                sqB[:], b_sqB, statB[:], b_statB)
                    for oc in range(8):
                        bank, bbk = C.pf[2 + (oc % 2)], C.b_pf[2 + (oc % 2)]
                        for kc in range(8):
                            P.op("tensor", lambda e, bank=bank, kc=kc, oc=oc, attnst=attnst:
                                 e.matmul(bank[:], wpr[:, kc, oc * 128:(oc + 1) * 128],
                                          attnst[:, kc, :], start=(kc == 0), stop=(kc == 7)),
                                 reads=[b_wpr, b_att], writes=[bbk], inc=(kc == 7))
                        P.op("scalar", lambda e, bank=bank, oc=oc, projst=projst:
                             e.activation(projst[:], bank[:], AF.Identity,
                                          bias=bpj[:, oc:oc + 1]),
                             reads=[bbk, C.b_const], writes=[b_projst])
                        for i in range(4):
                            tb, btb = C.pf[4 + (i % 2)], C.b_pf[4 + (i % 2)]
                            P.op("tensor", lambda e, tb=tb, i=i, projst=projst: e.transpose(
                                tb[0:128, 0:128], projst[:, i * 128:(i + 1) * 128],
                                C.idf_sb[:]),
                                reads=[b_projst, C.b_const], writes=[btb])
                            P.op("vector", lambda e, tb=tb, i=i, oc=oc, u1s=u1s: e.tensor_tensor(
                                u1s[:, i, oc * 128:(oc + 1) * 128], tb[0:128, 0:128],
                                u1s[:, i, oc * 128:(oc + 1) * 128], ALU.add),
                                reads=[btb, b_u1s], writes=[b_u1s])
                    for i in range(4):
                        tile = b * 16 + qs * 4 + i
                        P.op("sync", lambda e, t=tile, i=i, u1s=u1s: e.dma_start(
                            out=C.u1d[t * 128:(t + 1) * 128, :], in_=u1s[:, i, :]),
                            reads=[b_u1s], writes=[C.b_u1d[tile]])
            P.barrier()
    P.barrier()
    with ExitStack() as sC:
        EC = sC.enter_context
        wfc = EC(nc.sbuf_tensor("l0c_wfc", [128, 8, 4096], BF16))
        wml = EC(nc.sbuf_tensor("l0c_wml", [128, 32, 1024], BF16))
        u1sC = EC(nc.sbuf_tensor("l0c_u1", [128, 4, 1024], F32))
        ut = EC(nc.sbuf_tensor("l0c_ut", [128, 1024], F32))
        ttmC = EC(nc.sbuf_tensor("l0c_ttm", [128, 1024], BF16))
        tfmC = EC(nc.sbuf_tensor("l0c_tfm", [128, 2, 8, 512], BF16))
        hfmC = EC(nc.sbuf_tensor("l0c_hfm", [128, 32, 512], BF16))
        sqC = EC(nc.sbuf_tensor("l0c_sq", [128, 1024], F32))
        statC = EC(nc.sbuf_tensor("l0c_stat", [128, 8], F32))
        mlpstC = ttmC[:].bitcast(F32)
        b_wfc, b_wml, b_u1, b_ut, b_ttm2, b_tfm2, b_hfm = (Buf() for _ in range(7))
        b_mlpst = b_ttm2
        b_zt, b_xt2, b_xo2, b_sq2, b_stat2 = (Buf() for _ in range(5))
        P.op("gpsimd", lambda e: e.dma_start(
            out=wfc[:], in_=C.dram["w_fcT"].rearrange("(c p) o -> p c o", p=128)),
            writes=[b_wfc])
        P.op("gpsimd", lambda e: e.dma_start(
            out=wml[:], in_=C.dram["w_mlpT"].rearrange("(c p) o -> p c o", p=128)),
            writes=[b_wml])
        bfc = C.bias_sb["b_fc_fm"]
        bml = C.bias_sb["b_mlp_fm"]

        def prepC(s):
            par = s % 2
            for i in range(4):
                tile = s * 4 + i
                P.op("sync", lambda e, t=tile, ut=ut: e.dma_start(
                    out=ut[:], in_=C.u1d[t * 128:(t + 1) * 128, :]),
                    reads=[C.b_u1d[tile]], writes=[b_ut])
                ln_tile(C, ut[:], ttmC[:], C.epsc[:, 1:2], b_ut, b_ttm2,
                        sqC[:], b_sq2, statC[:], b_stat2)
                for c in range(8):
                    transpose128(C, ttmC[:, c * 128:(c + 1) * 128],
                                 tfmC[:, par, c, i * 128:(i + 1) * 128],
                                 b_ttm2, b_tfm2, BF16)

        prepC(0)
        for s in range(NSUP):
            par = s % 2
            for oc in range(32):
                bank, bbk = C.pf[oc % 3], C.b_pf[oc % 3]
                for kc in range(8):
                    P.op("tensor", lambda e, bank=bank, kc=kc, oc=oc, par=par:
                         e.matmul(bank[:], wfc[:, kc, oc * 128:(oc + 1) * 128],
                                  tfmC[:, par, kc, :], start=(kc == 0), stop=(kc == 7)),
                         reads=[b_wfc, b_tfm2], writes=[bbk], inc=(kc == 7))
                P.op("scalar", lambda e, bank=bank, oc=oc:
                     e.activation(hfmC[:, oc, :], bank[:], AF.Gelu_apprx_tanh,
                                  bias=bfc[:, oc:oc + 1]),
                     reads=[bbk, C.b_const], writes=[b_hfm])
            if s < NSUP - 1:
                prepC(s + 1)
            for i in range(4):
                tile = s * 4 + i
                P.op("sync", lambda e, t=tile, i=i: e.dma_start(
                    out=u1sC[:, i, :], in_=C.u1d[t * 128:(t + 1) * 128, :]),
                    reads=[C.b_u1d[tile]], writes=[b_u1])
            for oc in range(8):
                bank, bbk = C.pf[3 + (oc % 2)], C.b_pf[3 + (oc % 2)]
                for fc in range(32):
                    P.op("tensor", lambda e, bank=bank, fc=fc, oc=oc:
                         e.matmul(bank[:], wml[:, fc, oc * 128:(oc + 1) * 128],
                                  hfmC[:, fc, :], start=(fc == 0), stop=(fc == 31)),
                         reads=[b_wml, b_hfm], writes=[bbk], inc=(fc == 31))
                P.op("scalar", lambda e, bank=bank, oc=oc:
                     e.activation(mlpstC, bank[:], AF.Identity,
                                  bias=bml[:, oc:oc + 1]),
                     reads=[bbk, C.b_const], writes=[b_mlpst])
                for i in range(4):
                    tb, btb = C.pf[5], C.b_pf[5]
                    P.op("tensor", lambda e, tb=tb, i=i: e.transpose(
                        tb[0:128, 0:128], mlpstC[:, i * 128:(i + 1) * 128], C.idf_sb[:]),
                        reads=[b_mlpst, C.b_const], writes=[btb])
                    P.op("vector", lambda e, tb=tb, i=i, oc=oc: e.tensor_tensor(
                        u1sC[:, i, oc * 128:(oc + 1) * 128], tb[0:128, 0:128],
                        u1sC[:, i, oc * 128:(oc + 1) * 128], ALU.add),
                        reads=[btb, b_u1], writes=[b_u1])
            for i in range(4):
                tile = s * 4 + i
                ln_tile(C, u1sC[:, i, :], ut[:], C.epsc[:, 0:1], b_u1, b_ut,
                        sqC[:], b_sq2, statC[:], b_stat2)
                P.op("sync", lambda e, t=tile: e.dma_start(
                    out=sqC[:], in_=src[t * 128:(t + 1) * 128, :]),
                    reads=[b_src[tile]], writes=[b_sq2])
                P.op("vector", lambda e: e.tensor_tensor(ut[:], ut[:], sqC[:], ALU.add),
                     reads=[b_ut, b_sq2], writes=[b_ut])
                P.op("sync", lambda e, t=tile: e.dma_start(
                    out=dst[t * 128:(t + 1) * 128, :], in_=ut[:]),
                    reads=[b_ut], writes=[b_dst[tile]])
    P.barrier()


def emit_l0ffn(C, src, b_src, dst, b_dst):
    nc, P = C.nc, C.P
    with ExitStack() as bs:
        E = bs.enter_context
        wg = E(nc.sbuf_tensor("f_wg", [128, 8, 8], BF16))
        b2m = E(nc.sbuf_tensor("f_b2m", [8, 1024], BF16))
        gam = E(nc.sbuf_tensor("f_gam", [128, 1024], F32))
        bet = E(nc.sbuf_tensor("f_bet", [128, 1024], F32))
        b_wg, b_b2m, b_gam, b_bet, b_gbrow = (Buf() for _ in range(5))
        P.op("gpsimd", lambda e: e.dma_start(
            out=wg[:], in_=C.dram["wg"].rearrange("(c p) o -> p c o", p=128)),
            writes=[b_wg])
        P.op("gpsimd", lambda e: e.dma_start(out=b2m[:], in_=C.dram["b2_mat"].ap()),
             writes=[b_b2m])
        with nc.sbuf_tensor("f_gbrow", [1, 2, 1024], F32) as gbrow:
            P.op("sync", lambda e: e.dma_start(out=gbrow[0:1, 0, :], in_=C.dram["gamma"].ap()),
                 writes=[b_gbrow])
            P.op("sync", lambda e: e.dma_start(out=gbrow[0:1, 1, :], in_=C.dram["beta"].ap()),
                 writes=[b_gbrow])
            for (gi, tgt, bt) in ((0, gam, b_gam), (1, bet, b_bet)):
                for half in range(2):
                    P.op("tensor", lambda e, gi=gi, half=half: e.matmul(
                        C.pf[4][:, 0:512], C.onesf_sb[0:1, 0:128],
                        gbrow[0:1, gi, half * 512:(half + 1) * 512],
                        start=True, stop=True, skip_group_check=True),
                        reads=[b_gbrow, C.b_const], writes=[C.b_pf[4]])
                    P.op("vector", lambda e, tgt=tgt, half=half: e.tensor_copy(
                        tgt[:, half * 512:(half + 1) * 512], C.pf[4][:, 0:512]),
                        reads=[C.b_pf[4]], writes=[bt])
        P.barrier()

        with ExitStack() as sP:
            EP = sP.enter_context
            xt = EP(nc.sbuf_tensor("f_xt", [128, 1024], F32))
            ttm = EP(nc.sbuf_tensor("f_ttm", [128, 1024], BF16))
            u3st = EP(nc.sbuf_tensor("f_u3st", [128, 8, 512], BF16))
            lgst = EP(nc.sbuf_tensor("f_lgst", [8, 512], F32))
            lgtm = EP(nc.sbuf_tensor("f_lgtm", [128, 4, 8], F32))
            g1 = EP(nc.sbuf_tensor("f_g1", [128, 4, 8], F32))
            g2 = EP(nc.sbuf_tensor("f_g2", [128, 4, 8], F32))
            g3 = EP(nc.sbuf_tensor("f_g3", [128, 4, 8], F32))
            g4 = EP(nc.sbuf_tensor("f_g4", [128, 4, 8], F32))
            gsc = EP(nc.sbuf_tensor("f_gsc", [128, 20], F32))
            grst = EP(nc.sbuf_tensor("f_grst", [8, 4, 128], F32))
            sq = EP(nc.sbuf_tensor("f_sq", [128, 1024], F32))
            stat = EP(nc.sbuf_tensor("f_stat", [128, 8], F32))
            b_xt, b_ttm, b_u3st, b_lgst, b_lgtm = (Buf() for _ in range(5))
            b_g1, b_g2, b_g3, b_g4, b_gsc, b_grst, b_sq, b_stat = (Buf() for _ in range(8))
            for s in range(NSUP):
                for i in range(4):
                    tile = s * 4 + i
                    P.op("sync", lambda e, t=tile: e.dma_start(
                        out=xt[:], in_=src[t * 128:(t + 1) * 128, :]),
                        reads=[b_src[tile]], writes=[b_xt])
                    ln_tile(C, xt[:], ttm[:], C.epsc[:, 0:1], b_xt, b_ttm,
                            sq[:], b_sq, stat[:], b_stat)
                    for c in range(8):
                        transpose128(C, ttm[:, c * 128:(c + 1) * 128],
                                     u3st[:, c, i * 128:(i + 1) * 128],
                                     b_ttm, b_u3st, BF16)
                P.op("sync", lambda e, s=s: e.dma_start(
                    out=C.u3fmd[s].rearrange("p (c n) -> p c n", c=8), in_=u3st[:]),
                    reads=[b_u3st], writes=[C.b_u3fmd[s]])
                for c in range(8):
                    P.op("tensor", lambda e, c=c: e.matmul(
                        C.pf[3][0:8, 0:512], wg[:, c, :], u3st[:, c, :],
                        start=(c == 0), stop=(c == 7), skip_group_check=True),
                        reads=[b_wg, b_u3st], writes=[C.b_pf[3]], inc=(c == 7))
                P.op("scalar", lambda e: e.activation(
                    lgst[:], C.pf[3][0:8, 0:512], AF.Identity,
                    bias=C.bias_sb["bg_col"][:]),
                    reads=[C.b_pf[3], C.b_const], writes=[b_lgst])
                for i in range(4):
                    P.op("tensor", lambda e, i=i: e.transpose(
                        C.pf[2][0:128, i * 8:(i + 1) * 8],
                        lgst[0:8, i * 128:(i + 1) * 128], C.idf_sb[0:8, 0:8]),
                        reads=[b_lgst, C.b_const], writes=[C.b_pf[2]])
                    P.op("vector", lambda e, i=i: e.tensor_copy(
                        lgtm[:, i, :], C.pf[2][0:128, i * 8:(i + 1) * 8]),
                        reads=[C.b_pf[2]], writes=[b_lgtm])
                m1 = gsc[:, 0:4]
                m2 = gsc[:, 4:8]
                dd = gsc[:, 8:12]
                w1 = gsc[:, 12:16]
                w2 = gsc[:, 16:20]
                m1b = m1.rearrange("p (a o) -> p a o", a=4).broadcast_to([128, 4, 8])
                m2b = m2.rearrange("p (a o) -> p a o", a=4).broadcast_to([128, 4, 8])
                w1b = w1.rearrange("p (a o) -> p a o", a=4).broadcast_to([128, 4, 8])
                w2b = w2.rearrange("p (a o) -> p a o", a=4).broadcast_to([128, 4, 8])
                P.op("vector", lambda e: e.reduce_max(m1, lgtm[:], AX.X),
                     reads=[b_lgtm], writes=[b_gsc])
                P.op("vector", lambda e: e.tensor_tensor(g1[:], lgtm[:], m1b, ALU.is_equal),
                     reads=[b_lgtm, b_gsc], writes=[b_g1])
                P.op("vector", lambda e: e.scalar_tensor_tensor(
                    g2[:], g1[:], -1e30, lgtm[:], ALU.mult, ALU.add),
                    reads=[b_g1, b_lgtm], writes=[b_g2])
                P.op("vector", lambda e: e.reduce_max(m2, g2[:], AX.X),
                     reads=[b_g2], writes=[b_gsc])
                P.op("vector", lambda e: e.tensor_tensor(dd, m1, m2, ALU.subtract),
                     reads=[b_gsc], writes=[b_gsc])
                P.op("scalar", lambda e: e.activation(w1, dd, AF.Sigmoid),
                     reads=[b_gsc], writes=[b_gsc])
                P.op("scalar", lambda e: e.activation(w2, dd, AF.Sigmoid, scale=-1.0),
                     reads=[b_gsc], writes=[b_gsc])
                P.op("vector", lambda e: e.tensor_tensor(g3[:], g2[:], m2b, ALU.is_equal),
                     reads=[b_g2, b_gsc], writes=[b_g3])
                P.op("vector", lambda e: e.tensor_tensor(g4[:], g1[:], w1b, ALU.mult),
                     reads=[b_g1, b_gsc], writes=[b_g4])
                P.op("vector", lambda e: e.tensor_tensor(g1[:], g3[:], w2b, ALU.mult),
                     reads=[b_g3, b_gsc], writes=[b_g1])
                P.op("vector", lambda e: e.tensor_tensor(g4[:], g4[:], g1[:], ALU.add),
                     reads=[b_g4, b_g1], writes=[b_g4])
                for i in range(4):
                    P.op("tensor", lambda e, i=i: e.transpose(
                        C.pf[1][0:8, i * 128:(i + 1) * 128],
                        g4[:, i, :], C.idf_sb[:]),
                        reads=[b_g4, C.b_const], writes=[C.b_pf[1]])
                    P.op("vector", lambda e, i=i: e.tensor_copy(
                        grst[0:8, i, :], C.pf[1][0:8, i * 128:(i + 1) * 128]),
                        reads=[C.b_pf[1]], writes=[b_grst])
                P.op("sync", lambda e, s=s: e.dma_start(
                    out=C.gd[:, s * 512:(s + 1) * 512],
                    in_=grst[:].rearrange("p a n -> p (a n)")),
                    reads=[b_grst], writes=[C.b_gd[s]])
        P.barrier()

        with ExitStack() as sM:
            EM = sM.enter_context
            w1bufs = [EM(nc.sbuf_tensor(f"f_w1{i}", [128, 8, 2048], BF16)) for i in range(2)]
            w2sb = EM(nc.sbuf_tensor("f_w2", [128, 16, 1024], BF16))
            u3t = EM(nc.sbuf_tensor("f_u3t", [128, 2, 2, 8, 512], BF16))
            hb = EM(nc.sbuf_tensor("f_h", [128, 16, 512], BF16))
            gbc = EM(nc.sbuf_tensor("f_gbc", [128, 512], BF16))
            g1r = EM(nc.sbuf_tensor("f_g1r", [1, 512], F32))
            g8 = EM(nc.sbuf_tensor("f_g8", [8, 2, 512], BF16))
            mixed = EM(nc.sbuf_tensor("f_mixed", [128, 2, 8, 512], F32))
            xt2 = EM(nc.sbuf_tensor("f_xt2", [128, 1024], F32))
            u3r = EM(nc.sbuf_tensor("f_u3r", [128, 1024], F32))
            zt = EM(nc.sbuf_tensor("f_zt", [128, 1024], F32))
            sq2 = zt
            stat2 = EM(nc.sbuf_tensor("f_stat2", [128, 8], F32))
            b_w1 = [Buf(), Buf()]
            b_w2, b_u3t, b_h, b_gbc, b_g1r, b_g8, b_mixed = (Buf() for _ in range(7))
            b_xt2, b_u3r, b_zt, b_stat2 = (Buf() for _ in range(4))
            b_sq2 = b_zt
            b1f = C.bias_sb["b1_fm"]
            for H in range(8):
                sts = [2 * H, 2 * H + 1]
                hpar = H % 2
                for si, s in enumerate(sts):
                    P.op("sync", lambda e, s=s, si=si, hpar=hpar: e.dma_start(
                        out=u3t[:, hpar, si, :, :],
                        in_=C.u3fmd[s].rearrange("p (c n) -> p c n", c=8)),
                        reads=[C.b_u3fmd[s]], writes=[b_u3t])
                    P.op("gpsimd", lambda e, s=s, si=si: e.dma_start(
                        out=g8[:, si, :], in_=C.gd[:, s * 512:(s + 1) * 512]),
                        reads=[C.b_gd[s]], writes=[b_g8])
                for e_ in range(8):
                    wb, bwb = w1bufs[e_ % 2], b_w1[e_ % 2]
                    P.op("gpsimd", lambda e, e_=e_, wb=wb: e.dma_start(
                        out=wb[:], in_=C.dram["moe_w1"][e_].rearrange(
                            "(c p) f -> p c f", p=128)),
                        writes=[bwb])
                    P.op("gpsimd", lambda e, e_=e_: e.dma_start(
                        out=w2sb[:], in_=C.dram["moe_w2"][e_].rearrange(
                            "(c p) f -> p c f", p=128)),
                        writes=[b_w2])
                    for si, s in enumerate(sts):
                        P.op("sync", lambda e, s=s, e_=e_: e.dma_start(
                            out=g1r[:], in_=C.gd[e_:e_ + 1, s * 512:(s + 1) * 512]),
                            reads=[C.b_gd[s]], writes=[b_g1r])
                        P.op("tensor", lambda e: e.matmul(
                            C.pf[4][:, 0:512], C.onesf_sb[0:1, 0:128], g1r[:],
                            start=True, stop=True, skip_group_check=True),
                            reads=[b_g1r, C.b_const], writes=[C.b_pf[4]])
                        P.op("vector", lambda e: e.tensor_copy(gbc[:], C.pf[4][:, 0:512]),
                             reads=[C.b_pf[4]], writes=[b_gbc])
                        for foc in range(16):
                            bank, bbk = C.pf[foc % 2], C.b_pf[foc % 2]
                            for kc in range(8):
                                P.op("tensor", lambda e, bank=bank, kc=kc, foc=foc, si=si, wb=wb, hpar=hpar:
                                     e.matmul(bank[:], wb[:, kc, foc * 128:(foc + 1) * 128],
                                              u3t[:, hpar, si, kc, :], start=(kc == 0),
                                              stop=(kc == 7)),
                                     reads=[bwb, b_u3t], writes=[bbk], inc=(kc == 7))
                            P.op("scalar", lambda e, bank=bank, foc=foc, e_=e_:
                                 e.activation(hb[:, foc, :], bank[:], AF.Relu,
                                              bias=b1f[:, e_, foc:foc + 1]),
                                 reads=[bbk, C.b_const], writes=[b_h])
                            P.op("vector", lambda e, foc=foc:
                                 e.tensor_tensor(hb[:, foc, :], hb[:, foc, :],
                                                 gbc[:], ALU.mult),
                                 reads=[b_h, b_gbc], writes=[b_h])
                        for dc in range(8):
                            bank, bbk = C.pf[2 + dc % 2], C.b_pf[2 + dc % 2]
                            for fc in range(16):
                                P.op("tensor", lambda e, bank=bank, fc=fc, dc=dc, e_=e_:
                                     e.matmul(bank[:], w2sb[:, fc, dc * 128:(dc + 1) * 128],
                                              hb[:, fc, :], start=(fc == 0),
                                              stop=(fc == 15 and e_ != 0)),
                                     reads=[b_w2, b_h], writes=[bbk],
                                     inc=(fc == 15 and e_ != 0))
                            if e_ == 0:
                                P.op("tensor", lambda e, bank=bank, dc=dc, si=si:
                                     e.matmul(bank[:], b2m[0:8, dc * 128:(dc + 1) * 128],
                                              g8[:, si, :], start=False, stop=True,
                                              skip_group_check=True),
                                     reads=[b_b2m, b_g8], writes=[bbk], inc=True)
                            if e_ == 0:
                                P.op("vector", lambda e, bank=bank, dc=dc, si=si:
                                     e.tensor_copy(mixed[:, si, dc, :], bank[:]),
                                     reads=[bbk], writes=[b_mixed])
                            else:
                                P.op("vector", lambda e, bank=bank, dc=dc, si=si:
                                     e.tensor_tensor(mixed[:, si, dc, :], bank[:],
                                                     mixed[:, si, dc, :], ALU.add),
                                     reads=[bbk, b_mixed], writes=[b_mixed])
                for si, s in enumerate(sts):
                    for i in range(4):
                        tile = s * 4 + i
                        P.op("sync", lambda e, t=tile: e.dma_start(
                            out=xt2[:], in_=src[t * 128:(t + 1) * 128, :]),
                            reads=[b_src[tile]], writes=[b_xt2])
                        ln_tile(C, xt2[:], u3r[:], C.epsc[:, 0:1], b_xt2, b_u3r,
                                sq2[:], b_sq2, stat2[:], b_stat2)
                        for dc in range(8):
                            tb, btb = C.pf[4 + (dc % 2)], C.b_pf[4 + (dc % 2)]
                            P.op("tensor", lambda e, tb=tb, si=si, dc=dc, i=i: e.transpose(
                                tb[0:128, 0:128], mixed[:, si, dc, i * 128:(i + 1) * 128],
                                C.idf_sb[:]),
                                reads=[b_mixed, C.b_const], writes=[btb])
                            P.op("vector", lambda e, tb=tb, dc=dc: e.tensor_tensor(
                                u3r[:, dc * 128:(dc + 1) * 128], tb[0:128, 0:128],
                                u3r[:, dc * 128:(dc + 1) * 128], ALU.add),
                                reads=[btb, b_u3r], writes=[b_u3r])
                        rstd, nmr = ln_tile(C, u3r[:], zt[:], C.epsc[:, 0:1], b_u3r, b_zt,
                                            sq2[:], b_sq2, stat2[:], b_stat2)
                        P.op("vector", lambda e: e.tensor_tensor(zt[:], zt[:], gam[:], ALU.mult),
                             reads=[b_zt, b_gam], writes=[b_zt])
                        P.op("vector", lambda e: e.tensor_tensor(zt[:], zt[:], bet[:], ALU.add),
                             reads=[b_zt, b_bet], writes=[b_zt])
                        ln_tile(C, zt[:], u3r[:], C.epsc[:, 0:1], b_zt, b_u3r,
                                sq2[:], b_sq2, stat2[:], b_stat2)
                        P.op("vector", lambda e: e.tensor_tensor(u3r[:], u3r[:], xt2[:], ALU.add),
                             reads=[b_u3r, b_xt2], writes=[b_u3r])
                        P.op("sync", lambda e, t=tile: e.dma_start(
                            out=dst[t * 128:(t + 1) * 128, :], in_=u3r[:]),
                            reads=[b_u3r], writes=[b_dst[tile]])
    P.barrier()


def transpose128_v(C, src_ap, vtm, st_idx, c, r_src, w_dst):
    P = C.P
    bank, bbuf = C.next_pbf()
    pv = bank[:].bitcast(BF16)[0:128, 0:128]
    P.op("tensor", lambda e, o=pv, i=src_ap: e.transpose(o, i, C.idb_sb[:]),
         reads=[r_src, C.b_const], writes=[bbuf])
    P.op("vector", lambda e, pv=pv, st_idx=st_idx, c=c: e.tensor_copy(
        vtm[:, st_idx, c, 0:64], pv[:, 0:64]),
        reads=[bbuf], writes=[w_dst])
    P.op("vector", lambda e, pv=pv, st_idx=st_idx, c=c: e.tensor_copy(
        vtm[:, st_idx, c, 65:129], pv[:, 64:128]),
        reads=[bbuf], writes=[w_dst])


def tap(C, name, ap, buf):
    if not getattr(C, "want_taps", None) or name not in C.want_taps:
        return
    nc, P = C.nc, C.P
    sh = list(ap.shape)
    d = nc.dram_tensor("tap_" + name, sh, ap.dtype, kind="ExternalOutput")
    P.op("sync", lambda e: e.dma_start(out=d.ap(), in_=ap), reads=[buf])


def negc_calc(C, nrm, b_nrm, ncs, b_ncs, ncrow, b_ncrow, negc, b_negc):
    P = C.P
    P.op("tensor", lambda e: e.transpose(C.pf[2][0:16, 0:128], nrm[:, 0:16], C.idf_sb[:]),
         reads=[b_nrm, C.b_const], writes=[C.b_pf[2]])
    P.op("vector", lambda e: e.reduce_max(ncs[:, 0:1], C.pf[2][0:16, 0:128], AX.X),
         reads=[C.b_pf[2]], writes=[b_ncs])
    P.op("tensor", lambda e: e.transpose(C.pf[2][0:16, 128:256], nrm[:, 16:32], C.idf_sb[:]),
         reads=[b_nrm, C.b_const], writes=[C.b_pf[2]])
    P.op("vector", lambda e: e.reduce_max(ncs[:, 1:2], C.pf[2][0:16, 128:256], AX.X),
         reads=[C.b_pf[2]], writes=[b_ncs])
    P.op("scalar", lambda e: e.activation(ncs[:, 2:3], ncs[:, 0:1], AF.Sqrt),
         reads=[b_ncs], writes=[b_ncs])
    P.op("scalar", lambda e: e.activation(ncs[:, 3:4], ncs[:, 1:2], AF.Sqrt),
         reads=[b_ncs], writes=[b_ncs])
    P.op("vector", lambda e: e.tensor_tensor(ncs[:, 4:5], ncs[:, 2:3], ncs[:, 3:4], ALU.mult),
         reads=[b_ncs], writes=[b_ncs])
    P.op("vector", lambda e: e.tensor_single_scalar(ncs[:, 5:6], ncs[:, 4:5], -0.125, ALU.mult),
         reads=[b_ncs], writes=[b_ncs])
    P.op("tensor", lambda e: e.transpose(C.pf[2][0:1, 256:272], ncs[:, 5:6],
                                         C.idf_sb[0:16, 0:16]),
         reads=[b_ncs, C.b_const], writes=[C.b_pf[2]])
    P.op("vector", lambda e: e.tensor_copy(ncrow[:], C.pf[2][0:1, 256:272]),
         reads=[C.b_pf[2]], writes=[b_ncrow])
    P.op("tensor", lambda e: e.matmul(C.pf[2][:, 272:288], C.onesf_sb[0:1, 0:128],
                                      ncrow[:], start=True, stop=True,
                                      skip_group_check=True),
         reads=[b_ncrow, C.b_const], writes=[C.b_pf[2]])
    P.op("vector", lambda e: e.tensor_copy(negc[:], C.pf[2][:, 272:288]),
         reads=[C.b_pf[2]], writes=[b_negc])


def emit_l1attn(C, src, b_src, dst, b_dst):
    nc, P = C.nc, C.P
    with ExitStack() as bs:
        E = bs.enter_context
        wq = E(nc.sbuf_tensor("l1_wq", [128, 8, 1024], BF16))
        wdn = E(nc.sbuf_tensor("l1_wdn", [128, 8, 512], BF16))
        wku = E(nc.sbuf_tensor("l1_wku", [128, 4, 1024], BF16))
        wvu = E(nc.sbuf_tensor("l1_wvu", [128, 4, 1024], BF16))
        qfm = E(nc.sbuf_tensor("l1_qfm", [128, 8, 2048], BF16))
        kfm = E(nc.sbuf_tensor("l1_kfm", [128, 8, 2048], BF16))
        vtm = E(nc.sbuf_tensor("l1_vtm", [128, 16, 8, 130], BF16))
        negc = E(nc.sbuf_tensor("l1_negc", [128, 16], F32))
        nrm = E(nc.sbuf_tensor("l1_nrm", [128, 32], F32))
        ncs = E(nc.sbuf_tensor("l1_ncs", [16, 8], F32))
        ncrow = E(nc.sbuf_tensor("l1_ncrow", [1, 16], F32))
        cos_sb = E(nc.sbuf_tensor("l1_cos", [128, 16, 32], F32))
        sin_sb = E(nc.sbuf_tensor("l1_sin", [128, 16, 32], F32))
        b_wq, b_wdn, b_wku, b_wvu = (Buf() for _ in range(4))
        b_q, b_k, b_v, b_negc, b_nrm, b_ncs, b_ncrow = (Buf() for _ in range(7))
        P.op("vector", lambda e: e.memset(vtm[:, :, :, 64:65], 1.0), writes=[b_v])
        for (sb, nm, bw) in [(wq, "wqP", b_wq), (wdn, "w_down", b_wdn),
                             (wku, "wk_upP", b_wku), (wvu, "wv_up", b_wvu)]:
            P.op("gpsimd", lambda e, sb=sb, nm=nm: e.dma_start(
                out=sb[:], in_=C.dram[nm].rearrange("(c p) o -> p c o", p=128)),
                writes=[bw])
        P.op("sync", lambda e: e.dma_start(
            out=cos_sb[:], in_=C.dram["ropecos"].rearrange("(t p) d -> p t d", p=128)),
            writes=[C.b_const])
        P.op("sync", lambda e: e.dma_start(
            out=sin_sb[:], in_=C.dram["ropesin"].rearrange("(t p) d -> p t d", p=128)),
            writes=[C.b_const])

        for b in range(NB):
            with ExitStack() as sA:
                EA = sA.enter_context
                xt = EA(nc.sbuf_tensor(_un("l1a_xt"), [128, 1024], F32))
                ttm = EA(nc.sbuf_tensor(_un("l1a_ttm"), [128, 1024], BF16))
                tfm = EA(nc.sbuf_tensor(_un("l1a_tfm"), [128, 8, 512], BF16))
                latfm = EA(nc.sbuf_tensor(_un("l1a_lat"), [128, 4, 512], BF16))
                qst = EA(nc.sbuf_tensor(_un("l1a_qst"), [128, 512], F32))
                qtm = EA(nc.sbuf_tensor(_un("l1a_qtm"), [128, 4, 1024], F32))
                rout = EA(nc.sbuf_tensor(_un("l1a_rout"), [128, 1024], BF16))
                rscr = EA(nc.sbuf_tensor(_un("l1a_rscr"), [128, 1024], F32))
                vst = EA(nc.sbuf_tensor(_un("l1a_vst"), [128, 8, 512], BF16))
                sq = EA(nc.sbuf_tensor(_un("l1a_sq"), [128, 1024], F32))
                nrmt = EA(nc.sbuf_tensor(_un("l1a_nrmt"), [128, 16], F32))
                stat = EA(nc.sbuf_tensor(_un("l1a_stat"), [128, 8], F32))
                b_xt, b_ttm, b_tfm, b_lat, b_qst = (Buf() for _ in range(5))
                b_qtm, b_rout, b_rscr, b_vst, b_sq, b_nrmt, b_stat = (Buf() for _ in range(7))
                P.op("vector", lambda e: e.memset(nrm[:], 0.0), writes=[b_nrm])

                def fm_grid(wsb, b_w, nkc, noc, rhs, b_rhs, evac):
                    for oc in range(noc):
                        bank, bbk = C.pf[oc % 3], C.b_pf[oc % 3]
                        for kc in range(nkc):
                            P.op("tensor", lambda e, bank=bank, kc=kc, oc=oc:
                                 e.matmul(bank[:], wsb[:, kc, oc * 128:(oc + 1) * 128],
                                          rhs[:, kc, :], start=(kc == 0),
                                          stop=(kc == nkc - 1)),
                                 reads=[b_w, b_rhs], writes=[bbk], inc=(kc == nkc - 1))
                        evac(oc, bank, bbk)

                def tm_rope_path(j, srcname, b_srcw, off):
                    pass

                for j in range(4):
                    span = slice(j * 512, (j + 1) * 512)
                    for i in range(4):
                        tile = b * 16 + j * 4 + i
                        P.op("sync", lambda e, t=tile, xt=xt: e.dma_start(
                            out=xt[:], in_=src[t * 128:(t + 1) * 128, :]),
                            reads=[b_src[tile]], writes=[b_xt])
                        ln_tile(C, xt[:], ttm[:], C.epsc[:, 0:1], b_xt, b_ttm,
                                sq[:], b_sq, stat[:], b_stat)
                        for c in range(8):
                            transpose128(C, ttm[:, c * 128:(c + 1) * 128],
                                         tfm[:, c, i * 128:(i + 1) * 128],
                                         b_ttm, b_tfm, BF16)

                    def evac_lat(oc, bank, bbk):
                        P.op("scalar", lambda e, bank=bank, oc=oc, latfm=latfm: e.activation(
                            latfm[:, oc, :], bank[:], AF.Identity),
                            reads=[bbk], writes=[b_lat])
                    fm_grid(wdn, b_wdn, 8, 4, tfm, b_tfm, evac_lat)

                    def evac_v(oc, bank, bbk):
                        P.op("scalar", lambda e, bank=bank, oc=oc, vst=vst: e.activation(
                            vst[:, oc, :], bank[:], AF.Identity),
                            reads=[bbk], writes=[b_vst])
                    fm_grid(wvu, b_wvu, 4, 8, latfm, b_lat, evac_v)
                    for c in range(8):
                        for i in range(4):
                            transpose128_v(C, vst[:, c, i * 128:(i + 1) * 128],
                                           vtm, j * 4 + i, c, b_vst, b_v)

                    for (wsb, b_w, nkc, rhsbuf, b_rhs, dstfm, b_dst2, off) in (
                            (wq, b_wq, 8, tfm, b_tfm, qfm, b_q, 0),
                            (wku, b_wku, 4, latfm, b_lat, kfm, b_k, 16)):
                        def evac_q(oc, bank, bbk):
                            P.op("scalar", lambda e, bank=bank, qst=qst: e.activation(
                                qst[:], bank[:], AF.Identity),
                                reads=[bbk], writes=[b_qst])
                            for i in range(4):
                                tb, btb = C.pf[3 + (i % 2)], C.b_pf[3 + (i % 2)]
                                P.op("tensor", lambda e, tb=tb, i=i, qst=qst: e.transpose(
                                    tb[0:128, 0:128], qst[:, i * 128:(i + 1) * 128],
                                    C.idf_sb[:]),
                                    reads=[b_qst, C.b_const], writes=[btb])
                                P.op("vector", lambda e, tb=tb, i=i, oc=oc, qtm=qtm:
                                     e.tensor_copy(qtm[:, i, oc * 128:(oc + 1) * 128],
                                                   tb[0:128, 0:128]),
                                     reads=[btb], writes=[b_qtm])
                        fm_grid(wsb, b_w, nkc, 8, rhsbuf, b_rhs, evac_q)
                        for i in range(4):
                            q3 = qtm[:, i, :].rearrange("p (h two d) -> p h two d",
                                                        h=16, two=2)
                            x1 = q3[:, :, 0, :]
                            x2 = q3[:, :, 1, :]
                            ti = j * 4 + i
                            cb = cos_sb[:, ti, :].rearrange("p (o d) -> p o d", o=1
                                                            ).broadcast_to([128, 16, 32])
                            sb_ = sin_sb[:, ti, :].rearrange("p (o d) -> p o d", o=1
                                                             ).broadcast_to([128, 16, 32])
                            r1 = rscr[:, 0:512].rearrange("p (h d) -> p h d", h=16)
                            r2 = rscr[:, 512:1024].rearrange("p (h d) -> p h d", h=16)
                            ro = rout[:].rearrange("p (h two d) -> p h two d", h=16, two=2)
                            P.op("vector", lambda e, x1=x1, cb=cb, r1=r1:
                                 e.tensor_tensor(r1, x1, cb, ALU.mult),
                                 reads=[b_qtm, C.b_const], writes=[b_rscr])
                            P.op("vector", lambda e, x2=x2, sbx=sb_, r2=r2:
                                 e.tensor_tensor(r2, x2, sbx, ALU.mult),
                                 reads=[b_qtm, C.b_const], writes=[b_rscr])
                            P.op("vector", lambda e, r1=r1, r2=r2, ro=ro:
                                 e.tensor_tensor(ro[:, :, 0, :], r1, r2, ALU.subtract),
                                 reads=[b_rscr], writes=[b_rout])
                            P.op("vector", lambda e, x1=x1, sbx=sb_, r1=r1:
                                 e.tensor_tensor(r1, x1, sbx, ALU.mult),
                                 reads=[b_qtm, C.b_const], writes=[b_rscr])
                            P.op("vector", lambda e, x2=x2, cb=cb, r2=r2:
                                 e.tensor_tensor(r2, x2, cb, ALU.mult),
                                 reads=[b_qtm, C.b_const], writes=[b_rscr])
                            P.op("vector", lambda e, r1=r1, r2=r2, ro=ro:
                                 e.tensor_tensor(ro[:, :, 1, :], r1, r2, ALU.add),
                                 reads=[b_rscr], writes=[b_rout])
                            P.op("vector", lambda e, i=i, sq=sq, qtm=qtm: e.tensor_tensor(
                                sq[:], qtm[:, i, :], qtm[:, i, :], ALU.mult),
                                reads=[b_qtm], writes=[b_sq])
                            P.op("vector", lambda e, sq=sq, nrmt=nrmt: e.reduce_sum(
                                nrmt[:], sq[:].rearrange("p (h d) -> p h d", h=16), AX.X),
                                reads=[b_sq], writes=[b_nrmt])
                            P.op("vector", lambda e, off=off, nrmt=nrmt: e.tensor_tensor(
                                nrm[:, off:off + 16], nrm[:, off:off + 16],
                                nrmt[:], ALU.max),
                                reads=[b_nrm, b_nrmt], writes=[b_nrm])
                            for c in range(8):
                                transpose128(C, rout[:, c * 128:(c + 1) * 128],
                                             dstfm[:, c, (j * 4 + i) * 128:(j * 4 + i + 1) * 128],
                                             b_rout, b_dst2, BF16)
                negc_calc(C, nrm, b_nrm, ncs, b_ncs, ncrow, b_ncrow, negc, b_negc)
                if b == 0:
                    tap(C, "l1_qfm", qfm[:], b_q)
                    tap(C, "l1_kfm", kfm[:], b_k)
                    tap(C, "l1_vtm", vtm[:], b_v)
                    tap(C, "l1_negc", negc[:], b_negc)
            P.barrier()
            with ExitStack() as sB:
                EB = sB.enter_context
                wo = EB(nc.sbuf_tensor(_un("l1b_wo"), [128, 8, 1024], BF16))
                b_wo = Buf()
                P.op("gpsimd", lambda e: e.dma_start(
                    out=wo[:], in_=C.dram["wo"].rearrange("(c p) o -> p c o", p=128)),
                    writes=[b_wo])
                Pb = EB(nc.sbuf_tensor(_un("l1b_pb"), [128, 2, 2, 512], BF16))
                attnst = EB(nc.sbuf_tensor(_un("l1b_attnst"), [128, 8, 512], BF16))
                denb = EB(nc.sbuf_tensor(_un("l1b_denb"), [128, 512], F32))
                rb = EB(nc.sbuf_tensor(_un("l1b_rb"), [128, 512], F32))
                wost = EB(nc.sbuf_tensor(_un("l1b_wost"), [128, 512], F32))
                otm = EB(nc.sbuf_tensor(_un("l1b_otm"), [128, 4, 1024], F32))
                zt = EB(nc.sbuf_tensor(_un("l1b_zt"), [128, 1024], F32))
                xtB = EB(nc.sbuf_tensor(_un("l1b_xt"), [128, 1024], F32))
                sqB = EB(nc.sbuf_tensor(_un("l1b_sq"), [128, 1024], F32))
                statB = EB(nc.sbuf_tensor(_un("l1b_stat"), [128, 8], F32))
                b_Pb, b_att, b_den, b_rb, b_wost = (Buf() for _ in range(5))
                b_otm, b_zt, b_xtB, b_sqB, b_statB = (Buf() for _ in range(5))
                C.tap_stage = (wost, b_wost)
                C.tap_stage2 = (zt, b_zt)
                for qs in range(4):
                    attn_phaseB(C, qs, 4 * qs + 4,
                                lambda st, qs=qs: max(0, (st & ~1) - 4 * qs) * 128,
                                False, qfm, b_q, kfm, b_k, vtm, b_v,
                                negc, b_negc, attnst, b_att, Pb, b_Pb,
                                denb, b_den, rb, b_rb)
                    if b == 0 and qs == 0:
                        tap(C, "l1_attnst0", attnst[:], b_att)
                    for oc in range(8):
                        bank, bbk = C.pf[2 + (oc % 2)], C.b_pf[2 + (oc % 2)]
                        for kc in range(8):
                            P.op("tensor", lambda e, bank=bank, kc=kc, oc=oc, wo=wo, attnst=attnst:
                                 e.matmul(bank[:], wo[:, kc, oc * 128:(oc + 1) * 128],
                                          attnst[:, kc, :], start=(kc == 0), stop=(kc == 7)),
                                 reads=[b_wo, b_att], writes=[bbk], inc=(kc == 7))
                        P.op("scalar", lambda e, bank=bank, wost=wost: e.activation(
                            wost[:], bank[:], AF.Identity),
                            reads=[bbk], writes=[b_wost])
                        for i in range(4):
                            tb, btb = C.pf[4 + (i % 2)], C.b_pf[4 + (i % 2)]
                            P.op("tensor", lambda e, tb=tb, i=i, wost=wost: e.transpose(
                                tb[0:128, 0:128], wost[:, i * 128:(i + 1) * 128],
                                C.idf_sb[:]),
                                reads=[b_wost, C.b_const], writes=[btb])
                            P.op("vector", lambda e, tb=tb, i=i, oc=oc, otm=otm: e.tensor_copy(
                                otm[:, i, oc * 128:(oc + 1) * 128], tb[0:128, 0:128]),
                                reads=[btb], writes=[b_otm])
                    for i in range(4):
                        tile = b * 16 + qs * 4 + i
                        ln_tile(C, otm[:, i, :], zt[:], C.epsc[:, 0:1], b_otm, b_zt,
                                sqB[:], b_sqB, statB[:], b_statB)
                        P.op("sync", lambda e, t=tile, xtB=xtB: e.dma_start(
                            out=xtB[:], in_=src[t * 128:(t + 1) * 128, :]),
                            reads=[b_src[tile]], writes=[b_xtB])
                        P.op("vector", lambda e, zt=zt, xtB=xtB: e.tensor_tensor(xtB[:], zt[:], xtB[:], ALU.add),
                             reads=[b_zt, b_xtB], writes=[b_xtB])
                        P.op("sync", lambda e, t=tile, xtB=xtB: e.dma_start(
                            out=dst[t * 128:(t + 1) * 128, :], in_=xtB[:]),
                            reads=[b_xtB], writes=[b_dst[tile]])
            P.barrier()
    P.barrier()


def build_program(blocks=("l0a", "l0f", "l1a", "l1f"), want_taps=None, paranoia=0, **cflags):
    nc = bass.Bass("TRN2", target_bir_lowering=False)
    C = Ctx()
    C.nc = nc
    C.want_taps = want_taps
    C.paranoia = paranoia
    for k_, v_ in cflags.items():
        setattr(C, k_, v_)
    C.dram = {}

    def din(name, shape, dtype=F32):
        C.dram[name] = nc.dram_tensor(name, list(shape), dtype, kind="ExternalInput")
        return C.dram[name]

    din("x", [NB * T, D])
    dout = nc.dram_tensor("out", [NB * T, D], F32, kind="ExternalOutput")
    C.dram["out"] = dout

    din("idb", [128, 128], BF16)
    din("idf", [128, 128], F32)
    din("onesf", [128, 128], F32)
    din("onescb", [128, 1], BF16)
    din("epsc", [128, 2], F32)
    if "l0a" in blocks or "l1a" in blocks:
        din("eh", [128, 2], F32)
        din("ehb", [128, 2], BF16)
    if "l0a" in blocks:
        din("tri", [128, 128], BF16)
        din("w_attnT", [D, 3 * D])
        din("b_attn_fm", [128, 24])
        din("w_projT", [D, D])
        din("b_proj_fm", [128, 8])
        din("w_fcT", [D, 4 * D])
        din("b_fc_fm", [128, 32])
        din("w_mlpT", [4 * D, D])
        din("b_mlp_fm", [128, 8])
    if "l0f" in blocks:
        din("wg", [D, 8])
        din("bg_col", [8, 1])
        din("moe_w1", [8, D, 2 * D])
        din("b1_fm", [128, 8, 16])
        din("moe_w2", [8, 2 * D, D])
        din("b2_mat", [8, D])
        din("gamma", [1, D])
        din("beta", [1, D])
    if "l1a" in blocks:
        din("ropecos", [T, 32])
        din("ropesin", [T, 32])
        din("wqP", [D, D])
        din("w_down", [D, 512])
        din("wk_upP", [512, D])
        din("wv_up", [512, D])
        din("wo", [D, D])
    if "l1f" in blocks:
        din("l1f_w", [D, D])
        din("l1f_b_fm", [128, 8])

    need_xres = len(blocks) > 1
    if need_xres:
        xres = nc.dram_tensor("xres", [NB * T, D], F32, kind="Internal")
    if "l0a" in blocks:
        u1d = nc.dram_tensor("u1d", [NB * T, D], F32, kind="Internal")
        C.u1d = u1d
        C.b_u1d = [Buf() for _ in range(NTILES)]
    if "l0f" in blocks:
        u3fmd = nc.dram_tensor("u3fmd", [NSUP, 128, 8 * 512], BF16, kind="Internal")
        gd = nc.dram_tensor("gd", [8, NB * T], F32, kind="Internal")
        C.u3fmd, C.gd = u3fmd, gd
        C.b_u3fmd = [Buf() for _ in range(NSUP)]
        C.b_gd = [Buf() for _ in range(NSUP)]

    with ExitStack() as ctx:
        E = ctx.enter_context
        P = Prog(nc)
        C.P = P
        sems = {}
        for e in ENGINES:
            if e in DMA_ENG:
                for r in range(NLANE):
                    sems[(e, r)] = E(nc.semaphore(f"sem_{e}{r}"))
            else:
                sems[(e, 0)] = E(nc.semaphore("sem_" + e))
        bar_sem = E(nc.semaphore("sem_bar"))

        C.idb_sb = E(nc.sbuf_tensor("idb_sb", [128, 128], BF16))
        C.idf_sb = E(nc.sbuf_tensor("idf_sb", [128, 128], F32))
        C.onesf_sb = E(nc.sbuf_tensor("onesf_sb", [128, 128], F32))
        C.onescb_sb = E(nc.sbuf_tensor("onescb_sb", [128, 1], BF16))
        C.epsc = E(nc.sbuf_tensor("epsc_sb", [128, 2], F32))
        C.b_const = Buf("const")
        loads = [(C.idb_sb, "idb"), (C.idf_sb, "idf"), (C.onesf_sb, "onesf"),
                 (C.onescb_sb, "onescb"), (C.epsc, "epsc")]
        C.bias_sb = {}
        if "l0a" in blocks or "l1a" in blocks:
            C.eh_sb = E(nc.sbuf_tensor("eh_sb", [128, 2], F32))
            loads.append((C.eh_sb, "eh"))
            C.ehb_sb = E(nc.sbuf_tensor("ehb_sb", [128, 2], BF16))
            loads.append((C.ehb_sb, "ehb"))
        if "l0a" in blocks:
            C.tri_sb = E(nc.sbuf_tensor("tri_sb", [128, 128], BF16))
            loads.append((C.tri_sb, "tri"))
            for nm, sh in [("b_attn_fm", [128, 24]), ("b_proj_fm", [128, 8]),
                           ("b_fc_fm", [128, 32]), ("b_mlp_fm", [128, 8])]:
                C.bias_sb[nm] = E(nc.sbuf_tensor(nm + "_sb", sh, F32))
                loads.append((C.bias_sb[nm], nm))
        if "l0f" in blocks:
            for nm, sh in [("bg_col", [8, 1]), ("b1_fm", [128, 8, 16])]:
                C.bias_sb[nm] = E(nc.sbuf_tensor(nm + "_sb", sh, F32))
                loads.append((C.bias_sb[nm], nm))
        if "l1f" in blocks:
            C.bias_sb["l1f_b"] = E(nc.sbuf_tensor("l1f_b_sb", [128, 8], F32))
            loads.append((C.bias_sb["l1f_b"], "l1f_b_fm"))

        C.pf = [E(nc.psum_tensor(f"pf{i}", [128, 512], F32)) for i in range(8)]
        C.b_pf = [Buf(f"pf{i}") for i in range(8)]
        C.pbf = [C.pf[6], C.pf[7]]
        C.b_pbf = [C.b_pf[6], C.b_pf[7]]
        C._pf_rot = 0
        C._pbf_rot = 0

        def next_pf():
            i = C._pf_rot % len(C.pf)
            C._pf_rot += 1
            return C.pf[i], C.b_pf[i]

        def next_pbf():
            i = C._pbf_rot % len(C.pbf)
            C._pbf_rot += 1
            return C.pbf[i], C.b_pbf[i]

        C.next_pf = next_pf
        C.next_pbf = next_pbf

        blk = E(nc.Block())

        for sb, dr_name in loads:
            P.op("sync", lambda e, sb=sb, d=C.dram[dr_name]: e.dma_start(
                out=sb[:], in_=d.ap()), writes=[C.b_const])

        def dram_bufs(n):
            return [Buf() for _ in range(n)]

        b_x = dram_bufs(NTILES)
        b_out = dram_bufs(NTILES)
        b_xres = dram_bufs(NTILES) if need_xres else None

        seq = []
        for bi, name in enumerate(blocks):
            src = C.dram["x"].ap() if bi == 0 else xres.ap()
            bsrc = b_x if bi == 0 else b_xres
            last = bi == len(blocks) - 1
            dstap = dout.ap() if last else xres.ap()
            bdst = b_out if last else b_xres
            seq.append((name, src, bsrc, dstap, bdst))

        for name, src, bsrc, dstap, bdst in seq:
            if name == "l1f":
                emit_l1ffn(C, src, bsrc, dstap, bdst)
            elif name == "l0a":
                emit_l0attn(C, src, bsrc, dstap, bdst)
            elif name == "l0f":
                emit_l0ffn(C, src, bsrc, dstap, bdst)
            elif name == "l1a":
                emit_l1attn(C, src, bsrc, dstap, bdst)

        P.emit(blk, sems, bar_sem)
    return nc


def _bias_fm(b):
    nck = b.shape[0] // 128
    return np.ascontiguousarray(b.reshape(nck, 128).T).astype(np.float32)


def make_weight_arrays(inputs, blocks=("l0a", "l0f", "l1a", "l1f")):
    g = {}
    g["idb"] = np.eye(128, dtype=ml_dtypes.bfloat16)
    g["idf"] = np.eye(128, dtype=np.float32)
    g["onesf"] = np.ones((128, 128), dtype=np.float32)
    g["onescb"] = np.ones((128, 1), dtype=ml_dtypes.bfloat16)
    g["epsc"] = np.broadcast_to(np.array([1e-5, 1e-10], dtype=np.float32), (128, 2)).copy()
    if "l0a" in blocks or "l1a" in blocks:
        eh = np.zeros((128, 2), dtype=np.float32)
        eh[0:64, 0] = 1.0
        eh[64:128, 1] = 1.0
        g["eh"] = eh
        g["ehb"] = eh.astype(ml_dtypes.bfloat16)
    if "l0a" in blocks:
        s_le_q = (np.arange(128)[:, None] <= np.arange(128)[None, :])
        g["tri"] = s_le_q.astype(ml_dtypes.bfloat16)
        g["w_attnT"] = np.ascontiguousarray(inputs["L0_attn_w_attn"].T)
        g["b_attn_fm"] = _bias_fm(inputs["L0_attn_b_attn"])
        g["w_projT"] = np.ascontiguousarray(inputs["L0_attn_w_proj"].T)
        g["b_proj_fm"] = _bias_fm(inputs["L0_attn_b_proj"])
        g["w_fcT"] = np.ascontiguousarray(inputs["L0_attn_w_fc"].T)
        g["b_fc_fm"] = _bias_fm(inputs["L0_attn_b_fc"])
        g["w_mlpT"] = np.ascontiguousarray(inputs["L0_attn_w_mlp_proj"].T)
        g["b_mlp_fm"] = _bias_fm(inputs["L0_attn_b_mlp_proj"])
    if "l0f" in blocks:
        g["wg"] = np.ascontiguousarray(inputs["L0_ffn_Wg"])
        g["bg_col"] = np.ascontiguousarray(inputs["L0_ffn_bg"].reshape(8, 1))
        g["moe_w1"] = np.ascontiguousarray(inputs["L0_ffn_W1"])
        g["b1_fm"] = np.ascontiguousarray(
            inputs["L0_ffn_b1"].reshape(8, 16, 128).transpose(2, 0, 1))
        g["moe_w2"] = np.ascontiguousarray(inputs["L0_ffn_W2"])
        g["b2_mat"] = np.ascontiguousarray(inputs["L0_ffn_b2"])
        g["gamma"] = np.ascontiguousarray(inputs["L0_ffn_gamma"].reshape(1, D))
        g["beta"] = np.ascontiguousarray(inputs["L0_ffn_beta"].reshape(1, D))
    if "l1a" in blocks:
        inv = 1.0 / 10000.0 ** (np.arange(0, 64, 2, dtype=np.float32) / 64)
        ang = np.outer(np.arange(T, dtype=np.float32), inv)
        g["ropecos"] = np.cos(ang).astype(np.float32)
        g["ropesin"] = np.sin(ang).astype(np.float32)
        perm = np.concatenate([
            np.concatenate([h * 64 + 2 * np.arange(32), h * 64 + 2 * np.arange(32) + 1])
            for h in range(16)])
        g["wqP"] = np.ascontiguousarray(inputs["L1_attn_wq"][:, perm])
        g["w_down"] = np.ascontiguousarray(inputs["L1_attn_w_down"])
        g["wk_upP"] = np.ascontiguousarray(inputs["L1_attn_wk_up"][:, perm])
        g["wv_up"] = np.ascontiguousarray(inputs["L1_attn_wv_up"])
        g["wo"] = np.ascontiguousarray(inputs["L1_attn_wo"])
    if "l1f" in blocks:
        g["l1f_w"] = np.ascontiguousarray(inputs["L1_ffn_weight"])
        g["l1f_b_fm"] = _bias_fm(inputs["L1_ffn_bias"])
    return g


FUSED = True


def kernel(**inputs):
    blocks = ("l0a", "l0f", "l1a", "l1f")
    x = np.asarray(inputs["L0_attn_x"], dtype=np.float32).reshape(32 * T, D)
    if FUSED:
        g = make_weight_arrays(inputs, blocks)
        nc = build_program(blocks)
        in_maps = []
        for c in range(8):
            m = dict(g)
            m["x"] = np.ascontiguousarray(x[c * NB * T:(c + 1) * NB * T])
            in_maps.append(m)
        res = run_bass_kernel_spmd(nc, in_maps, core_ids=list(range(8)))
        out = np.concatenate([r["out"] for r in res.results], axis=0)
        return out.reshape(32, T, D)
    cur = x
    for bname in blocks:
        g = make_weight_arrays(inputs, (bname,))
        nc = build_program((bname,))
        in_maps = []
        for c in range(8):
            m = dict(g)
            m["x"] = np.ascontiguousarray(cur[c * NB * T:(c + 1) * NB * T])
            in_maps.append(m)
        res = run_bass_kernel_spmd(nc, in_maps, core_ids=list(range(8)))
        cur = np.concatenate([r["out"] for r in res.results], axis=0)
    return cur.reshape(32, T, D)


# version 07
# speedup vs baseline: 1.5814x; 1.1596x over previous
import numpy as np
import ml_dtypes
from contextlib import ExitStack

import concourse.bass as bass
import concourse.mybir as mybir
from concourse.bass_utils import run_bass_kernel_spmd

F32 = mybir.dt.float32
BF16 = mybir.dt.bfloat16
AF = mybir.ActivationFunctionType
ALU = mybir.AluOpType
AX = mybir.AxisListType

ENGINES = ("sync", "tensor", "vector", "scalar", "gpsimd")
DMA_ENG = ("sync", "gpsimd")
NLANE = 8

NB = 4
T = 2048
D = 1024
NTILES = NB * T // 128
NSUP = NB * T // 512
MAGIC = 12582912.0


class Buf:
    __slots__ = ("writer", "readers", "name")

    def __init__(self, name=""):
        self.writer = None
        self.readers = []
        self.name = name


class Prog:
    def __init__(self, nc):
        self.nc = nc
        self.steps = []
        self.eng_steps = {e: [] for e in ENGINES}
        self.nbar = 0
        self._inc_count = {}

    def op(self, eng, fn, reads=(), writes=(), deps=(), inc=True):
        d = list(deps)
        for b in reads:
            if b.writer is not None:
                d.append(b.writer)
        for b in writes:
            d.extend(b.readers)
            if b.writer is not None:
                d.append(b.writer)
        seq = len(self.steps)
        st = {"eng": eng, "fn": fn, "deps": d, "inc": inc, "seq": seq, "bar": None}
        if inc:
            self._inc_count[eng] = self._inc_count.get(eng, 0) + 1
        self.steps.append(st)
        self.eng_steps[eng].append(st)
        tok = (eng, seq)
        for b in reads:
            b.readers.append(tok)
        for b in writes:
            b.writer = tok
            b.readers = []
        return tok

    def barrier(self):
        snap = {}
        for e in ENGINES:
            if e in DMA_ENG:
                n = self._inc_count.get(e, 0)
                snap[e] = [(n + NLANE - 1 - r) // NLANE for r in range(NLANE)]
            else:
                snap[e] = self._inc_count.get(e, 0)
        for e in ENGINES:
            st = {"eng": e, "fn": None, "deps": (), "inc": False,
                  "seq": len(self.steps), "bar": ("waits", snap)}
            self.steps.append(st)
            self.eng_steps[e].append(st)

    def _resolve(self):
        self.cover = {}
        self.cover_seq = {}
        self.total_incs = {}
        for e in ENGINES:
            arr = self.eng_steps[e]
            if e in DMA_ENG:
                lane_cnt = [0] * NLANE
                for st in arr:
                    if st["bar"] is not None:
                        continue
                    assert st["inc"], "DMA steps must inc"
                    r = (sum(lane_cnt)) % NLANE
                    lane_cnt[r] += 1
                    st["lane"] = r
                    self.cover[(e, st["seq"])] = (r, lane_cnt[r] * 16)
                    self.cover_seq[(e, st["seq"])] = st["seq"]
                self.total_incs[e] = list(lane_cnt)
            else:
                cum = 0
                cums = []
                for st in arr:
                    if st["inc"]:
                        cum += 1
                    cums.append(cum)
                self.total_incs[e] = cum
                nxt = [None] * len(arr)
                nxtseq = [None] * len(arr)
                cur = None
                curseq = None
                for i in range(len(arr) - 1, -1, -1):
                    if arr[i]["inc"]:
                        cur = cums[i]
                        curseq = arr[i]["seq"]
                    nxt[i] = cur
                    nxtseq[i] = curseq
                for i, st in enumerate(arr):
                    if nxt[i] is not None:
                        self.cover[(e, st["seq"])] = (0, nxt[i])
                        self.cover_seq[(e, st["seq"])] = nxtseq[i]

    def emit(self, blk, sems, bar_sem):
        self._resolve()
        for ename in ENGINES:
            def mk(ename):
                def run(eng):
                    waited = {}
                    for st in self.eng_steps[ename]:
                        if st["bar"] is not None:
                            kind, val = st["bar"]
                            assert kind == "waits"
                            for e2 in ENGINES:
                                if e2 in DMA_ENG:
                                    for r in range(NLANE):
                                        v2 = val[e2][r] * 16
                                        if v2 > waited.get((e2, r), 0):
                                            eng.wait_ge(sems[(e2, r)], v2)
                                            waited[(e2, r)] = v2
                                else:
                                    v2 = val[e2]
                                    if v2 > waited.get((e2, 0), 0):
                                        eng.wait_ge(sems[(e2, 0)], v2)
                                        waited[(e2, 0)] = v2
                            continue
                        need = {}
                        for (de, dseq) in st["deps"]:
                            if de == ename and ename == "tensor":
                                continue
                            cv = self.cover.get((de, dseq))
                            assert cv is not None, \
                                f"dep on ({de},{dseq}) has no covering inc"
                            lane, v = cv
                            cseq = self.cover_seq[(de, dseq)]
                            assert cseq < st["seq"], \
                                f"dep of step {st['seq']} covered only by later step {cseq} on {de}"
                            if v > need.get((de, lane), -1):
                                need[(de, lane)] = v
                        for key in sorted(need):
                            v = need[key]
                            if v > waited.get(key, 0):
                                eng.wait_ge(sems[key], v)
                                waited[key] = v
                        ins = st["fn"](eng)
                        if st["inc"]:
                            if ename in DMA_ENG:
                                ins.then_inc(sems[(ename, st["lane"])], 16)
                            else:
                                ins.then_inc(sems[(ename, 0)], 1)
                    if ename == "sync":
                        for r in range(NLANE):
                            eng.wait_ge(sems[("sync", r)],
                                        self.total_incs["sync"][r] * 16)
                return run
            getattr(blk, ename)(mk(ename))



_NAME_N = [0]


def _un(s):
    _NAME_N[0] += 1
    return f"{s}_{_NAME_N[0]}"



class Ctx:
    pass


def ln_tile(C, src_ap, out_ap, eps_ap, r_src, w_out, sq_ap, b_sq, stat_ap, b_stat,
            extra_out=None):
    P = C.P
    n = src_ap.shape[-1]
    sums = stat_ap[:, 0:1]
    sumsq = stat_ap[:, 1:2]
    mu = stat_ap[:, 2:3]
    var = stat_ap[:, 3:4]
    rstd = stat_ap[:, 4:5]
    nmr = stat_ap[:, 5:6]
    P.op("vector", lambda e, a=src_ap, o=sums: e.reduce_sum(o, a, AX.X),
         reads=[r_src], writes=[b_stat])
    if out_ap.dtype == F32:
        P.op("scalar", lambda e, a=src_ap, o=out_ap, ac=sumsq:
             e.activation(o, a, AF.Square, accum_out=ac),
             reads=[r_src], writes=[w_out, b_stat])
    else:
        P.op("scalar", lambda e, a=src_ap, o=sq_ap, ac=sumsq:
             e.activation(o, a, AF.Square, accum_out=ac),
             reads=[r_src], writes=[b_sq, b_stat])
    P.op("vector", lambda e, o=mu, a=sums: e.tensor_single_scalar(o, a, 1.0 / n, ALU.mult),
         reads=[b_stat], writes=[b_stat])
    P.op("vector", lambda e, o=var, a=mu: e.tensor_tensor(o, a, a, ALU.mult),
         reads=[b_stat], writes=[b_stat])
    P.op("vector", lambda e, o=var, a=sumsq, m=var:
         e.scalar_tensor_tensor(o, a, 1.0 / n, m, ALU.mult, ALU.subtract),
         reads=[b_stat], writes=[b_stat])
    P.op("scalar", lambda e, o=rstd, a=var, b=eps_ap: e.activation(o, a, AF.Ln, bias=b),
         reads=[b_stat, C.b_const], writes=[b_stat])
    P.op("scalar", lambda e, o=rstd, a=rstd: e.activation(o, a, AF.Exp, scale=-0.5),
         reads=[b_stat], writes=[b_stat])
    P.op("vector", lambda e, o=nmr, a=mu, r=rstd:
         e.scalar_tensor_tensor(o, a, -1.0, r, ALU.mult, ALU.mult),
         reads=[b_stat], writes=[b_stat])
    P.op("scalar", lambda e, o=out_ap, a=src_ap, r=rstd, b=nmr:
         e.activation(o, a, AF.Identity, bias=b, scale=r),
         reads=[r_src, b_stat], writes=[w_out])
    return rstd, nmr


def transpose128(C, src_ap, dst_ap, r_src, w_dst, dtype, copy_eng="vector"):
    P = C.P
    ncols = src_ap.shape[-1]
    if dtype == BF16:
        bank, bbuf = C.next_pbf()
        ident = C.idb_sb
        pslice = bank[:].bitcast(BF16)[0:ncols, 0:128]
    else:
        bank, bbuf = C.next_pf()
        ident = C.idf_sb
        pslice = bank[0:ncols, 0:128]
    P.op("tensor", lambda e, o=pslice, i=src_ap, idt=ident[:]: e.transpose(o, i, idt),
         reads=[r_src, C.b_const], writes=[bbuf])
    P.op(copy_eng, lambda e, o=dst_ap, i=pslice:
         (e.tensor_copy(o, i) if copy_eng == "vector" else e.activation(o, i, AF.Identity)),
         reads=[bbuf], writes=[w_dst])


def emit_l1ffn(C, src, b_src, dst, b_dst):
    nc, P = C.nc, C.P

    def quant_chain(u_ap, out_ap, r_u, w_out, s1, s2, s3, s4, bs, bufs):
        b1, b2, b3, b4, bb = bufs
        bmax = bs[:, 0:32]
        scl = bs[:, 32:64]
        rscl = bs[:, 64:96]
        u3d = u_ap.rearrange("p (a b) -> p a b", a=32)
        P.op("vector", lambda e: e.reduce_max(bmax, u3d, AX.X, apply_absolute_value=True),
             reads=[r_u], writes=[bb])
        P.op("vector", lambda e: e.tensor_scalar(scl, bmax, 1.0 / 7.5, 1e-30, ALU.mult, ALU.max),
             reads=[bb], writes=[bb])
        P.op("vector", lambda e: e.reciprocal(rscl, scl),
             reads=[bb], writes=[bb])
        rb = rscl.rearrange("p (a o) -> p a o", a=32).broadcast_to([128, 32, 32])
        sb = scl.rearrange("p (a o) -> p a o", a=32).broadcast_to([128, 32, 32])
        s1d = s1.rearrange("p (a b) -> p a b", a=32)
        P.op("vector", lambda e: e.tensor_tensor(s1d, u3d, rb, ALU.mult),
             reads=[r_u, bb], writes=[b1])
        P.op("scalar", lambda e: e.activation(s2, s1, AF.Abs),
             reads=[b1], writes=[b2])
        P.op("vector", lambda e: e.tensor_single_scalar(s3, s2, 2.0, ALU.is_ge),
             reads=[b2], writes=[b3])
        P.op("vector", lambda e: e.tensor_single_scalar(s2, s2, 4.0, ALU.is_ge),
             reads=[b2], writes=[b2])
        P.op("vector", lambda e: e.tensor_scalar(s4, s3, -4.0, 8.0, ALU.mult, ALU.add),
             reads=[b3], writes=[b4])
        P.op("vector", lambda e: e.scalar_tensor_tensor(s4, s2, -2.0, s4, ALU.mult, ALU.add),
             reads=[b2, b4], writes=[b4])
        P.op("vector", lambda e: e.tensor_scalar(s3, s3, 0.125, 0.125, ALU.mult, ALU.add),
             reads=[b3], writes=[b3])
        P.op("vector", lambda e: e.scalar_tensor_tensor(s3, s2, 0.25, s3, ALU.mult, ALU.add),
             reads=[b2, b3], writes=[b3])
        P.op("vector", lambda e: e.tensor_tensor(s2, s1, s4, ALU.mult),
             reads=[b1, b4], writes=[b2])
        P.op("vector", lambda e: e.tensor_scalar(s2, s2, MAGIC, MAGIC, ALU.add, ALU.subtract),
             reads=[b2], writes=[b2])
        P.op("vector", lambda e: e.tensor_tensor(s1, s2, s3, ALU.mult),
             reads=[b2, b3], writes=[b1])
        od = out_ap.rearrange("p (a b) -> p a b", a=32)
        P.op("vector", lambda e: e.tensor_tensor(od, s1d, sb, ALU.mult),
             reads=[b1, bb], writes=[w_out])

    with ExitStack() as ctx:
        E = ctx.enter_context
        wf = E(nc.sbuf_tensor("l1f_wf", [128, 8, 1024], F32))
        wdq = E(nc.sbuf_tensor("l1f_wdq", [128, 8, 1024], BF16))
        wdT = E(nc.sbuf_tensor("l1f_wdT", [128, 8, 1024], BF16))
        s1 = E(nc.sbuf_tensor("l1f_s1", [128, 1024], F32))
        s2 = E(nc.sbuf_tensor("l1f_s2", [128, 1024], F32))
        s3 = E(nc.sbuf_tensor("l1f_s3", [128, 1024], F32))
        s4 = E(nc.sbuf_tensor("l1f_s4", [128, 1024], F32))
        bs = E(nc.sbuf_tensor("l1f_bs", [128, 96], F32))
        b_wf, b_wdq, b_wdT = Buf("wf"), Buf("wdq"), Buf("wdT")
        b1, b2, b3, b4, bb = Buf(), Buf(), Buf(), Buf(), Buf()

        P.op("sync", lambda e: e.dma_start(
            out=wf[:], in_=C.dram["l1f_w"].rearrange("(c p) k -> p c k", p=128)),
            writes=[b_wf])
        for jc in range(8):
            quant_chain(wf[:, jc, :], wdq[:, jc, :], b_wf, b_wdq,
                        s1[:], s2[:], s3[:], s4[:], bs[:], (b1, b2, b3, b4, bb))
        for jc in range(8):
            for kc in range(8):
                transpose128(C, wdq[:, jc, kc * 128:(kc + 1) * 128],
                             wdT[:, kc, jc * 128:(jc + 1) * 128],
                             b_wdq, b_wdT, BF16)
        P.barrier()

        xt = E(nc.sbuf_tensor("l1f_xt", [128, 4, 1024], F32))
        u5 = E(nc.sbuf_tensor("l1f_u5", [128, 1024], F32))
        xdtm = E(nc.sbuf_tensor("l1f_xdtm", [128, 1024], BF16))
        xdfm = E(nc.sbuf_tensor("l1f_xdfm", [128, 8, 512], BF16))
        outst = E(nc.sbuf_tensor("l1f_outst", [128, 512], F32))
        otm = E(nc.sbuf_tensor("l1f_otm", [128, 4, 1024], F32))
        zt = E(nc.sbuf_tensor("l1f_zt", [128, 1024], F32))
        xo = E(nc.sbuf_tensor("l1f_xo", [128, 1024], F32))
        stat = E(nc.sbuf_tensor("l1f_stat", [128, 8], F32))
        b_xt = [Buf() for _ in range(4)]
        b_u5, b_xdtm, b_xdfm = Buf(), Buf(), Buf()
        b_outst, b_otm, b_zt, b_xo, b_stat = Buf(), Buf(), Buf(), Buf(), Buf()

        for s in range(NSUP):
            for i in range(4):
                tile = s * 4 + i
                P.op("sync", lambda e, t=tile, i=i: e.dma_start(
                    out=xt[:, i, :], in_=src[t * 128:(t + 1) * 128, :]),
                    reads=[b_src[tile]], writes=[b_xt[i]])
                ln_tile(C, xt[:, i, :], u5[:], C.epsc[:, 0:1], b_xt[i], b_u5,
                        s1[:], b1, stat[:], b_stat)
                quant_chain(u5[:], xdtm[:], b_u5, b_xdtm,
                            s1[:], s2[:], s3[:], s4[:], bs[:], (b1, b2, b3, b4, bb))
                for c in range(8):
                    transpose128(C, xdtm[:, c * 128:(c + 1) * 128],
                                 xdfm[:, c, i * 128:(i + 1) * 128],
                                 b_xdtm, b_xdfm, BF16)
            for jc in range(8):
                bank, bbk = C.next_pf()
                for kc in range(8):
                    P.op("tensor", lambda e, bk=bank, kc=kc, jc=jc: e.matmul(
                        bk[:], wdT[:, kc, jc * 128:(jc + 1) * 128], xdfm[:, kc, :],
                        start=(kc == 0), stop=(kc == 7)),
                        reads=[b_wdT, b_xdfm], writes=[bbk], inc=(kc == 7))
                P.op("scalar", lambda e, bk=bank, jc=jc: e.activation(
                    outst[:], bk[:], AF.Identity, bias=C.bias_sb["l1f_b"][:, jc:jc + 1]),
                    reads=[bbk, C.b_const], writes=[b_outst])
                for i in range(4):
                    transpose128(C, outst[:, i * 128:(i + 1) * 128],
                                 otm[:, i, jc * 128:(jc + 1) * 128],
                                 b_outst, b_otm, F32)
            for i in range(4):
                tile = s * 4 + i
                rstd, nmr = ln_tile(C, otm[:, i, :], zt[:], C.epsc[:, 0:1], b_otm, b_zt,
                                    s1[:], b1, stat[:], b_stat)
                P.op("vector", lambda e, i=i: e.tensor_tensor(
                    xo[:], zt[:], xt[:, i, :], ALU.add),
                    reads=[b_zt, b_xt[i]], writes=[b_xo])
                P.op("sync", lambda e, t=tile: e.dma_start(
                    out=dst[t * 128:(t + 1) * 128, :], in_=xo[:]),
                    reads=[b_xo], writes=[b_dst[tile]])
        P.barrier()


def attn_phaseB(C, qs, nst, colstart_fn, mask_diag, qfm, b_q, kfm, b_k, vtm, b_v,
                negc, b_negc, attnst, b_att, Pb, b_Pb, denb, b_den, rb, b_rb):
    P = C.P
    pf, bpf = C.pf, C.b_pf
    SPAR = ((0, 1), (2, 3), (6, 7))

    for hp in range(8):
        def emit_S(st):
            cs = colstart_fn(st)
            pspan = slice(cs, 512)
            span = slice(qs * 512 + cs, qs * 512 + 512)
            for h in range(2):
                base = h * 64
                bi = SPAR[st % 3][h]
                P.op("tensor", lambda e, bi=bi, hp=hp, st=st, base=base, sp=span, ps=pspan:
                     e.matmul(pf[bi][:, ps], kfm[base:base + 64, hp, st * 128:(st + 1) * 128],
                              qfm[base:base + 64, hp, sp], start=True, stop=True,
                              skip_group_check=True),
                     reads=[b_q, b_k], writes=[bpf[bi]])

        def emit_exp(st):
            cs = colstart_fn(st)
            pspan = slice(cs, 512)
            par = st % 2
            for h in range(2):
                bi = SPAR[st % 3][h]
                P.op("scalar", lambda e, bi=bi, hp=hp, h=h, ps=pspan, par=par:
                     e.activation(Pb[:, par, h, ps], pf[bi][:, ps], AF.Exp,
                                  bias=negc[:, 2 * hp + h:2 * hp + h + 1], scale=0.125),
                     reads=[bpf[bi], b_negc], writes=[b_Pb[2 * par + h]])
                if mask_diag and st >= 4 * qs:
                    P.op("vector", lambda e, h=h, cs=cs, par=par:
                         e.tensor_tensor(Pb[:, par, h, cs:cs + 128],
                                         Pb[:, par, h, cs:cs + 128],
                                         C.tri_sb[:], ALU.mult),
                         reads=[b_Pb[2 * par + h], C.b_const], writes=[b_Pb[2 * par + h]])

        def emit_av(st):
            cs = colstart_fn(st)
            ps = slice(cs, 512)
            par = st % 2
            first = st == 0
            last = st == nst - 1
            P.op("tensor", lambda e, hp=hp, st=st, ps=ps, first=first, last=last, par=par:
                 e.matmul(pf[4][0:65, ps], vtm[:, st, hp, 0:65], Pb[:, par, 0, ps],
                          start=first, stop=last, tile_position=(0, 0),
                          skip_group_check=True),
                 reads=[b_v, b_Pb[2 * par]], writes=[C.b_pf[4]], inc=True)
            P.op("tensor", lambda e, hp=hp, st=st, ps=ps, first=first, last=last, par=par:
                 e.matmul(pf[5][64:128, ps], vtm[:, st, hp, 65:129], Pb[:, par, 1, ps],
                          start=first, stop=last, tile_position=(0, 64),
                          skip_group_check=True),
                 reads=[b_v, b_Pb[2 * par + 1]], writes=[C.b_pf[5]], inc=True)
            P.op("tensor", lambda e, ps=ps, first=first, last=last, par=par:
                 e.matmul(pf[5][0:1, ps], C.onescb_sb[:], Pb[:, par, 1, ps],
                          start=first, stop=last, tile_position=(0, 0),
                          skip_group_check=True),
                 reads=[C.b_const, b_Pb[2 * par + 1]], writes=[C.b_pf[5]], inc=True)

        emit_S(0)
        if nst > 1:
            emit_S(1)
        for st in range(nst):
            emit_exp(st)
            if st + 2 < nst:
                emit_S(st + 2)
            emit_av(st)

        P.op("scalar", lambda e: e.activation(denb[64:65, :], pf[4][64:65, 0:512], AF.Ln),
             reads=[C.b_pf[4]], writes=[b_den])
        P.op("scalar", lambda e: e.activation(denb[64:65, :], denb[64:65, :], AF.Exp, scale=-1.0),
             reads=[b_den], writes=[b_den])
        P.op("scalar", lambda e: e.activation(denb[0:1, :], pf[5][0:1, 0:512], AF.Ln),
             reads=[C.b_pf[5]], writes=[b_den])
        P.op("scalar", lambda e: e.activation(denb[0:1, :], denb[0:1, :], AF.Exp, scale=-1.0),
             reads=[b_den], writes=[b_den])
        P.op("tensor", lambda e: e.matmul(pf[0][0:64, 0:512], C.onesf_sb[64:65, 0:64],
                                          denb[64:65, :], start=True, stop=True,
                                          tile_position=(64, 0), skip_group_check=True),
             reads=[b_den, C.b_const], writes=[bpf[0]])
        P.op("tensor", lambda e: e.matmul(pf[1][64:128, 0:512], C.onesf_sb[0:1, 0:64],
                                          denb[0:1, :], start=True, stop=True,
                                          tile_position=(0, 64), skip_group_check=True),
             reads=[b_den, C.b_const], writes=[bpf[1]])
        P.op("scalar", lambda e: e.activation(rb[0:64, :], pf[0][0:64, 0:512], AF.Identity),
             reads=[bpf[0]], writes=[b_rb])
        P.op("scalar", lambda e: e.activation(rb[64:128, :], pf[1][64:128, 0:512], AF.Identity),
             reads=[bpf[1]], writes=[b_rb])
        P.op("vector", lambda e, hp=hp: e.tensor_tensor(
            attnst[0:64, hp, :], pf[4][0:64, 0:512], rb[0:64, :], ALU.mult),
             reads=[C.b_pf[4], b_rb], writes=[b_att])
        P.op("vector", lambda e, hp=hp: e.tensor_tensor(
            attnst[64:128, hp, :], pf[5][64:128, 0:512], rb[64:128, :], ALU.mult),
             reads=[C.b_pf[5], b_rb], writes=[b_att])


def emit_l0attn(C, src, b_src, dst, b_dst):
    nc, P = C.nc, C.P
    with ExitStack() as bs:
        E = bs.enter_context
        wat = E(nc.sbuf_tensor("l0_watt", [128, 8, 3072], BF16))
        wpr = E(nc.sbuf_tensor("l0_wproj", [128, 8, 1024], BF16))
        qfm = E(nc.sbuf_tensor("l0_qfm", [128, 8, 2048], BF16))
        kfm = E(nc.sbuf_tensor("l0_kfm", [128, 8, 2048], BF16))
        vtm = E(nc.sbuf_tensor("l0_vtm", [128, 16, 8, 130], BF16))
        negc = E(nc.sbuf_tensor("l0_negc", [128, 16], F32))
        nrm = E(nc.sbuf_tensor("l0_nrm", [128, 32], F32))
        ncs = E(nc.sbuf_tensor("l0_ncs", [16, 8], F32))
        ncrow = E(nc.sbuf_tensor("l0_ncrow", [1, 16], F32))
        b_wat, b_wpr, b_q, b_k, b_v = Buf(), Buf(), Buf(), Buf(), Buf()
        b_negc, b_nrm, b_ncs, b_ncrow = Buf(), Buf(), Buf(), Buf()
        P.op("vector", lambda e: e.memset(vtm[:, :, :, 64:65], 1.0), writes=[b_v])
        P.op("gpsimd", lambda e: e.dma_start(
            out=wat[:], in_=C.dram["w_attnT"].rearrange("(c p) o -> p c o", p=128)),
            writes=[b_wat])
        P.op("gpsimd", lambda e: e.dma_start(
            out=wpr[:], in_=C.dram["w_projT"].rearrange("(c p) o -> p c o", p=128)),
            writes=[b_wpr])
        bat = C.bias_sb["b_attn_fm"]
        bpj = C.bias_sb["b_proj_fm"]

        for b in range(NB):
            with ExitStack() as sA:
                EA = sA.enter_context
                xt = EA(nc.sbuf_tensor(_un("l0a_xt"), [128, 2, 1024], F32))
                u0t = EA(nc.sbuf_tensor(_un("l0a_u0t"), [128, 1024], F32))
                ttm = EA(nc.sbuf_tensor(_un("l0a_ttm"), [128, 1024], BF16))
                tfm = EA(nc.sbuf_tensor(_un("l0a_tfm"), [128, 2, 8, 512], BF16))
                vst = EA(nc.sbuf_tensor(_un("l0a_vst"), [128, 8, 512], BF16))
                sq = EA(nc.sbuf_tensor(_un("l0a_sq"), [128, 1024], F32))
                stat = EA(nc.sbuf_tensor(_un("l0a_stat"), [128, 8], F32))
                b_xt = [Buf(), Buf()]
                b_u0t, b_ttm, b_tfm, b_vst, b_sq, b_stat = (Buf() for _ in range(6))
                P.op("vector", lambda e: e.memset(nrm[:], 0.0), writes=[b_nrm])

                def prep(j):
                    par = j % 2
                    for i in range(4):
                        tile = b * 16 + j * 4 + i
                        P.op("sync", lambda e, t=tile, xt=xt, par=par: e.dma_start(
                            out=xt[:, par, :], in_=src[t * 128:(t + 1) * 128, :]),
                            reads=[b_src[tile]], writes=[b_xt[par]])
                        ln_tile(C, xt[:, par, :], u0t[:], C.epsc[:, 0:1], b_xt[par],
                                b_u0t, sq[:], b_sq, stat[:], b_stat)
                        ln_tile(C, u0t[:], ttm[:], C.epsc[:, 1:2], b_u0t, b_ttm,
                                sq[:], b_sq, stat[:], b_stat)
                        for c in range(8):
                            transpose128(C, ttm[:, c * 128:(c + 1) * 128],
                                         tfm[:, par, c, i * 128:(i + 1) * 128],
                                         b_ttm, b_tfm, BF16)

                prep(0)
                ttv = ttm[:].rearrange("p (s n) -> p s n", s=2)
                for j in range(4):
                    par = j % 2
                    span = slice(j * 512, (j + 1) * 512)
                    for oc in range(24):
                        bank, bbk = C.pf[oc % 3], C.b_pf[oc % 3]
                        for kc in range(8):
                            P.op("tensor", lambda e, bank=bank, kc=kc, oc=oc, tfm=tfm, par=par:
                                 e.matmul(bank[:], wat[:, kc, oc * 128:(oc + 1) * 128],
                                          tfm[:, par, kc, :], start=(kc == 0), stop=(kc == 7)),
                                 reads=[b_wat, b_tfm], writes=[bbk], inc=(kc == 7))
                        if oc < 8:
                            dstap, bd = qfm[:, oc, span], b_q
                        elif oc < 16:
                            dstap, bd = kfm[:, oc - 8, span], b_k
                        else:
                            dstap, bd = vst[:, oc - 16, :], b_vst
                        P.op("scalar", lambda e, bank=bank, oc=oc, dstap=dstap:
                             e.activation(dstap, bank[:], AF.Identity,
                                          bias=bat[:, oc:oc + 1]),
                             reads=[bbk, C.b_const], writes=[bd])
                        if oc < 16:
                            slot = oc % 2
                            P.op("scalar", lambda e, d=dstap, slot=slot, ttm=ttm:
                                 e.activation(ttv[:, slot, :], d, AF.Square),
                                 reads=[bd], writes=[b_ttm])
                            for i in range(4):
                                c = oc % 8
                                P.op("tensor", lambda e, slot=slot, i=i, c=c, ttm=ttm:
                                     e.matmul(
                                         C.pf[3][:, i * 16 + 2 * c:i * 16 + 2 * c + 2],
                                         ttv[:, slot, i * 128:(i + 1) * 128], C.ehb_sb[:],
                                         start=True, stop=True, skip_group_check=True),
                                     reads=[b_ttm, C.b_const], writes=[C.b_pf[3]])
                        if oc == 7 or oc == 15:
                            off = 0 if oc == 7 else 16
                            for i in range(4):
                                P.op("vector", lambda e, off=off, i=i: e.tensor_tensor(
                                    nrm[:, off:off + 16], nrm[:, off:off + 16],
                                    C.pf[3][:, i * 16:(i + 1) * 16], ALU.max),
                                    reads=[b_nrm, C.b_pf[3]], writes=[b_nrm])
                    for c in range(8):
                        for i in range(4):
                            transpose128_v(C, vst[:, c, i * 128:(i + 1) * 128],
                                           vtm, j * 4 + i, c, b_vst, b_v)
                    if j < 3:
                        prep(j + 1)
                negc_calc(C, nrm, b_nrm, ncs, b_ncs, ncrow, b_ncrow, negc, b_negc)
            P.barrier()
            with ExitStack() as sB:
                EB = sB.enter_context
                Pb = EB(nc.sbuf_tensor(_un("l0b_pb"), [128, 2, 2, 512], BF16))
                attnst = EB(nc.sbuf_tensor(_un("l0b_attnst"), [128, 8, 512], BF16))
                denb = EB(nc.sbuf_tensor(_un("l0b_denb"), [128, 512], F32))
                rb = EB(nc.sbuf_tensor(_un("l0b_rb"), [128, 512], F32))
                projst = EB(nc.sbuf_tensor(_un("l0b_projst"), [128, 512], F32))
                u1s = EB(nc.sbuf_tensor(_un("l0b_u1s"), [128, 4, 1024], F32))
                xtB = EB(nc.sbuf_tensor(_un("l0b_xt"), [128, 1024], F32))
                sqB = EB(nc.sbuf_tensor(_un("l0b_sq"), [128, 1024], F32))
                statB = EB(nc.sbuf_tensor(_un("l0b_stat"), [128, 8], F32))
                b_Pb = [Buf() for _ in range(4)]
                b_att, b_den, b_rb = Buf(), Buf(), Buf()
                b_projst, b_u1s, b_xtB, b_sqB, b_statB = Buf(), Buf(), Buf(), Buf(), Buf()
                for qs in range(4):
                    attn_phaseB(C, qs, 4 * qs + 4,
                                lambda st, qs=qs: max(0, st - 4 * qs) * 128,
                                True, qfm, b_q, kfm, b_k, vtm, b_v,
                                negc, b_negc, attnst, b_att, Pb, b_Pb,
                                denb, b_den, rb, b_rb)
                    for i in range(4):
                        tile = b * 16 + qs * 4 + i
                        P.op("sync", lambda e, t=tile, i=i, xtB=xtB: e.dma_start(
                            out=xtB[:], in_=src[t * 128:(t + 1) * 128, :]),
                            reads=[b_src[tile]], writes=[b_xtB])
                        ln_tile(C, xtB[:], u1s[:, i, :], C.epsc[:, 0:1], b_xtB, b_u1s,
                                sqB[:], b_sqB, statB[:], b_statB)
                    for oc in range(8):
                        bank, bbk = C.pf[2 + (oc % 2)], C.b_pf[2 + (oc % 2)]
                        for kc in range(8):
                            P.op("tensor", lambda e, bank=bank, kc=kc, oc=oc, attnst=attnst:
                                 e.matmul(bank[:], wpr[:, kc, oc * 128:(oc + 1) * 128],
                                          attnst[:, kc, :], start=(kc == 0), stop=(kc == 7)),
                                 reads=[b_wpr, b_att], writes=[bbk], inc=(kc == 7))
                        P.op("scalar", lambda e, bank=bank, oc=oc, projst=projst:
                             e.activation(projst[:], bank[:], AF.Identity,
                                          bias=bpj[:, oc:oc + 1]),
                             reads=[bbk, C.b_const], writes=[b_projst])
                        for i in range(4):
                            tb, btb = C.pf[4 + (i % 2)], C.b_pf[4 + (i % 2)]
                            P.op("tensor", lambda e, tb=tb, i=i, projst=projst: e.transpose(
                                tb[0:128, 0:128], projst[:, i * 128:(i + 1) * 128],
                                C.idf_sb[:]),
                                reads=[b_projst, C.b_const], writes=[btb])
                            P.op("vector", lambda e, tb=tb, i=i, oc=oc, u1s=u1s: e.tensor_tensor(
                                u1s[:, i, oc * 128:(oc + 1) * 128], tb[0:128, 0:128],
                                u1s[:, i, oc * 128:(oc + 1) * 128], ALU.add),
                                reads=[btb, b_u1s], writes=[b_u1s])
                    for i in range(4):
                        tile = b * 16 + qs * 4 + i
                        P.op("sync", lambda e, t=tile, i=i, u1s=u1s: e.dma_start(
                            out=C.u1d[t * 128:(t + 1) * 128, :], in_=u1s[:, i, :]),
                            reads=[b_u1s], writes=[C.b_u1d[tile]])
            P.barrier()
    P.barrier()
    with ExitStack() as sC:
        EC = sC.enter_context
        wfc = EC(nc.sbuf_tensor("l0c_wfc", [128, 8, 4096], BF16))
        wml = EC(nc.sbuf_tensor("l0c_wml", [128, 32, 1024], BF16))
        u1sC = EC(nc.sbuf_tensor("l0c_u1", [128, 4, 1024], F32))
        ut = EC(nc.sbuf_tensor("l0c_ut", [128, 1024], F32))
        ttmC = EC(nc.sbuf_tensor("l0c_ttm", [128, 1024], BF16))
        tfmC = EC(nc.sbuf_tensor("l0c_tfm", [128, 2, 8, 512], BF16))
        hfmC = EC(nc.sbuf_tensor("l0c_hfm", [128, 32, 512], BF16))
        sqC = EC(nc.sbuf_tensor("l0c_sq", [128, 1024], F32))
        statC = EC(nc.sbuf_tensor("l0c_stat", [128, 8], F32))
        mlpstC = ttmC[:].bitcast(F32)
        b_wfc, b_wml, b_u1, b_ut, b_ttm2, b_tfm2, b_hfm = (Buf() for _ in range(7))
        b_mlpst = b_ttm2
        b_zt, b_xt2, b_xo2, b_sq2, b_stat2 = (Buf() for _ in range(5))
        P.op("gpsimd", lambda e: e.dma_start(
            out=wfc[:], in_=C.dram["w_fcT"].rearrange("(c p) o -> p c o", p=128)),
            writes=[b_wfc])
        P.op("gpsimd", lambda e: e.dma_start(
            out=wml[:], in_=C.dram["w_mlpT"].rearrange("(c p) o -> p c o", p=128)),
            writes=[b_wml])
        bfc = C.bias_sb["b_fc_fm"]
        bml = C.bias_sb["b_mlp_fm"]

        def prepC(s):
            par = s % 2
            for i in range(4):
                tile = s * 4 + i
                P.op("sync", lambda e, t=tile, ut=ut: e.dma_start(
                    out=ut[:], in_=C.u1d[t * 128:(t + 1) * 128, :]),
                    reads=[C.b_u1d[tile]], writes=[b_ut])
                ln_tile(C, ut[:], ttmC[:], C.epsc[:, 1:2], b_ut, b_ttm2,
                        sqC[:], b_sq2, statC[:], b_stat2)
                for c in range(8):
                    transpose128(C, ttmC[:, c * 128:(c + 1) * 128],
                                 tfmC[:, par, c, i * 128:(i + 1) * 128],
                                 b_ttm2, b_tfm2, BF16)

        prepC(0)
        for s in range(NSUP):
            par = s % 2
            for oc in range(32):
                bank, bbk = C.pf[oc % 3], C.b_pf[oc % 3]
                for kc in range(8):
                    P.op("tensor", lambda e, bank=bank, kc=kc, oc=oc, par=par:
                         e.matmul(bank[:], wfc[:, kc, oc * 128:(oc + 1) * 128],
                                  tfmC[:, par, kc, :], start=(kc == 0), stop=(kc == 7)),
                         reads=[b_wfc, b_tfm2], writes=[bbk], inc=(kc == 7))
                P.op("scalar", lambda e, bank=bank, oc=oc:
                     e.activation(hfmC[:, oc, :], bank[:], AF.Gelu_apprx_tanh,
                                  bias=bfc[:, oc:oc + 1]),
                     reads=[bbk, C.b_const], writes=[b_hfm])
            if s < NSUP - 1:
                prepC(s + 1)
            for i in range(4):
                tile = s * 4 + i
                P.op("sync", lambda e, t=tile, i=i: e.dma_start(
                    out=u1sC[:, i, :], in_=C.u1d[t * 128:(t + 1) * 128, :]),
                    reads=[C.b_u1d[tile]], writes=[b_u1])
            for oc in range(8):
                bank, bbk = C.pf[3 + (oc % 2)], C.b_pf[3 + (oc % 2)]
                for fc in range(32):
                    P.op("tensor", lambda e, bank=bank, fc=fc, oc=oc:
                         e.matmul(bank[:], wml[:, fc, oc * 128:(oc + 1) * 128],
                                  hfmC[:, fc, :], start=(fc == 0), stop=(fc == 31)),
                         reads=[b_wml, b_hfm], writes=[bbk], inc=(fc == 31))
                P.op("scalar", lambda e, bank=bank, oc=oc:
                     e.activation(mlpstC, bank[:], AF.Identity,
                                  bias=bml[:, oc:oc + 1]),
                     reads=[bbk, C.b_const], writes=[b_mlpst])
                for i in range(4):
                    tb, btb = C.pf[5], C.b_pf[5]
                    P.op("tensor", lambda e, tb=tb, i=i: e.transpose(
                        tb[0:128, 0:128], mlpstC[:, i * 128:(i + 1) * 128], C.idf_sb[:]),
                        reads=[b_mlpst, C.b_const], writes=[btb])
                    P.op("vector", lambda e, tb=tb, i=i, oc=oc: e.tensor_tensor(
                        u1sC[:, i, oc * 128:(oc + 1) * 128], tb[0:128, 0:128],
                        u1sC[:, i, oc * 128:(oc + 1) * 128], ALU.add),
                        reads=[btb, b_u1], writes=[b_u1])
            for i in range(4):
                tile = s * 4 + i
                ln_tile(C, u1sC[:, i, :], ut[:], C.epsc[:, 0:1], b_u1, b_ut,
                        sqC[:], b_sq2, statC[:], b_stat2)
                P.op("sync", lambda e, t=tile: e.dma_start(
                    out=sqC[:], in_=src[t * 128:(t + 1) * 128, :]),
                    reads=[b_src[tile]], writes=[b_sq2])
                P.op("vector", lambda e: e.tensor_tensor(ut[:], ut[:], sqC[:], ALU.add),
                     reads=[b_ut, b_sq2], writes=[b_ut])
                P.op("sync", lambda e, t=tile: e.dma_start(
                    out=dst[t * 128:(t + 1) * 128, :], in_=ut[:]),
                    reads=[b_ut], writes=[b_dst[tile]])
    P.barrier()


def emit_l0ffn(C, src, b_src, dst, b_dst):
    nc, P = C.nc, C.P
    with ExitStack() as bs:
        E = bs.enter_context
        wg = E(nc.sbuf_tensor("f_wg", [128, 8, 8], BF16))
        b2m = E(nc.sbuf_tensor("f_b2m", [8, 1024], BF16))
        gam = E(nc.sbuf_tensor("f_gam", [128, 1024], F32))
        bet = E(nc.sbuf_tensor("f_bet", [128, 1024], F32))
        b_wg, b_b2m, b_gam, b_bet, b_gbrow = (Buf() for _ in range(5))
        P.op("gpsimd", lambda e: e.dma_start(
            out=wg[:], in_=C.dram["wg"].rearrange("(c p) o -> p c o", p=128)),
            writes=[b_wg])
        P.op("gpsimd", lambda e: e.dma_start(out=b2m[:], in_=C.dram["b2_mat"].ap()),
             writes=[b_b2m])
        with nc.sbuf_tensor("f_gbrow", [1, 2, 1024], F32) as gbrow:
            P.op("sync", lambda e: e.dma_start(out=gbrow[0:1, 0, :], in_=C.dram["gamma"].ap()),
                 writes=[b_gbrow])
            P.op("sync", lambda e: e.dma_start(out=gbrow[0:1, 1, :], in_=C.dram["beta"].ap()),
                 writes=[b_gbrow])
            for (gi, tgt, bt) in ((0, gam, b_gam), (1, bet, b_bet)):
                for half in range(2):
                    P.op("tensor", lambda e, gi=gi, half=half: e.matmul(
                        C.pf[4][:, 0:512], C.onesf_sb[0:1, 0:128],
                        gbrow[0:1, gi, half * 512:(half + 1) * 512],
                        start=True, stop=True, skip_group_check=True),
                        reads=[b_gbrow, C.b_const], writes=[C.b_pf[4]])
                    P.op("vector", lambda e, tgt=tgt, half=half: e.tensor_copy(
                        tgt[:, half * 512:(half + 1) * 512], C.pf[4][:, 0:512]),
                        reads=[C.b_pf[4]], writes=[bt])
        P.barrier()

        with ExitStack() as sP:
            EP = sP.enter_context
            xt = EP(nc.sbuf_tensor("f_xt", [128, 1024], F32))
            ttm = EP(nc.sbuf_tensor("f_ttm", [128, 1024], BF16))
            u3st = EP(nc.sbuf_tensor("f_u3st", [128, 8, 512], BF16))
            lgst = EP(nc.sbuf_tensor("f_lgst", [8, 512], F32))
            lgtm = EP(nc.sbuf_tensor("f_lgtm", [128, 4, 8], F32))
            g1 = EP(nc.sbuf_tensor("f_g1", [128, 4, 8], F32))
            g2 = EP(nc.sbuf_tensor("f_g2", [128, 4, 8], F32))
            g3 = EP(nc.sbuf_tensor("f_g3", [128, 4, 8], F32))
            g4 = EP(nc.sbuf_tensor("f_g4", [128, 4, 8], F32))
            gsc = EP(nc.sbuf_tensor("f_gsc", [128, 20], F32))
            grst = EP(nc.sbuf_tensor("f_grst", [8, 4, 128], F32))
            sq = EP(nc.sbuf_tensor("f_sq", [128, 1024], F32))
            stat = EP(nc.sbuf_tensor("f_stat", [128, 8], F32))
            b_xt, b_ttm, b_u3st, b_lgst, b_lgtm = (Buf() for _ in range(5))
            b_g1, b_g2, b_g3, b_g4, b_gsc, b_grst, b_sq, b_stat = (Buf() for _ in range(8))
            for s in range(NSUP):
                for i in range(4):
                    tile = s * 4 + i
                    P.op("sync", lambda e, t=tile: e.dma_start(
                        out=xt[:], in_=src[t * 128:(t + 1) * 128, :]),
                        reads=[b_src[tile]], writes=[b_xt])
                    ln_tile(C, xt[:], ttm[:], C.epsc[:, 0:1], b_xt, b_ttm,
                            sq[:], b_sq, stat[:], b_stat)
                    for c in range(8):
                        transpose128(C, ttm[:, c * 128:(c + 1) * 128],
                                     u3st[:, c, i * 128:(i + 1) * 128],
                                     b_ttm, b_u3st, BF16)
                P.op("sync", lambda e, s=s: e.dma_start(
                    out=C.u3fmd[s].rearrange("p (c n) -> p c n", c=8), in_=u3st[:]),
                    reads=[b_u3st], writes=[C.b_u3fmd[s]])
                for c in range(8):
                    P.op("tensor", lambda e, c=c: e.matmul(
                        C.pf[3][0:8, 0:512], wg[:, c, :], u3st[:, c, :],
                        start=(c == 0), stop=(c == 7), skip_group_check=True),
                        reads=[b_wg, b_u3st], writes=[C.b_pf[3]], inc=(c == 7))
                P.op("scalar", lambda e: e.activation(
                    lgst[:], C.pf[3][0:8, 0:512], AF.Identity,
                    bias=C.bias_sb["bg_col"][:]),
                    reads=[C.b_pf[3], C.b_const], writes=[b_lgst])
                for i in range(4):
                    P.op("tensor", lambda e, i=i: e.transpose(
                        C.pf[2][0:128, i * 8:(i + 1) * 8],
                        lgst[0:8, i * 128:(i + 1) * 128], C.idf_sb[0:8, 0:8]),
                        reads=[b_lgst, C.b_const], writes=[C.b_pf[2]])
                    P.op("vector", lambda e, i=i: e.tensor_copy(
                        lgtm[:, i, :], C.pf[2][0:128, i * 8:(i + 1) * 8]),
                        reads=[C.b_pf[2]], writes=[b_lgtm])
                m1 = gsc[:, 0:4]
                m2 = gsc[:, 4:8]
                dd = gsc[:, 8:12]
                w1 = gsc[:, 12:16]
                w2 = gsc[:, 16:20]
                m1b = m1.rearrange("p (a o) -> p a o", a=4).broadcast_to([128, 4, 8])
                m2b = m2.rearrange("p (a o) -> p a o", a=4).broadcast_to([128, 4, 8])
                w1b = w1.rearrange("p (a o) -> p a o", a=4).broadcast_to([128, 4, 8])
                w2b = w2.rearrange("p (a o) -> p a o", a=4).broadcast_to([128, 4, 8])
                P.op("vector", lambda e: e.reduce_max(m1, lgtm[:], AX.X),
                     reads=[b_lgtm], writes=[b_gsc])
                P.op("vector", lambda e: e.tensor_tensor(g1[:], lgtm[:], m1b, ALU.is_equal),
                     reads=[b_lgtm, b_gsc], writes=[b_g1])
                P.op("vector", lambda e: e.scalar_tensor_tensor(
                    g2[:], g1[:], -1e30, lgtm[:], ALU.mult, ALU.add),
                    reads=[b_g1, b_lgtm], writes=[b_g2])
                P.op("vector", lambda e: e.reduce_max(m2, g2[:], AX.X),
                     reads=[b_g2], writes=[b_gsc])
                P.op("vector", lambda e: e.tensor_tensor(dd, m1, m2, ALU.subtract),
                     reads=[b_gsc], writes=[b_gsc])
                P.op("scalar", lambda e: e.activation(w1, dd, AF.Sigmoid),
                     reads=[b_gsc], writes=[b_gsc])
                P.op("scalar", lambda e: e.activation(w2, dd, AF.Sigmoid, scale=-1.0),
                     reads=[b_gsc], writes=[b_gsc])
                P.op("vector", lambda e: e.tensor_tensor(g3[:], g2[:], m2b, ALU.is_equal),
                     reads=[b_g2, b_gsc], writes=[b_g3])
                P.op("vector", lambda e: e.tensor_tensor(g4[:], g1[:], w1b, ALU.mult),
                     reads=[b_g1, b_gsc], writes=[b_g4])
                P.op("vector", lambda e: e.tensor_tensor(g1[:], g3[:], w2b, ALU.mult),
                     reads=[b_g3, b_gsc], writes=[b_g1])
                P.op("vector", lambda e: e.tensor_tensor(g4[:], g4[:], g1[:], ALU.add),
                     reads=[b_g4, b_g1], writes=[b_g4])
                for i in range(4):
                    P.op("tensor", lambda e, i=i: e.transpose(
                        C.pf[1][0:8, i * 128:(i + 1) * 128],
                        g4[:, i, :], C.idf_sb[:]),
                        reads=[b_g4, C.b_const], writes=[C.b_pf[1]])
                    P.op("vector", lambda e, i=i: e.tensor_copy(
                        grst[0:8, i, :], C.pf[1][0:8, i * 128:(i + 1) * 128]),
                        reads=[C.b_pf[1]], writes=[b_grst])
                P.op("sync", lambda e, s=s: e.dma_start(
                    out=C.gd[:, s * 512:(s + 1) * 512],
                    in_=grst[:].rearrange("p a n -> p (a n)")),
                    reads=[b_grst], writes=[C.b_gd[s]])
        P.barrier()

        with ExitStack() as sM:
            EM = sM.enter_context
            w1bufs = [EM(nc.sbuf_tensor(f"f_w1{i}", [128, 8, 2048], BF16)) for i in range(2)]
            w2sb = EM(nc.sbuf_tensor("f_w2", [128, 16, 1024], BF16))
            u3t = EM(nc.sbuf_tensor("f_u3t", [128, 2, 2, 8, 512], BF16))
            hb = EM(nc.sbuf_tensor("f_h", [128, 16, 512], BF16))
            gbc = EM(nc.sbuf_tensor("f_gbc", [128, 512], BF16))
            g1r = EM(nc.sbuf_tensor("f_g1r", [1, 512], F32))
            g8 = EM(nc.sbuf_tensor("f_g8", [8, 2, 512], BF16))
            mixed = EM(nc.sbuf_tensor("f_mixed", [128, 2, 8, 512], F32))
            xt2 = EM(nc.sbuf_tensor("f_xt2", [128, 1024], F32))
            u3r = EM(nc.sbuf_tensor("f_u3r", [128, 1024], F32))
            zt = EM(nc.sbuf_tensor("f_zt", [128, 1024], F32))
            sq2 = zt
            stat2 = EM(nc.sbuf_tensor("f_stat2", [128, 8], F32))
            b_w1 = [Buf(), Buf()]
            b_w2, b_u3t, b_h, b_gbc, b_g1r, b_g8, b_mixed = (Buf() for _ in range(7))
            b_xt2, b_u3r, b_zt, b_stat2 = (Buf() for _ in range(4))
            b_sq2 = b_zt
            b1f = C.bias_sb["b1_fm"]
            for H in range(8):
                sts = [2 * H, 2 * H + 1]
                hpar = H % 2
                for si, s in enumerate(sts):
                    P.op("sync", lambda e, s=s, si=si, hpar=hpar: e.dma_start(
                        out=u3t[:, hpar, si, :, :],
                        in_=C.u3fmd[s].rearrange("p (c n) -> p c n", c=8)),
                        reads=[C.b_u3fmd[s]], writes=[b_u3t])
                    P.op("gpsimd", lambda e, s=s, si=si: e.dma_start(
                        out=g8[:, si, :], in_=C.gd[:, s * 512:(s + 1) * 512]),
                        reads=[C.b_gd[s]], writes=[b_g8])
                for e_ in range(8):
                    wb, bwb = w1bufs[e_ % 2], b_w1[e_ % 2]
                    P.op("gpsimd", lambda e, e_=e_, wb=wb: e.dma_start(
                        out=wb[:], in_=C.dram["moe_w1"][e_].rearrange(
                            "(c p) f -> p c f", p=128)),
                        writes=[bwb])
                    P.op("gpsimd", lambda e, e_=e_: e.dma_start(
                        out=w2sb[:], in_=C.dram["moe_w2"][e_].rearrange(
                            "(c p) f -> p c f", p=128)),
                        writes=[b_w2])
                    for si, s in enumerate(sts):
                        P.op("sync", lambda e, s=s, e_=e_: e.dma_start(
                            out=g1r[:], in_=C.gd[e_:e_ + 1, s * 512:(s + 1) * 512]),
                            reads=[C.b_gd[s]], writes=[b_g1r])
                        P.op("tensor", lambda e: e.matmul(
                            C.pf[4][:, 0:512], C.onesf_sb[0:1, 0:128], g1r[:],
                            start=True, stop=True, skip_group_check=True),
                            reads=[b_g1r, C.b_const], writes=[C.b_pf[4]])
                        P.op("vector", lambda e: e.tensor_copy(gbc[:], C.pf[4][:, 0:512]),
                             reads=[C.b_pf[4]], writes=[b_gbc])
                        for foc in range(16):
                            bank, bbk = C.pf[foc % 2], C.b_pf[foc % 2]
                            for kc in range(8):
                                P.op("tensor", lambda e, bank=bank, kc=kc, foc=foc, si=si, wb=wb, hpar=hpar:
                                     e.matmul(bank[:], wb[:, kc, foc * 128:(foc + 1) * 128],
                                              u3t[:, hpar, si, kc, :], start=(kc == 0),
                                              stop=(kc == 7)),
                                     reads=[bwb, b_u3t], writes=[bbk], inc=(kc == 7))
                            P.op("scalar", lambda e, bank=bank, foc=foc, e_=e_:
                                 e.activation(hb[:, foc, :], bank[:], AF.Relu,
                                              bias=b1f[:, e_, foc:foc + 1]),
                                 reads=[bbk, C.b_const], writes=[b_h])
                            P.op("vector", lambda e, foc=foc:
                                 e.tensor_tensor(hb[:, foc, :], hb[:, foc, :],
                                                 gbc[:], ALU.mult),
                                 reads=[b_h, b_gbc], writes=[b_h])
                        for dc in range(8):
                            bank, bbk = C.pf[2 + dc % 2], C.b_pf[2 + dc % 2]
                            for fc in range(16):
                                P.op("tensor", lambda e, bank=bank, fc=fc, dc=dc, e_=e_:
                                     e.matmul(bank[:], w2sb[:, fc, dc * 128:(dc + 1) * 128],
                                              hb[:, fc, :], start=(fc == 0),
                                              stop=(fc == 15 and e_ != 0)),
                                     reads=[b_w2, b_h], writes=[bbk],
                                     inc=(fc == 15 and e_ != 0))
                            if e_ == 0:
                                P.op("tensor", lambda e, bank=bank, dc=dc, si=si:
                                     e.matmul(bank[:], b2m[0:8, dc * 128:(dc + 1) * 128],
                                              g8[:, si, :], start=False, stop=True,
                                              skip_group_check=True),
                                     reads=[b_b2m, b_g8], writes=[bbk], inc=True)
                            if e_ == 0:
                                P.op("vector", lambda e, bank=bank, dc=dc, si=si:
                                     e.tensor_copy(mixed[:, si, dc, :], bank[:]),
                                     reads=[bbk], writes=[b_mixed])
                            else:
                                P.op("vector", lambda e, bank=bank, dc=dc, si=si:
                                     e.tensor_tensor(mixed[:, si, dc, :], bank[:],
                                                     mixed[:, si, dc, :], ALU.add),
                                     reads=[bbk, b_mixed], writes=[b_mixed])
                for si, s in enumerate(sts):
                    for i in range(4):
                        tile = s * 4 + i
                        P.op("sync", lambda e, t=tile: e.dma_start(
                            out=xt2[:], in_=src[t * 128:(t + 1) * 128, :]),
                            reads=[b_src[tile]], writes=[b_xt2])
                        ln_tile(C, xt2[:], u3r[:], C.epsc[:, 0:1], b_xt2, b_u3r,
                                sq2[:], b_sq2, stat2[:], b_stat2)
                        for dc in range(8):
                            tb, btb = C.pf[4 + (dc % 2)], C.b_pf[4 + (dc % 2)]
                            P.op("tensor", lambda e, tb=tb, si=si, dc=dc, i=i: e.transpose(
                                tb[0:128, 0:128], mixed[:, si, dc, i * 128:(i + 1) * 128],
                                C.idf_sb[:]),
                                reads=[b_mixed, C.b_const], writes=[btb])
                            P.op("vector", lambda e, tb=tb, dc=dc: e.tensor_tensor(
                                u3r[:, dc * 128:(dc + 1) * 128], tb[0:128, 0:128],
                                u3r[:, dc * 128:(dc + 1) * 128], ALU.add),
                                reads=[btb, b_u3r], writes=[b_u3r])
                        rstd, nmr = ln_tile(C, u3r[:], zt[:], C.epsc[:, 0:1], b_u3r, b_zt,
                                            sq2[:], b_sq2, stat2[:], b_stat2)
                        P.op("vector", lambda e: e.tensor_tensor(zt[:], zt[:], gam[:], ALU.mult),
                             reads=[b_zt, b_gam], writes=[b_zt])
                        P.op("vector", lambda e: e.tensor_tensor(zt[:], zt[:], bet[:], ALU.add),
                             reads=[b_zt, b_bet], writes=[b_zt])
                        ln_tile(C, zt[:], u3r[:], C.epsc[:, 0:1], b_zt, b_u3r,
                                sq2[:], b_sq2, stat2[:], b_stat2)
                        P.op("vector", lambda e: e.tensor_tensor(u3r[:], u3r[:], xt2[:], ALU.add),
                             reads=[b_u3r, b_xt2], writes=[b_u3r])
                        P.op("sync", lambda e, t=tile: e.dma_start(
                            out=dst[t * 128:(t + 1) * 128, :], in_=u3r[:]),
                            reads=[b_u3r], writes=[b_dst[tile]])
    P.barrier()


def transpose128_v(C, src_ap, vtm, st_idx, c, r_src, w_dst):
    P = C.P
    bank, bbuf = C.next_pbf()
    pv = bank[:].bitcast(BF16)[0:128, 0:128]
    P.op("tensor", lambda e, o=pv, i=src_ap: e.transpose(o, i, C.idb_sb[:]),
         reads=[r_src, C.b_const], writes=[bbuf])
    P.op("vector", lambda e, pv=pv, st_idx=st_idx, c=c: e.tensor_copy(
        vtm[:, st_idx, c, 0:64], pv[:, 0:64]),
        reads=[bbuf], writes=[w_dst])
    P.op("vector", lambda e, pv=pv, st_idx=st_idx, c=c: e.tensor_copy(
        vtm[:, st_idx, c, 65:129], pv[:, 64:128]),
        reads=[bbuf], writes=[w_dst])


def tap(C, name, ap, buf):
    if not getattr(C, "want_taps", None) or name not in C.want_taps:
        return
    nc, P = C.nc, C.P
    sh = list(ap.shape)
    d = nc.dram_tensor("tap_" + name, sh, ap.dtype, kind="ExternalOutput")
    P.op("sync", lambda e: e.dma_start(out=d.ap(), in_=ap), reads=[buf])


def negc_calc(C, nrm, b_nrm, ncs, b_ncs, ncrow, b_ncrow, negc, b_negc):
    P = C.P
    P.op("tensor", lambda e: e.transpose(C.pf[2][0:16, 0:128], nrm[:, 0:16], C.idf_sb[:]),
         reads=[b_nrm, C.b_const], writes=[C.b_pf[2]])
    P.op("vector", lambda e: e.reduce_max(ncs[:, 0:1], C.pf[2][0:16, 0:128], AX.X),
         reads=[C.b_pf[2]], writes=[b_ncs])
    P.op("tensor", lambda e: e.transpose(C.pf[2][0:16, 128:256], nrm[:, 16:32], C.idf_sb[:]),
         reads=[b_nrm, C.b_const], writes=[C.b_pf[2]])
    P.op("vector", lambda e: e.reduce_max(ncs[:, 1:2], C.pf[2][0:16, 128:256], AX.X),
         reads=[C.b_pf[2]], writes=[b_ncs])
    P.op("scalar", lambda e: e.activation(ncs[:, 2:3], ncs[:, 0:1], AF.Sqrt),
         reads=[b_ncs], writes=[b_ncs])
    P.op("scalar", lambda e: e.activation(ncs[:, 3:4], ncs[:, 1:2], AF.Sqrt),
         reads=[b_ncs], writes=[b_ncs])
    P.op("vector", lambda e: e.tensor_tensor(ncs[:, 4:5], ncs[:, 2:3], ncs[:, 3:4], ALU.mult),
         reads=[b_ncs], writes=[b_ncs])
    P.op("vector", lambda e: e.tensor_single_scalar(ncs[:, 5:6], ncs[:, 4:5], -0.125, ALU.mult),
         reads=[b_ncs], writes=[b_ncs])
    P.op("tensor", lambda e: e.transpose(C.pf[2][0:1, 256:272], ncs[:, 5:6],
                                         C.idf_sb[0:16, 0:16]),
         reads=[b_ncs, C.b_const], writes=[C.b_pf[2]])
    P.op("vector", lambda e: e.tensor_copy(ncrow[:], C.pf[2][0:1, 256:272]),
         reads=[C.b_pf[2]], writes=[b_ncrow])
    P.op("tensor", lambda e: e.matmul(C.pf[2][:, 272:288], C.onesf_sb[0:1, 0:128],
                                      ncrow[:], start=True, stop=True,
                                      skip_group_check=True),
         reads=[b_ncrow, C.b_const], writes=[C.b_pf[2]])
    P.op("vector", lambda e: e.tensor_copy(negc[:], C.pf[2][:, 272:288]),
         reads=[C.b_pf[2]], writes=[b_negc])


def emit_l1attn(C, src, b_src, dst, b_dst):
    nc, P = C.nc, C.P
    with ExitStack() as bs:
        E = bs.enter_context
        wq = E(nc.sbuf_tensor("l1_wq", [128, 8, 1024], BF16))
        wdn = E(nc.sbuf_tensor("l1_wdn", [128, 8, 512], BF16))
        wku = E(nc.sbuf_tensor("l1_wku", [128, 4, 1024], BF16))
        wvu = E(nc.sbuf_tensor("l1_wvu", [128, 4, 1024], BF16))
        qfm = E(nc.sbuf_tensor("l1_qfm", [128, 8, 2048], BF16))
        kfm = E(nc.sbuf_tensor("l1_kfm", [128, 8, 2048], BF16))
        vtm = E(nc.sbuf_tensor("l1_vtm", [128, 16, 8, 130], BF16))
        negc = E(nc.sbuf_tensor("l1_negc", [128, 16], F32))
        nrm = E(nc.sbuf_tensor("l1_nrm", [128, 32], F32))
        ncs = E(nc.sbuf_tensor("l1_ncs", [16, 8], F32))
        ncrow = E(nc.sbuf_tensor("l1_ncrow", [1, 16], F32))
        cos_sb = E(nc.sbuf_tensor("l1_cos", [128, 16, 32], F32))
        sin_sb = E(nc.sbuf_tensor("l1_sin", [128, 16, 32], F32))
        b_wq, b_wdn, b_wku, b_wvu = (Buf() for _ in range(4))
        b_q, b_k, b_v, b_negc, b_nrm, b_ncs, b_ncrow = (Buf() for _ in range(7))
        P.op("vector", lambda e: e.memset(vtm[:, :, :, 64:65], 1.0), writes=[b_v])
        for (sb, nm, bw) in [(wq, "wqP", b_wq), (wdn, "w_down", b_wdn),
                             (wku, "wk_upP", b_wku), (wvu, "wv_up", b_wvu)]:
            P.op("gpsimd", lambda e, sb=sb, nm=nm: e.dma_start(
                out=sb[:], in_=C.dram[nm].rearrange("(c p) o -> p c o", p=128)),
                writes=[bw])
        P.op("sync", lambda e: e.dma_start(
            out=cos_sb[:], in_=C.dram["ropecos"].rearrange("(t p) d -> p t d", p=128)),
            writes=[C.b_const])
        P.op("sync", lambda e: e.dma_start(
            out=sin_sb[:], in_=C.dram["ropesin"].rearrange("(t p) d -> p t d", p=128)),
            writes=[C.b_const])

        for b in range(NB):
            with ExitStack() as sA:
                EA = sA.enter_context
                xt = EA(nc.sbuf_tensor(_un("l1a_xt"), [128, 1024], F32))
                ttm = EA(nc.sbuf_tensor(_un("l1a_ttm"), [128, 1024], BF16))
                tfm = EA(nc.sbuf_tensor(_un("l1a_tfm"), [128, 8, 512], BF16))
                latfm = EA(nc.sbuf_tensor(_un("l1a_lat"), [128, 4, 512], BF16))
                qst = EA(nc.sbuf_tensor(_un("l1a_qst"), [128, 512], F32))
                qtm = EA(nc.sbuf_tensor(_un("l1a_qtm"), [128, 4, 1024], F32))
                rout = EA(nc.sbuf_tensor(_un("l1a_rout"), [128, 1024], BF16))
                rscr = EA(nc.sbuf_tensor(_un("l1a_rscr"), [128, 1024], F32))
                vst = EA(nc.sbuf_tensor(_un("l1a_vst"), [128, 8, 512], BF16))
                sq = EA(nc.sbuf_tensor(_un("l1a_sq"), [128, 1024], F32))
                nrmt = EA(nc.sbuf_tensor(_un("l1a_nrmt"), [128, 16], F32))
                stat = EA(nc.sbuf_tensor(_un("l1a_stat"), [128, 8], F32))
                b_xt, b_ttm, b_tfm, b_lat, b_qst = (Buf() for _ in range(5))
                b_qtm, b_rout, b_rscr, b_vst, b_sq, b_nrmt, b_stat = (Buf() for _ in range(7))
                P.op("vector", lambda e: e.memset(nrm[:], 0.0), writes=[b_nrm])

                def fm_grid(wsb, b_w, nkc, noc, rhs, b_rhs, evac):
                    for oc in range(noc):
                        bank, bbk = C.pf[oc % 3], C.b_pf[oc % 3]
                        for kc in range(nkc):
                            P.op("tensor", lambda e, bank=bank, kc=kc, oc=oc:
                                 e.matmul(bank[:], wsb[:, kc, oc * 128:(oc + 1) * 128],
                                          rhs[:, kc, :], start=(kc == 0),
                                          stop=(kc == nkc - 1)),
                                 reads=[b_w, b_rhs], writes=[bbk], inc=(kc == nkc - 1))
                        evac(oc, bank, bbk)

                def tm_rope_path(j, srcname, b_srcw, off):
                    pass

                for j in range(4):
                    span = slice(j * 512, (j + 1) * 512)
                    for i in range(4):
                        tile = b * 16 + j * 4 + i
                        P.op("sync", lambda e, t=tile, xt=xt: e.dma_start(
                            out=xt[:], in_=src[t * 128:(t + 1) * 128, :]),
                            reads=[b_src[tile]], writes=[b_xt])
                        ln_tile(C, xt[:], ttm[:], C.epsc[:, 0:1], b_xt, b_ttm,
                                sq[:], b_sq, stat[:], b_stat)
                        for c in range(8):
                            transpose128(C, ttm[:, c * 128:(c + 1) * 128],
                                         tfm[:, c, i * 128:(i + 1) * 128],
                                         b_ttm, b_tfm, BF16)

                    def evac_lat(oc, bank, bbk):
                        P.op("scalar", lambda e, bank=bank, oc=oc, latfm=latfm: e.activation(
                            latfm[:, oc, :], bank[:], AF.Identity),
                            reads=[bbk], writes=[b_lat])
                    fm_grid(wdn, b_wdn, 8, 4, tfm, b_tfm, evac_lat)

                    def evac_v(oc, bank, bbk):
                        P.op("scalar", lambda e, bank=bank, oc=oc, vst=vst: e.activation(
                            vst[:, oc, :], bank[:], AF.Identity),
                            reads=[bbk], writes=[b_vst])
                    fm_grid(wvu, b_wvu, 4, 8, latfm, b_lat, evac_v)
                    for c in range(8):
                        for i in range(4):
                            transpose128_v(C, vst[:, c, i * 128:(i + 1) * 128],
                                           vtm, j * 4 + i, c, b_vst, b_v)

                    for (wsb, b_w, nkc, rhsbuf, b_rhs, dstfm, b_dst2, off) in (
                            (wq, b_wq, 8, tfm, b_tfm, qfm, b_q, 0),
                            (wku, b_wku, 4, latfm, b_lat, kfm, b_k, 16)):
                        def evac_q(oc, bank, bbk):
                            P.op("scalar", lambda e, bank=bank, qst=qst: e.activation(
                                qst[:], bank[:], AF.Identity),
                                reads=[bbk], writes=[b_qst])
                            for i in range(4):
                                tb, btb = C.pf[3 + (i % 2)], C.b_pf[3 + (i % 2)]
                                P.op("tensor", lambda e, tb=tb, i=i, qst=qst: e.transpose(
                                    tb[0:128, 0:128], qst[:, i * 128:(i + 1) * 128],
                                    C.idf_sb[:]),
                                    reads=[b_qst, C.b_const], writes=[btb])
                                P.op("vector", lambda e, tb=tb, i=i, oc=oc, qtm=qtm:
                                     e.tensor_copy(qtm[:, i, oc * 128:(oc + 1) * 128],
                                                   tb[0:128, 0:128]),
                                     reads=[btb], writes=[b_qtm])
                        fm_grid(wsb, b_w, nkc, 8, rhsbuf, b_rhs, evac_q)
                        for i in range(4):
                            q3 = qtm[:, i, :].rearrange("p (h two d) -> p h two d",
                                                        h=16, two=2)
                            x1 = q3[:, :, 0, :]
                            x2 = q3[:, :, 1, :]
                            ti = j * 4 + i
                            cb = cos_sb[:, ti, :].rearrange("p (o d) -> p o d", o=1
                                                            ).broadcast_to([128, 16, 32])
                            sb_ = sin_sb[:, ti, :].rearrange("p (o d) -> p o d", o=1
                                                             ).broadcast_to([128, 16, 32])
                            r1 = rscr[:, 0:512].rearrange("p (h d) -> p h d", h=16)
                            r2 = rscr[:, 512:1024].rearrange("p (h d) -> p h d", h=16)
                            ro = rout[:].rearrange("p (h two d) -> p h two d", h=16, two=2)
                            P.op("vector", lambda e, x1=x1, cb=cb, r1=r1:
                                 e.tensor_tensor(r1, x1, cb, ALU.mult),
                                 reads=[b_qtm, C.b_const], writes=[b_rscr])
                            P.op("vector", lambda e, x2=x2, sbx=sb_, r2=r2:
                                 e.tensor_tensor(r2, x2, sbx, ALU.mult),
                                 reads=[b_qtm, C.b_const], writes=[b_rscr])
                            P.op("vector", lambda e, r1=r1, r2=r2, ro=ro:
                                 e.tensor_tensor(ro[:, :, 0, :], r1, r2, ALU.subtract),
                                 reads=[b_rscr], writes=[b_rout])
                            P.op("vector", lambda e, x1=x1, sbx=sb_, r1=r1:
                                 e.tensor_tensor(r1, x1, sbx, ALU.mult),
                                 reads=[b_qtm, C.b_const], writes=[b_rscr])
                            P.op("vector", lambda e, x2=x2, cb=cb, r2=r2:
                                 e.tensor_tensor(r2, x2, cb, ALU.mult),
                                 reads=[b_qtm, C.b_const], writes=[b_rscr])
                            P.op("vector", lambda e, r1=r1, r2=r2, ro=ro:
                                 e.tensor_tensor(ro[:, :, 1, :], r1, r2, ALU.add),
                                 reads=[b_rscr], writes=[b_rout])
                            P.op("vector", lambda e, i=i, sq=sq, qtm=qtm: e.tensor_tensor(
                                sq[:], qtm[:, i, :], qtm[:, i, :], ALU.mult),
                                reads=[b_qtm], writes=[b_sq])
                            P.op("vector", lambda e, sq=sq, nrmt=nrmt: e.reduce_sum(
                                nrmt[:], sq[:].rearrange("p (h d) -> p h d", h=16), AX.X),
                                reads=[b_sq], writes=[b_nrmt])
                            P.op("vector", lambda e, off=off, nrmt=nrmt: e.tensor_tensor(
                                nrm[:, off:off + 16], nrm[:, off:off + 16],
                                nrmt[:], ALU.max),
                                reads=[b_nrm, b_nrmt], writes=[b_nrm])
                            for c in range(8):
                                transpose128(C, rout[:, c * 128:(c + 1) * 128],
                                             dstfm[:, c, (j * 4 + i) * 128:(j * 4 + i + 1) * 128],
                                             b_rout, b_dst2, BF16)
                negc_calc(C, nrm, b_nrm, ncs, b_ncs, ncrow, b_ncrow, negc, b_negc)
                if b == 0:
                    tap(C, "l1_qfm", qfm[:], b_q)
                    tap(C, "l1_kfm", kfm[:], b_k)
                    tap(C, "l1_vtm", vtm[:], b_v)
                    tap(C, "l1_negc", negc[:], b_negc)
            P.barrier()
            with ExitStack() as sB:
                EB = sB.enter_context
                wo = EB(nc.sbuf_tensor(_un("l1b_wo"), [128, 8, 1024], BF16))
                b_wo = Buf()
                P.op("gpsimd", lambda e: e.dma_start(
                    out=wo[:], in_=C.dram["wo"].rearrange("(c p) o -> p c o", p=128)),
                    writes=[b_wo])
                Pb = EB(nc.sbuf_tensor(_un("l1b_pb"), [128, 2, 2, 512], BF16))
                attnst = EB(nc.sbuf_tensor(_un("l1b_attnst"), [128, 8, 512], BF16))
                denb = EB(nc.sbuf_tensor(_un("l1b_denb"), [128, 512], F32))
                rb = EB(nc.sbuf_tensor(_un("l1b_rb"), [128, 512], F32))
                wost = EB(nc.sbuf_tensor(_un("l1b_wost"), [128, 512], F32))
                otm = EB(nc.sbuf_tensor(_un("l1b_otm"), [128, 4, 1024], F32))
                zt = EB(nc.sbuf_tensor(_un("l1b_zt"), [128, 1024], F32))
                xtB = EB(nc.sbuf_tensor(_un("l1b_xt"), [128, 1024], F32))
                sqB = EB(nc.sbuf_tensor(_un("l1b_sq"), [128, 1024], F32))
                statB = EB(nc.sbuf_tensor(_un("l1b_stat"), [128, 8], F32))
                b_Pb = [Buf() for _ in range(4)]
                b_att, b_den, b_rb, b_wost = (Buf() for _ in range(4))
                b_otm, b_zt, b_xtB, b_sqB, b_statB = (Buf() for _ in range(5))
                C.tap_stage = (wost, b_wost)
                C.tap_stage2 = (zt, b_zt)
                for qs in range(4):
                    attn_phaseB(C, qs, 4 * qs + 4,
                                lambda st, qs=qs: max(0, (st & ~1) - 4 * qs) * 128,
                                False, qfm, b_q, kfm, b_k, vtm, b_v,
                                negc, b_negc, attnst, b_att, Pb, b_Pb,
                                denb, b_den, rb, b_rb)
                    if b == 0 and qs == 0:
                        tap(C, "l1_attnst0", attnst[:], b_att)
                    for oc in range(8):
                        bank, bbk = C.pf[2 + (oc % 2)], C.b_pf[2 + (oc % 2)]
                        for kc in range(8):
                            P.op("tensor", lambda e, bank=bank, kc=kc, oc=oc, wo=wo, attnst=attnst:
                                 e.matmul(bank[:], wo[:, kc, oc * 128:(oc + 1) * 128],
                                          attnst[:, kc, :], start=(kc == 0), stop=(kc == 7)),
                                 reads=[b_wo, b_att], writes=[bbk], inc=(kc == 7))
                        P.op("scalar", lambda e, bank=bank, wost=wost: e.activation(
                            wost[:], bank[:], AF.Identity),
                            reads=[bbk], writes=[b_wost])
                        for i in range(4):
                            tb, btb = C.pf[4 + (i % 2)], C.b_pf[4 + (i % 2)]
                            P.op("tensor", lambda e, tb=tb, i=i, wost=wost: e.transpose(
                                tb[0:128, 0:128], wost[:, i * 128:(i + 1) * 128],
                                C.idf_sb[:]),
                                reads=[b_wost, C.b_const], writes=[btb])
                            P.op("vector", lambda e, tb=tb, i=i, oc=oc, otm=otm: e.tensor_copy(
                                otm[:, i, oc * 128:(oc + 1) * 128], tb[0:128, 0:128]),
                                reads=[btb], writes=[b_otm])
                    for i in range(4):
                        tile = b * 16 + qs * 4 + i
                        ln_tile(C, otm[:, i, :], zt[:], C.epsc[:, 0:1], b_otm, b_zt,
                                sqB[:], b_sqB, statB[:], b_statB)
                        P.op("sync", lambda e, t=tile, xtB=xtB: e.dma_start(
                            out=xtB[:], in_=src[t * 128:(t + 1) * 128, :]),
                            reads=[b_src[tile]], writes=[b_xtB])
                        P.op("vector", lambda e, zt=zt, xtB=xtB: e.tensor_tensor(xtB[:], zt[:], xtB[:], ALU.add),
                             reads=[b_zt, b_xtB], writes=[b_xtB])
                        P.op("sync", lambda e, t=tile, xtB=xtB: e.dma_start(
                            out=dst[t * 128:(t + 1) * 128, :], in_=xtB[:]),
                            reads=[b_xtB], writes=[b_dst[tile]])
            P.barrier()
    P.barrier()


def build_program(blocks=("l0a", "l0f", "l1a", "l1f"), want_taps=None, paranoia=0, **cflags):
    nc = bass.Bass("TRN2", target_bir_lowering=False)
    C = Ctx()
    C.nc = nc
    C.want_taps = want_taps
    C.paranoia = paranoia
    for k_, v_ in cflags.items():
        setattr(C, k_, v_)
    C.dram = {}

    def din(name, shape, dtype=F32):
        C.dram[name] = nc.dram_tensor(name, list(shape), dtype, kind="ExternalInput")
        return C.dram[name]

    din("x", [NB * T, D])
    dout = nc.dram_tensor("out", [NB * T, D], F32, kind="ExternalOutput")
    C.dram["out"] = dout

    din("idb", [128, 128], BF16)
    din("idf", [128, 128], F32)
    din("onesf", [128, 128], F32)
    din("onescb", [128, 1], BF16)
    din("epsc", [128, 2], F32)
    if "l0a" in blocks or "l1a" in blocks:
        din("eh", [128, 2], F32)
        din("ehb", [128, 2], BF16)
    if "l0a" in blocks:
        din("tri", [128, 128], BF16)
        din("w_attnT", [D, 3 * D])
        din("b_attn_fm", [128, 24])
        din("w_projT", [D, D])
        din("b_proj_fm", [128, 8])
        din("w_fcT", [D, 4 * D])
        din("b_fc_fm", [128, 32])
        din("w_mlpT", [4 * D, D])
        din("b_mlp_fm", [128, 8])
    if "l0f" in blocks:
        din("wg", [D, 8])
        din("bg_col", [8, 1])
        din("moe_w1", [8, D, 2 * D])
        din("b1_fm", [128, 8, 16])
        din("moe_w2", [8, 2 * D, D])
        din("b2_mat", [8, D])
        din("gamma", [1, D])
        din("beta", [1, D])
    if "l1a" in blocks:
        din("ropecos", [T, 32])
        din("ropesin", [T, 32])
        din("wqP", [D, D])
        din("w_down", [D, 512])
        din("wk_upP", [512, D])
        din("wv_up", [512, D])
        din("wo", [D, D])
    if "l1f" in blocks:
        din("l1f_w", [D, D])
        din("l1f_b_fm", [128, 8])

    need_xres = len(blocks) > 1
    if need_xres:
        xres = nc.dram_tensor("xres", [NB * T, D], F32, kind="Internal")
    if "l0a" in blocks:
        u1d = nc.dram_tensor("u1d", [NB * T, D], F32, kind="Internal")
        C.u1d = u1d
        C.b_u1d = [Buf() for _ in range(NTILES)]
    if "l0f" in blocks:
        u3fmd = nc.dram_tensor("u3fmd", [NSUP, 128, 8 * 512], BF16, kind="Internal")
        gd = nc.dram_tensor("gd", [8, NB * T], F32, kind="Internal")
        C.u3fmd, C.gd = u3fmd, gd
        C.b_u3fmd = [Buf() for _ in range(NSUP)]
        C.b_gd = [Buf() for _ in range(NSUP)]

    with ExitStack() as ctx:
        E = ctx.enter_context
        P = Prog(nc)
        C.P = P
        sems = {}
        for e in ENGINES:
            if e in DMA_ENG:
                for r in range(NLANE):
                    sems[(e, r)] = E(nc.semaphore(f"sem_{e}{r}"))
            else:
                sems[(e, 0)] = E(nc.semaphore("sem_" + e))
        bar_sem = E(nc.semaphore("sem_bar"))

        C.idb_sb = E(nc.sbuf_tensor("idb_sb", [128, 128], BF16))
        C.idf_sb = E(nc.sbuf_tensor("idf_sb", [128, 128], F32))
        C.onesf_sb = E(nc.sbuf_tensor("onesf_sb", [128, 128], F32))
        C.onescb_sb = E(nc.sbuf_tensor("onescb_sb", [128, 1], BF16))
        C.epsc = E(nc.sbuf_tensor("epsc_sb", [128, 2], F32))
        C.b_const = Buf("const")
        loads = [(C.idb_sb, "idb"), (C.idf_sb, "idf"), (C.onesf_sb, "onesf"),
                 (C.onescb_sb, "onescb"), (C.epsc, "epsc")]
        C.bias_sb = {}
        if "l0a" in blocks or "l1a" in blocks:
            C.eh_sb = E(nc.sbuf_tensor("eh_sb", [128, 2], F32))
            loads.append((C.eh_sb, "eh"))
            C.ehb_sb = E(nc.sbuf_tensor("ehb_sb", [128, 2], BF16))
            loads.append((C.ehb_sb, "ehb"))
        if "l0a" in blocks:
            C.tri_sb = E(nc.sbuf_tensor("tri_sb", [128, 128], BF16))
            loads.append((C.tri_sb, "tri"))
            for nm, sh in [("b_attn_fm", [128, 24]), ("b_proj_fm", [128, 8]),
                           ("b_fc_fm", [128, 32]), ("b_mlp_fm", [128, 8])]:
                C.bias_sb[nm] = E(nc.sbuf_tensor(nm + "_sb", sh, F32))
                loads.append((C.bias_sb[nm], nm))
        if "l0f" in blocks:
            for nm, sh in [("bg_col", [8, 1]), ("b1_fm", [128, 8, 16])]:
                C.bias_sb[nm] = E(nc.sbuf_tensor(nm + "_sb", sh, F32))
                loads.append((C.bias_sb[nm], nm))
        if "l1f" in blocks:
            C.bias_sb["l1f_b"] = E(nc.sbuf_tensor("l1f_b_sb", [128, 8], F32))
            loads.append((C.bias_sb["l1f_b"], "l1f_b_fm"))

        C.pf = [E(nc.psum_tensor(f"pf{i}", [128, 512], F32)) for i in range(8)]
        C.b_pf = [Buf(f"pf{i}") for i in range(8)]
        C.pbf = [C.pf[6], C.pf[7]]
        C.b_pbf = [C.b_pf[6], C.b_pf[7]]
        C._pf_rot = 0
        C._pbf_rot = 0

        def next_pf():
            i = C._pf_rot % len(C.pf)
            C._pf_rot += 1
            return C.pf[i], C.b_pf[i]

        def next_pbf():
            i = C._pbf_rot % len(C.pbf)
            C._pbf_rot += 1
            return C.pbf[i], C.b_pbf[i]

        C.next_pf = next_pf
        C.next_pbf = next_pbf

        blk = E(nc.Block())

        for sb, dr_name in loads:
            P.op("sync", lambda e, sb=sb, d=C.dram[dr_name]: e.dma_start(
                out=sb[:], in_=d.ap()), writes=[C.b_const])

        def dram_bufs(n):
            return [Buf() for _ in range(n)]

        b_x = dram_bufs(NTILES)
        b_out = dram_bufs(NTILES)
        b_xres = dram_bufs(NTILES) if need_xres else None

        seq = []
        for bi, name in enumerate(blocks):
            src = C.dram["x"].ap() if bi == 0 else xres.ap()
            bsrc = b_x if bi == 0 else b_xres
            last = bi == len(blocks) - 1
            dstap = dout.ap() if last else xres.ap()
            bdst = b_out if last else b_xres
            seq.append((name, src, bsrc, dstap, bdst))

        for name, src, bsrc, dstap, bdst in seq:
            if name == "l1f":
                emit_l1ffn(C, src, bsrc, dstap, bdst)
            elif name == "l0a":
                emit_l0attn(C, src, bsrc, dstap, bdst)
            elif name == "l0f":
                emit_l0ffn(C, src, bsrc, dstap, bdst)
            elif name == "l1a":
                emit_l1attn(C, src, bsrc, dstap, bdst)

        P.emit(blk, sems, bar_sem)
    return nc


def _bias_fm(b):
    nck = b.shape[0] // 128
    return np.ascontiguousarray(b.reshape(nck, 128).T).astype(np.float32)


def make_weight_arrays(inputs, blocks=("l0a", "l0f", "l1a", "l1f")):
    g = {}
    g["idb"] = np.eye(128, dtype=ml_dtypes.bfloat16)
    g["idf"] = np.eye(128, dtype=np.float32)
    g["onesf"] = np.ones((128, 128), dtype=np.float32)
    g["onescb"] = np.ones((128, 1), dtype=ml_dtypes.bfloat16)
    g["epsc"] = np.broadcast_to(np.array([1e-5, 1e-10], dtype=np.float32), (128, 2)).copy()
    if "l0a" in blocks or "l1a" in blocks:
        eh = np.zeros((128, 2), dtype=np.float32)
        eh[0:64, 0] = 1.0
        eh[64:128, 1] = 1.0
        g["eh"] = eh
        g["ehb"] = eh.astype(ml_dtypes.bfloat16)
    if "l0a" in blocks:
        s_le_q = (np.arange(128)[:, None] <= np.arange(128)[None, :])
        g["tri"] = s_le_q.astype(ml_dtypes.bfloat16)
        g["w_attnT"] = np.ascontiguousarray(inputs["L0_attn_w_attn"].T)
        g["b_attn_fm"] = _bias_fm(inputs["L0_attn_b_attn"])
        g["w_projT"] = np.ascontiguousarray(inputs["L0_attn_w_proj"].T)
        g["b_proj_fm"] = _bias_fm(inputs["L0_attn_b_proj"])
        g["w_fcT"] = np.ascontiguousarray(inputs["L0_attn_w_fc"].T)
        g["b_fc_fm"] = _bias_fm(inputs["L0_attn_b_fc"])
        g["w_mlpT"] = np.ascontiguousarray(inputs["L0_attn_w_mlp_proj"].T)
        g["b_mlp_fm"] = _bias_fm(inputs["L0_attn_b_mlp_proj"])
    if "l0f" in blocks:
        g["wg"] = np.ascontiguousarray(inputs["L0_ffn_Wg"])
        g["bg_col"] = np.ascontiguousarray(inputs["L0_ffn_bg"].reshape(8, 1))
        g["moe_w1"] = np.ascontiguousarray(inputs["L0_ffn_W1"])
        g["b1_fm"] = np.ascontiguousarray(
            inputs["L0_ffn_b1"].reshape(8, 16, 128).transpose(2, 0, 1))
        g["moe_w2"] = np.ascontiguousarray(inputs["L0_ffn_W2"])
        g["b2_mat"] = np.ascontiguousarray(inputs["L0_ffn_b2"])
        g["gamma"] = np.ascontiguousarray(inputs["L0_ffn_gamma"].reshape(1, D))
        g["beta"] = np.ascontiguousarray(inputs["L0_ffn_beta"].reshape(1, D))
    if "l1a" in blocks:
        inv = 1.0 / 10000.0 ** (np.arange(0, 64, 2, dtype=np.float32) / 64)
        ang = np.outer(np.arange(T, dtype=np.float32), inv)
        g["ropecos"] = np.cos(ang).astype(np.float32)
        g["ropesin"] = np.sin(ang).astype(np.float32)
        perm = np.concatenate([
            np.concatenate([h * 64 + 2 * np.arange(32), h * 64 + 2 * np.arange(32) + 1])
            for h in range(16)])
        g["wqP"] = np.ascontiguousarray(inputs["L1_attn_wq"][:, perm])
        g["w_down"] = np.ascontiguousarray(inputs["L1_attn_w_down"])
        g["wk_upP"] = np.ascontiguousarray(inputs["L1_attn_wk_up"][:, perm])
        g["wv_up"] = np.ascontiguousarray(inputs["L1_attn_wv_up"])
        g["wo"] = np.ascontiguousarray(inputs["L1_attn_wo"])
    if "l1f" in blocks:
        g["l1f_w"] = np.ascontiguousarray(inputs["L1_ffn_weight"])
        g["l1f_b_fm"] = _bias_fm(inputs["L1_ffn_bias"])
    return g


FUSED = True


def kernel(**inputs):
    blocks = ("l0a", "l0f", "l1a", "l1f")
    x = np.asarray(inputs["L0_attn_x"], dtype=np.float32).reshape(32 * T, D)
    if FUSED:
        g = make_weight_arrays(inputs, blocks)
        nc = build_program(blocks)
        in_maps = []
        for c in range(8):
            m = dict(g)
            m["x"] = np.ascontiguousarray(x[c * NB * T:(c + 1) * NB * T])
            in_maps.append(m)
        res = run_bass_kernel_spmd(nc, in_maps, core_ids=list(range(8)))
        out = np.concatenate([r["out"] for r in res.results], axis=0)
        return out.reshape(32, T, D)
    cur = x
    for bname in blocks:
        g = make_weight_arrays(inputs, (bname,))
        nc = build_program((bname,))
        in_maps = []
        for c in range(8):
            m = dict(g)
            m["x"] = np.ascontiguousarray(cur[c * NB * T:(c + 1) * NB * T])
            in_maps.append(m)
        res = run_bass_kernel_spmd(nc, in_maps, core_ids=list(range(8)))
        cur = np.concatenate([r["out"] for r in res.results], axis=0)
    return cur.reshape(32, T, D)
